# version 02
# baseline (speedup 1.0000x reference)
import math
import numpy as np
from contextlib import ExitStack

import concourse.bass as bass
import concourse.mybir as mybir
from concourse.bass_utils import run_bass_kernel_spmd

F32 = mybir.dt.float32
BF16 = mybir.dt.bfloat16
FP8 = mybir.dt.float8e4
I32 = mybir.dt.int32
U32 = mybir.dt.uint32
AF = mybir.ActivationFunctionType
ALU = mybir.AluOpType
AXL = mybir.AxisListType

B, T, D, NCORE = 2, 8192, 1024, 8
TPC = 2048
NH, HD, MF = 16, 64, 256
KSEL = 15
LN_EPS = 1e-5
PEPS = 1e-6
DATA_NORM = HD ** -0.25
NEG = -1.0e38
ENGS = ("sync", "vector", "scalar", "tensor", "gpsimd")
NT = TPC // 128


def tok_blocks(c):
    return list(range(c % 4, 64, 4))


def tok_index(c):
    return np.concatenate([np.arange(128) + 128 * bb for bb in tok_blocks(c)])


class Prog:

    def __init__(self, nc):
        self.nc = nc
        self.q = {e: [] for e in ENGS}
        self.cnt = {}

    def mksem(self, ctx, name):
        h = ctx.enter_context(self.nc.semaphore(name))
        self.cnt[name] = 0
        return h

    def op(self, eng, build, waits=(), inc=None, amt=1, dma=False):
        waits = tuple(w for w in waits if w is not None)
        tok = None
        if inc is not None:
            self.cnt[inc.name] += amt
            tok = (inc, self.cnt[inc.name])
        need_drain = eng in ("vector", "scalar", "gpsimd", "tensor") and not dma

        def f(e, build=build, waits=waits, inc=inc, amt=amt, need_drain=need_drain):
            for (s, v) in waits:
                e.wait_ge(s, v)
            ins = build(e)
            if need_drain:
                ins = e.drain()
            if inc is not None:
                ins.then_inc(inc, amt)

        self.q[eng].append(f)
        return tok

    def barrier(self, sems):
        toks = [(s, self.cnt[s.name]) for s in sems]
        for e in ENGS:
            self.fence(e, toks)

    def fence(self, eng, waits):
        waits = tuple(w for w in waits if w is not None)

        def f(e, waits=waits):
            for (s, v) in waits:
                e.wait_ge(s, v)

        self.q[eng].append(f)

    def setup_consts(self, ctx, values, sem):
        E = ctx.enter_context
        vals = sorted(set(float(v) for v in values))
        tile = E(self.nc.sbuf_tensor("consts", [128, max(1, len(vals))], F32))
        for i, v in enumerate(vals):
            self.nc.const_aps.register(F32, v, tile[:, i:i + 1])

        def f(e, vals=vals, tile=tile):
            ins = None
            for i, v in enumerate(vals):
                ins = e.memset(tile[:, i:i + 1], v)
            ins.then_inc(sem, 1)
        self.cnt[sem.name] += 1
        tok = (sem, self.cnt[sem.name])
        self.q["gpsimd"].append(f)
        for e in ENGS:
            if e != "gpsimd":
                self.fence(e, [tok])
        return tok

    def emit(self, block):
        for e in ENGS:
            fns = self.q[e]

            def runner(eng, fns=fns):
                for f in fns:
                    f(eng)

            getattr(block, e)(runner)


def build_A(hp_count=8, ln_only=False, stage='kv'):
    nc = bass.Bass("TRN2", target_bir_lowering=False)
    xa = nc.dram_tensor("xa", [TPC, D], F32, kind="ExternalInput").ap()
    wqkv = nc.dram_tensor("wqkv", [D, 3 * D], F32, kind="ExternalInput").ap()
    bqkv = nc.dram_tensor("bqkv", [128, 24], F32, kind="ExternalInput").ap()
    proj = nc.dram_tensor("proj", [128, 8 * 2 * MF], F32, kind="ExternalInput").ap()
    ident = nc.dram_tensor("ident", [128, 128], F32, kind="ExternalInput").ap()

    o_xn1 = nc.dram_tensor("o_xn1", [TPC, D], F32, kind="ExternalOutput").ap()
    o_qp = nc.dram_tensor("o_qp", [NH, TPC, MF], BF16, kind="ExternalOutput").ap()
    o_kv = nc.dram_tensor("o_kv", [NH, 2, 128, 65], F32, kind="ExternalOutput").ap()
    o_amax = nc.dram_tensor("o_amax", [1, 1], F32, kind="ExternalOutput").ap()

    with ExitStack() as ctx:
        E = ctx.enter_context
        sx = [E(nc.sbuf_tensor(f"sx{j}", [128, D], F32)) for j in range(2)]
        sxn = [E(nc.sbuf_tensor(f"sxn{j}", [128, D], F32)) for j in range(2)]
        ssq = E(nc.sbuf_tensor("ssq", [128, D], F32))
        sstat = [E(nc.sbuf_tensor(f"sstat{j}", [128, 8], F32)) for j in range(2)]
        samax = E(nc.sbuf_tensor("samax", [128, 1], F32))
        samax_r = E(nc.sbuf_tensor("samax_r", [1, 128], F32))
        samax_f = E(nc.sbuf_tensor("samax_f", [1, 1], F32))
        sident = E(nc.sbuf_tensor("sident", [128, 128], F32))
        sident_b = E(nc.sbuf_tensor("sident_b", [128, 128], BF16))
        sxnT = E(nc.sbuf_tensor("sxnT", [128, 8 * TPC], BF16))
        swq = E(nc.sbuf_tensor("swq", [128, 8 * 3 * D], BF16))
        sbias = E(nc.sbuf_tensor("sbias", [128, 24], F32))
        sproj = E(nc.sbuf_tensor("sproj", [128, 8 * 2 * MF], BF16))
        sproj0 = E(nc.sbuf_tensor("sproj0", [128, 8 * 2 * MF], BF16))
        sq = E(nc.sbuf_tensor("sq", [128, TPC], BF16))
        sk = E(nc.sbuf_tensor("sk", [128, TPC], BF16))
        sv = E(nc.sbuf_tensor("svv", [128, TPC], BF16))
        sexp = E(nc.sbuf_tensor("sexp", [128, 512], F32))
        snmax = E(nc.sbuf_tensor("snmax", [128, 4], F32))
        sqp = E(nc.sbuf_tensor("sqp", [128, 512], BF16))
        skp = E(nc.sbuf_tensor("skp", [128, 16 * 512], BF16))
        svt = E(nc.sbuf_tensor("svt", [128, 16 * 130], BF16))
        skv = E(nc.sbuf_tensor("skv", [128, 65], F32))

        p_mm = [E(nc.psum_tensor(f"p_mm{j}", [128, 512], F32)) for j in range(2)]
        p_xp = [E(nc.psum_tensor(f"p_xp{j}", [128, 512], F32)) for j in range(2)]
        p_vt = E(nc.psum_tensor("p_vt", [128, 64], BF16))
        p_kv = [E(nc.psum_tensor(f"p_kv{j}", [128, 65], F32)) for j in range(2)]
        p_tr = [p_xp[0][:, 0:128], p_xp[1][:, 0:128]]

        p = Prog(nc)
        ld = p.mksem(ctx, "ld")
        lda = [p.mksem(ctx, "lda0"), p.mksem(ctx, "lda1")]
        ldg = p.mksem(ctx, "ldg")
        sv_ = p.mksem(ctx, "sv_")
        ss = p.mksem(ctx, "ss")
        st_ = p.mksem(ctx, "st_")
        sg = p.mksem(ctx, "sg")
        sgx = [p.mksem(ctx, "sgx0"), p.mksem(ctx, "sgx1")]
        sgq = p.mksem(ctx, "sgq")
        sgk = p.mksem(ctx, "sgk")
        block = E(nc.Block())
        p.setup_consts(ctx, [0.0, 1.0, -0.5, -1.0, LN_EPS, 1.0 / D, DATA_NORM, 1.0 / 16.0, PEPS / 16.0], sg)

        t_ident = p.op("sync", lambda e: e.dma_start(out=sident[:], in_=ident), inc=ld, amt=16)
        t_bias = p.op("sync", lambda e: e.dma_start(out=sbias[:], in_=bqkv), inc=ld, amt=16)
        t_ident = t_bias = (ld, 32)
        t_wq = []
        for k in range(8):
            t_wq.append(p.op("gpsimd",
                             lambda e, k=k: e.dma_start(out=swq[:, k * 3 * D:(k + 1) * 3 * D],
                                                        in_=wqkv[k * 128:(k + 1) * 128, :]),
                             inc=ldg, amt=16, dma=True))
        t_proj = p.op("gpsimd", lambda e: e.dma_start(out=sproj0[:, :], in_=proj), inc=ldg, amt=16, dma=True)
        t_identb = p.op("vector", lambda e: e.tensor_copy(out=sident_b[:], in_=sident[:]),
                        waits=[t_ident], inc=sv_)
        t_ldg_all = (ldg, p.cnt["ldg"])
        t_projs = p.op("vector",
                       lambda e: e.tensor_scalar(out=sproj[:], in0=sproj0[:], scalar1=DATA_NORM,
                                                 scalar2=None, op0=ALU.mult),
                       waits=[t_ldg_all], inc=sv_)
        t_ones = p.op("gpsimd", lambda e: e.memset(svt[:], 1.0), inc=sg, amt=16)

        xn_done = [None] * 16
        mean_done = [None] * 16
        ev_done = [None] * 16
        out_done = [None] * 16
        am_prev = None
        trbank_prev = {0: None, 1: None}
        for i in range(16):
            j = i % 2
            tl = p.op("sync", lambda e, i=i, j=j: e.dma_start(out=sx[j][:], in_=xa[i * 128:(i + 1) * 128, :]),
                      waits=([xn_done[i - 2], mean_done[i - 2]] if i >= 2 else []), inc=lda[j], amt=16)
            p.op("vector", lambda e, j=j: e.reduce_sum(out=sstat[j][:, 0:1], in_=sx[j][:], axis=AXL.X),
                 waits=[tl] + ([xn_done[i - 2]] if i >= 2 else []), inc=sv_)
            mean_done[i] = p.op("vector", lambda e, j=j: e.tensor_scalar(
                out=sstat[j][:, 0:1], in0=sstat[j][:, 0:1], scalar1=1.0 / D, scalar2=None, op0=ALU.mult),
                inc=sv_)
            t_sq = p.op("scalar", lambda e, j=j: e.activation(out=ssq[:], in_=sx[j][:], func=AF.Square,
                                                              accum_out=sstat[j][:, 1:2]),
                        waits=[tl, mean_done[i]], inc=ss)
            p.op("vector", lambda e, j=j: e.tensor_scalar(out=sstat[j][:, 2:3], in0=sstat[j][:, 1:2],
                                                          scalar1=1.0 / D, scalar2=None, op0=ALU.mult),
                 waits=[t_sq], inc=sv_)
            p.op("vector", lambda e, j=j: e.tensor_tensor(out=sstat[j][:, 3:4], in0=sstat[j][:, 0:1],
                                                          in1=sstat[j][:, 0:1], op=ALU.mult), inc=sv_)
            t_var = p.op("vector", lambda e, j=j: e.tensor_tensor(out=sstat[j][:, 2:3], in0=sstat[j][:, 2:3],
                                                                  in1=sstat[j][:, 3:4], op=ALU.subtract), inc=sv_)
            p.op("scalar", lambda e, j=j: e.activation(out=sstat[j][:, 4:5], in_=sstat[j][:, 2:3],
                                                       func=AF.Ln, bias=LN_EPS, scale=1.0),
                 waits=[t_var], inc=ss)
            t_rstd = p.op("scalar", lambda e, j=j: e.activation(out=sstat[j][:, 4:5], in_=sstat[j][:, 4:5],
                                                                func=AF.Exp, scale=-0.5), inc=ss)
            p.op("vector", lambda e, j=j: e.tensor_tensor(out=sstat[j][:, 5:6], in0=sstat[j][:, 0:1],
                                                          in1=sstat[j][:, 4:5], op=ALU.mult),
                 waits=[t_rstd], inc=sv_)
            t_bneg = p.op("vector", lambda e, j=j: e.tensor_scalar(out=sstat[j][:, 5:6], in0=sstat[j][:, 5:6],
                                                                   scalar1=-1.0, scalar2=None, op0=ALU.mult),
                          inc=sv_)
            xn_done[i] = p.op("vector", lambda e, j=j: e.tensor_scalar(
                out=sxn[j][:], in0=sx[j][:], scalar1=sstat[j][:, 4:5], scalar2=sstat[j][:, 5:6],
                op0=ALU.mult, op1=ALU.add),
                waits=[t_bneg] + ([ev_done[i - 2], out_done[i - 2]] if i >= 2 else []), inc=sv_)
            p.op("vector", lambda e, j=j: e.tensor_reduce(out=sstat[j][:, 6:7], in_=sxn[j][:], axis=AXL.X,
                                                          op=ALU.max, apply_absolute_value=True),
                 waits=[xn_done[i]], inc=sv_)
            if i == 0:
                am_prev = p.op("vector", lambda e, j=j: e.tensor_copy(out=samax[:], in_=sstat[j][:, 6:7]),
                               inc=sv_)
            else:
                am_prev = p.op("vector", lambda e, j=j: e.tensor_tensor(out=samax[:], in0=samax[:],
                                                                        in1=sstat[j][:, 6:7], op=ALU.max),
                               inc=sv_)
            out_done[i] = p.op("gpsimd", lambda e, i=i, j=j: e.dma_start(
                out=o_xn1[i * 128:(i + 1) * 128, :], in_=sxn[j][:]),
                waits=[xn_done[i]], inc=sgx[j], amt=16, dma=True)
            ev_tok = {}
            for k in range(8):
                tb = k % 2
                wt = [xn_done[i], t_ident]
                if k >= 2:
                    wt.append(ev_tok[k - 2])
                else:
                    wt.append(trbank_prev[tb])
                tt = p.op("tensor", lambda e, j=j, k=k, tb=tb: e.transpose(
                    out=p_tr[tb], in_=sxn[j][:, k * 128:(k + 1) * 128], identity=sident[:]),
                    waits=wt, inc=st_)
                ev_tok[k] = p.op("vector", lambda e, i=i, k=k, tb=tb: e.tensor_copy(
                    out=sxnT[:, k * TPC + i * 128: k * TPC + (i + 1) * 128], in_=p_tr[tb]),
                    waits=[tt], inc=sv_)
            trbank_prev = {0: ev_tok[6], 1: ev_tok[7]}
            ev_done[i] = ev_tok[7]

        t_amT = p.op("tensor", lambda e: e.transpose(out=p_xp[0][0:1, 0:128], in_=samax[:],
                                                     identity=sident[:]),
                     waits=[am_prev, trbank_prev[0]], inc=st_)
        t_amE = p.op("vector", lambda e: e.tensor_copy(out=samax_r[:], in_=p_xp[0][0:1, 0:128]),
                     waits=[t_amT], inc=sv_)
        t_amR = p.op("vector", lambda e: e.tensor_reduce(out=samax_f[:], in_=samax_r[:], axis=AXL.X,
                                                         op=ALU.max), waits=[t_amE], inc=sv_)
        t_amO = p.op("gpsimd", lambda e: e.dma_start(out=o_amax, in_=samax_f[:]),
                     waits=[t_amR], inc=sg, amt=16, dma=True)

        if ln_only:
            p.fence("gpsimd", [(sg, p.cnt["sg"]), (sgx[0], p.cnt["sgx0"]), (sgx[1], p.cnt["sgx1"]),
                               t_amO, (sv_, p.cnt["sv_"]), (ss, p.cnt["ss"]), (st_, p.cnt["st_"])])
            p.emit(block)
            return nc
        all_xnT = ev_done[15]
        mm_prev = {0: None, 1: None}
        xp_prev = {0: t_amE, 1: None}
        vt_prev = None
        kv_prev = {0: None, 1: None}
        exp_cons = None
        nm_cons = None
        qp_cons = None
        kv_last_mm = None
        skv_dma_prev = None
        for hp in range(hp_count):
            slab_toks = {}
            for sl, (dst, col0) in enumerate(((sq, 128 * hp), (sk, D + 128 * hp), (sv, 2 * D + 128 * hp))):
                last_ev = None
                for ntc in range(4):
                    pb = (sl * 4 + ntc) % 2
                    t_mm = None
                    for kt in range(8):
                        wt = []
                        if kt == 0:
                            wt = [mm_prev[pb], all_xnT] + ([t_ldg_all, t_bias] if hp == 0 and sl == 0 and ntc == 0 else [])
                            if sl == 0 and ntc == 0:
                                wt.append(kv_last_mm)
                                wt.append(exp_cons)
                        t_mm = p.op("tensor", lambda e, kt=kt, col0=col0, ntc=ntc, pb=pb: e.matmul(
                            out=p_mm[pb][:],
                            lhsT=swq[:, kt * 3 * D + col0: kt * 3 * D + col0 + 128],
                            rhs=sxnT[:, kt * TPC + ntc * 512: kt * TPC + (ntc + 1) * 512],
                            start=(kt == 0), stop=(kt == 7)),
                            waits=wt, inc=(st_ if kt == 7 else None))
                    bias_col = col0 // 128
                    last_ev = p.op("vector", lambda e, dst=dst, ntc=ntc, pb=pb, bias_col=bias_col: e.tensor_scalar(
                        out=dst[:, ntc * 512:(ntc + 1) * 512], in0=p_mm[pb][:],
                        scalar1=sbias[:, bias_col:bias_col + 1], scalar2=None, op0=ALU.add),
                        waits=[t_mm], inc=sv_)
                    mm_prev[pb] = last_ev
                slab_toks[sl] = last_ev

            for ic in range(NT if stage != 'slab' else 0):
                for h2 in range(2):
                    wt = [slab_toks[2], t_ones, vt_prev, t_identb]
                    ttv = p.op("tensor", lambda e, ic=ic, h2=h2: e.transpose(
                        out=p_vt[:], in_=sv[h2 * 64:(h2 + 1) * 64, ic * 128:(ic + 1) * 128],
                        identity=sident_b[h2 * 64:(h2 + 1) * 64, h2 * 64:(h2 + 1) * 64]),
                        waits=wt, inc=st_)
                    vt_prev = p.op("vector", lambda e, ic=ic, h2=h2: e.tensor_copy(
                        out=svt[:, ic * 130 + h2 * 65: ic * 130 + h2 * 65 + 64], in_=p_vt[0:128, 0:64]),
                        waits=[ttv, kv_last_mm if ic == 0 and h2 == 0 else None], inc=sv_)
            vt_all = vt_prev

            kp_last = None
            for src, is_q in (((sq, True), (sk, False)) if stage in ('fm', 'kv') else ()):
                for ic in range(NT):
                    pb = ic % 2
                    mm2 = p.op("tensor", lambda e, src=src, ic=ic, pb=pb, hp=hp: e.matmul(
                        out=p_xp[pb][:], lhsT=src[:, ic * 128:(ic + 1) * 128],
                        rhs=sproj[:, hp * 512:(hp + 1) * 512], start=True, stop=True),
                        waits=[slab_toks[0] if is_q else slab_toks[1], t_projs, xp_prev[pb]], inc=st_)
                    t_nm = p.op("vector", lambda e, pb=pb: e.tensor_reduce(
                        out=snmax[:, 0:2], in_=p_xp[pb][:].rearrange("p (h m) -> p h m", h=2), axis=AXL.X,
                        op=ALU.max, negate=True),
                        waits=[mm2, nm_cons], inc=sv_)
                    e1 = p.op("scalar", lambda e, pb=pb: e.activation(
                        out=sexp[:, 0:256], in_=p_xp[pb][:, 0:256], func=AF.Exp,
                        bias=snmax[:, 0:1], scale=1.0), waits=[t_nm, exp_cons], inc=ss)
                    e2 = p.op("scalar", lambda e, pb=pb: e.activation(
                        out=sexp[:, 256:512], in_=p_xp[pb][:, 256:512], func=AF.Exp,
                        bias=snmax[:, 1:2], scale=1.0), inc=ss)
                    xp_prev[pb] = e2
                    if is_q:
                        t_qp = p.op("vector", lambda e: e.tensor_scalar(
                            out=sqp[:], in0=sexp[:], scalar1=1.0 / 16.0, scalar2=PEPS / 16.0,
                            op0=ALU.mult, op1=ALU.add), waits=[e2, qp_cons], inc=sv_)
                        exp_cons = t_qp
                        nm_cons = e2
                        qp_cons = p.op("gpsimd", lambda e, ic=ic, hp=hp: e.dma_start(
                            out=o_qp.rearrange("h t m -> t h m")[ic * 128:(ic + 1) * 128,
                                                                 2 * hp:2 * hp + 2, :],
                            in_=sqp[:].rearrange("p (h m) -> p h m", h=2)),
                            waits=[t_qp], inc=sgq, amt=16, dma=True)
                    else:
                        kp_last = p.op("vector", lambda e, ic=ic: e.tensor_scalar(
                            out=skp[:, ic * 512:(ic + 1) * 512], in0=sexp[:], scalar1=1.0 / 16.0,
                            scalar2=PEPS / 16.0, op0=ALU.mult, op1=ALU.add),
                            waits=[e2, kv_last_mm if ic == 0 else None], inc=sv_)
                        exp_cons = kp_last
                        nm_cons = e2

            for h2 in range(2 if stage == 'kv' else 0):
                head = 2 * hp + h2
                for mt in range(2):
                    t_mm = None
                    for ic in range(NT):
                        wt = [kp_last, vt_all, kv_prev[mt]] if ic == 0 else []
                        t_mm = p.op("tensor", lambda e, ic=ic, h2=h2, mt=mt: e.matmul(
                            out=p_kv[mt][:],
                            lhsT=skp[:, ic * 512 + h2 * 256 + mt * 128: ic * 512 + h2 * 256 + (mt + 1) * 128],
                            rhs=svt[:, ic * 130 + h2 * 65: ic * 130 + h2 * 65 + 65],
                            start=(ic == 0), stop=(ic == NT - 1)),
                            waits=wt, inc=(st_ if ic == NT - 1 else None))
                        kv_last_mm = t_mm if ic == NT - 1 else kv_last_mm
                    evk = p.op("vector", lambda e, mt=mt: e.tensor_copy(out=skv[:], in_=p_kv[mt][:]),
                               waits=[t_mm, skv_dma_prev], inc=sv_)
                    skv_dma_prev = p.op("gpsimd", lambda e, head=head, mt=mt: e.dma_start(
                        out=o_kv[head, mt, :, :], in_=skv[:]), waits=[evk], inc=sgk, amt=16, dma=True)
                    kv_prev[mt] = skv_dma_prev

        p.fence("gpsimd", [(sg, p.cnt["sg"]), (sgx[0], p.cnt["sgx0"]), (sgx[1], p.cnt["sgx1"]),
                           (sgq, p.cnt["sgq"]), (sgk, p.cnt["sgk"]), t_amO])
        p.emit(block)
    return nc



def build_B():
    nc = bass.Bass("TRN2", target_bir_lowering=False)
    xa = nc.dram_tensor("xa", [TPC, D], F32, kind="ExternalInput").ap()
    xn1T = nc.dram_tensor("xn1T", [D, TPC], F32, kind="ExternalInput").ap()
    qpT = nc.dram_tensor("qpT", [NH, MF, TPC], BF16, kind="ExternalInput").ap()
    kv4 = nc.dram_tensor("kv4", [128, NH * 2 * 4 * 65], F32, kind="ExternalInput").ap()
    amax8 = nc.dram_tensor("amax8", [1, 8], F32, kind="ExternalInput").ap()
    wffnT = nc.dram_tensor("wffnT", [D, D], F32, kind="ExternalInput").ap()
    bffn = nc.dram_tensor("bffn", [1, D], F32, kind="ExternalInput").ap()
    wout = nc.dram_tensor("wout", [D, D], F32, kind="ExternalInput").ap()
    bout = nc.dram_tensor("bout", [1, D], F32, kind="ExternalInput").ap()
    wip = nc.dram_tensor("wip", [D, 2 * D], F32, kind="ExternalInput").ap()
    gwT = nc.dram_tensor("gwT", [D, 8], F32, kind="ExternalInput").ap()
    wexp = nc.dram_tensor("wexp", [8, D, D], F32, kind="ExternalInput").ap()
    cosb = nc.dram_tensor("cosb", [TPC, 512], BF16, kind="ExternalInput").ap()
    sinb = nc.dram_tensor("sinb", [TPC, 512], BF16, kind="ExternalInput").ap()
    ident = nc.dram_tensor("ident", [128, 128], F32, kind="ExternalInput").ap()
    iota8 = nc.dram_tensor("iota8", [1, 8], F32, kind="ExternalInput").ap()

    o_x3p = nc.dram_tensor("o_x3p", [TPC, D], F32, kind="ExternalOutput").ap()
    o_d1 = nc.dram_tensor("o_d1", [128, D], F32, kind="ExternalOutput").ap()
    o_d2 = nc.dram_tensor("o_d2", [128, D], F32, kind="ExternalOutput").ap()
    o_d3 = nc.dram_tensor("o_d3", [128, TPC], F32, kind="ExternalOutput").ap()
    o_d0 = nc.dram_tensor("o_d0", [1, 16], F32, kind="ExternalOutput").ap()
    o_P = nc.dram_tensor("o_P", [TPC, D], BF16, kind="ExternalOutput").ap()
    o_Ph = nc.dram_tensor("o_Ph", [TPC, D], BF16, kind="ExternalOutput").ap()
    o_I = nc.dram_tensor("o_I", [TPC, D], BF16, kind="ExternalOutput").ap()
    o_Ih = nc.dram_tensor("o_Ih", [TPC, D], BF16, kind="ExternalOutput").ap()

    with ExitStack() as ctx:
        E = ctx.enter_context
        p = Prog(nc)
        ld1 = p.mksem(ctx, "ld1")
        ldg1 = p.mksem(ctx, "ldg1")
        ldx = [p.mksem(ctx, "ldx0"), p.mksem(ctx, "ldx1")]
        ldw = [p.mksem(ctx, "ldw0"), p.mksem(ctx, "ldw1")]
        lda = [p.mksem(ctx, "lda0"), p.mksem(ctx, "lda1")]
        sv_ = p.mksem(ctx, "sv_")
        ss = p.mksem(ctx, "ss")
        st_ = p.mksem(ctx, "st_")
        sgo = [p.mksem(ctx, "sgo0"), p.mksem(ctx, "sgo1")]
        block = E(nc.Block())
        p.setup_consts(ctx, [0.0, 1.0, -0.5, -1.0, LN_EPS, 1.0 / D, 448.0, 4.0, 0.5, 1e-12,
                             1000.0, 1e-24, -1000.0], sgo[0])

        sident = E(nc.sbuf_tensor("sident", [128, 128], F32))
        sident_b = E(nc.sbuf_tensor("sident_b", [128, 128], BF16))
        sones = E(nc.sbuf_tensor("sones", [1, 128], F32))
        ssc = E(nc.sbuf_tensor("ssc", [1, 16], F32))
        sscb = E(nc.sbuf_tensor("sscb", [128, 4], F32))
        sx2 = E(nc.sbuf_tensor("sx2", [128, NT * D], F32))
        sstat = E(nc.sbuf_tensor("sstat", [128, 8], F32))
        sbffn = E(nc.sbuf_tensor("sbffn", [128, D], F32))
        sbout = E(nc.sbuf_tensor("sbout", [128, D], F32))
        ssc_f = E(nc.sbuf_tensor("ssc_f", [8, TPC], F32))
        sgw32 = E(nc.sbuf_tensor("sgw32", [128, 64], F32))

        p_a = [E(nc.psum_tensor(f"p_a{j}", [128, 512], F32)) for j in range(4)]
        p_b = [E(nc.psum_tensor(f"p_b{j}", [128, 512], F32)) for j in range(2)]
        p_t = [E(nc.psum_tensor(f"p_t{j}", [128, 128], F32)) for j in range(2)]

        p.op("sync", lambda e: e.dma_start(out=sident[:], in_=ident), inc=ld1, amt=16)
        p.op("sync", lambda e: e.dma_start(out=ssc[0:1, 0:8], in_=amax8), inc=ld1, amt=16)
        p.op("sync", lambda e: e.dma_start(out=sbffn[:], in_=bffn.partition_broadcast(128)), inc=ld1, amt=16)
        p.op("sync", lambda e: e.dma_start(out=sbout[:], in_=bout.partition_broadcast(128)), inc=ld1, amt=16)
        t_ld1 = (ld1, p.cnt["ld1"])
        t_identb = p.op("vector", lambda e: e.tensor_copy(out=sident_b[:], in_=sident[:]),
                        waits=[t_ld1], inc=sv_)
        t_ones = p.op("gpsimd", lambda e: e.memset(sones[:], 1.0), inc=sv_)

        c12 = ctx.enter_context(ExitStack())
        sffo = c12.enter_context(nc.sbuf_tensor("sffo", [128, NT * D], BF16))
        with ExitStack() as c1:
            E1 = c1.enter_context
            sxf = [E1(nc.sbuf_tensor(f"sxf{j}", [128, TPC], F32)) for j in range(2)]
            swfb = [E1(nc.sbuf_tensor(f"swfb{j}", [128, D], F32)) for j in range(2)]
            sx8 = E1(nc.sbuf_tensor("sx8", [128, 8 * TPC], FP8))
            sw8 = E1(nc.sbuf_tensor("sw8", [128, 8 * D], FP8))
            t_am = p.op("vector", lambda e: e.tensor_reduce(out=ssc[0:1, 8:9], in_=ssc[0:1, 0:8],
                                                            axis=AXL.X, op=ALU.max),
                        waits=[t_ld1], inc=sv_)
            wl_prev = {0: None, 1: None}
            am_acc = None
            for k in range(8):
                j = k % 2
                t_wl = p.op("sync", lambda e, k=k, j=j: e.dma_start(
                    out=swfb[j][:], in_=wffnT[k * 128:(k + 1) * 128, :]),
                    waits=[wl_prev[j]], inc=lda[j], amt=16)
                t_r = p.op("vector", lambda e, j=j, k=k: e.tensor_reduce(
                    out=sstat[:, 1:2], in_=swfb[j][:], axis=AXL.X, op=ALU.max,
                    apply_absolute_value=True), waits=[t_wl], inc=sv_)
                if k == 0:
                    am_acc = p.op("vector", lambda e: e.tensor_copy(out=sstat[:, 0:1], in_=sstat[:, 1:2]),
                                  inc=sv_)
                else:
                    am_acc = p.op("vector", lambda e: e.tensor_tensor(
                        out=sstat[:, 0:1], in0=sstat[:, 0:1], in1=sstat[:, 1:2], op=ALU.max), inc=sv_)
                wl_prev[j] = am_acc
            t_wT = p.op("tensor", lambda e: e.transpose(out=p_t[0][0:1, 0:128], in_=sstat[:, 0:1],
                                                        identity=sident[:]), waits=[am_acc, t_ld1], inc=st_)
            t_wr3 = p.op("vector", lambda e: e.tensor_reduce(out=ssc[0:1, 9:10],
                                                             in_=p_t[0][0:1, 0:128], axis=AXL.X,
                                                             op=ALU.max), waits=[t_wT], inc=sv_)

            def scales(e):
                e.tensor_scalar(out=ssc[0:1, 10:11], in0=ssc[0:1, 8:9], scalar1=1e-12, scalar2=None,
                                op0=ALU.max)
                e.drain()
                e.reciprocal(out=ssc[0:1, 10:11], in_=ssc[0:1, 10:11])
                e.drain()
                e.tensor_scalar(out=ssc[0:1, 10:11], in0=ssc[0:1, 10:11], scalar1=448.0, scalar2=448.0,
                                op0=ALU.mult, op1=ALU.min)
                e.drain()
                e.tensor_scalar(out=ssc[0:1, 11:12], in0=ssc[0:1, 9:10], scalar1=1e-12, scalar2=None,
                                op0=ALU.max)
                e.drain()
                e.reciprocal(out=ssc[0:1, 11:12], in_=ssc[0:1, 11:12])
                e.drain()
                e.tensor_scalar(out=ssc[0:1, 11:12], in0=ssc[0:1, 11:12], scalar1=448.0, scalar2=448.0,
                                op0=ALU.mult, op1=ALU.min)
                e.drain()
                e.tensor_tensor(out=ssc[0:1, 12:13], in0=ssc[0:1, 10:11], in1=ssc[0:1, 11:12],
                                op=ALU.mult)
                e.drain()
                e.reciprocal(out=ssc[0:1, 12:13], in_=ssc[0:1, 12:13])
                e.drain()
                e.tensor_scalar(out=ssc[0:1, 12:13], in0=ssc[0:1, 12:13], scalar1=4.0, scalar2=None,
                                op0=ALU.mult)
                e.drain()
                e.tensor_scalar(out=ssc[0:1, 10:11], in0=ssc[0:1, 10:11], scalar1=0.5, scalar2=None,
                                op0=ALU.mult)
                e.drain()
                return e.tensor_scalar(out=ssc[0:1, 11:12], in0=ssc[0:1, 11:12], scalar1=0.5,
                                       scalar2=None, op0=ALU.mult)
            t_sc = p.op("vector", scales, waits=[t_am, t_wr3], inc=sv_)
            t_bc = []
            for i_src, i_dst in ((10, 0), (11, 1), (12, 2)):
                tm = p.op("tensor", lambda e, i_src=i_src: e.matmul(
                    out=p_t[1][:, 0:1], lhsT=sones[:], rhs=ssc[0:1, i_src:i_src + 1],
                    start=True, stop=True), waits=[t_sc, t_ones, t_bc[-1] if t_bc else None], inc=st_)
                t_bc.append(p.op("vector", lambda e, i_dst=i_dst: e.tensor_copy(
                    out=sscb[:, i_dst:i_dst + 1], in_=p_t[1][:, 0:1]), waits=[tm], inc=sv_))
            t_x8 = []
            for k in range(8):
                j = k % 2
                t_xl = p.op("sync", lambda e, k=k, j=j: e.dma_start(
                    out=sxf[j][:], in_=xn1T[k * 128:(k + 1) * 128, :]),
                    waits=[t_x8[k - 2] if k >= 2 else t_sc], inc=lda[j], amt=16)
                t_x8.append(p.op("vector", lambda e, k=k, j=j: e.tensor_scalar(
                    out=sx8[:, k * TPC:(k + 1) * TPC], in0=sxf[j][:],
                    scalar1=sscb[:, 0:1], scalar2=None, op0=ALU.mult),
                    waits=[t_xl, t_bc[0]], inc=sv_))
            t_w8 = []
            for k in range(8):
                j = k % 2
                t_wl = p.op("sync", lambda e, k=k, j=j: e.dma_start(
                    out=swfb[j][:], in_=wffnT[k * 128:(k + 1) * 128, :]),
                    waits=[t_w8[k - 2] if k >= 2 else t_x8[6 + k]], inc=lda[j], amt=16)
                t_w8.append(p.op("vector", lambda e, k=k, j=j: e.tensor_scalar(
                    out=sw8[:, k * D:(k + 1) * D], in0=swfb[j][:],
                    scalar1=sscb[:, 1:2], scalar2=None, op0=ALU.mult),
                    waits=[t_wl, t_bc[1]], inc=sv_))
            prev_ev = {0: None, 1: None}
            for icc in range(NT):
                for mh in range(2):
                    pb = (icc * 2 + mh) % 2
                    t_mm = None
                    for kt in range(8):
                        wt = [prev_ev[pb], t_x8[7], t_w8[7]] if kt == 0 else []
                        t_mm = p.op("tensor", lambda e, icc=icc, mh=mh, kt=kt, pb=pb: e.matmul(
                            out=p_a[pb][:], lhsT=sx8[:, kt * TPC + icc * 128: kt * TPC + (icc + 1) * 128],
                            rhs=sw8[:, kt * D + mh * 512: kt * D + (mh + 1) * 512],
                            start=(kt == 0), stop=(kt == 7)),
                            waits=wt, inc=(st_ if kt == 7 else None))
                    prev_ev[pb] = p.op("vector", lambda e, icc=icc, mh=mh, pb=pb: e.scalar_tensor_tensor(
                        out=sffo[:, icc * D + mh * 512: icc * D + (mh + 1) * 512], in0=p_a[pb][:],
                        scalar=sscb[:, 2:3], in1=sbffn[:, mh * 512:(mh + 1) * 512],
                        op0=ALU.mult, op1=ALU.add),
                        waits=[t_mm, t_bc[2], t_ld1], inc=sv_)
            p.barrier([sv_, ss, st_])
        p.op("gpsimd", lambda e: e.dma_start(out=o_d1, in_=sffo[:, 0:D]), inc=sgo[0], amt=16, dma=True)
        p.op("gpsimd", lambda e: e.dma_start(out=o_d0, in_=ssc[:]), inc=sgo[0], amt=16, dma=True)

        with ExitStack() as c2:
            E2 = c2.enter_context
            skvb = E2(nc.sbuf_tensor("skvb", [128, NH * 2 * 65], BF16))
            with ExitStack() as c2a:
                skvs = c2a.enter_context(nc.sbuf_tensor("skvs", [128, NH * 2 * 4 * 65], F32))
                t_kv = p.op("sync", lambda e: e.dma_start(out=skvs[:], in_=kv4), inc=ld1, amt=16)

                def kvsum(e):
                    v = skvs[:].rearrange("p (h m c f) -> p (h m) c f", h=NH, m=2, c=4)
                    e.tensor_tensor(out=v[:, :, 0, :], in0=v[:, :, 0, :], in1=v[:, :, 1, :], op=ALU.add)
                    e.drain()
                    e.tensor_tensor(out=v[:, :, 2, :], in0=v[:, :, 2, :], in1=v[:, :, 3, :], op=ALU.add)
                    e.drain()
                    e.tensor_tensor(out=v[:, :, 0, :], in0=v[:, :, 0, :], in1=v[:, :, 2, :], op=ALU.add)
                    e.drain()
                    return e.tensor_copy(out=skvb[:].rearrange("p (g f) -> p g f", f=65),
                                         in_=v[:, :, 0, :])
                t_kvs = p.op("vector", kvsum, waits=[t_kv], inc=sv_)
                p.barrier([sv_])
            shc = E2(nc.sbuf_tensor("shc", [128, 8 * TPC], BF16))
            sqp2 = [E2(nc.sbuf_tensor(f"sqp2{j}", [128, 2 * TPC], BF16)) for j in range(2)]
            srec = E2(nc.sbuf_tensor("srec", [1, 512], F32))
            srecb = E2(nc.sbuf_tensor("srecb", [128, 512], BF16))
            swo = E2(nc.sbuf_tensor("swo", [128, 8 * D], BF16))
            sxc = [E2(nc.sbuf_tensor(f"sxc{j}", [128, D], F32)) for j in range(2)]
            for k in range(8):
                p.op("gpsimd", lambda e, k=k: e.dma_start(
                    out=swo[:, k * D:(k + 1) * D], in_=wout[k * 128:(k + 1) * 128, :]),
                    inc=ldg1, amt=16, dma=True)
            t_ldg_all2 = (ldg1, p.cnt["ldg1"])
            hc_toks = []
            pnum_prev = {0: None, 1: None}
            qh_prev = {0: None, 1: None}
            for h in range(NH):
                jh = h % 2
                t_qh = p.op("sync", lambda e, h=h, jh=jh: e.dma_start(
                    out=sqp2[jh][:].rearrange("p (a t) -> p a t", a=2),
                    in_=qpT[h].rearrange("(a p) t -> p a t", p=128)),
                    waits=[qh_prev[jh]], inc=lda[jh], amt=16)
                for icc in range(4):
                    pb = icc % 2
                    t_mm = None
                    for mt in range(2):
                        wt = [t_kvs, t_qh, pnum_prev[pb]] if mt == 0 else []
                        t_mm = p.op("tensor", lambda e, h=h, jh=jh, icc=icc, mt=mt, pb=pb: e.matmul(
                            out=p_a[pb][0:65, :],
                            lhsT=skvb[:, (h * 2 + mt) * 65:(h * 2 + mt + 1) * 65],
                            rhs=sqp2[jh][:, mt * TPC + icc * 512: mt * TPC + (icc + 1) * 512],
                            start=(mt == 0), stop=(mt == 1)),
                            waits=wt, inc=(st_ if mt == 1 else None))
                    def recip_den(e, pb=pb):
                        e.tensor_scalar(out=srec[:], in0=p_a[pb][64:65, :], scalar1=PEPS, scalar2=None,
                                        op0=ALU.add)
                        e.drain()
                        return e.reciprocal(out=srec[:], in_=srec[:])
                    t_rc = p.op("vector", recip_den, waits=[t_mm], inc=sv_)
                    t_rb = p.op("tensor", lambda e: e.matmul(out=p_b[0][:], lhsT=sones[:], rhs=srec[:],
                                                             start=True, stop=True),
                                waits=[t_rc, t_ones], inc=st_)
                    t_re = p.op("vector", lambda e: e.tensor_copy(out=srecb[:], in_=p_b[0][:]),
                                waits=[t_rb], inc=sv_)
                    pnum_prev[pb] = p.op("vector", lambda e, h=h, icc=icc, pb=pb: e.tensor_tensor(
                        out=shc[(h % 2) * 64:(h % 2) * 64 + 64,
                                (h // 2) * TPC + icc * 512:(h // 2) * TPC + (icc + 1) * 512],
                        in0=p_a[pb][0:64, :], in1=srecb[0:64, :], op=ALU.mult),
                        waits=[t_re], inc=sv_)
                    hc_toks.append(pnum_prev[pb])
                qh_prev[jh] = hc_toks[-1]
            hc_all = [hc_toks[-1], hc_toks[-2]]
            x2_done = [None] * NT
            wo_prev = {0: None, 1: None}
            for icc in range(NT):
                jj = icc % 2
                t_xl = p.op("sync", lambda e, icc=icc, jj=jj: e.dma_start(
                    out=sxc[jj][:], in_=xa[icc * 128:(icc + 1) * 128, :]),
                    waits=([x2_done[icc - 2]] if icc >= 2 else []), inc=ldx[jj], amt=16)
                for mh in range(2):
                    pb = mh
                    t_mm = None
                    for kt in range(8):
                        wt = [wo_prev[pb], t_ldg_all2] + hc_all if kt == 0 else []
                        t_mm = p.op("tensor", lambda e, icc=icc, mh=mh, kt=kt, pb=pb: e.matmul(
                            out=p_b[pb][:], lhsT=shc[:, kt * TPC + icc * 128: kt * TPC + (icc + 1) * 128],
                            rhs=swo[:, kt * D + mh * 512: kt * D + (mh + 1) * 512],
                            start=(kt == 0), stop=(kt == 7)),
                            waits=wt, inc=(st_ if kt == 7 else None))
                    p.op("vector", lambda e, icc=icc, mh=mh, pb=pb: e.scalar_tensor_tensor(
                        out=sx2[:, icc * D + mh * 512: icc * D + (mh + 1) * 512], in0=p_b[pb][:],
                        scalar=1.0, in1=sbout[:, mh * 512:(mh + 1) * 512], op0=ALU.mult, op1=ALU.add),
                        waits=[t_mm, t_ld1], inc=sv_)
                    p.op("vector", lambda e, icc=icc, mh=mh, jj=jj: e.tensor_tensor(
                        out=sx2[:, icc * D + mh * 512: icc * D + (mh + 1) * 512],
                        in0=sx2[:, icc * D + mh * 512: icc * D + (mh + 1) * 512],
                        in1=sxc[jj][:, mh * 512:(mh + 1) * 512], op=ALU.add),
                        waits=[t_xl], inc=sv_)
                    wo_prev[pb] = p.op("vector", lambda e, icc=icc, mh=mh: e.tensor_tensor(
                        out=sx2[:, icc * D + mh * 512: icc * D + (mh + 1) * 512],
                        in0=sx2[:, icc * D + mh * 512: icc * D + (mh + 1) * 512],
                        in1=sffo[:, icc * D + mh * 512:icc * D + (mh + 1) * 512],
                        op=ALU.add), inc=sv_)
                x2_done[icc] = wo_prev[1]
            p.barrier([sv_, ss, st_])

        p.op("gpsimd", lambda e: e.dma_start(out=o_d2, in_=sx2[:, 0:D]), inc=sgo[0], amt=16, dma=True)
        p.barrier([sv_, ss, st_])
        c12.close()
        sxn2T = E(nc.sbuf_tensor("sxn2T", [128, 8 * TPC], BF16))
        with ExitStack() as c3:
            E3 = c3.enter_context
            ssq3 = E3(nc.sbuf_tensor("ssq3", [128, D], F32))
            sxn2 = E3(nc.sbuf_tensor("sxn2", [128, D], F32))
            sxn232 = E3(nc.sbuf_tensor("sxn232", [128, 8 * TPC], F32))
            for k in range(0, 8):
                p.op("sync", lambda e, k=k: e.dma_start(out=sgw32[:, k * 8:(k + 1) * 8],
                                                        in_=gwT[k * 128:(k + 1) * 128, :]), inc=ld1, amt=16)
            t_gw32 = (ld1, p.cnt["ld1"])
            trbank_prev = {0: None, 1: None}
            for i in range(NT):
                x2t = sx2[:, i * D:(i + 1) * D]
                p.op("vector", lambda e, x2t=x2t: e.reduce_sum(out=sstat[:, 0:1], in_=x2t, axis=AXL.X),
                     inc=sv_)
                p.op("vector", lambda e: e.tensor_scalar(out=sstat[:, 0:1], in0=sstat[:, 0:1],
                                                         scalar1=1.0 / D, scalar2=None, op0=ALU.mult),
                     inc=sv_)
                t_sq = p.op("scalar", lambda e, x2t=x2t: e.activation(out=ssq3[:], in_=x2t, func=AF.Square,
                                                                     accum_out=sstat[:, 1:2]), inc=ss)
                p.op("vector", lambda e: e.tensor_scalar(out=sstat[:, 2:3], in0=sstat[:, 1:2],
                                                         scalar1=1.0 / D, scalar2=None, op0=ALU.mult),
                     waits=[t_sq], inc=sv_)
                p.op("vector", lambda e: e.tensor_tensor(out=sstat[:, 3:4], in0=sstat[:, 0:1],
                                                         in1=sstat[:, 0:1], op=ALU.mult), inc=sv_)
                t_var = p.op("vector", lambda e: e.tensor_tensor(out=sstat[:, 2:3], in0=sstat[:, 2:3],
                                                                 in1=sstat[:, 3:4], op=ALU.subtract), inc=sv_)
                p.op("scalar", lambda e: e.activation(out=sstat[:, 4:5], in_=sstat[:, 2:3],
                                                      func=AF.Ln, bias=LN_EPS, scale=1.0),
                     waits=[t_var], inc=ss)
                t_rstd = p.op("scalar", lambda e: e.activation(out=sstat[:, 4:5], in_=sstat[:, 4:5],
                                                               func=AF.Exp, scale=-0.5), inc=ss)
                p.op("vector", lambda e: e.tensor_tensor(out=sstat[:, 5:6], in0=sstat[:, 0:1],
                                                         in1=sstat[:, 4:5], op=ALU.mult),
                     waits=[t_rstd], inc=sv_)
                t_bneg = p.op("vector", lambda e: e.tensor_scalar(out=sstat[:, 5:6], in0=sstat[:, 5:6],
                                                                  scalar1=-1.0, scalar2=None, op0=ALU.mult),
                              inc=sv_)
                t_xn = p.op("vector", lambda e, x2t=x2t: e.tensor_scalar(
                    out=sxn2[:], in0=x2t, scalar1=sstat[:, 4:5], scalar2=sstat[:, 5:6],
                    op0=ALU.mult, op1=ALU.add), waits=[t_bneg], inc=sv_)
                ev_tok = {}
                for k in range(8):
                    tb = k % 2
                    wt = [t_xn, t_ld1]
                    if k >= 2:
                        wt.append(ev_tok[k - 2])
                    else:
                        wt.append(trbank_prev[tb])
                    tt = p.op("tensor", lambda e, k=k, tb=tb: e.transpose(
                        out=p_t[tb][:], in_=sxn2[:, k * 128:(k + 1) * 128], identity=sident[:]),
                        waits=wt, inc=st_)
                    ev_tok[k] = p.op("vector", lambda e, i=i, k=k, tb=tb: e.tensor_copy(
                        out=sxn232[:, k * TPC + i * 128: k * TPC + (i + 1) * 128], in_=p_t[tb][:]),
                        waits=[tt], inc=sv_)
                trbank_prev = {0: ev_tok[6], 1: ev_tok[7]}
                p.op("vector", lambda e, x2t=x2t: e.tensor_tensor(
                    out=x2t, in0=x2t, in1=sxn2[:], op=ALU.add), inc=sv_)
            t_cast = None
            for k in range(8):
                t_cast = p.op("vector", lambda e, k=k: e.tensor_copy(
                    out=sxn2T[:, k * TPC:(k + 1) * TPC], in_=sxn232[:, k * TPC:(k + 1) * TPC]),
                    inc=sv_)
            sc_ev = None
            for nch in range(4):
                t_mm = None
                for kt in range(8):
                    wt = [t_gw32, sc_ev, t_cast] if kt == 0 else []
                    t_mm = p.op("tensor", lambda e, nch=nch, kt=kt: e.matmul(
                        out=p_a[0][0:8, :], lhsT=sgw32[:, kt * 8:(kt + 1) * 8],
                        rhs=sxn232[:, kt * TPC + nch * 512:kt * TPC + (nch + 1) * 512],
                        start=(kt == 0), stop=(kt == 7)),
                        waits=wt, inc=(st_ if kt == 7 else None))
                sc_ev = p.op("vector", lambda e, nch=nch: e.tensor_copy(
                    out=ssc_f[:, nch * 512:(nch + 1) * 512], in_=p_a[0][0:8, :]),
                    waits=[t_mm], inc=sv_)
            p.barrier([sv_, ss, st_])

        p.op("gpsimd", lambda e: e.dma_start(out=o_d3, in_=sxn2T[:, 0:TPC]), inc=sgo[0], amt=16, dma=True)
        with ExitStack() as c4:
            E4 = c4.enter_context
            swip = E4(nc.sbuf_tensor("swip", [128, 8 * 2 * D], BF16))
            scs = E4(nc.sbuf_tensor("scs", [128, 2 * D], BF16))
            sipc = [E4(nc.sbuf_tensor(f"sipc{j}", [128, 2 * D], F32)) for j in range(2)]
            srp = [E4(nc.sbuf_tensor(f"srp{j}", [128, 2 * D], BF16)) for j in range(2)]
            srh = [E4(nc.sbuf_tensor(f"srh{j}", [128, 2 * D], BF16)) for j in range(2)]
            snrm = E4(nc.sbuf_tensor("snrm", [128, 8], F32))
            ssq4 = E4(nc.sbuf_tensor("ssq4", [128, D], F32))
            for k in range(8):
                p.op("gpsimd", lambda e, k=k: e.dma_start(
                    out=swip[:, k * 2 * D:(k + 1) * 2 * D], in_=wip[k * 128:(k + 1) * 128, :]),
                    inc=ldg1, amt=16, dma=True)
            t_l4 = (ld1, p.cnt["ld1"])
            t_g4 = (ldg1, p.cnt["ldg1"])
            ip_prev = {0: None, 1: None, 2: None, 3: None}
            buf_prev = {0: None, 1: None}
            st_prev = {0: None, 1: None}
            cs_prev = {0: None, 1: None}
            for i in range(NT):
                jj = i % 2
                t_cs = p.op("sync", lambda e, i=i, jj=jj: e.dma_start(
                    out=scs[:, jj * D: jj * D + 512], in_=cosb[i * 128:(i + 1) * 128, :]),
                    waits=[cs_prev[jj]], inc=lda[jj], amt=16)
                t_sn = p.op("sync", lambda e, i=i, jj=jj: e.dma_start(
                    out=scs[:, jj * D + 512: jj * D + D], in_=sinb[i * 128:(i + 1) * 128, :]),
                    inc=lda[jj], amt=16)
                ev4 = []
                for q4 in range(4):
                    t_mm = None
                    for kt in range(8):
                        wt = [ip_prev[q4], t_g4] if kt == 0 else []
                        t_mm = p.op("tensor", lambda e, i=i, q4=q4, kt=kt: e.matmul(
                            out=p_a[q4][:], lhsT=sxn2T[:, kt * TPC + i * 128: kt * TPC + (i + 1) * 128],
                            rhs=swip[:, kt * 2 * D + q4 * 512: kt * 2 * D + (q4 + 1) * 512],
                            start=(kt == 0), stop=(kt == 7)),
                            waits=wt, inc=(st_ if kt == 7 else None))
                    ev4.append(p.op("vector", lambda e, jj=jj, q4=q4: e.tensor_copy(
                        out=sipc[jj][:, q4 * 512:(q4 + 1) * 512], in_=p_a[q4][:]),
                        waits=[t_mm, buf_prev[jj]], inc=sv_))
                    ip_prev[q4] = ev4[-1]
                def rope2(e, jj=jj, i=i):
                    cc = scs[:, jj * D: jj * D + 512]
                    sn = scs[:, jj * D + 512: jj * D + D]
                    tmp = srh[jj][:, 0:512]
                    for base in (0, D):
                        x1 = sipc[jj][:, base: base + 512]
                        x2_ = sipc[jj][:, base + 512: base + D]
                        o1 = srp[jj][:, base: base + 512]
                        o2 = srp[jj][:, base + 512: base + D]
                        e.tensor_tensor(out=o1, in0=x1, in1=cc, op=ALU.mult)
                        e.drain()
                        e.tensor_tensor(out=tmp, in0=x2_, in1=sn, op=ALU.mult)
                        e.drain()
                        e.tensor_tensor(out=o1, in0=o1, in1=tmp, op=ALU.subtract)
                        e.drain()
                        e.tensor_tensor(out=o2, in0=x2_, in1=cc, op=ALU.mult)
                        e.drain()
                        e.tensor_tensor(out=tmp, in0=x1, in1=sn, op=ALU.mult)
                        e.drain()
                        e.tensor_tensor(out=o2, in0=o2, in1=tmp, op=ALU.add)
                        e.drain()
                    return e.engine_nop()
                t_rope = p.op("vector", rope2, waits=[ev4[3], t_sn, st_prev[jj]], inc=sv_)
                cs_prev[jj] = t_rope
                def norms(e, jj=jj):
                    e.activation(out=ssq4[:], in_=srp[jj][:, 0:D], func=AF.Square,
                                 accum_out=snrm[:, 0:1])
                    e.drain()
                    e.activation(out=ssq4[:], in_=srp[jj][:, D:2 * D], func=AF.Square,
                                 accum_out=snrm[:, 1:2])
                    return e.drain()
                t_sq4 = p.op("scalar", norms, waits=[t_rope], inc=ss, dma=True)
                t_gq = p.op("vector", lambda e: e.tensor_scalar(
                    out=snrm[:, 0:2], in0=snrm[:, 0:2], scalar1=1e-24, scalar2=None, op0=ALU.max),
                    waits=[t_sq4], inc=sv_)

                def norms2(e):
                    e.activation(out=snrm[:, 2:3], in_=snrm[:, 0:1], func=AF.Ln, bias=0.0, scale=1.0)
                    e.drain()
                    e.activation(out=snrm[:, 2:3], in_=snrm[:, 2:3], func=AF.Exp, scale=-0.5)
                    e.drain()
                    e.activation(out=snrm[:, 3:4], in_=snrm[:, 1:2], func=AF.Ln, bias=0.0, scale=1.0)
                    e.drain()
                    return e.activation(out=snrm[:, 3:4], in_=snrm[:, 3:4], func=AF.Exp, scale=-0.5)
                t_nrm = p.op("scalar", norms2, waits=[t_gq], inc=ss)
                t_nh = p.op("vector", lambda e, jj=jj: e.tensor_scalar(
                    out=srh[jj][:, 0:D], in0=srp[jj][:, 0:D], scalar1=snrm[:, 2:3], scalar2=None,
                    op0=ALU.mult), waits=[t_nrm], inc=sv_)
                t_ph = p.op("vector", lambda e, jj=jj: e.tensor_scalar(
                    out=srh[jj][:, D:2 * D], in0=srp[jj][:, D:2 * D], scalar1=snrm[:, 3:4], scalar2=None,
                    op0=ALU.mult), inc=sv_)
                st1 = p.op("gpsimd", lambda e, i=i, jj=jj: e.dma_start(
                    out=o_I[i * 128:(i + 1) * 128, :], in_=srp[jj][:, 0:D]),
                    waits=[t_rope], inc=sgo[jj], amt=16, dma=True)
                st2 = p.op("gpsimd", lambda e, i=i, jj=jj: e.dma_start(
                    out=o_P[i * 128:(i + 1) * 128, :], in_=srp[jj][:, D:2 * D]),
                    waits=[t_rope], inc=sgo[jj], amt=16, dma=True)
                st3 = p.op("gpsimd", lambda e, i=i, jj=jj: e.dma_start(
                    out=o_Ih[i * 128:(i + 1) * 128, :], in_=srh[jj][:, 0:D]),
                    waits=[t_nh], inc=sgo[jj], amt=16, dma=True)
                st4 = p.op("gpsimd", lambda e, i=i, jj=jj: e.dma_start(
                    out=o_Ph[i * 128:(i + 1) * 128, :], in_=srh[jj][:, D:2 * D]),
                    waits=[t_ph], inc=sgo[jj], amt=16, dma=True)
                buf_prev[jj] = t_rope
                st_prev[jj] = st4
            p.barrier([sv_, ss, st_])

        with ExitStack() as c5:
            E5 = c5.enter_context
            scot = E5(nc.sbuf_tensor("scot", [128, 8], F32))
            smk = E5(nc.sbuf_tensor("smk", [128, NT * 8], F32))
            siota = E5(nc.sbuf_tensor("siota", [128, 8], F32))
            swe = [E5(nc.sbuf_tensor(f"swe{j}", [128, 8 * D], BF16)) for j in range(2)]
            t_io = p.op("gpsimd", lambda e: e.dma_start(out=siota[:], in_=iota8.partition_broadcast(128)),
                        inc=ldg1, amt=16, dma=True)
            sc_ev = None
            mk_prev = None
            tr_prev = None
            for i in range(NT):
                tt = p.op("tensor", lambda e, i=i: e.transpose(
                    out=p_t[0][:, 0:8], in_=ssc_f[:, i * 128:(i + 1) * 128],
                    identity=sident[0:8, 0:8]), waits=[sc_ev, tr_prev], inc=st_)
                t_sc2 = p.op("vector", lambda e: e.tensor_copy(out=scot[:], in_=p_t[0][:, 0:8]),
                             waits=[tt], inc=sv_)
                tr_prev = t_sc2

                def mkmask(e, i=i):
                    e.tensor_reduce(out=sstat[:, 6:7], in_=scot[:], axis=AXL.X, op=ALU.max)
                    e.drain()
                    e.tensor_scalar(out=smk[:, i * 8:(i + 1) * 8], in0=scot[:], scalar1=sstat[:, 6:7],
                                    scalar2=None, op0=ALU.is_ge)
                    e.drain()
                    e.tensor_scalar(out=smk[:, i * 8:(i + 1) * 8], in0=smk[:, i * 8:(i + 1) * 8],
                                    scalar1=-1000.0, scalar2=1000.0, op0=ALU.mult, op1=ALU.add)
                    e.drain()
                    e.tensor_tensor(out=smk[:, i * 8:(i + 1) * 8], in0=smk[:, i * 8:(i + 1) * 8],
                                    in1=siota[:], op=ALU.add)
                    e.drain()
                    e.tensor_reduce(out=sstat[:, 7:8], in_=smk[:, i * 8:(i + 1) * 8],
                                    axis=AXL.X, op=ALU.min)
                    e.drain()
                    return e.tensor_scalar(out=smk[:, i * 8:(i + 1) * 8], in0=siota[:],
                                           scalar1=sstat[:, 7:8], scalar2=None, op0=ALU.is_equal)
                mk_prev = p.op("vector", mkmask, waits=[t_sc2, t_io], inc=sv_)
            we_tok = [None, None]
            last_mm_of_expert = {}
            moe_ev = {0: None, 1: None}
            for ee in range(8):
                jw = ee % 2
                we_tok[jw] = p.op("gpsimd", lambda e, ee=ee: e.dma_start(
                    out=swe[ee % 2][:].rearrange("p (k d) -> p k d", k=8),
                    in_=wexp[ee].rearrange("(k p) d -> p k d", p=128)),
                    waits=([last_mm_of_expert.get(ee - 2)]), inc=ldw[jw], amt=16, dma=True)
                for i in range(NT):
                    for mh in range(2):
                        pb = mh
                        t_mm = None
                        for kt in range(8):
                            wt = [we_tok[jw], mk_prev, moe_ev[pb]] if kt == 0 else []
                            t_mm = p.op("tensor", lambda e, ee=ee, i=i, mh=mh, kt=kt, pb=pb, jw=jw: e.matmul(
                                out=p_b[pb][:],
                                lhsT=sxn2T[:, kt * TPC + i * 128: kt * TPC + (i + 1) * 128],
                                rhs=swe[jw][:, kt * D + mh * 512: kt * D + (mh + 1) * 512],
                                start=(kt == 0), stop=(kt == 7)),
                                waits=wt, inc=(st_ if kt == 7 else None))
                        last_mm_of_expert[ee] = t_mm
                        moe_ev[pb] = p.op("vector", lambda e, ee=ee, i=i, mh=mh, pb=pb: e.scalar_tensor_tensor(
                            out=sx2[:, i * D + mh * 512: i * D + (mh + 1) * 512], in0=p_b[pb][:],
                            scalar=smk[:, i * 8 + ee: i * 8 + ee + 1],
                            in1=sx2[:, i * D + mh * 512: i * D + (mh + 1) * 512],
                            op0=ALU.mult, op1=ALU.add),
                            waits=[t_mm], inc=sv_)
            st5 = None
            for i in range(NT):
                st5 = p.op("gpsimd", lambda e, i=i: e.dma_start(
                    out=o_x3p[i * 128:(i + 1) * 128, :], in_=sx2[:, i * D:(i + 1) * D]),
                    waits=[moe_ev[0], moe_ev[1]], inc=sgo[0], amt=16, dma=True)
            p.barrier([sv_, ss, st_])
        p.fence("gpsimd", [(sgo[0], p.cnt["sgo0"]), (sgo[1], p.cnt["sgo1"])])
        p.emit(block)
    return nc



def build_C(blocks):
    nc = bass.Bass("TRN2", target_bir_lowering=False)
    PT = nc.dram_tensor("PT", [D, T], BF16, kind="ExternalInput").ap()
    Phat = nc.dram_tensor("Phat", [T, D], BF16, kind="ExternalInput").ap()
    If_ = nc.dram_tensor("If_", [D, TPC], BF16, kind="ExternalInput").ap()
    Ihf = nc.dram_tensor("Ihf", [D, TPC], BF16, kind="ExternalInput").ap()
    x3pT = nc.dram_tensor("x3pT", [D, TPC], F32, kind="ExternalInput").ap()
    w1p = nc.dram_tensor("w1p", [32, 64], F32, kind="ExternalInput").ap()
    b1 = nc.dram_tensor("b1", [64, 1], F32, kind="ExternalInput").ap()
    w2 = nc.dram_tensor("w2", [64, 64], F32, kind="ExternalInput").ap()
    b2 = nc.dram_tensor("b2", [64, 1], F32, kind="ExternalInput").ap()
    wc = nc.dram_tensor("wc", [64, 2], F32, kind="ExternalInput").ap()
    bc = nc.dram_tensor("bc", [2, 1], F32, kind="ExternalInput").ap()
    wm = nc.dram_tensor("wm", [64, 1], F32, kind="ExternalInput").ap()
    bm = nc.dram_tensor("bm", [1, 1], F32, kind="ExternalInput").ap()
    pal = nc.dram_tensor("pal", [D, 256], F32, kind="ExternalInput").ap()
    wo = nc.dram_tensor("wo", [D, D], F32, kind="ExternalInput").ap()
    ident = nc.dram_tensor("ident", [128, 128], F32, kind="ExternalInput").ap()
    cmask = nc.dram_tensor("cmask", [128, 128], F32, kind="ExternalInput").ap()
    iotas = nc.dram_tensor("iotas", [128, 512], I32, kind="ExternalInput").ap()
    iot128 = nc.dram_tensor("iot128", [128, 1], F32, kind="ExternalInput").ap()
    ioy = nc.dram_tensor("ioy", [128, 256], BF16, kind="ExternalInput").ap()
    kmG = nc.dram_tensor("kmG", [32, 256], BF16, kind="ExternalInput").ap()
    kmM = nc.dram_tensor("kmM", [128, 16], F32, kind="ExternalInput").ap()
    o_yf = nc.dram_tensor("o_yf", [D, TPC], F32, kind="ExternalOutput").ap()
    idx_scr = nc.dram_tensor("idx_scr", [128, 16], I32, kind="Internal").ap()
    row_scr = nc.dram_tensor("row_scr", [1, TPC], F32, kind="Internal").ap()
    tok_scr = nc.dram_tensor("tok_scr", [128, 16], F32, kind="Internal").ap()

    with ExitStack() as ctx:
        E = ctx.enter_context
        p = Prog(nc)
        ld1 = p.mksem(ctx, "ld1")
        ldp = [p.mksem(ctx, "ldp0"), p.mksem(ctx, "ldp1")]
        ldb = [p.mksem(ctx, "ldb0"), p.mksem(ctx, "ldb1")]
        ldgx = p.mksem(ctx, "ldgx")
        ldsc = p.mksem(ctx, "ldsc")
        sv_ = p.mksem(ctx, "sv_")
        ss = p.mksem(ctx, "ss")
        st_ = p.mksem(ctx, "st_")
        sgo = p.mksem(ctx, "sgo")
        block = E(nc.Block())
        p.setup_consts(ctx, [0.0, 1.0, -1.0, -0.5, 7.5, 15.0, NEG, PEPS, 1.0 / 8192.0, 1e-24,
                             1.0 / 16.0], sgo)

        sident = E(nc.sbuf_tensor("sident", [128, 128], F32))
        sident_b = E(nc.sbuf_tensor("sident_b", [128, 128], BF16))
        scm = E(nc.sbuf_tensor("scm", [128, 128], F32))
        siotas = E(nc.sbuf_tensor("siotas", [128, 512], I32))
        siot128 = E(nc.sbuf_tensor("siot128", [128, 1], F32))
        sioy = E(nc.sbuf_tensor("sioy", [128, 256], BF16))
        skmG = E(nc.sbuf_tensor("skmG", [32, 256], BF16))
        skmM = E(nc.sbuf_tensor("skmM", [128, 16], F32))
        sw1 = E(nc.sbuf_tensor("sw1", [32, 64], BF16))
        sw2 = E(nc.sbuf_tensor("sw2", [64, 64], BF16))
        swc = E(nc.sbuf_tensor("swc", [64, 2], BF16))
        swm = E(nc.sbuf_tensor("swm", [64, 1], BF16))
        sb1 = E(nc.sbuf_tensor("sb1", [64, 1], F32))
        sb2 = E(nc.sbuf_tensor("sb2", [64, 1], F32))
        sbc = E(nc.sbuf_tensor("sbc", [2, 1], F32))
        sbm = E(nc.sbuf_tensor("sbm", [1, 1], F32))
        sPW = E(nc.sbuf_tensor("sPW", [128, 2 * D], BF16))
        with ExitStack() as c0:
            E0 = c0.enter_context
            spal = E0(nc.sbuf_tensor("spal", [128, 8 * 256], BF16))
            swo0 = E0(nc.sbuf_tensor("swo0", [128, 8 * D], BF16))
            for k in range(8):
                p.op("gpsimd", lambda e, k=k: e.dma_start(
                    out=spal[:, k * 256:(k + 1) * 256], in_=pal[k * 128:(k + 1) * 128, :]),
                    inc=ldgx, amt=16, dma=True)
                p.op("gpsimd", lambda e, k=k: e.dma_start(
                    out=swo0[:, k * D:(k + 1) * D], in_=wo[k * 128:(k + 1) * 128, :]),
                    inc=ldgx, amt=16, dma=True)
            t_pw0 = (ldgx, p.cnt["ldgx"])
            pw_prev = None
            for cc in range(2):
                for nn in range(2):
                    t_mm = None
                    for kt in range(8):
                        wt = [t_pw0, pw_prev] if kt == 0 else []
                        t_mm = p.op("tensor", lambda e, cc=cc, nn=nn, kt=kt: e.matmul(
                            out=p_m[:], lhsT=spal[:, kt * 256 + cc * 128: kt * 256 + (cc + 1) * 128],
                            rhs=swo0[:, kt * D + nn * 512: kt * D + (nn + 1) * 512],
                            start=(kt == 0), stop=(kt == 7)),
                            waits=wt, inc=(st_ if kt == 7 else None))
                    pw_prev = p.op("vector", lambda e, cc=cc, nn=nn: e.tensor_copy(
                        out=sPW[:, cc * D + nn * 512: cc * D + (nn + 1) * 512], in_=p_m[:]),
                        waits=[t_mm], inc=sv_)
            p.barrier([sv_, ss, st_])

        sVp = E(nc.sbuf_tensor("sVp", [128, T], F32))
        sZ = E(nc.sbuf_tensor("sZ", [128, 8 * 4096], BF16))
        sptb = [E(nc.sbuf_tensor(f"sptb{j}", [128, 8 * 512], BF16)) for j in range(2)]
        sifb = [E(nc.sbuf_tensor(f"sifb{j}", [128, 8 * 128], BF16)) for j in range(2)]
        sihb = [E(nc.sbuf_tensor(f"sihb{j}", [128, 8 * 128], BF16)) for j in range(2)]
        sx3b = [E(nc.sbuf_tensor(f"sx3b{j}", [128, 8 * 128], F32)) for j in range(2)]
        sm16 = E(nc.sbuf_tensor("sm16", [128, 16], F32))
        sidx = E(nc.sbuf_tensor("sidx", [128, 16], I32))
        sidxf = E(nc.sbuf_tensor("sidxf", [128, 16], F32))
        sidxg = E(nc.sbuf_tensor("sidxg", [128, 16], I32))
        sgt = [E(nc.sbuf_tensor(f"sgt{j}", [128, D], BF16)) for j in range(2)]
        sG = E(nc.sbuf_tensor("sG", [128, 4096], BF16))
        srel = E(nc.sbuf_tensor("srel", [32, 2048], BF16))
        sh1 = E(nc.sbuf_tensor("sh1", [64, 2048], BF16))
        sh2 = E(nc.sbuf_tensor("sh2", [64, 2048], BF16))
        smixr = E(nc.sbuf_tensor("smixr", [1, 2048], F32))
        szrow = E(nc.sbuf_tensor("szrow", [2, 2048], F32))
        smixt = E(nc.sbuf_tensor("smixt", [128, 16], F32))
        sstat = E(nc.sbuf_tensor("sstat", [128, 8], F32))
        sw_ = E(nc.sbuf_tensor("sw_", [128, 16], F32))
        sfx = E(nc.sbuf_tensor("sfx", [128, 16], F32))
        sfy = E(nc.sbuf_tensor("sfy", [128, 16], F32))
        sfxi = E(nc.sbuf_tensor("sfxi", [128, 16], I32))
        shx = E(nc.sbuf_tensor("shx", [128, 256], BF16))
        shy = E(nc.sbuf_tensor("shy", [128, 256], BF16))
        sYe = E(nc.sbuf_tensor("sYe", [128, 4096], BF16))
        stmp = E(nc.sbuf_tensor("stmp", [128, 256], BF16))
        sXe = E(nc.sbuf_tensor("sXe", [128, 4096], BF16))
        sA = E(nc.sbuf_tensor("sA", [128, 256], F32))
        sAb = E(nc.sbuf_tensor("sAb", [128, 256], BF16))
        sAf = E(nc.sbuf_tensor("sAf", [128, 256], BF16))
        souf = E(nc.sbuf_tensor("souf", [128, 8 * 128], F32))

        p_l = [E(nc.psum_tensor(f"p_l{j}", [128, 512], F32)) for j in range(2)]
        p_g = E(nc.psum_tensor("p_g", [128, 128], F32))
        p_gb = E(nc.psum_tensor("p_gb", [128, 128], BF16))
        p_t2 = [E(nc.psum_tensor(f"p_t2{j}", [128, 128], BF16)) for j in range(2)]
        p_m = E(nc.psum_tensor("p_m", [128, 512], F32))

        for dst, srcap in ((sident, ident), (scm, cmask), (siotas, iotas), (siot128, iot128),
                           (skmM, kmM), (sb1, b1), (sb2, b2), (sbc, bc), (sbm, bm),
                           (sioy, ioy), (skmG, kmG)):
            p.op("sync", lambda e, dst=dst, srcap=srcap: e.dma_start(out=dst[:], in_=srcap),
                 inc=ld1, amt=16)
        for dst, srcap in ((sw1, w1p), (sw2, w2), (swc, wc), (swm, wm)):
            p.op("gpsimd", lambda e, dst=dst, srcap=srcap: e.dma_start(out=dst[:], in_=srcap),
                 inc=ld1, amt=16, dma=True)
        t_ld1 = (ld1, p.cnt["ld1"])
        t_identb = p.op("vector", lambda e: e.tensor_copy(out=sident_b[:], in_=sident[:]),
                        waits=[t_ld1], inc=sv_)
        t_zz = p.op("vector", lambda e: e.memset(sZ[:], 0.0), inc=sv_)
        ptb_prev = {0: None, 1: None}
        blk_prev = {0: None, 1: None}
        gx_prev = None
        sc_prev = None
        out_prev = None
        for rb, gb in enumerate(blocks):
            t0 = gb * 128
            ncol = t0 + 128
            nch = (ncol + 511) // 512
            jb = rb % 2
            t_if = p.op("sync", lambda e, rb=rb, jb=jb: e.dma_start(
                out=sifb[jb][:].rearrange("p (k c) -> p k c", k=8),
                in_=If_.rearrange("(k p) t -> p k t", p=128)[:, :, rb * 128:(rb + 1) * 128]),
                waits=[blk_prev[jb]], inc=ldb[jb], amt=16)
            t_ih = p.op("sync", lambda e, rb=rb, jb=jb: e.dma_start(
                out=sihb[jb][:].rearrange("p (k c) -> p k c", k=8),
                in_=Ihf.rearrange("(k p) t -> p k t", p=128)[:, :, rb * 128:(rb + 1) * 128]),
                inc=ldb[jb], amt=16)
            t_x3 = p.op("sync", lambda e, rb=rb, jb=jb: e.dma_start(
                out=sx3b[jb][:].rearrange("p (k c) -> p k c", k=8),
                in_=x3pT.rearrange("(k p) t -> p k t", p=128)[:, :, rb * 128:(rb + 1) * 128]),
                inc=ldb[jb], amt=16)
            ev_prev = None
            for ch in range(nch):
                jp = ch % 2
                t_pt = p.op("sync", lambda e, ch=ch, jp=jp: e.dma_start(
                    out=sptb[jp][:].rearrange("p (k s) -> p k s", k=8),
                    in_=PT.rearrange("(k p) t -> p k t", p=128)[:, :, ch * 512:(ch + 1) * 512]),
                    waits=[ptb_prev[jp]], inc=ldp[jp], amt=16)
                pb = ch % 2
                t_mm = None
                for kt in range(8):
                    wt = [t_pt, t_if] if kt == 0 else []
                    if kt == 0 and ev_prev is not None:
                        wt.append(ev_prev if ch < 2 else ev_toks[ch - 2])
                    t_mm = p.op("tensor", lambda e, ch=ch, kt=kt, pb=pb, rb=rb, jb=jb: e.matmul(
                        out=p_l[pb][:], lhsT=sifb[jb][:, kt * 128:(kt + 1) * 128],
                        rhs=sptb[ch % 2][:, kt * 512:(kt + 1) * 512],
                        start=(kt == 0), stop=(kt == 7)),
                        waits=wt, inc=(st_ if kt == 7 else None))
                if ch == nch - 1:
                    off = t0 - ch * 512
                    t_mk = p.op("vector", lambda e, pb=pb, off=off: e.tensor_tensor(
                        out=p_l[pb][:, off:off + 128], in0=p_l[pb][:, off:off + 128],
                        in1=scm[:], op=ALU.add), waits=[t_mm, t_ld1], inc=sv_)
                else:
                    t_mk = t_mm
                def tagev(e, ch=ch, pb=pb):
                    e.tensor_scalar(
                        out=sVp[:, ch * 512:(ch + 1) * 512].bitcast(I32),
                        in0=p_l[pb][:].bitcast(I32), scalar1=-8192, scalar2=ch * 512,
                        op0=ALU.bitwise_and, op1=ALU.bitwise_or)
                    e.drain()
                    return e.tensor_tensor(
                        out=sVp[:, ch * 512:(ch + 1) * 512].bitcast(I32),
                        in0=sVp[:, ch * 512:(ch + 1) * 512].bitcast(I32),
                        in1=siotas[:], op=ALU.bitwise_or)
                tg = p.op("vector", tagev, waits=[t_mk], inc=sv_)
                if ch == 0:
                    ev_toks = {}
                ev_toks[ch] = tg
                ev_prev = tg
                ptb_prev[ch % 2] = tg
            if ncol < nch * 512:
                ev_prev = p.op("vector", lambda e, ncol=ncol, nch=nch: e.memset(
                    sVp[:, ncol:nch * 512], NEG), waits=[ev_prev], inc=sv_)
            def top16(e, nch=nch):
                e.max(out=sm16[:, 0:8], in_=sVp[:, 0:nch * 512])
                e.drain()
                e.match_replace(out=sVp[:, 0:nch * 512], in_to_replace=sm16[:, 0:8],
                                in_values=sVp[:, 0:nch * 512], imm_value=NEG)
                e.drain()
                e.max(out=sm16[:, 8:16], in_=sVp[:, 0:nch * 512])
                e.drain()
                e.tensor_scalar(out=sidx[:], in0=sm16[:].bitcast(I32), scalar1=8191, scalar2=None,
                                op0=ALU.bitwise_and)
                e.drain()
                return e.tensor_copy(out=sidxf[:], in_=sidx[:])
            t_tk = p.op("vector", top16, waits=[ev_prev, sc_prev, gx_prev], inc=sv_)
            gt_prev = {0: None, 1: None}
            tb_prev = {0: None, 1: None}
            zev = []
            for g in range(15):
                jg = g % 2
                t_ga = p.op("gpsimd", lambda e, g=g, jg=jg: e.indirect_dma_start(
                    out=sgt[jg][:], out_offset=None, in_=Phat,
                    in_offset=bass.IndirectOffsetOnAxis(sidx[:, g:g + 1], 0)),
                    waits=[t_tk, gt_prev[jg]], inc=ldgx, amt=16, dma=True)
                for kt in range(8):
                    tb = kt % 2
                    t_tt = p.op("tensor", lambda e, jg=jg, kt=kt, tb=tb: e.transpose(
                        out=p_t2[tb][:], in_=sgt[jg][:, kt * 128:(kt + 1) * 128],
                        identity=sident_b[:]),
                        waits=[t_ga, tb_prev[tb], t_identb], inc=st_)
                    ev = p.op("vector", lambda e, g=g, kt=kt, tb=tb: e.tensor_copy(
                        out=sZ[:, kt * 4096 + g: (kt + 1) * 4096: 32].rearrange("p f -> p f"),
                        in_=p_t2[tb][:]),
                        waits=[t_tt, t_zz], inc=sv_)
                    tb_prev[tb] = ev
                    if kt == 7:
                        gt_prev[jg] = ev
                        zev.append(ev)
            ih_ev = []
            for kt in range(8):
                ih_ev.append(p.op("vector", lambda e, kt=kt, jb=jb: e.tensor_copy(
                    out=sZ[:, kt * 4096 + 15: (kt + 1) * 4096: 32].rearrange("p f -> p f"),
                    in_=sihb[jb][:, kt * 128:(kt + 1) * 128]),
                    waits=[t_ih, zev[-1]], inc=sv_))
            gr_ev = []
            gr_prev = None
            for g in range(32):
                t_mm = None
                for kt in range(8):
                    wt = [ih_ev[-1], gr_prev] if kt == 0 else []
                    t_mm = p.op("tensor", lambda e, g=g, kt=kt: e.matmul(
                        out=p_g[:], lhsT=sZ[:, kt * 4096 + g * 128: kt * 4096 + (g + 1) * 128],
                        rhs=sZ[:, kt * 4096 + g * 128: kt * 4096 + (g + 1) * 128],
                        start=(kt == 0), stop=(kt == 7)),
                        waits=wt, inc=(st_ if kt == 7 else None))
                gr_prev = p.op("vector", lambda e, g=g: e.tensor_scalar(
                    out=sG[:, g * 128:(g + 1) * 128], in0=p_g[:], scalar1=1.0, scalar2=-1.0,
                    op0=ALU.min, op1=ALU.max), waits=[t_mm], inc=sv_)
                gr_ev.append(gr_prev)
            def relg(e):
                e.memset(srel[:], 0.0)
                e.drain()
                for tsub in range(4):
                    e.tensor_copy(
                        out=srel[0:16, :].rearrange("p (g k) -> p g k", g=32)[:, :, 0:16].rearrange(
                            "p g k -> p (g k)").rearrange("p (g k) -> p g k", g=32)[:, :, 0:16]
                        if False else
                        srel[0:16, tsub * 16 + 0: 2048: 64].rearrange("p f -> p f"),
                        in_=sG[tsub * 32: tsub * 32 + 16, tsub * 32: 4096: 128])
                    e.drain()
                return e.drain()
            def relg2(e):
                e.memset(srel[:], 0.0)
                e.drain()
                for tsub in range(4):
                    src = sG[tsub * 32: tsub * 32 + 16, :].rearrange("p (g c) -> p g c", g=32)[
                        :, :, tsub * 32: tsub * 32 + 16]
                    dst = srel[0:16, :].rearrange("p (g t k) -> p g t k", g=32, t=4)[:, :, tsub, :]
                    e.tensor_copy(out=dst, in_=src)
                    e.drain()
                return e.drain()
            t_rel = p.op("vector", relg2, waits=[gr_ev[-1]], inc=sv_, dma=True)
            def delta(e, t0=t0):
                e.tensor_scalar(out=sm16[:], in0=sidxf[:], scalar1=siot128[:, 0:1], scalar2=None,
                                op0=ALU.subtract)
                e.drain()
                e.tensor_scalar(out=sm16[:], in0=sm16[:], scalar1=-1.0 / 8192.0,
                                scalar2=float(t0) / 8192.0, op0=ALU.mult, op1=ALU.add)
                e.drain()
                return e.tensor_scalar(out=sm16[:], in0=sm16[:], scalar1=0.0, scalar2=None,
                                       op0=ALU.max)
            t_dl = p.op("vector", delta, waits=[t_tk], inc=sv_)
            t_dw = p.op("gpsimd", lambda e: e.dma_start(out=tok_scr, in_=sm16[:]),
                        waits=[t_dl], inc=ldsc, amt=16, dma=True)
            t_dr = p.op("gpsimd", lambda e: e.dma_start(
                out=srel[16:17, :], in_=tok_scr.rearrange("p k -> (p k)")[None, :]),
                waits=[t_dw, t_rel], inc=ldsc, amt=16, dma=True)
            if t0 == 0:
                t_rel2 = p.op("vector", lambda e: e.tensor_tensor(
                    out=srel[:, 0:224].bitcast(BF16), in0=srel[:, 0:224].bitcast(BF16),
                    in1=skmG[:, 0:224], op=ALU.mult), waits=[(ldsc, p.cnt["ldsc"]), t_ld1], inc=sv_)
            else:
                t_rel2 = (ldsc, p.cnt["ldsc"])
            mlp_ev = []
            for nn in range(4):
                t_m1 = p.op("tensor", lambda e, nn=nn: e.matmul(
                    out=p_m[0:64, :], lhsT=sw1[:], rhs=srel[:, nn * 512:(nn + 1) * 512],
                    start=True, stop=True), waits=[t_rel2, t_ld1] if nn == 0 else [mlp_ev[-1]],
                    inc=st_)
                mlp_ev.append(p.op("scalar", lambda e, nn=nn: e.activation(
                    out=sh1[:, nn * 512:(nn + 1) * 512], in_=p_m[0:64, :], func=AF.Gelu,
                    bias=sb1[:], scale=1.0), waits=[t_m1], inc=ss))
            for nn in range(4):
                t_m2 = p.op("tensor", lambda e, nn=nn: e.matmul(
                    out=p_m[0:64, :], lhsT=sw2[:], rhs=sh1[:, nn * 512:(nn + 1) * 512],
                    start=True, stop=True), waits=[mlp_ev[-1] if nn == 0 else mlp_ev[-1]], inc=st_)
                mlp_ev.append(p.op("scalar", lambda e, nn=nn: e.activation(
                    out=sh2[:, nn * 512:(nn + 1) * 512], in_=p_m[0:64, :], func=AF.Gelu,
                    bias=sb2[:], scale=1.0), waits=[t_m2], inc=ss))
            zm_ev = []
            for nn in range(4):
                t_m3 = p.op("tensor", lambda e, nn=nn: e.matmul(
                    out=p_m[0:2, :], lhsT=swc[:], rhs=sh2[:, nn * 512:(nn + 1) * 512],
                    start=True, stop=True), waits=[mlp_ev[-1]] if nn == 0 else [zm_ev[-1]], inc=st_)
                zm_ev.append(p.op("scalar", lambda e, nn=nn: e.activation(
                    out=szrow[:, nn * 512:(nn + 1) * 512], in_=p_m[0:2, :], func=AF.Tanh,
                    bias=sbc[:], scale=1.0), waits=[t_m3], inc=ss))
            for nn in range(4):
                t_m4 = p.op("tensor", lambda e, nn=nn: e.matmul(
                    out=p_m[0:1, :], lhsT=swm[:], rhs=sh2[:, nn * 512:(nn + 1) * 512],
                    start=True, stop=True), waits=[zm_ev[-1]] if nn == 0 else [zm_ev[-1]], inc=st_)
                zm_ev.append(p.op("vector", lambda e, nn=nn: e.tensor_scalar(
                    out=smixr[:, nn * 512:(nn + 1) * 512], in0=p_m[0:1, :], scalar1=sbm[:],
                    scalar2=None, op0=ALU.add), waits=[t_m4], inc=sv_))
            t_mw = p.op("gpsimd", lambda e: e.dma_start(out=row_scr, in_=smixr[:]),
                        waits=[zm_ev[-1]], inc=ldsc, amt=16, dma=True)
            t_mr = p.op("gpsimd", lambda e: e.dma_start(
                out=smixt[:], in_=row_scr.rearrange("o (p k) -> (o p) k", p=128)),
                waits=[t_mw, sc_prev], inc=ldsc, amt=16, dma=True)

            def smax(e, t0=t0):
                e.memset(smixt[:, 15:16], NEG)
                e.drain()
                if t0 == 0:
                    e.tensor_tensor(out=smixt[:], in0=smixt[:], in1=skmM[:], op=ALU.add)
                    e.drain()
                e.tensor_reduce(out=sstat[:, 0:1], in_=smixt[:], axis=AXL.X, op=ALU.max, negate=True)
                return e.drain()
            t_sm = p.op("vector", smax, waits=[t_mr, t_ld1], inc=sv_, dma=True)
            t_ex = p.op("scalar", lambda e: e.activation(out=sw_[:], in_=smixt[:], func=AF.Exp,
                                                         bias=sstat[:, 0:1], scale=1.0),
                        waits=[t_sm], inc=ss)

            def wnorm(e):
                e.tensor_reduce(out=sstat[:, 1:2], in_=sw_[:], axis=AXL.X, op=ALU.add)
                e.drain()
                e.reciprocal(out=sstat[:, 1:2], in_=sstat[:, 1:2])
                e.drain()
                return e.tensor_scalar(out=sw_[:], in0=sw_[:], scalar1=sstat[:, 1:2], scalar2=None,
                                       op0=ALU.mult)
            t_wn = p.op("vector", wnorm, waits=[t_ex], inc=sv_)
            t_zw = p.op("gpsimd", lambda e: e.dma_start(out=row_scr, in_=szrow[0:1, :]),
                        waits=[zm_ev[-1], t_mr], inc=ldsc, amt=16, dma=True)
            t_zr = p.op("gpsimd", lambda e: e.dma_start(
                out=sfx[:], in_=row_scr.rearrange("o (p k) -> (o p) k", p=128)),
                waits=[t_zw], inc=ldsc, amt=16, dma=True)
            t_zw2 = p.op("gpsimd", lambda e: e.dma_start(out=row_scr, in_=szrow[1:2, :]),
                         waits=[t_zr], inc=ldsc, amt=16, dma=True)
            t_zr2 = p.op("gpsimd", lambda e: e.dma_start(
                out=sfy[:], in_=row_scr.rearrange("o (p k) -> (o p) k", p=128)),
                waits=[t_zw2], inc=ldsc, amt=16, dma=True)

            def mkf(e, buf):
                e.tensor_scalar(out=buf[:], in0=buf[:], scalar1=7.5, scalar2=7.5,
                                op0=ALU.mult, op1=ALU.add)
                e.drain()
                e.tensor_scalar(out=buf[:], in0=buf[:], scalar1=0.0, scalar2=15.0,
                                op0=ALU.max, op1=ALU.min)
                return e.drain()
            t_fx = p.op("vector", lambda e: mkf(e, sfx), waits=[t_zr], inc=sv_, dma=True)
            t_fy = p.op("vector", lambda e: mkf(e, sfy), waits=[t_zr2], inc=sv_, dma=True)
            def hats(e, src, dst, wmul):
                for j in range(16):
                    e.tensor_copy(out=dst[:, j * 16:(j + 1) * 16], in_=src[:])
                    e.drain()
                e.tensor_tensor(out=dst[:], in0=dst[:], in1=sioy[:], op=ALU.subtract)
                e.drain()
                e.tensor_scalar(out=stmp[:], in0=dst[:], scalar1=1.0, scalar2=None, op0=ALU.add)
                e.drain()
                e.tensor_scalar(out=dst[:], in0=dst[:], scalar1=-1.0, scalar2=1.0,
                                op0=ALU.mult, op1=ALU.add)
                e.drain()
                e.tensor_tensor(out=dst[:], in0=dst[:], in1=stmp[:], op=ALU.min)
                e.drain()
                e.tensor_scalar(out=dst[:], in0=dst[:], scalar1=0.0, scalar2=None, op0=ALU.max)
                e.drain()
                if wmul:
                    for j in range(16):
                        e.tensor_tensor(out=dst[:, j * 16:(j + 1) * 16],
                                        in0=dst[:, j * 16:(j + 1) * 16], in1=sw_[:], op=ALU.mult)
                        e.drain()
                return e.drain()
            t_hx = p.op("vector", lambda e: hats(e, sfx, shx, False), waits=[t_fx], inc=sv_, dma=True)
            t_hy = p.op("vector", lambda e: hats(e, sfy, shy, True), waits=[t_fy, t_wn], inc=sv_, dma=True)
            def abuild(e):
                for x in range(16):
                    e.tensor_copy(out=sYe[:].rearrange("p (y x k) -> p y x k", y=16, x=16)[:, :, x, :],
                                  in_=shy[:].rearrange("p (y k) -> p y k", y=16))
                    e.drain()
                for y in range(16):
                    e.tensor_copy(out=sXe[:, y * 256:(y + 1) * 256], in_=shx[:])
                    e.drain()
                e.tensor_tensor(out=sYe[:], in0=sYe[:], in1=sXe[:], op=ALU.mult)
                e.drain()
                e.tensor_reduce(out=sA[:], in_=sYe[:].rearrange("p (c k) -> p c k", k=16),
                                axis=AXL.X, op=ALU.add)
                e.drain()
                e.tensor_copy(out=sAb[:], in_=sA[:])
                return e.drain()
            t_ab = p.op("vector", abuild, waits=[t_hx, t_hy], inc=sv_, dma=True)
            ta_ev = []
            for half in range(2):
                t_at = p.op("tensor", lambda e, half=half: e.transpose(
                    out=p_t2[half][:], in_=sAb[:, half * 128:(half + 1) * 128],
                    identity=sident_b[:]), waits=[t_ab, ta_ev[-1] if ta_ev else None], inc=st_)
                ta_ev.append(p.op("vector", lambda e, half=half: e.tensor_copy(
                    out=sAf[:, half * 128:(half + 1) * 128], in_=p_t2[half][:]),
                    waits=[t_at], inc=sv_))
                tb_prev[half] = ta_ev[-1]
            o_ev = []
            for mt in range(8):
                t_my = p.op("tensor", lambda e, mt=mt: e.matmul(
                    out=p_m[:, 0:128], lhsT=sPW[:, 0 * D + mt * 128: 0 * D + (mt + 1) * 128],
                    rhs=sAf[:, 0:128], start=True, stop=False),
                    waits=[ta_ev[-1]] + ([o_ev[-1]] if o_ev else [out_prev]), inc=None)
                t_my = p.op("tensor", lambda e, mt=mt: e.matmul(
                    out=p_m[:, 0:128], lhsT=sPW[:, 1 * D + mt * 128: 1 * D + (mt + 1) * 128],
                    rhs=sAf[:, 128:256], start=False, stop=True), inc=st_)
                o_ev.append(p.op("vector", lambda e, mt=mt, jb=jb: e.tensor_tensor(
                    out=souf[:, mt * 128:(mt + 1) * 128], in0=p_m[:, 0:128],
                    in1=sx3b[jb][:, mt * 128:(mt + 1) * 128], op=ALU.add),
                    waits=[t_my, t_x3], inc=sv_))
            out_prev = p.op("gpsimd", lambda e, rb=rb: e.dma_start(
                out=o_yf.rearrange("(k p) t -> p k t", p=128)[:, :, rb * 128:(rb + 1) * 128],
                in_=souf[:].rearrange("p (k c) -> p k c", k=8)),
                waits=[o_ev[-1]], inc=sgo, amt=16, dma=True)
            blk_prev[jb] = o_ev[-1]
            sc_prev = t_wn
            gx_prev = (ldgx, p.cnt["ldgx"])
        p.fence("gpsimd", [(sgo, p.cnt["sgo"])])
        p.barrier([sv_, ss, st_])
        p.emit(block)
    return nc



_IDENT = np.eye(128, dtype=np.float32)


def _prep_A(inputs):
    x = inputs["L0_attn_x"]
    wqkv = np.ascontiguousarray(inputs["L0_attn_w_qkv"])
    bqkv = np.ascontiguousarray(inputs["L0_attn_b_qkv"].reshape(24, 128).T)
    pt = inputs["L0_attn_proj_matrix"].transpose(0, 2, 1)
    proj = np.zeros((128, 8 * 2 * MF), dtype=np.float32)
    for hp in range(8):
        proj[0:64, hp * 512: hp * 512 + 256] = pt[2 * hp]
        proj[64:128, hp * 512 + 256: hp * 512 + 512] = pt[2 * hp + 1]
    proj = np.ascontiguousarray(proj)
    maps = []
    for c in range(NCORE):
        b = c // 4
        xa = np.ascontiguousarray(x[b][tok_index(c)])
        maps.append({"xa": xa, "wqkv": wqkv, "bqkv": bqkv, "proj": proj, "ident": _IDENT})
    return maps


def run_A(inputs, **kw):
    nc = build_A(**kw)
    maps = _prep_A(inputs)
    res = run_bass_kernel_spmd(nc, maps, core_ids=list(range(NCORE)))
    return res.results




def _performer_ref_np(xn, w_qkv, b_qkv, w_out, b_out, proj):
    qkv = xn @ w_qkv + b_qkv
    q, k, v = np.split(qkv, 3, axis=-1)
    DATA_NORM_ = 64 ** -0.25
    out = np.zeros_like(xn)
    for h in range(16):
        qh = q[:, h * 64:(h + 1) * 64] * DATA_NORM_
        kh = k[:, h * 64:(h + 1) * 64] * DATA_NORM_
        vh = v[:, h * 64:(h + 1) * 64]

        def fm(z):
            xp = z @ proj[h].T
            xp = xp - xp.max(-1, keepdims=True)
            return (np.exp(xp) + 1e-6) / 16.0
        qp = fm(qh)
        kp = fm(kh)
        kv = kp.T @ vh
        ks = kp.sum(0)
        num = qp @ kv
        den = qp @ ks + 1e-6
        out[:, h * 64:(h + 1) * 64] = num / den[:, None]
    return out @ w_out + b_out


def _prep_B(inputs, resA):
    x = inputs["L0_attn_x"]
    wffnT = np.ascontiguousarray(inputs["L0_ffn_weight"].T)
    bffn = np.ascontiguousarray(inputs["L0_ffn_bias"][None, :])
    wout = np.ascontiguousarray(inputs["L0_attn_w_out"])
    bout = np.ascontiguousarray(inputs["L0_attn_b_out"][None, :])
    wip = np.ascontiguousarray(np.concatenate([inputs["L1_attn_Wi"], inputs["L1_attn_Wp"]], axis=1))
    gwT = np.ascontiguousarray(inputs["L1_ffn_gate_w"].T)
    wexp = np.ascontiguousarray(inputs["L1_ffn_expert_w"].transpose(0, 2, 1))
    amax8 = np.ascontiguousarray(np.stack([resA[c]["o_amax"][0, 0] for c in range(NCORE)])[None, :])
    iota8 = np.arange(8, dtype=np.float32)[None, :]
    import ml_dtypes
    half = 512
    inv_freq = (1.0 / 10000.0 ** (np.arange(half, dtype=np.float32) / half)).astype(np.float32)
    maps = []
    for c in range(NCORE):
        b = c // 4
        tok = tok_index(c)
        ang = (tok[:, None].astype(np.float32) * inv_freq[None, :]).astype(np.float32)
        cosb = np.cos(ang).astype(ml_dtypes.bfloat16)
        sinb = np.sin(ang).astype(ml_dtypes.bfloat16)
        kvstack = np.stack([resA[4 * b + j]["o_kv"] for j in range(4)])
        kv4 = np.ascontiguousarray(kvstack.transpose(3, 1, 2, 0, 4).reshape(128, -1))
        maps.append({
            "xa": np.ascontiguousarray(x[b][tok]),
            "xn1T": np.ascontiguousarray(resA[c]["o_xn1"].T),
            "qpT": np.ascontiguousarray(resA[c]["o_qp"].transpose(0, 2, 1)),
            "kv4": kv4, "amax8": amax8, "wffnT": wffnT, "bffn": bffn,
            "wout": wout, "bout": bout, "wip": wip, "gwT": gwT, "wexp": wexp,
            "cosb": cosb, "sinb": sinb, "ident": _IDENT, "iota8": iota8,
        })
    return maps


def run_B(inputs, resA):
    nc = build_B()
    maps = _prep_B(inputs, resA)
    res = run_bass_kernel_spmd(nc, maps, core_ids=list(range(NCORE)))
    return res.results




def _prep_C(inputs, resB):
    import ml_dtypes
    Pfull = [np.zeros((T, D), ml_dtypes.bfloat16) for _ in range(B)]
    Phfull = [np.zeros((T, D), ml_dtypes.bfloat16) for _ in range(B)]
    for c in range(NCORE):
        b = c // 4
        Pfull[b][tok_index(c)] = resB[c]["o_P"]
        Phfull[b][tok_index(c)] = resB[c]["o_Ph"]
    PTb = [np.ascontiguousarray(Pfull[b].T) for b in range(B)]
    w1p = np.zeros((32, 64), np.float32)
    w1p[0:17] = inputs["L1_attn_W1"]
    cmask = np.where(np.arange(128)[None, :] <= np.arange(128)[:, None], 0.0, NEG).astype(np.float32)
    iotas = np.tile(np.arange(512, dtype=np.int32)[None, :], (128, 1))
    iot128 = np.arange(128, dtype=np.float32)[:, None]
    ioy = np.tile(np.repeat(np.arange(16, dtype=np.float32), 16)[None, :], (128, 1)).astype(ml_dtypes.bfloat16)
    tt = (np.arange(14)[:, None] >= np.arange(16)[None, :])
    kmG = np.ones((32, 256), np.float32)
    for t in range(14):
        for k in range(16):
            kk = bool(tt[t, k]) if k < 16 else False
            for j in range(16):
                keep_j = (j <= t) if j <= 14 else True
                if j == 15:
                    keep_j = True
                kmG[j, 16 * t + k] = float((k <= t) and (keep_j if j < 15 else True) and ((j <= t) if j < 15 else True))
            kmG[16, 16 * t + k] = float(k <= t)
    kmG_b = kmG[:, 0:256].astype(ml_dtypes.bfloat16)
    kmM = np.zeros((128, 16), np.float32)
    for t in range(14):
        for k in range(16):
            if k > t:
                kmM[t, k] = NEG
    maps = []
    ncs = {}
    for c in range(NCORE):
        b = c // 4
        blocks = tok_blocks(c)
        key = c % 4
        if key not in ncs:
            ncs[key] = build_C(blocks)
        maps.append({
            "PT": PTb[b], "Phat": np.ascontiguousarray(Phfull[b]),
            "If_": np.ascontiguousarray(resB[c]["o_I"].T),
            "Ihf": np.ascontiguousarray(resB[c]["o_Ih"].T),
            "x3pT": np.ascontiguousarray(resB[c]["o_x3p"].T),
            "w1p": w1p, "b1": np.ascontiguousarray(inputs["L1_attn_b1"][:, None]),
            "w2": np.ascontiguousarray(inputs["L1_attn_W2"]),
            "b2": np.ascontiguousarray(inputs["L1_attn_b2"][:, None]),
            "wc": np.ascontiguousarray(inputs["L1_attn_Wc"]),
            "bc": np.ascontiguousarray(inputs["L1_attn_bc"][:, None]),
            "wm": np.ascontiguousarray(inputs["L1_attn_Wm"]),
            "bm": np.ascontiguousarray(inputs["L1_attn_bm"][:, None]),
            "pal": np.ascontiguousarray(inputs["L1_attn_palette"].reshape(D, 256)),
            "wo": np.ascontiguousarray(inputs["L1_attn_Wo"][0]),
            "ident": _IDENT, "cmask": cmask, "iotas": iotas, "iot128": iot128,
            "ioy": np.ascontiguousarray(ioy), "kmG": np.ascontiguousarray(kmG_b), "kmM": kmM,
        })
    return ncs, maps


def run_C(inputs, resB):
    ncs, maps = _prep_C(inputs, resB)
    out = [None] * NCORE
    for key in range(4):
        cores = [key, key + 4]
        res = run_bass_kernel_spmd(ncs[key], [maps[c] for c in cores], core_ids=cores)
        for i, c in enumerate(cores):
            out[c] = res.results[i]
    return out


def kernel(**inputs):
    inputs = {k: np.asarray(v) for k, v in inputs.items()}
    resA = run_A(inputs)
    resB = run_B(inputs, resA)
    resC = run_C(inputs, resB)
    out = np.zeros((B, T, D), dtype=np.float32)
    for c in range(NCORE):
        b = c // 4
        out[b][tok_index(c)] = resC[c]["o_yf"].T
    return out


# version 03
# speedup vs baseline: 2.6697x; 2.6697x over previous
import math
import numpy as np
from contextlib import ExitStack

import concourse.bass as bass
import concourse.mybir as mybir
from concourse.bass_utils import run_bass_kernel_spmd

F32 = mybir.dt.float32
BF16 = mybir.dt.bfloat16
FP8 = mybir.dt.float8e4
I32 = mybir.dt.int32
U32 = mybir.dt.uint32
AF = mybir.ActivationFunctionType
ALU = mybir.AluOpType
AXL = mybir.AxisListType

B, T, D, NCORE = 2, 8192, 1024, 8
TPC = 2048
NH, HD, MF = 16, 64, 256
KSEL = 15
LN_EPS = 1e-5
PEPS = 1e-6
DATA_NORM = HD ** -0.25
NEG = -1.0e38
ENGS = ("sync", "vector", "scalar", "tensor", "gpsimd")
NT = TPC // 128


def tok_blocks(c):
    return list(range(c % 4, 64, 4))


def tok_index(c):
    return np.concatenate([np.arange(128) + 128 * bb for bb in tok_blocks(c)])


class Prog:

    def __init__(self, nc):
        self.nc = nc
        self.q = {e: [] for e in ENGS}
        self.cnt = {}

    def mksem(self, ctx, name):
        h = ctx.enter_context(self.nc.semaphore(name))
        self.cnt[name] = 0
        return h

    def op(self, eng, build, waits=(), inc=None, amt=1, dma=False):
        waits = tuple(w for w in waits if w is not None)
        tok = None
        if inc is not None:
            self.cnt[inc.name] += amt
            tok = (inc, self.cnt[inc.name])
        need_drain = eng in ("vector", "scalar", "gpsimd", "tensor") and not dma

        def f(e, build=build, waits=waits, inc=inc, amt=amt, need_drain=need_drain):
            for (s, v) in waits:
                e.wait_ge(s, v)
            ins = build(e)
            if need_drain:
                ins = e.drain()
            if inc is not None:
                ins.then_inc(inc, amt)

        self.q[eng].append(f)
        return tok

    def barrier(self, sems):
        toks = [(s, self.cnt[s.name]) for s in sems]
        for e in ENGS:
            self.fence(e, toks)

    def fence(self, eng, waits):
        waits = tuple(w for w in waits if w is not None)

        def f(e, waits=waits):
            for (s, v) in waits:
                e.wait_ge(s, v)

        self.q[eng].append(f)

    def setup_consts(self, ctx, values, sem=None):
        sem = self.mksem(ctx, "scst")
        E = ctx.enter_context
        vals = sorted(set(float(v) for v in values))
        tile = E(self.nc.sbuf_tensor("consts", [128, max(1, len(vals))], F32))
        for i, v in enumerate(vals):
            self.nc.const_aps.register(F32, v, tile[:, i:i + 1])

        def f(e, vals=vals, tile=tile):
            ins = None
            for i, v in enumerate(vals):
                ins = e.memset(tile[:, i:i + 1], v)
            ins.then_inc(sem, 1)
        self.cnt[sem.name] += 1
        tok = (sem, self.cnt[sem.name])
        self.q["gpsimd"].append(f)
        for e in ENGS:
            if e != "gpsimd":
                self.fence(e, [tok])
        return tok

    def emit(self, block):
        for e in ENGS:
            fns = self.q[e]

            def runner(eng, fns=fns):
                for f in fns:
                    f(eng)

            getattr(block, e)(runner)


def build_A(hp_count=8, ln_only=False, stage='kv'):
    nc = bass.Bass("TRN2", target_bir_lowering=False)
    xa = nc.dram_tensor("xa", [TPC, D], F32, kind="ExternalInput").ap()
    wqkv = nc.dram_tensor("wqkv", [D, 3 * D], F32, kind="ExternalInput").ap()
    bqkv = nc.dram_tensor("bqkv", [128, 24], F32, kind="ExternalInput").ap()
    proj = nc.dram_tensor("proj", [128, 8 * 2 * MF], F32, kind="ExternalInput").ap()
    ident = nc.dram_tensor("ident", [128, 128], F32, kind="ExternalInput").ap()

    o_xn1 = nc.dram_tensor("o_xn1", [TPC, D], F32, kind="ExternalOutput").ap()
    o_qp = nc.dram_tensor("o_qp", [NH, TPC, MF], BF16, kind="ExternalOutput").ap()
    o_kv = nc.dram_tensor("o_kv", [NH, 2, 128, 65], F32, kind="ExternalOutput").ap()
    o_amax = nc.dram_tensor("o_amax", [1, 1], F32, kind="ExternalOutput").ap()

    with ExitStack() as ctx:
        E = ctx.enter_context
        sx = [E(nc.sbuf_tensor(f"sx{j}", [128, D], F32)) for j in range(2)]
        sxn = [E(nc.sbuf_tensor(f"sxn{j}", [128, D], F32)) for j in range(2)]
        ssq = E(nc.sbuf_tensor("ssq", [128, D], F32))
        sstat = [E(nc.sbuf_tensor(f"sstat{j}", [128, 8], F32)) for j in range(2)]
        samax = E(nc.sbuf_tensor("samax", [128, 1], F32))
        samax_r = E(nc.sbuf_tensor("samax_r", [1, 128], F32))
        samax_f = E(nc.sbuf_tensor("samax_f", [1, 1], F32))
        sident = E(nc.sbuf_tensor("sident", [128, 128], F32))
        sident_b = E(nc.sbuf_tensor("sident_b", [128, 128], BF16))
        sxnT = E(nc.sbuf_tensor("sxnT", [128, 8 * TPC], BF16))
        swq = E(nc.sbuf_tensor("swq", [128, 8 * 3 * D], BF16))
        sbias = E(nc.sbuf_tensor("sbias", [128, 24], F32))
        sproj = E(nc.sbuf_tensor("sproj", [128, 8 * 2 * MF], BF16))
        sproj0 = E(nc.sbuf_tensor("sproj0", [128, 8 * 2 * MF], BF16))
        sq = E(nc.sbuf_tensor("sq", [128, TPC], BF16))
        sk = E(nc.sbuf_tensor("sk", [128, TPC], BF16))
        sv = E(nc.sbuf_tensor("svv", [128, TPC], BF16))
        sexp = E(nc.sbuf_tensor("sexp", [128, 512], F32))
        snmax = E(nc.sbuf_tensor("snmax", [128, 4], F32))
        sqp = E(nc.sbuf_tensor("sqp", [128, 512], BF16))
        skp = E(nc.sbuf_tensor("skp", [128, 16 * 512], BF16))
        svt = E(nc.sbuf_tensor("svt", [128, 16 * 130], BF16))
        skv = E(nc.sbuf_tensor("skv", [128, 65], F32))

        p_mm = [E(nc.psum_tensor(f"p_mm{j}", [128, 512], F32)) for j in range(2)]
        p_xp = [E(nc.psum_tensor(f"p_xp{j}", [128, 512], F32)) for j in range(2)]
        p_vt = E(nc.psum_tensor("p_vt", [128, 64], BF16))
        p_kv = [E(nc.psum_tensor(f"p_kv{j}", [128, 65], F32)) for j in range(2)]
        p_tr = [p_xp[0][:, 0:128], p_xp[1][:, 0:128]]

        p = Prog(nc)
        ld = p.mksem(ctx, "ld")
        lda = [p.mksem(ctx, "lda0"), p.mksem(ctx, "lda1")]
        ldg = p.mksem(ctx, "ldg")
        sv_ = p.mksem(ctx, "sv_")
        ss = p.mksem(ctx, "ss")
        st_ = p.mksem(ctx, "st_")
        sg = p.mksem(ctx, "sg")
        sgx = [p.mksem(ctx, "sgx0"), p.mksem(ctx, "sgx1")]
        sgq = p.mksem(ctx, "sgq")
        sgk = p.mksem(ctx, "sgk")
        block = E(nc.Block())
        p.setup_consts(ctx, [0.0, 1.0, -0.5, -1.0, LN_EPS, 1.0 / D, DATA_NORM, 1.0 / 16.0, PEPS / 16.0], sg)

        t_ident = p.op("sync", lambda e: e.dma_start(out=sident[:], in_=ident), inc=ld, amt=16)
        t_bias = p.op("sync", lambda e: e.dma_start(out=sbias[:], in_=bqkv), inc=ld, amt=16)
        t_ident = t_bias = (ld, 32)
        t_wq = []
        for k in range(8):
            t_wq.append(p.op("gpsimd",
                             lambda e, k=k: e.dma_start(out=swq[:, k * 3 * D:(k + 1) * 3 * D],
                                                        in_=wqkv[k * 128:(k + 1) * 128, :]),
                             inc=ldg, amt=16, dma=True))
        t_proj = p.op("gpsimd", lambda e: e.dma_start(out=sproj0[:, :], in_=proj), inc=ldg, amt=16, dma=True)
        t_identb = p.op("vector", lambda e: e.tensor_copy(out=sident_b[:], in_=sident[:]),
                        waits=[t_ident], inc=sv_)
        t_ldg_all = (ldg, p.cnt["ldg"])
        t_projs = p.op("vector",
                       lambda e: e.tensor_scalar(out=sproj[:], in0=sproj0[:], scalar1=DATA_NORM,
                                                 scalar2=None, op0=ALU.mult),
                       waits=[t_ldg_all], inc=sv_)
        sgm = p.mksem(ctx, "sgm")
        t_ones = p.op("gpsimd", lambda e: e.memset(svt[:], 1.0), inc=sgm, amt=1)

        xn_done = [None] * 16
        mean_done = [None] * 16
        ev_done = [None] * 16
        out_done = [None] * 16
        am_prev = None
        trbank_prev = {0: None, 1: None}
        for i in range(16):
            j = i % 2
            tl = p.op("sync", lambda e, i=i, j=j: e.dma_start(out=sx[j][:], in_=xa[i * 128:(i + 1) * 128, :]),
                      waits=([xn_done[i - 2], mean_done[i - 2]] if i >= 2 else []), inc=lda[j], amt=16)
            p.op("vector", lambda e, j=j: e.reduce_sum(out=sstat[j][:, 0:1], in_=sx[j][:], axis=AXL.X),
                 waits=[tl] + ([xn_done[i - 2]] if i >= 2 else []), inc=sv_)
            mean_done[i] = p.op("vector", lambda e, j=j: e.tensor_scalar(
                out=sstat[j][:, 0:1], in0=sstat[j][:, 0:1], scalar1=1.0 / D, scalar2=None, op0=ALU.mult),
                inc=sv_)
            t_sq = p.op("scalar", lambda e, j=j: e.activation(out=ssq[:], in_=sx[j][:], func=AF.Square,
                                                              accum_out=sstat[j][:, 1:2]),
                        waits=[tl, mean_done[i]], inc=ss)
            p.op("vector", lambda e, j=j: e.tensor_scalar(out=sstat[j][:, 2:3], in0=sstat[j][:, 1:2],
                                                          scalar1=1.0 / D, scalar2=None, op0=ALU.mult),
                 waits=[t_sq], inc=sv_)
            p.op("vector", lambda e, j=j: e.tensor_tensor(out=sstat[j][:, 3:4], in0=sstat[j][:, 0:1],
                                                          in1=sstat[j][:, 0:1], op=ALU.mult), inc=sv_)
            t_var = p.op("vector", lambda e, j=j: e.tensor_tensor(out=sstat[j][:, 2:3], in0=sstat[j][:, 2:3],
                                                                  in1=sstat[j][:, 3:4], op=ALU.subtract), inc=sv_)
            p.op("scalar", lambda e, j=j: e.activation(out=sstat[j][:, 4:5], in_=sstat[j][:, 2:3],
                                                       func=AF.Ln, bias=LN_EPS, scale=1.0),
                 waits=[t_var], inc=ss)
            t_rstd = p.op("scalar", lambda e, j=j: e.activation(out=sstat[j][:, 4:5], in_=sstat[j][:, 4:5],
                                                                func=AF.Exp, scale=-0.5), inc=ss)
            p.op("vector", lambda e, j=j: e.tensor_tensor(out=sstat[j][:, 5:6], in0=sstat[j][:, 0:1],
                                                          in1=sstat[j][:, 4:5], op=ALU.mult),
                 waits=[t_rstd], inc=sv_)
            t_bneg = p.op("vector", lambda e, j=j: e.tensor_scalar(out=sstat[j][:, 5:6], in0=sstat[j][:, 5:6],
                                                                   scalar1=-1.0, scalar2=None, op0=ALU.mult),
                          inc=sv_)
            xn_done[i] = p.op("vector", lambda e, j=j: e.tensor_scalar(
                out=sxn[j][:], in0=sx[j][:], scalar1=sstat[j][:, 4:5], scalar2=sstat[j][:, 5:6],
                op0=ALU.mult, op1=ALU.add),
                waits=[t_bneg] + ([ev_done[i - 2], out_done[i - 2]] if i >= 2 else []), inc=sv_)
            p.op("vector", lambda e, j=j: e.tensor_reduce(out=sstat[j][:, 6:7], in_=sxn[j][:], axis=AXL.X,
                                                          op=ALU.max, apply_absolute_value=True),
                 waits=[xn_done[i]], inc=sv_)
            if i == 0:
                am_prev = p.op("vector", lambda e, j=j: e.tensor_copy(out=samax[:], in_=sstat[j][:, 6:7]),
                               inc=sv_)
            else:
                am_prev = p.op("vector", lambda e, j=j: e.tensor_tensor(out=samax[:], in0=samax[:],
                                                                        in1=sstat[j][:, 6:7], op=ALU.max),
                               inc=sv_)
            out_done[i] = p.op("gpsimd", lambda e, i=i, j=j: e.dma_start(
                out=o_xn1[i * 128:(i + 1) * 128, :], in_=sxn[j][:]),
                waits=[xn_done[i]], inc=sgx[j], amt=16, dma=True)
            ev_tok = {}
            for k in range(8):
                tb = k % 2
                wt = [xn_done[i], t_ident]
                if k >= 2:
                    wt.append(ev_tok[k - 2])
                else:
                    wt.append(trbank_prev[tb])
                tt = p.op("tensor", lambda e, j=j, k=k, tb=tb: e.transpose(
                    out=p_tr[tb], in_=sxn[j][:, k * 128:(k + 1) * 128], identity=sident[:]),
                    waits=wt, inc=st_)
                ev_tok[k] = p.op("vector", lambda e, i=i, k=k, tb=tb: e.tensor_copy(
                    out=sxnT[:, k * TPC + i * 128: k * TPC + (i + 1) * 128], in_=p_tr[tb]),
                    waits=[tt], inc=sv_)
            trbank_prev = {0: ev_tok[6], 1: ev_tok[7]}
            ev_done[i] = ev_tok[7]

        t_amT = p.op("tensor", lambda e: e.transpose(out=p_xp[0][0:1, 0:128], in_=samax[:],
                                                     identity=sident[:]),
                     waits=[am_prev, trbank_prev[0]], inc=st_)
        t_amE = p.op("vector", lambda e: e.tensor_copy(out=samax_r[:], in_=p_xp[0][0:1, 0:128]),
                     waits=[t_amT], inc=sv_)
        t_amR = p.op("vector", lambda e: e.tensor_reduce(out=samax_f[:], in_=samax_r[:], axis=AXL.X,
                                                         op=ALU.max), waits=[t_amE], inc=sv_)
        t_amO = p.op("gpsimd", lambda e: e.dma_start(out=o_amax, in_=samax_f[:]),
                     waits=[t_amR], inc=sg, amt=16, dma=True)

        if ln_only:
            p.fence("gpsimd", [(sg, p.cnt["sg"]), (sgx[0], p.cnt["sgx0"]), (sgx[1], p.cnt["sgx1"]),
                               t_amO, (sv_, p.cnt["sv_"]), (ss, p.cnt["ss"]), (st_, p.cnt["st_"])])
            p.emit(block)
            return nc
        all_xnT = ev_done[15]
        mm_prev = {0: None, 1: None}
        xp_prev = {0: t_amE, 1: None}
        vt_prev = None
        kv_prev = {0: None, 1: None}
        exp_cons = None
        nm_cons = None
        qp_cons = None
        kv_last_mm = None
        skv_dma_prev = None
        for hp in range(hp_count):
            slab_toks = {}
            for sl, (dst, col0) in enumerate(((sq, 128 * hp), (sk, D + 128 * hp), (sv, 2 * D + 128 * hp))):
                last_ev = None
                for ntc in range(4):
                    pb = (sl * 4 + ntc) % 2
                    t_mm = None
                    for kt in range(8):
                        wt = []
                        if kt == 0:
                            wt = [mm_prev[pb], all_xnT] + ([t_ldg_all, t_bias] if hp == 0 and sl == 0 and ntc == 0 else [])
                            if sl == 0 and ntc == 0:
                                wt.append(kv_last_mm)
                                wt.append(exp_cons)
                        t_mm = p.op("tensor", lambda e, kt=kt, col0=col0, ntc=ntc, pb=pb: e.matmul(
                            out=p_mm[pb][:],
                            lhsT=swq[:, kt * 3 * D + col0: kt * 3 * D + col0 + 128],
                            rhs=sxnT[:, kt * TPC + ntc * 512: kt * TPC + (ntc + 1) * 512],
                            start=(kt == 0), stop=(kt == 7)),
                            waits=wt, inc=(st_ if kt == 7 else None))
                    bias_col = col0 // 128
                    last_ev = p.op("vector", lambda e, dst=dst, ntc=ntc, pb=pb, bias_col=bias_col: e.tensor_scalar(
                        out=dst[:, ntc * 512:(ntc + 1) * 512], in0=p_mm[pb][:],
                        scalar1=sbias[:, bias_col:bias_col + 1], scalar2=None, op0=ALU.add),
                        waits=[t_mm], inc=sv_)
                    mm_prev[pb] = last_ev
                slab_toks[sl] = last_ev

            for ic in range(NT if stage != 'slab' else 0):
                for h2 in range(2):
                    wt = [slab_toks[2], t_ones, vt_prev, t_identb]
                    ttv = p.op("tensor", lambda e, ic=ic, h2=h2: e.transpose(
                        out=p_vt[:], in_=sv[h2 * 64:(h2 + 1) * 64, ic * 128:(ic + 1) * 128],
                        identity=sident_b[h2 * 64:(h2 + 1) * 64, h2 * 64:(h2 + 1) * 64]),
                        waits=wt, inc=st_)
                    vt_prev = p.op("vector", lambda e, ic=ic, h2=h2: e.tensor_copy(
                        out=svt[:, ic * 130 + h2 * 65: ic * 130 + h2 * 65 + 64], in_=p_vt[0:128, 0:64]),
                        waits=[ttv, kv_last_mm if ic == 0 and h2 == 0 else None], inc=sv_)
            vt_all = vt_prev

            kp_last = None
            for src, is_q in (((sq, True), (sk, False)) if stage in ('fm', 'kv') else ()):
                for ic in range(NT):
                    pb = ic % 2
                    mm2 = p.op("tensor", lambda e, src=src, ic=ic, pb=pb, hp=hp: e.matmul(
                        out=p_xp[pb][:], lhsT=src[:, ic * 128:(ic + 1) * 128],
                        rhs=sproj[:, hp * 512:(hp + 1) * 512], start=True, stop=True),
                        waits=[slab_toks[0] if is_q else slab_toks[1], t_projs, xp_prev[pb]], inc=st_)
                    t_nm = p.op("vector", lambda e, pb=pb: e.tensor_reduce(
                        out=snmax[:, 0:2], in_=p_xp[pb][:].rearrange("p (h m) -> p h m", h=2), axis=AXL.X,
                        op=ALU.max, negate=True),
                        waits=[mm2, nm_cons], inc=sv_)
                    e1 = p.op("scalar", lambda e, pb=pb: e.activation(
                        out=sexp[:, 0:256], in_=p_xp[pb][:, 0:256], func=AF.Exp,
                        bias=snmax[:, 0:1], scale=1.0), waits=[t_nm, exp_cons], inc=ss)
                    e2 = p.op("scalar", lambda e, pb=pb: e.activation(
                        out=sexp[:, 256:512], in_=p_xp[pb][:, 256:512], func=AF.Exp,
                        bias=snmax[:, 1:2], scale=1.0), inc=ss)
                    xp_prev[pb] = e2
                    if is_q:
                        t_qp = p.op("vector", lambda e: e.tensor_scalar(
                            out=sqp[:], in0=sexp[:], scalar1=1.0 / 16.0, scalar2=PEPS / 16.0,
                            op0=ALU.mult, op1=ALU.add), waits=[e2, qp_cons], inc=sv_)
                        exp_cons = t_qp
                        nm_cons = e2
                        qp_cons = p.op("gpsimd", lambda e, ic=ic, hp=hp: e.dma_start(
                            out=o_qp.rearrange("h t m -> t h m")[ic * 128:(ic + 1) * 128,
                                                                 2 * hp:2 * hp + 2, :],
                            in_=sqp[:].rearrange("p (h m) -> p h m", h=2)),
                            waits=[t_qp], inc=sgq, amt=16, dma=True)
                    else:
                        kp_last = p.op("vector", lambda e, ic=ic: e.tensor_scalar(
                            out=skp[:, ic * 512:(ic + 1) * 512], in0=sexp[:], scalar1=1.0 / 16.0,
                            scalar2=PEPS / 16.0, op0=ALU.mult, op1=ALU.add),
                            waits=[e2, kv_last_mm if ic == 0 else None], inc=sv_)
                        exp_cons = kp_last
                        nm_cons = e2

            for h2 in range(2 if stage == 'kv' else 0):
                head = 2 * hp + h2
                for mt in range(2):
                    t_mm = None
                    for ic in range(NT):
                        wt = [kp_last, vt_all, kv_prev[mt]] if ic == 0 else []
                        t_mm = p.op("tensor", lambda e, ic=ic, h2=h2, mt=mt: e.matmul(
                            out=p_kv[mt][:],
                            lhsT=skp[:, ic * 512 + h2 * 256 + mt * 128: ic * 512 + h2 * 256 + (mt + 1) * 128],
                            rhs=svt[:, ic * 130 + h2 * 65: ic * 130 + h2 * 65 + 65],
                            start=(ic == 0), stop=(ic == NT - 1)),
                            waits=wt, inc=(st_ if ic == NT - 1 else None))
                        kv_last_mm = t_mm if ic == NT - 1 else kv_last_mm
                    evk = p.op("vector", lambda e, mt=mt: e.tensor_copy(out=skv[:], in_=p_kv[mt][:]),
                               waits=[t_mm, skv_dma_prev], inc=sv_)
                    skv_dma_prev = p.op("gpsimd", lambda e, head=head, mt=mt: e.dma_start(
                        out=o_kv[head, mt, :, :], in_=skv[:]), waits=[evk], inc=sgk, amt=16, dma=True)
                    kv_prev[mt] = skv_dma_prev

        p.fence("gpsimd", [(sg, p.cnt["sg"]), (sgx[0], p.cnt["sgx0"]), (sgx[1], p.cnt["sgx1"]),
                           (sgq, p.cnt["sgq"]), (sgk, p.cnt["sgk"]), t_amO])
        p.emit(block)
    return nc



def build_B():
    nc = bass.Bass("TRN2", target_bir_lowering=False)
    xa = nc.dram_tensor("xa", [TPC, D], F32, kind="ExternalInput").ap()
    xn1T = nc.dram_tensor("xn1T", [D, TPC], F32, kind="ExternalInput").ap()
    qpT = nc.dram_tensor("qpT", [NH, MF, TPC], BF16, kind="ExternalInput").ap()
    kv4 = nc.dram_tensor("kv4", [128, NH * 2 * 4 * 65], F32, kind="ExternalInput").ap()
    amax8 = nc.dram_tensor("amax8", [1, 8], F32, kind="ExternalInput").ap()
    wffnT = nc.dram_tensor("wffnT", [D, D], F32, kind="ExternalInput").ap()
    bffn = nc.dram_tensor("bffn", [1, D], F32, kind="ExternalInput").ap()
    wout = nc.dram_tensor("wout", [D, D], F32, kind="ExternalInput").ap()
    bout = nc.dram_tensor("bout", [1, D], F32, kind="ExternalInput").ap()
    wip = nc.dram_tensor("wip", [D, 2 * D], F32, kind="ExternalInput").ap()
    gwT = nc.dram_tensor("gwT", [D, 8], F32, kind="ExternalInput").ap()
    wexp = nc.dram_tensor("wexp", [8, D, D], F32, kind="ExternalInput").ap()
    cosb = nc.dram_tensor("cosb", [TPC, 512], BF16, kind="ExternalInput").ap()
    sinb = nc.dram_tensor("sinb", [TPC, 512], BF16, kind="ExternalInput").ap()
    ident = nc.dram_tensor("ident", [128, 128], F32, kind="ExternalInput").ap()
    iota8 = nc.dram_tensor("iota8", [1, 8], F32, kind="ExternalInput").ap()

    o_x3p = nc.dram_tensor("o_x3p", [TPC, D], F32, kind="ExternalOutput").ap()
    o_d1 = nc.dram_tensor("o_d1", [128, D], F32, kind="ExternalOutput").ap()
    o_d2 = nc.dram_tensor("o_d2", [128, D], F32, kind="ExternalOutput").ap()
    o_d3 = nc.dram_tensor("o_d3", [128, TPC], F32, kind="ExternalOutput").ap()
    o_d0 = nc.dram_tensor("o_d0", [1, 16], F32, kind="ExternalOutput").ap()
    o_P = nc.dram_tensor("o_P", [TPC, D], BF16, kind="ExternalOutput").ap()
    o_Ph = nc.dram_tensor("o_Ph", [TPC, D], BF16, kind="ExternalOutput").ap()
    o_I = nc.dram_tensor("o_I", [TPC, D], BF16, kind="ExternalOutput").ap()
    o_Ih = nc.dram_tensor("o_Ih", [TPC, D], BF16, kind="ExternalOutput").ap()

    with ExitStack() as ctx:
        E = ctx.enter_context
        p = Prog(nc)
        ld1 = p.mksem(ctx, "ld1")
        ldg1 = p.mksem(ctx, "ldg1")
        ldx = [p.mksem(ctx, "ldx0"), p.mksem(ctx, "ldx1")]
        ldw = [p.mksem(ctx, "ldw0"), p.mksem(ctx, "ldw1")]
        lda = [p.mksem(ctx, "lda0"), p.mksem(ctx, "lda1")]
        sv_ = p.mksem(ctx, "sv_")
        ss = p.mksem(ctx, "ss")
        st_ = p.mksem(ctx, "st_")
        sgo = [p.mksem(ctx, "sgo0"), p.mksem(ctx, "sgo1")]
        block = E(nc.Block())
        p.setup_consts(ctx, [0.0, 1.0, -0.5, -1.0, LN_EPS, 1.0 / D, 448.0, 4.0, 0.5, 1e-12,
                             1000.0, 1e-24, -1000.0], sgo[0])

        sident = E(nc.sbuf_tensor("sident", [128, 128], F32))
        sident_b = E(nc.sbuf_tensor("sident_b", [128, 128], BF16))
        sones = E(nc.sbuf_tensor("sones", [1, 128], F32))
        ssc = E(nc.sbuf_tensor("ssc", [1, 16], F32))
        sscb = E(nc.sbuf_tensor("sscb", [128, 4], F32))
        sx2 = E(nc.sbuf_tensor("sx2", [128, NT * D], F32))
        sstat = E(nc.sbuf_tensor("sstat", [128, 8], F32))
        sbffn = E(nc.sbuf_tensor("sbffn", [128, D], F32))
        sbout = E(nc.sbuf_tensor("sbout", [128, D], F32))
        ssc_f = E(nc.sbuf_tensor("ssc_f", [8, TPC], F32))
        sgw32 = E(nc.sbuf_tensor("sgw32", [128, 64], F32))

        p_a = [E(nc.psum_tensor(f"p_a{j}", [128, 512], F32)) for j in range(4)]
        p_b = [E(nc.psum_tensor(f"p_b{j}", [128, 512], F32)) for j in range(2)]
        p_t = [E(nc.psum_tensor(f"p_t{j}", [128, 128], F32)) for j in range(2)]

        p.op("sync", lambda e: e.dma_start(out=sident[:], in_=ident), inc=ld1, amt=16)
        p.op("sync", lambda e: e.dma_start(out=ssc[0:1, 0:8], in_=amax8), inc=ld1, amt=16)
        p.op("sync", lambda e: e.dma_start(out=sbffn[:], in_=bffn.partition_broadcast(128)), inc=ld1, amt=16)
        p.op("sync", lambda e: e.dma_start(out=sbout[:], in_=bout.partition_broadcast(128)), inc=ld1, amt=16)
        t_ld1 = (ld1, p.cnt["ld1"])
        t_identb = p.op("vector", lambda e: e.tensor_copy(out=sident_b[:], in_=sident[:]),
                        waits=[t_ld1], inc=sv_)
        sgm = p.mksem(ctx, "sgm")
        t_ones = p.op("gpsimd", lambda e: e.memset(sones[:], 1.0), inc=sgm)

        c12 = ctx.enter_context(ExitStack())
        sffo = c12.enter_context(nc.sbuf_tensor("sffo", [128, NT * D], BF16))
        with ExitStack() as c1:
            E1 = c1.enter_context
            sxf = [E1(nc.sbuf_tensor(f"sxf{j}", [128, TPC], F32)) for j in range(2)]
            swfb = [E1(nc.sbuf_tensor(f"swfb{j}", [128, D], F32)) for j in range(2)]
            sx8 = E1(nc.sbuf_tensor("sx8", [128, 8 * TPC], FP8))
            sw8 = E1(nc.sbuf_tensor("sw8", [128, 8 * D], FP8))
            t_am = p.op("vector", lambda e: e.tensor_reduce(out=ssc[0:1, 8:9], in_=ssc[0:1, 0:8],
                                                            axis=AXL.X, op=ALU.max),
                        waits=[t_ld1], inc=sv_)
            wl_prev = {0: None, 1: None}
            am_acc = None
            for k in range(8):
                j = k % 2
                t_wl = p.op("sync", lambda e, k=k, j=j: e.dma_start(
                    out=swfb[j][:], in_=wffnT[k * 128:(k + 1) * 128, :]),
                    waits=[wl_prev[j]], inc=lda[j], amt=16)
                t_r = p.op("vector", lambda e, j=j, k=k: e.tensor_reduce(
                    out=sstat[:, 1:2], in_=swfb[j][:], axis=AXL.X, op=ALU.max,
                    apply_absolute_value=True), waits=[t_wl], inc=sv_)
                if k == 0:
                    am_acc = p.op("vector", lambda e: e.tensor_copy(out=sstat[:, 0:1], in_=sstat[:, 1:2]),
                                  inc=sv_)
                else:
                    am_acc = p.op("vector", lambda e: e.tensor_tensor(
                        out=sstat[:, 0:1], in0=sstat[:, 0:1], in1=sstat[:, 1:2], op=ALU.max), inc=sv_)
                wl_prev[j] = am_acc
            t_wT = p.op("tensor", lambda e: e.transpose(out=p_t[0][0:1, 0:128], in_=sstat[:, 0:1],
                                                        identity=sident[:]), waits=[am_acc, t_ld1], inc=st_)
            t_wr3 = p.op("vector", lambda e: e.tensor_reduce(out=ssc[0:1, 9:10],
                                                             in_=p_t[0][0:1, 0:128], axis=AXL.X,
                                                             op=ALU.max), waits=[t_wT], inc=sv_)

            def scales(e):
                e.tensor_scalar(out=ssc[0:1, 10:11], in0=ssc[0:1, 8:9], scalar1=1e-12, scalar2=None,
                                op0=ALU.max)
                e.drain()
                e.reciprocal(out=ssc[0:1, 10:11], in_=ssc[0:1, 10:11])
                e.drain()
                e.tensor_scalar(out=ssc[0:1, 10:11], in0=ssc[0:1, 10:11], scalar1=448.0, scalar2=448.0,
                                op0=ALU.mult, op1=ALU.min)
                e.drain()
                e.tensor_scalar(out=ssc[0:1, 11:12], in0=ssc[0:1, 9:10], scalar1=1e-12, scalar2=None,
                                op0=ALU.max)
                e.drain()
                e.reciprocal(out=ssc[0:1, 11:12], in_=ssc[0:1, 11:12])
                e.drain()
                e.tensor_scalar(out=ssc[0:1, 11:12], in0=ssc[0:1, 11:12], scalar1=448.0, scalar2=448.0,
                                op0=ALU.mult, op1=ALU.min)
                e.drain()
                e.tensor_tensor(out=ssc[0:1, 12:13], in0=ssc[0:1, 10:11], in1=ssc[0:1, 11:12],
                                op=ALU.mult)
                e.drain()
                e.reciprocal(out=ssc[0:1, 12:13], in_=ssc[0:1, 12:13])
                e.drain()
                e.tensor_scalar(out=ssc[0:1, 12:13], in0=ssc[0:1, 12:13], scalar1=4.0, scalar2=None,
                                op0=ALU.mult)
                e.drain()
                e.tensor_scalar(out=ssc[0:1, 10:11], in0=ssc[0:1, 10:11], scalar1=0.5, scalar2=None,
                                op0=ALU.mult)
                e.drain()
                return e.tensor_scalar(out=ssc[0:1, 11:12], in0=ssc[0:1, 11:12], scalar1=0.5,
                                       scalar2=None, op0=ALU.mult)
            t_sc = p.op("vector", scales, waits=[t_am, t_wr3], inc=sv_)
            t_bc = []
            for i_src, i_dst in ((10, 0), (11, 1), (12, 2)):
                tm = p.op("tensor", lambda e, i_src=i_src: e.matmul(
                    out=p_t[1][:, 0:1], lhsT=sones[:], rhs=ssc[0:1, i_src:i_src + 1],
                    start=True, stop=True), waits=[t_sc, t_ones, t_bc[-1] if t_bc else None], inc=st_)
                t_bc.append(p.op("vector", lambda e, i_dst=i_dst: e.tensor_copy(
                    out=sscb[:, i_dst:i_dst + 1], in_=p_t[1][:, 0:1]), waits=[tm], inc=sv_))
            t_x8 = []
            for k in range(8):
                j = k % 2
                t_xl = p.op("sync", lambda e, k=k, j=j: e.dma_start(
                    out=sxf[j][:], in_=xn1T[k * 128:(k + 1) * 128, :]),
                    waits=[t_x8[k - 2] if k >= 2 else t_sc], inc=lda[j], amt=16)
                t_x8.append(p.op("vector", lambda e, k=k, j=j: e.tensor_scalar(
                    out=sx8[:, k * TPC:(k + 1) * TPC], in0=sxf[j][:],
                    scalar1=sscb[:, 0:1], scalar2=None, op0=ALU.mult),
                    waits=[t_xl, t_bc[0]], inc=sv_))
            t_w8 = []
            for k in range(8):
                j = k % 2
                t_wl = p.op("sync", lambda e, k=k, j=j: e.dma_start(
                    out=swfb[j][:], in_=wffnT[k * 128:(k + 1) * 128, :]),
                    waits=[t_w8[k - 2] if k >= 2 else t_x8[6 + k]], inc=lda[j], amt=16)
                t_w8.append(p.op("vector", lambda e, k=k, j=j: e.tensor_scalar(
                    out=sw8[:, k * D:(k + 1) * D], in0=swfb[j][:],
                    scalar1=sscb[:, 1:2], scalar2=None, op0=ALU.mult),
                    waits=[t_wl, t_bc[1]], inc=sv_))
            prev_ev = {0: None, 1: None}
            for icc in range(NT):
                for mh in range(2):
                    pb = (icc * 2 + mh) % 2
                    t_mm = None
                    for kt in range(8):
                        wt = [prev_ev[pb], t_x8[7], t_w8[7]] if kt == 0 else []
                        t_mm = p.op("tensor", lambda e, icc=icc, mh=mh, kt=kt, pb=pb: e.matmul(
                            out=p_a[pb][:], lhsT=sx8[:, kt * TPC + icc * 128: kt * TPC + (icc + 1) * 128],
                            rhs=sw8[:, kt * D + mh * 512: kt * D + (mh + 1) * 512],
                            start=(kt == 0), stop=(kt == 7)),
                            waits=wt, inc=(st_ if kt == 7 else None))
                    prev_ev[pb] = p.op("vector", lambda e, icc=icc, mh=mh, pb=pb: e.scalar_tensor_tensor(
                        out=sffo[:, icc * D + mh * 512: icc * D + (mh + 1) * 512], in0=p_a[pb][:],
                        scalar=sscb[:, 2:3], in1=sbffn[:, mh * 512:(mh + 1) * 512],
                        op0=ALU.mult, op1=ALU.add),
                        waits=[t_mm, t_bc[2], t_ld1], inc=sv_)
            p.barrier([sv_, ss, st_])
        p.op("gpsimd", lambda e: e.dma_start(out=o_d1, in_=sffo[:, 0:D]), inc=sgo[0], amt=16, dma=True)
        p.op("gpsimd", lambda e: e.dma_start(out=o_d0, in_=ssc[:]), inc=sgo[0], amt=16, dma=True)

        with ExitStack() as c2:
            E2 = c2.enter_context
            skvb = E2(nc.sbuf_tensor("skvb", [128, NH * 2 * 65], BF16))
            with ExitStack() as c2a:
                skvs = c2a.enter_context(nc.sbuf_tensor("skvs", [128, NH * 2 * 4 * 65], F32))
                t_kv = p.op("sync", lambda e: e.dma_start(out=skvs[:], in_=kv4), inc=ld1, amt=16)

                def kvsum(e):
                    v = skvs[:].rearrange("p (h m c f) -> p (h m) c f", h=NH, m=2, c=4)
                    e.tensor_tensor(out=v[:, :, 0, :], in0=v[:, :, 0, :], in1=v[:, :, 1, :], op=ALU.add)
                    e.drain()
                    e.tensor_tensor(out=v[:, :, 2, :], in0=v[:, :, 2, :], in1=v[:, :, 3, :], op=ALU.add)
                    e.drain()
                    e.tensor_tensor(out=v[:, :, 0, :], in0=v[:, :, 0, :], in1=v[:, :, 2, :], op=ALU.add)
                    e.drain()
                    return e.tensor_copy(out=skvb[:].rearrange("p (g f) -> p g f", f=65),
                                         in_=v[:, :, 0, :])
                t_kvs = p.op("vector", kvsum, waits=[t_kv], inc=sv_)
                p.barrier([sv_])
            shc = E2(nc.sbuf_tensor("shc", [128, 8 * TPC], BF16))
            sqp2 = [E2(nc.sbuf_tensor(f"sqp2{j}", [128, 2 * TPC], BF16)) for j in range(2)]
            srec = E2(nc.sbuf_tensor("srec", [1, 512], F32))
            srecb = E2(nc.sbuf_tensor("srecb", [128, 512], BF16))
            swo = E2(nc.sbuf_tensor("swo", [128, 8 * D], BF16))
            sxc = [E2(nc.sbuf_tensor(f"sxc{j}", [128, D], F32)) for j in range(2)]
            for k in range(8):
                p.op("gpsimd", lambda e, k=k: e.dma_start(
                    out=swo[:, k * D:(k + 1) * D], in_=wout[k * 128:(k + 1) * 128, :]),
                    inc=ldg1, amt=16, dma=True)
            t_ldg_all2 = (ldg1, p.cnt["ldg1"])
            hc_toks = []
            pnum_prev = {0: None, 1: None}
            qh_prev = {0: None, 1: None}
            for h in range(NH):
                jh = h % 2
                t_qh = p.op("sync", lambda e, h=h, jh=jh: e.dma_start(
                    out=sqp2[jh][:].rearrange("p (a t) -> p a t", a=2),
                    in_=qpT[h].rearrange("(a p) t -> p a t", p=128)),
                    waits=[qh_prev[jh]], inc=lda[jh], amt=16)
                for icc in range(4):
                    pb = icc % 2
                    t_mm = None
                    for mt in range(2):
                        wt = [t_kvs, t_qh, pnum_prev[pb]] if mt == 0 else []
                        t_mm = p.op("tensor", lambda e, h=h, jh=jh, icc=icc, mt=mt, pb=pb: e.matmul(
                            out=p_a[pb][0:65, :],
                            lhsT=skvb[:, (h * 2 + mt) * 65:(h * 2 + mt + 1) * 65],
                            rhs=sqp2[jh][:, mt * TPC + icc * 512: mt * TPC + (icc + 1) * 512],
                            start=(mt == 0), stop=(mt == 1)),
                            waits=wt, inc=(st_ if mt == 1 else None))
                    def recip_den(e, pb=pb):
                        e.tensor_scalar(out=srec[:], in0=p_a[pb][64:65, :], scalar1=PEPS, scalar2=None,
                                        op0=ALU.add)
                        e.drain()
                        return e.reciprocal(out=srec[:], in_=srec[:])
                    t_rc = p.op("vector", recip_den, waits=[t_mm], inc=sv_)
                    t_rb = p.op("tensor", lambda e: e.matmul(out=p_b[0][:], lhsT=sones[:], rhs=srec[:],
                                                             start=True, stop=True),
                                waits=[t_rc, t_ones], inc=st_)
                    t_re = p.op("vector", lambda e: e.tensor_copy(out=srecb[:], in_=p_b[0][:]),
                                waits=[t_rb], inc=sv_)
                    pnum_prev[pb] = p.op("vector", lambda e, h=h, icc=icc, pb=pb: e.tensor_tensor(
                        out=shc[(h % 2) * 64:(h % 2) * 64 + 64,
                                (h // 2) * TPC + icc * 512:(h // 2) * TPC + (icc + 1) * 512],
                        in0=p_a[pb][0:64, :], in1=srecb[0:64, :], op=ALU.mult),
                        waits=[t_re], inc=sv_)
                    hc_toks.append(pnum_prev[pb])
                qh_prev[jh] = hc_toks[-1]
            hc_all = [hc_toks[-1], hc_toks[-2]]
            x2_done = [None] * NT
            wo_prev = {0: None, 1: None}
            for icc in range(NT):
                jj = icc % 2
                t_xl = p.op("sync", lambda e, icc=icc, jj=jj: e.dma_start(
                    out=sxc[jj][:], in_=xa[icc * 128:(icc + 1) * 128, :]),
                    waits=([x2_done[icc - 2]] if icc >= 2 else []), inc=ldx[jj], amt=16)
                for mh in range(2):
                    pb = mh
                    t_mm = None
                    for kt in range(8):
                        wt = [wo_prev[pb], t_ldg_all2] + hc_all if kt == 0 else []
                        t_mm = p.op("tensor", lambda e, icc=icc, mh=mh, kt=kt, pb=pb: e.matmul(
                            out=p_b[pb][:], lhsT=shc[:, kt * TPC + icc * 128: kt * TPC + (icc + 1) * 128],
                            rhs=swo[:, kt * D + mh * 512: kt * D + (mh + 1) * 512],
                            start=(kt == 0), stop=(kt == 7)),
                            waits=wt, inc=(st_ if kt == 7 else None))
                    p.op("vector", lambda e, icc=icc, mh=mh, pb=pb: e.scalar_tensor_tensor(
                        out=sx2[:, icc * D + mh * 512: icc * D + (mh + 1) * 512], in0=p_b[pb][:],
                        scalar=1.0, in1=sbout[:, mh * 512:(mh + 1) * 512], op0=ALU.mult, op1=ALU.add),
                        waits=[t_mm, t_ld1], inc=sv_)
                    p.op("vector", lambda e, icc=icc, mh=mh, jj=jj: e.tensor_tensor(
                        out=sx2[:, icc * D + mh * 512: icc * D + (mh + 1) * 512],
                        in0=sx2[:, icc * D + mh * 512: icc * D + (mh + 1) * 512],
                        in1=sxc[jj][:, mh * 512:(mh + 1) * 512], op=ALU.add),
                        waits=[t_xl], inc=sv_)
                    wo_prev[pb] = p.op("vector", lambda e, icc=icc, mh=mh: e.tensor_tensor(
                        out=sx2[:, icc * D + mh * 512: icc * D + (mh + 1) * 512],
                        in0=sx2[:, icc * D + mh * 512: icc * D + (mh + 1) * 512],
                        in1=sffo[:, icc * D + mh * 512:icc * D + (mh + 1) * 512],
                        op=ALU.add), inc=sv_)
                x2_done[icc] = wo_prev[1]
            p.barrier([sv_, ss, st_])

        p.op("gpsimd", lambda e: e.dma_start(out=o_d2, in_=sx2[:, 0:D]), inc=sgo[0], amt=16, dma=True)
        p.barrier([sv_, ss, st_])
        c12.close()
        sxn2T = E(nc.sbuf_tensor("sxn2T", [128, 8 * TPC], BF16))
        with ExitStack() as c3:
            E3 = c3.enter_context
            ssq3 = E3(nc.sbuf_tensor("ssq3", [128, D], F32))
            sxn2 = E3(nc.sbuf_tensor("sxn2", [128, D], F32))
            sxn232 = E3(nc.sbuf_tensor("sxn232", [128, 8 * TPC], F32))
            for k in range(0, 8):
                p.op("sync", lambda e, k=k: e.dma_start(out=sgw32[:, k * 8:(k + 1) * 8],
                                                        in_=gwT[k * 128:(k + 1) * 128, :]), inc=ld1, amt=16)
            t_gw32 = (ld1, p.cnt["ld1"])
            trbank_prev = {0: None, 1: None}
            for i in range(NT):
                x2t = sx2[:, i * D:(i + 1) * D]
                p.op("vector", lambda e, x2t=x2t: e.reduce_sum(out=sstat[:, 0:1], in_=x2t, axis=AXL.X),
                     inc=sv_)
                p.op("vector", lambda e: e.tensor_scalar(out=sstat[:, 0:1], in0=sstat[:, 0:1],
                                                         scalar1=1.0 / D, scalar2=None, op0=ALU.mult),
                     inc=sv_)
                t_sq = p.op("scalar", lambda e, x2t=x2t: e.activation(out=ssq3[:], in_=x2t, func=AF.Square,
                                                                     accum_out=sstat[:, 1:2]), inc=ss)
                p.op("vector", lambda e: e.tensor_scalar(out=sstat[:, 2:3], in0=sstat[:, 1:2],
                                                         scalar1=1.0 / D, scalar2=None, op0=ALU.mult),
                     waits=[t_sq], inc=sv_)
                p.op("vector", lambda e: e.tensor_tensor(out=sstat[:, 3:4], in0=sstat[:, 0:1],
                                                         in1=sstat[:, 0:1], op=ALU.mult), inc=sv_)
                t_var = p.op("vector", lambda e: e.tensor_tensor(out=sstat[:, 2:3], in0=sstat[:, 2:3],
                                                                 in1=sstat[:, 3:4], op=ALU.subtract), inc=sv_)
                p.op("scalar", lambda e: e.activation(out=sstat[:, 4:5], in_=sstat[:, 2:3],
                                                      func=AF.Ln, bias=LN_EPS, scale=1.0),
                     waits=[t_var], inc=ss)
                t_rstd = p.op("scalar", lambda e: e.activation(out=sstat[:, 4:5], in_=sstat[:, 4:5],
                                                               func=AF.Exp, scale=-0.5), inc=ss)
                p.op("vector", lambda e: e.tensor_tensor(out=sstat[:, 5:6], in0=sstat[:, 0:1],
                                                         in1=sstat[:, 4:5], op=ALU.mult),
                     waits=[t_rstd], inc=sv_)
                t_bneg = p.op("vector", lambda e: e.tensor_scalar(out=sstat[:, 5:6], in0=sstat[:, 5:6],
                                                                  scalar1=-1.0, scalar2=None, op0=ALU.mult),
                              inc=sv_)
                t_xn = p.op("vector", lambda e, x2t=x2t: e.tensor_scalar(
                    out=sxn2[:], in0=x2t, scalar1=sstat[:, 4:5], scalar2=sstat[:, 5:6],
                    op0=ALU.mult, op1=ALU.add), waits=[t_bneg], inc=sv_)
                ev_tok = {}
                for k in range(8):
                    tb = k % 2
                    wt = [t_xn, t_ld1]
                    if k >= 2:
                        wt.append(ev_tok[k - 2])
                    else:
                        wt.append(trbank_prev[tb])
                    tt = p.op("tensor", lambda e, k=k, tb=tb: e.transpose(
                        out=p_t[tb][:], in_=sxn2[:, k * 128:(k + 1) * 128], identity=sident[:]),
                        waits=wt, inc=st_)
                    ev_tok[k] = p.op("vector", lambda e, i=i, k=k, tb=tb: e.tensor_copy(
                        out=sxn232[:, k * TPC + i * 128: k * TPC + (i + 1) * 128], in_=p_t[tb][:]),
                        waits=[tt], inc=sv_)
                trbank_prev = {0: ev_tok[6], 1: ev_tok[7]}
                p.op("vector", lambda e, x2t=x2t: e.tensor_tensor(
                    out=x2t, in0=x2t, in1=sxn2[:], op=ALU.add), inc=sv_)
            t_cast = None
            for k in range(8):
                t_cast = p.op("vector", lambda e, k=k: e.tensor_copy(
                    out=sxn2T[:, k * TPC:(k + 1) * TPC], in_=sxn232[:, k * TPC:(k + 1) * TPC]),
                    inc=sv_)
            sc_ev = None
            for nch in range(4):
                t_mm = None
                for kt in range(8):
                    wt = [t_gw32, sc_ev, t_cast] if kt == 0 else []
                    t_mm = p.op("tensor", lambda e, nch=nch, kt=kt: e.matmul(
                        out=p_a[0][0:8, :], lhsT=sgw32[:, kt * 8:(kt + 1) * 8],
                        rhs=sxn232[:, kt * TPC + nch * 512:kt * TPC + (nch + 1) * 512],
                        start=(kt == 0), stop=(kt == 7)),
                        waits=wt, inc=(st_ if kt == 7 else None))
                sc_ev = p.op("vector", lambda e, nch=nch: e.tensor_copy(
                    out=ssc_f[:, nch * 512:(nch + 1) * 512], in_=p_a[0][0:8, :]),
                    waits=[t_mm], inc=sv_)
            p.barrier([sv_, ss, st_])

        p.op("gpsimd", lambda e: e.dma_start(out=o_d3, in_=sxn2T[:, 0:TPC]), inc=sgo[0], amt=16, dma=True)
        with ExitStack() as c4:
            E4 = c4.enter_context
            swip = E4(nc.sbuf_tensor("swip", [128, 8 * 2 * D], BF16))
            scs = E4(nc.sbuf_tensor("scs", [128, 2 * D], BF16))
            sipc = [E4(nc.sbuf_tensor(f"sipc{j}", [128, 2 * D], F32)) for j in range(2)]
            srp = [E4(nc.sbuf_tensor(f"srp{j}", [128, 2 * D], BF16)) for j in range(2)]
            srh = [E4(nc.sbuf_tensor(f"srh{j}", [128, 2 * D], BF16)) for j in range(2)]
            snrm = E4(nc.sbuf_tensor("snrm", [128, 8], F32))
            ssq4 = E4(nc.sbuf_tensor("ssq4", [128, D], F32))
            for k in range(8):
                p.op("gpsimd", lambda e, k=k: e.dma_start(
                    out=swip[:, k * 2 * D:(k + 1) * 2 * D], in_=wip[k * 128:(k + 1) * 128, :]),
                    inc=ldg1, amt=16, dma=True)
            t_l4 = (ld1, p.cnt["ld1"])
            t_g4 = (ldg1, p.cnt["ldg1"])
            ip_prev = {0: None, 1: None, 2: None, 3: None}
            buf_prev = {0: None, 1: None}
            st_prev = {0: None, 1: None}
            cs_prev = {0: None, 1: None}
            for i in range(NT):
                jj = i % 2
                t_cs = p.op("sync", lambda e, i=i, jj=jj: e.dma_start(
                    out=scs[:, jj * D: jj * D + 512], in_=cosb[i * 128:(i + 1) * 128, :]),
                    waits=[cs_prev[jj]], inc=lda[jj], amt=16)
                t_sn = p.op("sync", lambda e, i=i, jj=jj: e.dma_start(
                    out=scs[:, jj * D + 512: jj * D + D], in_=sinb[i * 128:(i + 1) * 128, :]),
                    inc=lda[jj], amt=16)
                ev4 = []
                for q4 in range(4):
                    t_mm = None
                    for kt in range(8):
                        wt = [ip_prev[q4], t_g4] if kt == 0 else []
                        t_mm = p.op("tensor", lambda e, i=i, q4=q4, kt=kt: e.matmul(
                            out=p_a[q4][:], lhsT=sxn2T[:, kt * TPC + i * 128: kt * TPC + (i + 1) * 128],
                            rhs=swip[:, kt * 2 * D + q4 * 512: kt * 2 * D + (q4 + 1) * 512],
                            start=(kt == 0), stop=(kt == 7)),
                            waits=wt, inc=(st_ if kt == 7 else None))
                    ev4.append(p.op("vector", lambda e, jj=jj, q4=q4: e.tensor_copy(
                        out=sipc[jj][:, q4 * 512:(q4 + 1) * 512], in_=p_a[q4][:]),
                        waits=[t_mm, buf_prev[jj]], inc=sv_))
                    ip_prev[q4] = ev4[-1]
                def rope2(e, jj=jj, i=i):
                    cc = scs[:, jj * D: jj * D + 512]
                    sn = scs[:, jj * D + 512: jj * D + D]
                    tmp = srh[jj][:, 0:512]
                    for base in (0, D):
                        x1 = sipc[jj][:, base: base + 512]
                        x2_ = sipc[jj][:, base + 512: base + D]
                        o1 = srp[jj][:, base: base + 512]
                        o2 = srp[jj][:, base + 512: base + D]
                        e.tensor_tensor(out=o1, in0=x1, in1=cc, op=ALU.mult)
                        e.drain()
                        e.tensor_tensor(out=tmp, in0=x2_, in1=sn, op=ALU.mult)
                        e.drain()
                        e.tensor_tensor(out=o1, in0=o1, in1=tmp, op=ALU.subtract)
                        e.drain()
                        e.tensor_tensor(out=o2, in0=x2_, in1=cc, op=ALU.mult)
                        e.drain()
                        e.tensor_tensor(out=tmp, in0=x1, in1=sn, op=ALU.mult)
                        e.drain()
                        e.tensor_tensor(out=o2, in0=o2, in1=tmp, op=ALU.add)
                        e.drain()
                    return e.engine_nop()
                t_rope = p.op("vector", rope2, waits=[ev4[3], t_sn, st_prev[jj]], inc=sv_)
                cs_prev[jj] = t_rope
                def norms(e, jj=jj):
                    e.activation(out=ssq4[:], in_=srp[jj][:, 0:D], func=AF.Square,
                                 accum_out=snrm[:, 0:1])
                    e.drain()
                    e.activation(out=ssq4[:], in_=srp[jj][:, D:2 * D], func=AF.Square,
                                 accum_out=snrm[:, 1:2])
                    return e.drain()
                t_sq4 = p.op("scalar", norms, waits=[t_rope], inc=ss, dma=True)
                t_gq = p.op("vector", lambda e: e.tensor_scalar(
                    out=snrm[:, 0:2], in0=snrm[:, 0:2], scalar1=1e-24, scalar2=None, op0=ALU.max),
                    waits=[t_sq4], inc=sv_)

                def norms2(e):
                    e.activation(out=snrm[:, 2:3], in_=snrm[:, 0:1], func=AF.Ln, bias=0.0, scale=1.0)
                    e.drain()
                    e.activation(out=snrm[:, 2:3], in_=snrm[:, 2:3], func=AF.Exp, scale=-0.5)
                    e.drain()
                    e.activation(out=snrm[:, 3:4], in_=snrm[:, 1:2], func=AF.Ln, bias=0.0, scale=1.0)
                    e.drain()
                    return e.activation(out=snrm[:, 3:4], in_=snrm[:, 3:4], func=AF.Exp, scale=-0.5)
                t_nrm = p.op("scalar", norms2, waits=[t_gq], inc=ss)
                t_nh = p.op("vector", lambda e, jj=jj: e.tensor_scalar(
                    out=srh[jj][:, 0:D], in0=srp[jj][:, 0:D], scalar1=snrm[:, 2:3], scalar2=None,
                    op0=ALU.mult), waits=[t_nrm], inc=sv_)
                t_ph = p.op("vector", lambda e, jj=jj: e.tensor_scalar(
                    out=srh[jj][:, D:2 * D], in0=srp[jj][:, D:2 * D], scalar1=snrm[:, 3:4], scalar2=None,
                    op0=ALU.mult), inc=sv_)
                st1 = p.op("gpsimd", lambda e, i=i, jj=jj: e.dma_start(
                    out=o_I[i * 128:(i + 1) * 128, :], in_=srp[jj][:, 0:D]),
                    waits=[t_rope], inc=sgo[jj], amt=16, dma=True)
                st2 = p.op("gpsimd", lambda e, i=i, jj=jj: e.dma_start(
                    out=o_P[i * 128:(i + 1) * 128, :], in_=srp[jj][:, D:2 * D]),
                    waits=[t_rope], inc=sgo[jj], amt=16, dma=True)
                st3 = p.op("gpsimd", lambda e, i=i, jj=jj: e.dma_start(
                    out=o_Ih[i * 128:(i + 1) * 128, :], in_=srh[jj][:, 0:D]),
                    waits=[t_nh], inc=sgo[jj], amt=16, dma=True)
                st4 = p.op("gpsimd", lambda e, i=i, jj=jj: e.dma_start(
                    out=o_Ph[i * 128:(i + 1) * 128, :], in_=srh[jj][:, D:2 * D]),
                    waits=[t_ph], inc=sgo[jj], amt=16, dma=True)
                buf_prev[jj] = t_rope
                st_prev[jj] = st4
            p.barrier([sv_, ss, st_])

        with ExitStack() as c5:
            E5 = c5.enter_context
            scot = E5(nc.sbuf_tensor("scot", [128, 8], F32))
            smk = E5(nc.sbuf_tensor("smk", [128, NT * 8], F32))
            siota = E5(nc.sbuf_tensor("siota", [128, 8], F32))
            swe = [E5(nc.sbuf_tensor(f"swe{j}", [128, 8 * D], BF16)) for j in range(2)]
            t_io = p.op("gpsimd", lambda e: e.dma_start(out=siota[:], in_=iota8.partition_broadcast(128)),
                        inc=ldg1, amt=16, dma=True)
            sc_ev = None
            mk_prev = None
            tr_prev = None
            for i in range(NT):
                tt = p.op("tensor", lambda e, i=i: e.transpose(
                    out=p_t[0][:, 0:8], in_=ssc_f[:, i * 128:(i + 1) * 128],
                    identity=sident[0:8, 0:8]), waits=[sc_ev, tr_prev], inc=st_)
                t_sc2 = p.op("vector", lambda e: e.tensor_copy(out=scot[:], in_=p_t[0][:, 0:8]),
                             waits=[tt], inc=sv_)
                tr_prev = t_sc2

                def mkmask(e, i=i):
                    e.tensor_reduce(out=sstat[:, 6:7], in_=scot[:], axis=AXL.X, op=ALU.max)
                    e.drain()
                    e.tensor_scalar(out=smk[:, i * 8:(i + 1) * 8], in0=scot[:], scalar1=sstat[:, 6:7],
                                    scalar2=None, op0=ALU.is_ge)
                    e.drain()
                    e.tensor_scalar(out=smk[:, i * 8:(i + 1) * 8], in0=smk[:, i * 8:(i + 1) * 8],
                                    scalar1=-1000.0, scalar2=1000.0, op0=ALU.mult, op1=ALU.add)
                    e.drain()
                    e.tensor_tensor(out=smk[:, i * 8:(i + 1) * 8], in0=smk[:, i * 8:(i + 1) * 8],
                                    in1=siota[:], op=ALU.add)
                    e.drain()
                    e.tensor_reduce(out=sstat[:, 7:8], in_=smk[:, i * 8:(i + 1) * 8],
                                    axis=AXL.X, op=ALU.min)
                    e.drain()
                    return e.tensor_scalar(out=smk[:, i * 8:(i + 1) * 8], in0=siota[:],
                                           scalar1=sstat[:, 7:8], scalar2=None, op0=ALU.is_equal)
                mk_prev = p.op("vector", mkmask, waits=[t_sc2, t_io], inc=sv_)
            we_tok = [None, None]
            last_mm_of_expert = {}
            moe_ev = {0: None, 1: None}
            for ee in range(8):
                jw = ee % 2
                we_tok[jw] = p.op("gpsimd", lambda e, ee=ee: e.dma_start(
                    out=swe[ee % 2][:].rearrange("p (k d) -> p k d", k=8),
                    in_=wexp[ee].rearrange("(k p) d -> p k d", p=128)),
                    waits=([last_mm_of_expert.get(ee - 2)]), inc=ldw[jw], amt=16, dma=True)
                for i in range(NT):
                    for mh in range(2):
                        pb = mh
                        t_mm = None
                        for kt in range(8):
                            wt = [we_tok[jw], mk_prev, moe_ev[pb]] if kt == 0 else []
                            t_mm = p.op("tensor", lambda e, ee=ee, i=i, mh=mh, kt=kt, pb=pb, jw=jw: e.matmul(
                                out=p_b[pb][:],
                                lhsT=sxn2T[:, kt * TPC + i * 128: kt * TPC + (i + 1) * 128],
                                rhs=swe[jw][:, kt * D + mh * 512: kt * D + (mh + 1) * 512],
                                start=(kt == 0), stop=(kt == 7)),
                                waits=wt, inc=(st_ if kt == 7 else None))
                        last_mm_of_expert[ee] = t_mm
                        moe_ev[pb] = p.op("vector", lambda e, ee=ee, i=i, mh=mh, pb=pb: e.scalar_tensor_tensor(
                            out=sx2[:, i * D + mh * 512: i * D + (mh + 1) * 512], in0=p_b[pb][:],
                            scalar=smk[:, i * 8 + ee: i * 8 + ee + 1],
                            in1=sx2[:, i * D + mh * 512: i * D + (mh + 1) * 512],
                            op0=ALU.mult, op1=ALU.add),
                            waits=[t_mm], inc=sv_)
            st5 = None
            for i in range(NT):
                st5 = p.op("gpsimd", lambda e, i=i: e.dma_start(
                    out=o_x3p[i * 128:(i + 1) * 128, :], in_=sx2[:, i * D:(i + 1) * D]),
                    waits=[moe_ev[0], moe_ev[1]], inc=sgo[0], amt=16, dma=True)
            p.barrier([sv_, ss, st_])
        p.fence("gpsimd", [(sgo[0], p.cnt["sgo0"]), (sgo[1], p.cnt["sgo1"])])
        p.emit(block)
    return nc



def build_C():
    nc = bass.Bass("TRN2", target_bir_lowering=False)
    PT = nc.dram_tensor("PT", [D, T], BF16, kind="ExternalInput").ap()
    Phat = nc.dram_tensor("Phat", [T, D], BF16, kind="ExternalInput").ap()
    If_ = nc.dram_tensor("If_", [D, TPC], BF16, kind="ExternalInput").ap()
    Ihf = nc.dram_tensor("Ihf", [D, TPC], BF16, kind="ExternalInput").ap()
    x3pT = nc.dram_tensor("x3pT", [D, TPC], F32, kind="ExternalInput").ap()
    w1p = nc.dram_tensor("w1p", [32, 64], F32, kind="ExternalInput").ap()
    b1 = nc.dram_tensor("b1", [64, 1], F32, kind="ExternalInput").ap()
    w2 = nc.dram_tensor("w2", [64, 64], F32, kind="ExternalInput").ap()
    b2 = nc.dram_tensor("b2", [64, 1], F32, kind="ExternalInput").ap()
    wc = nc.dram_tensor("wc", [64, 2], F32, kind="ExternalInput").ap()
    bc = nc.dram_tensor("bc", [2, 1], F32, kind="ExternalInput").ap()
    wm = nc.dram_tensor("wm", [64, 1], F32, kind="ExternalInput").ap()
    bm = nc.dram_tensor("bm", [1, 1], F32, kind="ExternalInput").ap()
    pal = nc.dram_tensor("pal", [D, 256], F32, kind="ExternalInput").ap()
    wo = nc.dram_tensor("wo", [D, D], F32, kind="ExternalInput").ap()
    ident = nc.dram_tensor("ident", [128, 128], F32, kind="ExternalInput").ap()
    cmask = nc.dram_tensor("cmask", [128, 512], F32, kind="ExternalInput").ap()
    rowg = nc.dram_tensor("rowg", [128, 16], F32, kind="ExternalInput").ap()
    iotas = nc.dram_tensor("iotas", [128, 512], I32, kind="ExternalInput").ap()
    iot128 = nc.dram_tensor("iot128", [128, 1], F32, kind="ExternalInput").ap()
    ioy = nc.dram_tensor("ioy", [128, 256], BF16, kind="ExternalInput").ap()
    kmG = nc.dram_tensor("kmG", [32, 256], BF16, kind="ExternalInput").ap()
    kmM = nc.dram_tensor("kmM", [128, 16], F32, kind="ExternalInput").ap()
    o_yf = nc.dram_tensor("o_yf", [D, TPC], F32, kind="ExternalOutput").ap()
    idx_scr = nc.dram_tensor("idx_scr", [128, 16], I32, kind="Internal").ap()
    row_scr = nc.dram_tensor("row_scr", [1, TPC], F32, kind="Internal").ap()
    tok_scr = nc.dram_tensor("tok_scr", [128, 16], F32, kind="Internal").ap()

    with ExitStack() as ctx:
        E = ctx.enter_context
        p = Prog(nc)
        ld1 = p.mksem(ctx, "ld1")
        ldp = [p.mksem(ctx, "ldp0"), p.mksem(ctx, "ldp1")]
        ldb = [p.mksem(ctx, "ldb0"), p.mksem(ctx, "ldb1")]
        ldgx = p.mksem(ctx, "ldgx")
        ldsc = p.mksem(ctx, "ldsc")
        sv_ = p.mksem(ctx, "sv_")
        ss = p.mksem(ctx, "ss")
        st_ = p.mksem(ctx, "st_")
        sgo = p.mksem(ctx, "sgo")
        block = E(nc.Block())
        p.setup_consts(ctx, [0.0, 1.0, -1.0, -0.5, 7.5, 15.0, NEG, PEPS, 1.0 / 8192.0, 1e-24,
                             1.0 / 16.0], sgo)

        sident = E(nc.sbuf_tensor("sident", [128, 128], F32))
        sident_b = E(nc.sbuf_tensor("sident_b", [128, 128], BF16))
        scm = E(nc.sbuf_tensor("scm", [128, 512], F32))
        srowg = E(nc.sbuf_tensor("srowg", [128, 16], F32))
        siotas = E(nc.sbuf_tensor("siotas", [128, 512], I32))
        siot128 = E(nc.sbuf_tensor("siot128", [128, 1], F32))
        sioy = E(nc.sbuf_tensor("sioy", [128, 256], BF16))
        skmG = E(nc.sbuf_tensor("skmG", [32, 256], BF16))
        skmM = E(nc.sbuf_tensor("skmM", [128, 16], F32))
        sw1 = E(nc.sbuf_tensor("sw1", [32, 64], BF16))
        sw2 = E(nc.sbuf_tensor("sw2", [64, 64], BF16))
        swc = E(nc.sbuf_tensor("swc", [64, 2], BF16))
        swm = E(nc.sbuf_tensor("swm", [64, 1], BF16))
        sb1 = E(nc.sbuf_tensor("sb1", [64, 1], F32))
        sb2 = E(nc.sbuf_tensor("sb2", [64, 1], F32))
        sbc = E(nc.sbuf_tensor("sbc", [2, 1], F32))
        sbm = E(nc.sbuf_tensor("sbm", [1, 1], F32))
        sPW = E(nc.sbuf_tensor("sPW", [128, 2 * D], BF16))
        with ExitStack() as c0:
            E0 = c0.enter_context
            spal = E0(nc.sbuf_tensor("spal", [128, 8 * 256], BF16))
            swo0 = E0(nc.sbuf_tensor("swo0", [128, 8 * D], BF16))
            for k in range(8):
                p.op("gpsimd", lambda e, k=k: e.dma_start(
                    out=spal[:, k * 256:(k + 1) * 256], in_=pal[k * 128:(k + 1) * 128, :]),
                    inc=ldgx, amt=16, dma=True)
                p.op("gpsimd", lambda e, k=k: e.dma_start(
                    out=swo0[:, k * D:(k + 1) * D], in_=wo[k * 128:(k + 1) * 128, :]),
                    inc=ldgx, amt=16, dma=True)
            t_pw0 = (ldgx, p.cnt["ldgx"])
            pw_prev = None
            for cc in range(2):
                for nn in range(2):
                    t_mm = None
                    for kt in range(8):
                        wt = [t_pw0, pw_prev] if kt == 0 else []
                        t_mm = p.op("tensor", lambda e, cc=cc, nn=nn, kt=kt: e.matmul(
                            out=p_m[:], lhsT=spal[:, kt * 256 + cc * 128: kt * 256 + (cc + 1) * 128],
                            rhs=swo0[:, kt * D + nn * 512: kt * D + (nn + 1) * 512],
                            start=(kt == 0), stop=(kt == 7)),
                            waits=wt, inc=(st_ if kt == 7 else None))
                    pw_prev = p.op("vector", lambda e, cc=cc, nn=nn: e.tensor_copy(
                        out=sPW[:, cc * D + nn * 512: cc * D + (nn + 1) * 512], in_=p_m[:]),
                        waits=[t_mm], inc=sv_)
            p.barrier([sv_, ss, st_])

        sVp = E(nc.sbuf_tensor("sVp", [128, T], F32))
        sZ = E(nc.sbuf_tensor("sZ", [128, 8 * 4096], BF16))
        sptb = [E(nc.sbuf_tensor(f"sptb{j}", [128, 8 * 512], BF16)) for j in range(2)]
        sifb = [E(nc.sbuf_tensor(f"sifb{j}", [128, 8 * 128], BF16)) for j in range(2)]
        sihb = [E(nc.sbuf_tensor(f"sihb{j}", [128, 8 * 128], BF16)) for j in range(2)]
        sx3b = [E(nc.sbuf_tensor(f"sx3b{j}", [128, 8 * 128], F32)) for j in range(2)]
        sm16 = E(nc.sbuf_tensor("sm16", [128, 16], F32))
        sidx = E(nc.sbuf_tensor("sidx", [128, 16], I32))
        sidxf = E(nc.sbuf_tensor("sidxf", [128, 16], F32))
        sidxg = E(nc.sbuf_tensor("sidxg", [128, 16], I32))
        sgt = [E(nc.sbuf_tensor(f"sgt{j}", [128, D], BF16)) for j in range(2)]
        sG = E(nc.sbuf_tensor("sG", [128, 4096], BF16))
        srel = E(nc.sbuf_tensor("srel", [32, 2048], BF16))
        sh1 = E(nc.sbuf_tensor("sh1", [64, 2048], BF16))
        sh2 = E(nc.sbuf_tensor("sh2", [64, 2048], BF16))
        smixr = E(nc.sbuf_tensor("smixr", [1, 2048], F32))
        szrow = E(nc.sbuf_tensor("szrow", [2, 2048], F32))
        smixt = E(nc.sbuf_tensor("smixt", [128, 16], F32))
        sstat = E(nc.sbuf_tensor("sstat", [128, 8], F32))
        sw_ = E(nc.sbuf_tensor("sw_", [128, 16], F32))
        sfx = E(nc.sbuf_tensor("sfx", [128, 16], F32))
        sfy = E(nc.sbuf_tensor("sfy", [128, 16], F32))
        sfxi = E(nc.sbuf_tensor("sfxi", [128, 16], I32))
        shx = E(nc.sbuf_tensor("shx", [128, 256], BF16))
        shy = E(nc.sbuf_tensor("shy", [128, 256], BF16))
        sYe = E(nc.sbuf_tensor("sYe", [128, 4096], BF16))
        stmp = E(nc.sbuf_tensor("stmp", [128, 256], BF16))
        sXe = E(nc.sbuf_tensor("sXe", [128, 4096], BF16))
        sA = E(nc.sbuf_tensor("sA", [128, 256], F32))
        sAb = E(nc.sbuf_tensor("sAb", [128, 256], BF16))
        sAf = E(nc.sbuf_tensor("sAf", [128, 256], BF16))
        souf = E(nc.sbuf_tensor("souf", [128, 8 * 128], F32))

        p_l = [E(nc.psum_tensor(f"p_l{j}", [128, 512], F32)) for j in range(2)]
        p_g = E(nc.psum_tensor("p_g", [128, 128], F32))
        p_gb = E(nc.psum_tensor("p_gb", [128, 128], BF16))
        p_t2 = [E(nc.psum_tensor(f"p_t2{j}", [128, 128], BF16)) for j in range(2)]
        p_m = E(nc.psum_tensor("p_m", [128, 512], F32))

        for dst, srcap in ((sident, ident), (scm, cmask), (siotas, iotas), (siot128, iot128),
                           (srowg, rowg),
                           (skmM, kmM), (sb1, b1), (sb2, b2), (sbc, bc), (sbm, bm),
                           (sioy, ioy), (skmG, kmG)):
            p.op("sync", lambda e, dst=dst, srcap=srcap: e.dma_start(out=dst[:], in_=srcap),
                 inc=ld1, amt=16)
        for dst, srcap in ((sw1, w1p), (sw2, w2), (swc, wc), (swm, wm)):
            p.op("gpsimd", lambda e, dst=dst, srcap=srcap: e.dma_start(out=dst[:], in_=srcap),
                 inc=ld1, amt=16, dma=True)
        t_ld1 = (ld1, p.cnt["ld1"])
        t_identb = p.op("vector", lambda e: e.tensor_copy(out=sident_b[:], in_=sident[:]),
                        waits=[t_ld1], inc=sv_)
        t_zz = p.op("vector", lambda e: e.memset(sZ[:], 0.0), inc=sv_)
        ptb_prev = {0: None, 1: None}
        blk_prev = {0: None, 1: None}
        gx_prev = None
        sc_prev = None
        out_prev = None
        for rb in range(16):
            nch = rb + 1
            jb = rb % 2
            t_if = p.op("sync", lambda e, rb=rb, jb=jb: e.dma_start(
                out=sifb[jb][:].rearrange("p (k c) -> p k c", k=8),
                in_=If_.rearrange("(k p) t -> p k t", p=128)[:, :, rb * 128:(rb + 1) * 128]),
                waits=[blk_prev[jb]], inc=ldb[jb], amt=16)
            t_ih = p.op("sync", lambda e, rb=rb, jb=jb: e.dma_start(
                out=sihb[jb][:].rearrange("p (k c) -> p k c", k=8),
                in_=Ihf.rearrange("(k p) t -> p k t", p=128)[:, :, rb * 128:(rb + 1) * 128]),
                inc=ldb[jb], amt=16)
            t_x3 = p.op("sync", lambda e, rb=rb, jb=jb: e.dma_start(
                out=sx3b[jb][:].rearrange("p (k c) -> p k c", k=8),
                in_=x3pT.rearrange("(k p) t -> p k t", p=128)[:, :, rb * 128:(rb + 1) * 128]),
                inc=ldb[jb], amt=16)
            ev_prev = None
            for ch in range(nch):
                jp = ch % 2
                t_pt = p.op("sync", lambda e, ch=ch, jp=jp: e.dma_start(
                    out=sptb[jp][:].rearrange("p (k s) -> p k s", k=8),
                    in_=PT.rearrange("(k p) t -> p k t", p=128)[:, :, ch * 512:(ch + 1) * 512]),
                    waits=[ptb_prev[jp]], inc=ldp[jp], amt=16)
                pb = ch % 2
                t_mm = None
                for kt in range(8):
                    wt = [t_pt, t_if] if kt == 0 else []
                    if kt == 0 and ev_prev is not None:
                        wt.append(ev_prev if ch < 2 else ev_toks[ch - 2])
                    t_mm = p.op("tensor", lambda e, ch=ch, kt=kt, pb=pb, rb=rb, jb=jb: e.matmul(
                        out=p_l[pb][:], lhsT=sifb[jb][:, kt * 128:(kt + 1) * 128],
                        rhs=sptb[ch % 2][:, kt * 512:(kt + 1) * 512],
                        start=(kt == 0), stop=(kt == 7)),
                        waits=wt, inc=(st_ if kt == 7 else None))
                if ch == nch - 1:
                    t_mk = p.op("vector", lambda e, pb=pb: e.tensor_tensor(
                        out=p_l[pb][:], in0=p_l[pb][:], in1=scm[:], op=ALU.add),
                        waits=[t_mm, t_ld1], inc=sv_)
                else:
                    t_mk = t_mm
                def tagev(e, ch=ch, pb=pb):
                    e.tensor_scalar(
                        out=sVp[:, ch * 512:(ch + 1) * 512].bitcast(I32),
                        in0=p_l[pb][:].bitcast(I32), scalar1=-8192, scalar2=ch * 512,
                        op0=ALU.bitwise_and, op1=ALU.bitwise_or)
                    e.drain()
                    return e.tensor_tensor(
                        out=sVp[:, ch * 512:(ch + 1) * 512].bitcast(I32),
                        in0=sVp[:, ch * 512:(ch + 1) * 512].bitcast(I32),
                        in1=siotas[:], op=ALU.bitwise_or)
                tg = p.op("vector", tagev, waits=[t_mk], inc=sv_)
                if ch == 0:
                    ev_toks = {}
                ev_toks[ch] = tg
                ev_prev = tg
                ptb_prev[ch % 2] = tg
            def top16(e, nch=nch):
                e.max(out=sm16[:, 0:8], in_=sVp[:, 0:nch * 512])
                e.drain()
                e.match_replace(out=sVp[:, 0:nch * 512], in_to_replace=sm16[:, 0:8],
                                in_values=sVp[:, 0:nch * 512], imm_value=NEG)
                e.drain()
                e.max(out=sm16[:, 8:16], in_=sVp[:, 0:nch * 512])
                e.drain()
                e.tensor_scalar(out=sidx[:], in0=sm16[:].bitcast(I32), scalar1=8191, scalar2=None,
                                op0=ALU.bitwise_and)
                e.drain()
                return e.tensor_copy(out=sidxf[:], in_=sidx[:])
            t_tk = p.op("vector", top16, waits=[ev_prev, sc_prev, gx_prev], inc=sv_)
            gt_prev = {0: None, 1: None}
            tb_prev = {0: None, 1: None}
            zev = []
            for g in range(15):
                jg = g % 2
                t_ga = p.op("gpsimd", lambda e, g=g, jg=jg: e.indirect_dma_start(
                    out=sgt[jg][:], out_offset=None, in_=Phat,
                    in_offset=bass.IndirectOffsetOnAxis(sidx[:, g:g + 1], 0)),
                    waits=[t_tk, gt_prev[jg]], inc=ldgx, amt=16, dma=True)
                for kt in range(8):
                    tb = kt % 2
                    t_tt = p.op("tensor", lambda e, jg=jg, kt=kt, tb=tb: e.transpose(
                        out=p_t2[tb][:], in_=sgt[jg][:, kt * 128:(kt + 1) * 128],
                        identity=sident_b[:]),
                        waits=[t_ga, tb_prev[tb], t_identb], inc=st_)
                    ev = p.op("vector", lambda e, g=g, kt=kt, tb=tb: e.tensor_copy(
                        out=sZ[:, kt * 4096 + g: (kt + 1) * 4096: 32].rearrange("p f -> p f"),
                        in_=p_t2[tb][:]),
                        waits=[t_tt, t_zz], inc=sv_)
                    tb_prev[tb] = ev
                    if kt == 7:
                        gt_prev[jg] = ev
                        zev.append(ev)
            ih_ev = []
            for kt in range(8):
                ih_ev.append(p.op("vector", lambda e, kt=kt, jb=jb: e.tensor_copy(
                    out=sZ[:, kt * 4096 + 15: (kt + 1) * 4096: 32].rearrange("p f -> p f"),
                    in_=sihb[jb][:, kt * 128:(kt + 1) * 128]),
                    waits=[t_ih, zev[-1]], inc=sv_))
            gr_ev = []
            gr_prev = None
            for g in range(32):
                t_mm = None
                for kt in range(8):
                    wt = [ih_ev[-1], gr_prev] if kt == 0 else []
                    t_mm = p.op("tensor", lambda e, g=g, kt=kt: e.matmul(
                        out=p_g[:], lhsT=sZ[:, kt * 4096 + g * 128: kt * 4096 + (g + 1) * 128],
                        rhs=sZ[:, kt * 4096 + g * 128: kt * 4096 + (g + 1) * 128],
                        start=(kt == 0), stop=(kt == 7)),
                        waits=wt, inc=(st_ if kt == 7 else None))
                gr_prev = p.op("vector", lambda e, g=g: e.tensor_scalar(
                    out=sG[:, g * 128:(g + 1) * 128], in0=p_g[:], scalar1=1.0, scalar2=-1.0,
                    op0=ALU.min, op1=ALU.max), waits=[t_mm], inc=sv_)
                gr_ev.append(gr_prev)
            def relg(e):
                e.memset(srel[:], 0.0)
                e.drain()
                for tsub in range(4):
                    e.tensor_copy(
                        out=srel[0:16, :].rearrange("p (g k) -> p g k", g=32)[:, :, 0:16].rearrange(
                            "p g k -> p (g k)").rearrange("p (g k) -> p g k", g=32)[:, :, 0:16]
                        if False else
                        srel[0:16, tsub * 16 + 0: 2048: 64].rearrange("p f -> p f"),
                        in_=sG[tsub * 32: tsub * 32 + 16, tsub * 32: 4096: 128])
                    e.drain()
                return e.drain()
            def relg2(e):
                e.memset(srel[:], 0.0)
                e.drain()
                for tsub in range(4):
                    src = sG[tsub * 32: tsub * 32 + 16, :].rearrange("p (g c) -> p g c", g=32)[
                        :, :, tsub * 32: tsub * 32 + 16]
                    dst = srel[0:16, :].rearrange("p (g t k) -> p g t k", g=32, t=4)[:, :, tsub, :]
                    e.tensor_copy(out=dst, in_=src)
                    e.drain()
                return e.drain()
            t_rel = p.op("vector", relg2, waits=[gr_ev[-1]], inc=sv_, dma=True)
            def delta(e, rb=rb):
                e.tensor_scalar(out=sm16[:], in0=sidxf[:], scalar1=srowg[:, rb:rb + 1], scalar2=None,
                                op0=ALU.subtract)
                e.drain()
                e.tensor_scalar(out=sm16[:], in0=sm16[:], scalar1=-1.0 / 8192.0,
                                scalar2=0.0, op0=ALU.mult, op1=ALU.add)
                e.drain()
                return e.tensor_scalar(out=sm16[:], in0=sm16[:], scalar1=0.0, scalar2=None,
                                       op0=ALU.max)
            t_dl = p.op("vector", delta, waits=[t_tk], inc=sv_)
            t_dw = p.op("gpsimd", lambda e: e.dma_start(out=tok_scr, in_=sm16[:]),
                        waits=[t_dl], inc=ldsc, amt=16, dma=True)
            t_dr = p.op("gpsimd", lambda e: e.dma_start(
                out=srel[16:17, :], in_=tok_scr.rearrange("p k -> (p k)")[None, :]),
                waits=[t_dw, t_rel], inc=ldsc, amt=16, dma=True)
            if rb == 0:
                t_rel2 = p.op("vector", lambda e: e.tensor_tensor(
                    out=srel[:, 0:224].bitcast(BF16), in0=srel[:, 0:224].bitcast(BF16),
                    in1=skmG[:, 0:224], op=ALU.mult), waits=[(ldsc, p.cnt["ldsc"]), t_ld1], inc=sv_)
            else:
                t_rel2 = (ldsc, p.cnt["ldsc"])
            mlp_ev = []
            for nn in range(4):
                t_m1 = p.op("tensor", lambda e, nn=nn: e.matmul(
                    out=p_m[0:64, :], lhsT=sw1[:], rhs=srel[:, nn * 512:(nn + 1) * 512],
                    start=True, stop=True), waits=[t_rel2, t_ld1] if nn == 0 else [mlp_ev[-1]],
                    inc=st_)
                mlp_ev.append(p.op("scalar", lambda e, nn=nn: e.activation(
                    out=sh1[:, nn * 512:(nn + 1) * 512], in_=p_m[0:64, :], func=AF.Gelu,
                    bias=sb1[:], scale=1.0), waits=[t_m1], inc=ss))
            for nn in range(4):
                t_m2 = p.op("tensor", lambda e, nn=nn: e.matmul(
                    out=p_m[0:64, :], lhsT=sw2[:], rhs=sh1[:, nn * 512:(nn + 1) * 512],
                    start=True, stop=True), waits=[mlp_ev[-1] if nn == 0 else mlp_ev[-1]], inc=st_)
                mlp_ev.append(p.op("scalar", lambda e, nn=nn: e.activation(
                    out=sh2[:, nn * 512:(nn + 1) * 512], in_=p_m[0:64, :], func=AF.Gelu,
                    bias=sb2[:], scale=1.0), waits=[t_m2], inc=ss))
            zm_ev = []
            for nn in range(4):
                t_m3 = p.op("tensor", lambda e, nn=nn: e.matmul(
                    out=p_m[0:2, :], lhsT=swc[:], rhs=sh2[:, nn * 512:(nn + 1) * 512],
                    start=True, stop=True), waits=[mlp_ev[-1]] if nn == 0 else [zm_ev[-1]], inc=st_)
                zm_ev.append(p.op("scalar", lambda e, nn=nn: e.activation(
                    out=szrow[:, nn * 512:(nn + 1) * 512], in_=p_m[0:2, :], func=AF.Tanh,
                    bias=sbc[:], scale=1.0), waits=[t_m3], inc=ss))
            for nn in range(4):
                t_m4 = p.op("tensor", lambda e, nn=nn: e.matmul(
                    out=p_m[0:1, :], lhsT=swm[:], rhs=sh2[:, nn * 512:(nn + 1) * 512],
                    start=True, stop=True), waits=[zm_ev[-1]] if nn == 0 else [zm_ev[-1]], inc=st_)
                zm_ev.append(p.op("vector", lambda e, nn=nn: e.tensor_scalar(
                    out=smixr[:, nn * 512:(nn + 1) * 512], in0=p_m[0:1, :], scalar1=sbm[:],
                    scalar2=None, op0=ALU.add), waits=[t_m4], inc=sv_))
            t_mw = p.op("gpsimd", lambda e: e.dma_start(out=row_scr, in_=smixr[:]),
                        waits=[zm_ev[-1]], inc=ldsc, amt=16, dma=True)
            t_mr = p.op("gpsimd", lambda e: e.dma_start(
                out=smixt[:], in_=row_scr.rearrange("o (p k) -> (o p) k", p=128)),
                waits=[t_mw, sc_prev], inc=ldsc, amt=16, dma=True)

            def smax(e, rb=rb):
                e.memset(smixt[:, 15:16], NEG)
                e.drain()
                if rb == 0:
                    e.tensor_tensor(out=smixt[:], in0=smixt[:], in1=skmM[:], op=ALU.add)
                    e.drain()
                e.tensor_reduce(out=sstat[:, 0:1], in_=smixt[:], axis=AXL.X, op=ALU.max, negate=True)
                return e.drain()
            t_sm = p.op("vector", smax, waits=[t_mr, t_ld1], inc=sv_, dma=True)
            t_ex = p.op("scalar", lambda e: e.activation(out=sw_[:], in_=smixt[:], func=AF.Exp,
                                                         bias=sstat[:, 0:1], scale=1.0),
                        waits=[t_sm], inc=ss)

            def wnorm(e):
                e.tensor_reduce(out=sstat[:, 1:2], in_=sw_[:], axis=AXL.X, op=ALU.add)
                e.drain()
                e.reciprocal(out=sstat[:, 1:2], in_=sstat[:, 1:2])
                e.drain()
                return e.tensor_scalar(out=sw_[:], in0=sw_[:], scalar1=sstat[:, 1:2], scalar2=None,
                                       op0=ALU.mult)
            t_wn = p.op("vector", wnorm, waits=[t_ex], inc=sv_)
            t_zw = p.op("gpsimd", lambda e: e.dma_start(out=row_scr, in_=szrow[0:1, :]),
                        waits=[zm_ev[-1], t_mr], inc=ldsc, amt=16, dma=True)
            t_zr = p.op("gpsimd", lambda e: e.dma_start(
                out=sfx[:], in_=row_scr.rearrange("o (p k) -> (o p) k", p=128)),
                waits=[t_zw], inc=ldsc, amt=16, dma=True)
            t_zw2 = p.op("gpsimd", lambda e: e.dma_start(out=row_scr, in_=szrow[1:2, :]),
                         waits=[t_zr], inc=ldsc, amt=16, dma=True)
            t_zr2 = p.op("gpsimd", lambda e: e.dma_start(
                out=sfy[:], in_=row_scr.rearrange("o (p k) -> (o p) k", p=128)),
                waits=[t_zw2], inc=ldsc, amt=16, dma=True)

            def mkf(e, buf):
                e.tensor_scalar(out=buf[:], in0=buf[:], scalar1=7.5, scalar2=7.5,
                                op0=ALU.mult, op1=ALU.add)
                e.drain()
                e.tensor_scalar(out=buf[:], in0=buf[:], scalar1=0.0, scalar2=15.0,
                                op0=ALU.max, op1=ALU.min)
                return e.drain()
            t_fx = p.op("vector", lambda e: mkf(e, sfx), waits=[t_zr], inc=sv_, dma=True)
            t_fy = p.op("vector", lambda e: mkf(e, sfy), waits=[t_zr2], inc=sv_, dma=True)
            def hats(e, src, dst, wmul):
                for j in range(16):
                    e.tensor_copy(out=dst[:, j * 16:(j + 1) * 16], in_=src[:])
                    e.drain()
                e.tensor_tensor(out=dst[:], in0=dst[:], in1=sioy[:], op=ALU.subtract)
                e.drain()
                e.tensor_scalar(out=stmp[:], in0=dst[:], scalar1=1.0, scalar2=None, op0=ALU.add)
                e.drain()
                e.tensor_scalar(out=dst[:], in0=dst[:], scalar1=-1.0, scalar2=1.0,
                                op0=ALU.mult, op1=ALU.add)
                e.drain()
                e.tensor_tensor(out=dst[:], in0=dst[:], in1=stmp[:], op=ALU.min)
                e.drain()
                e.tensor_scalar(out=dst[:], in0=dst[:], scalar1=0.0, scalar2=None, op0=ALU.max)
                e.drain()
                if wmul:
                    for j in range(16):
                        e.tensor_tensor(out=dst[:, j * 16:(j + 1) * 16],
                                        in0=dst[:, j * 16:(j + 1) * 16], in1=sw_[:], op=ALU.mult)
                        e.drain()
                return e.drain()
            t_hx = p.op("vector", lambda e: hats(e, sfx, shx, False), waits=[t_fx], inc=sv_, dma=True)
            t_hy = p.op("vector", lambda e: hats(e, sfy, shy, True), waits=[t_fy, t_wn], inc=sv_, dma=True)
            def abuild(e):
                for x in range(16):
                    e.tensor_copy(out=sYe[:].rearrange("p (y x k) -> p y x k", y=16, x=16)[:, :, x, :],
                                  in_=shy[:].rearrange("p (y k) -> p y k", y=16))
                    e.drain()
                for y in range(16):
                    e.tensor_copy(out=sXe[:, y * 256:(y + 1) * 256], in_=shx[:])
                    e.drain()
                e.tensor_tensor(out=sYe[:], in0=sYe[:], in1=sXe[:], op=ALU.mult)
                e.drain()
                e.tensor_reduce(out=sA[:], in_=sYe[:].rearrange("p (c k) -> p c k", k=16),
                                axis=AXL.X, op=ALU.add)
                e.drain()
                e.tensor_copy(out=sAb[:], in_=sA[:])
                return e.drain()
            t_ab = p.op("vector", abuild, waits=[t_hx, t_hy], inc=sv_, dma=True)
            ta_ev = []
            for half in range(2):
                t_at = p.op("tensor", lambda e, half=half: e.transpose(
                    out=p_t2[half][:], in_=sAb[:, half * 128:(half + 1) * 128],
                    identity=sident_b[:]), waits=[t_ab, ta_ev[-1] if ta_ev else None], inc=st_)
                ta_ev.append(p.op("vector", lambda e, half=half: e.tensor_copy(
                    out=sAf[:, half * 128:(half + 1) * 128], in_=p_t2[half][:]),
                    waits=[t_at], inc=sv_))
                tb_prev[half] = ta_ev[-1]
            o_ev = []
            for mt in range(8):
                t_my = p.op("tensor", lambda e, mt=mt: e.matmul(
                    out=p_m[:, 0:128], lhsT=sPW[:, 0 * D + mt * 128: 0 * D + (mt + 1) * 128],
                    rhs=sAf[:, 0:128], start=True, stop=False),
                    waits=[ta_ev[-1]] + ([o_ev[-1]] if o_ev else [out_prev]), inc=None)
                t_my = p.op("tensor", lambda e, mt=mt: e.matmul(
                    out=p_m[:, 0:128], lhsT=sPW[:, 1 * D + mt * 128: 1 * D + (mt + 1) * 128],
                    rhs=sAf[:, 128:256], start=False, stop=True), inc=st_)
                o_ev.append(p.op("vector", lambda e, mt=mt, jb=jb: e.tensor_tensor(
                    out=souf[:, mt * 128:(mt + 1) * 128], in0=p_m[:, 0:128],
                    in1=sx3b[jb][:, mt * 128:(mt + 1) * 128], op=ALU.add),
                    waits=[t_my, t_x3], inc=sv_))
            out_prev = p.op("gpsimd", lambda e, rb=rb: e.dma_start(
                out=o_yf.rearrange("(k p) t -> p k t", p=128)[:, :, rb * 128:(rb + 1) * 128],
                in_=souf[:].rearrange("p (k c) -> p k c", k=8)),
                waits=[o_ev[-1]], inc=sgo, amt=16, dma=True)
            blk_prev[jb] = o_ev[-1]
            sc_prev = t_wn
            gx_prev = (ldgx, p.cnt["ldgx"])
        p.fence("gpsimd", [(sgo, p.cnt["sgo"])])
        p.barrier([sv_, ss, st_])
        p.emit(block)
    return nc



_IDENT = np.eye(128, dtype=np.float32)


def _prep_A(inputs):
    x = inputs["L0_attn_x"]
    wqkv = np.ascontiguousarray(inputs["L0_attn_w_qkv"])
    bqkv = np.ascontiguousarray(inputs["L0_attn_b_qkv"].reshape(24, 128).T)
    pt = inputs["L0_attn_proj_matrix"].transpose(0, 2, 1)
    proj = np.zeros((128, 8 * 2 * MF), dtype=np.float32)
    for hp in range(8):
        proj[0:64, hp * 512: hp * 512 + 256] = pt[2 * hp]
        proj[64:128, hp * 512 + 256: hp * 512 + 512] = pt[2 * hp + 1]
    proj = np.ascontiguousarray(proj)
    maps = []
    for c in range(NCORE):
        b = c // 4
        xa = np.ascontiguousarray(x[b][tok_index(c)])
        maps.append({"xa": xa, "wqkv": wqkv, "bqkv": bqkv, "proj": proj, "ident": _IDENT})
    return maps


def run_A(inputs, **kw):
    nc = build_A(**kw)
    maps = _prep_A(inputs)
    res = run_bass_kernel_spmd(nc, maps, core_ids=list(range(NCORE)))
    return res.results




def _performer_ref_np(xn, w_qkv, b_qkv, w_out, b_out, proj):
    qkv = xn @ w_qkv + b_qkv
    q, k, v = np.split(qkv, 3, axis=-1)
    DATA_NORM_ = 64 ** -0.25
    out = np.zeros_like(xn)
    for h in range(16):
        qh = q[:, h * 64:(h + 1) * 64] * DATA_NORM_
        kh = k[:, h * 64:(h + 1) * 64] * DATA_NORM_
        vh = v[:, h * 64:(h + 1) * 64]

        def fm(z):
            xp = z @ proj[h].T
            xp = xp - xp.max(-1, keepdims=True)
            return (np.exp(xp) + 1e-6) / 16.0
        qp = fm(qh)
        kp = fm(kh)
        kv = kp.T @ vh
        ks = kp.sum(0)
        num = qp @ kv
        den = qp @ ks + 1e-6
        out[:, h * 64:(h + 1) * 64] = num / den[:, None]
    return out @ w_out + b_out


def _prep_B(inputs, resA):
    x = inputs["L0_attn_x"]
    wffnT = np.ascontiguousarray(inputs["L0_ffn_weight"].T)
    bffn = np.ascontiguousarray(inputs["L0_ffn_bias"][None, :])
    wout = np.ascontiguousarray(inputs["L0_attn_w_out"])
    bout = np.ascontiguousarray(inputs["L0_attn_b_out"][None, :])
    wip = np.ascontiguousarray(np.concatenate([inputs["L1_attn_Wi"], inputs["L1_attn_Wp"]], axis=1))
    gwT = np.ascontiguousarray(inputs["L1_ffn_gate_w"].T)
    wexp = np.ascontiguousarray(inputs["L1_ffn_expert_w"].transpose(0, 2, 1))
    amax8 = np.ascontiguousarray(np.stack([resA[c]["o_amax"][0, 0] for c in range(NCORE)])[None, :])
    iota8 = np.arange(8, dtype=np.float32)[None, :]
    import ml_dtypes
    half = 512
    inv_freq = (1.0 / 10000.0 ** (np.arange(half, dtype=np.float32) / half)).astype(np.float32)
    maps = []
    for c in range(NCORE):
        b = c // 4
        tok = tok_index(c)
        ang = (tok[:, None].astype(np.float32) * inv_freq[None, :]).astype(np.float32)
        cosb = np.cos(ang).astype(ml_dtypes.bfloat16)
        sinb = np.sin(ang).astype(ml_dtypes.bfloat16)
        kvstack = np.stack([resA[4 * b + j]["o_kv"] for j in range(4)])
        kv4 = np.ascontiguousarray(kvstack.transpose(3, 1, 2, 0, 4).reshape(128, -1))
        maps.append({
            "xa": np.ascontiguousarray(x[b][tok]),
            "xn1T": np.ascontiguousarray(resA[c]["o_xn1"].T),
            "qpT": np.ascontiguousarray(resA[c]["o_qp"].transpose(0, 2, 1)),
            "kv4": kv4, "amax8": amax8, "wffnT": wffnT, "bffn": bffn,
            "wout": wout, "bout": bout, "wip": wip, "gwT": gwT, "wexp": wexp,
            "cosb": cosb, "sinb": sinb, "ident": _IDENT, "iota8": iota8,
        })
    return maps


def run_B(inputs, resA):
    nc = build_B()
    maps = _prep_B(inputs, resA)
    res = run_bass_kernel_spmd(nc, maps, core_ids=list(range(NCORE)))
    return res.results




def _prep_C(inputs, resB):
    import ml_dtypes
    Pfull = [np.zeros((T, D), ml_dtypes.bfloat16) for _ in range(B)]
    Phfull = [np.zeros((T, D), ml_dtypes.bfloat16) for _ in range(B)]
    for c in range(NCORE):
        b = c // 4
        Pfull[b][tok_index(c)] = resB[c]["o_P"]
        Phfull[b][tok_index(c)] = resB[c]["o_Ph"]
    PTb = [np.ascontiguousarray(Pfull[b].T) for b in range(B)]
    w1p = np.zeros((32, 64), np.float32)
    w1p[0:17] = inputs["L1_attn_W1"]
    iotas = np.tile(np.arange(512, dtype=np.int32)[None, :], (128, 1))
    iot128 = np.arange(128, dtype=np.float32)[:, None]
    ioy = np.tile(np.repeat(np.arange(16, dtype=np.float32), 16)[None, :], (128, 1)).astype(ml_dtypes.bfloat16)
    kmG0 = np.ones((32, 256), np.float32)
    for t in range(14):
        for k in range(16):
            for j in range(15):
                kmG0[j, 16 * t + k] = float((k <= t) and (j <= t))
            kmG0[15, 16 * t + k] = float(k <= t)
            kmG0[16, 16 * t + k] = float(k <= t)
    kmM0 = np.zeros((128, 16), np.float32)
    for t in range(14):
        for k in range(16):
            if k > t:
                kmM0[t, k] = NEG
    ones_kmG = np.ones((32, 256), np.float32)
    zeros_kmM = np.zeros((128, 16), np.float32)
    maps = []
    for c in range(NCORE):
        b = c // 4
        m = c % 4
        blocks = tok_blocks(c)
        sp = np.arange(512)[None, :]
        pp = np.arange(128)[:, None]
        cm = np.where(sp > 128 * m + pp, NEG, 0.0).astype(np.float32)
        rowg_np = np.stack([gb * 128 + np.arange(128, dtype=np.float32) for gb in blocks], 1)
        maps.append({
            "PT": PTb[b], "Phat": np.ascontiguousarray(Phfull[b]),
            "If_": np.ascontiguousarray(resB[c]["o_I"].T),
            "Ihf": np.ascontiguousarray(resB[c]["o_Ih"].T),
            "x3pT": np.ascontiguousarray(resB[c]["o_x3p"].T),
            "w1p": w1p, "b1": np.ascontiguousarray(inputs["L1_attn_b1"][:, None]),
            "w2": np.ascontiguousarray(inputs["L1_attn_W2"]),
            "b2": np.ascontiguousarray(inputs["L1_attn_b2"][:, None]),
            "wc": np.ascontiguousarray(inputs["L1_attn_Wc"]),
            "bc": np.ascontiguousarray(inputs["L1_attn_bc"][:, None]),
            "wm": np.ascontiguousarray(inputs["L1_attn_Wm"]),
            "bm": np.ascontiguousarray(inputs["L1_attn_bm"][:, None]),
            "pal": np.ascontiguousarray(inputs["L1_attn_palette"].reshape(D, 256)),
            "wo": np.ascontiguousarray(inputs["L1_attn_Wo"][0]),
            "ident": _IDENT, "cmask": cm, "rowg": np.ascontiguousarray(rowg_np),
            "iotas": iotas, "iot128": iot128,
            "ioy": np.ascontiguousarray(ioy),
            "kmG": np.ascontiguousarray((kmG0 if m == 0 else ones_kmG).astype(ml_dtypes.bfloat16)),
            "kmM": (kmM0 if m == 0 else zeros_kmM),
        })
    return maps


def run_C(inputs, resB):
    maps = _prep_C(inputs, resB)
    nc = build_C()
    res = run_bass_kernel_spmd(nc, maps, core_ids=list(range(NCORE)))
    return res.results


def kernel(**inputs):
    inputs = {k: np.asarray(v) for k, v in inputs.items()}
    resA = run_A(inputs)
    resB = run_B(inputs, resA)
    resC = run_C(inputs, resB)
    out = np.zeros((B, T, D), dtype=np.float32)
    for c in range(NCORE):
        b = c // 4
        out[b][tok_index(c)] = resC[c]["o_yf"].T
    return out


# version 04
# speedup vs baseline: 2.8121x; 1.0533x over previous
import math
import numpy as np
from contextlib import ExitStack

import concourse.bass as bass
import concourse.mybir as mybir
from concourse.bass_utils import run_bass_kernel_spmd

F32 = mybir.dt.float32
BF16 = mybir.dt.bfloat16
FP8 = mybir.dt.float8e4
I32 = mybir.dt.int32
U32 = mybir.dt.uint32
AF = mybir.ActivationFunctionType
ALU = mybir.AluOpType
AXL = mybir.AxisListType

B, T, D, NCORE = 2, 8192, 1024, 8
TPC = 2048
NH, HD, MF = 16, 64, 256
KSEL = 15
LN_EPS = 1e-5
PEPS = 1e-6
DATA_NORM = HD ** -0.25
NEG = -1.0e38
ENGS = ("sync", "vector", "scalar", "tensor", "gpsimd")
NT = TPC // 128


def tok_blocks(c):
    return list(range(c % 4, 64, 4))


def tok_index(c):
    return np.concatenate([np.arange(128) + 128 * bb for bb in tok_blocks(c)])


class Prog:

    def __init__(self, nc):
        self.nc = nc
        self.q = {e: [] for e in ENGS}
        self.cnt = {}

    def mksem(self, ctx, name):
        h = ctx.enter_context(self.nc.semaphore(name))
        self.cnt[name] = 0
        return h

    def op(self, eng, build, waits=(), inc=None, amt=1, dma=False):
        waits = tuple(w for w in waits if w is not None)
        tok = None
        if inc is not None:
            self.cnt[inc.name] += amt
            tok = (inc, self.cnt[inc.name])
        need_drain = eng in ("vector", "scalar", "gpsimd", "tensor") and not dma

        def f(e, build=build, waits=waits, inc=inc, amt=amt, need_drain=need_drain):
            for (s, v) in waits:
                e.wait_ge(s, v)
            ins = build(e)
            if need_drain:
                ins = e.drain()
            if inc is not None:
                ins.then_inc(inc, amt)

        self.q[eng].append(f)
        return tok

    def barrier(self, sems):
        toks = [(s, self.cnt[s.name]) for s in sems]
        for e in ENGS:
            self.fence(e, toks)

    def fence(self, eng, waits):
        waits = tuple(w for w in waits if w is not None)

        def f(e, waits=waits):
            for (s, v) in waits:
                e.wait_ge(s, v)

        self.q[eng].append(f)

    def setup_consts(self, ctx, values, sem=None):
        sem = self.mksem(ctx, "scst")
        E = ctx.enter_context
        vals = sorted(set(float(v) for v in values))
        tile = E(self.nc.sbuf_tensor("consts", [128, max(1, len(vals))], F32))
        for i, v in enumerate(vals):
            self.nc.const_aps.register(F32, v, tile[:, i:i + 1])

        def f(e, vals=vals, tile=tile):
            ins = None
            for i, v in enumerate(vals):
                ins = e.memset(tile[:, i:i + 1], v)
            ins.then_inc(sem, 1)
        self.cnt[sem.name] += 1
        tok = (sem, self.cnt[sem.name])
        self.q["gpsimd"].append(f)
        for e in ENGS:
            if e != "gpsimd":
                self.fence(e, [tok])
        return tok

    def emit(self, block):
        for e in ENGS:
            fns = self.q[e]

            def runner(eng, fns=fns):
                for f in fns:
                    f(eng)

            getattr(block, e)(runner)


def build_A(hp_count=8, ln_only=False, stage='kv'):
    nc = bass.Bass("TRN2", target_bir_lowering=False)
    xa = nc.dram_tensor("xa", [TPC, D], F32, kind="ExternalInput").ap()
    wqkv = nc.dram_tensor("wqkv", [D, 3 * D], F32, kind="ExternalInput").ap()
    bqkv = nc.dram_tensor("bqkv", [128, 24], F32, kind="ExternalInput").ap()
    proj = nc.dram_tensor("proj", [128, 8 * 2 * MF], F32, kind="ExternalInput").ap()
    ident = nc.dram_tensor("ident", [128, 128], F32, kind="ExternalInput").ap()

    o_xn1 = nc.dram_tensor("o_xn1", [TPC, D], F32, kind="ExternalOutput").ap()
    o_qp = nc.dram_tensor("o_qp", [NH, TPC, MF], BF16, kind="ExternalOutput").ap()
    o_kv = nc.dram_tensor("o_kv", [NH, 2, 128, 65], F32, kind="ExternalOutput").ap()
    o_amax = nc.dram_tensor("o_amax", [1, 1], F32, kind="ExternalOutput").ap()

    with ExitStack() as ctx:
        E = ctx.enter_context
        sx = [E(nc.sbuf_tensor(f"sx{j}", [128, D], F32)) for j in range(2)]
        sxn = [E(nc.sbuf_tensor(f"sxn{j}", [128, D], F32)) for j in range(2)]
        ssq = E(nc.sbuf_tensor("ssq", [128, D], F32))
        sstat = [E(nc.sbuf_tensor(f"sstat{j}", [128, 8], F32)) for j in range(2)]
        samax = E(nc.sbuf_tensor("samax", [128, 1], F32))
        samax_r = E(nc.sbuf_tensor("samax_r", [1, 128], F32))
        samax_f = E(nc.sbuf_tensor("samax_f", [1, 1], F32))
        sident = E(nc.sbuf_tensor("sident", [128, 128], F32))
        sident_b = E(nc.sbuf_tensor("sident_b", [128, 128], BF16))
        sxnT = E(nc.sbuf_tensor("sxnT", [128, 8 * TPC], BF16))
        swq = E(nc.sbuf_tensor("swq", [128, 8 * 3 * D], BF16))
        sbias = E(nc.sbuf_tensor("sbias", [128, 24], F32))
        sproj = E(nc.sbuf_tensor("sproj", [128, 8 * 2 * MF], BF16))
        sproj0 = E(nc.sbuf_tensor("sproj0", [128, 8 * 2 * MF], BF16))
        sq = E(nc.sbuf_tensor("sq", [128, TPC], BF16))
        sk = E(nc.sbuf_tensor("sk", [128, TPC], BF16))
        sv = E(nc.sbuf_tensor("svv", [128, TPC], BF16))
        sexp = E(nc.sbuf_tensor("sexp", [128, 512], F32))
        snmax = E(nc.sbuf_tensor("snmax", [128, 4], F32))
        sqp = E(nc.sbuf_tensor("sqp", [128, 512], BF16))
        skp = E(nc.sbuf_tensor("skp", [128, 16 * 512], BF16))
        svt = E(nc.sbuf_tensor("svt", [128, 16 * 130], BF16))
        skv = E(nc.sbuf_tensor("skv", [128, 65], F32))

        p_mm = [E(nc.psum_tensor(f"p_mm{j}", [128, 512], F32)) for j in range(2)]
        p_xp = [E(nc.psum_tensor(f"p_xp{j}", [128, 512], F32)) for j in range(2)]
        p_vt = E(nc.psum_tensor("p_vt", [128, 64], BF16))
        p_kv = [E(nc.psum_tensor(f"p_kv{j}", [128, 65], F32)) for j in range(2)]
        p_tr = [p_xp[0][:, 0:128], p_xp[1][:, 0:128]]

        p = Prog(nc)
        ld = p.mksem(ctx, "ld")
        lda = [p.mksem(ctx, "lda0"), p.mksem(ctx, "lda1")]
        ldg = p.mksem(ctx, "ldg")
        sv_ = p.mksem(ctx, "sv_")
        ss = p.mksem(ctx, "ss")
        st_ = p.mksem(ctx, "st_")
        sg = p.mksem(ctx, "sg")
        sgx = [p.mksem(ctx, "sgx0"), p.mksem(ctx, "sgx1")]
        sgq = p.mksem(ctx, "sgq")
        sgk = p.mksem(ctx, "sgk")
        block = E(nc.Block())
        p.setup_consts(ctx, [0.0, 1.0, -0.5, -1.0, LN_EPS, 1.0 / D, DATA_NORM, 1.0 / 16.0, PEPS / 16.0], sg)

        t_ident = p.op("sync", lambda e: e.dma_start(out=sident[:], in_=ident), inc=ld, amt=16)
        t_bias = p.op("sync", lambda e: e.dma_start(out=sbias[:], in_=bqkv), inc=ld, amt=16)
        t_ident = t_bias = (ld, 32)
        t_wq = []
        for k in range(8):
            t_wq.append(p.op("gpsimd",
                             lambda e, k=k: e.dma_start(out=swq[:, k * 3 * D:(k + 1) * 3 * D],
                                                        in_=wqkv[k * 128:(k + 1) * 128, :]),
                             inc=ldg, amt=16, dma=True))
        t_proj = p.op("gpsimd", lambda e: e.dma_start(out=sproj0[:, :], in_=proj), inc=ldg, amt=16, dma=True)
        t_identb = p.op("vector", lambda e: e.tensor_copy(out=sident_b[:], in_=sident[:]),
                        waits=[t_ident], inc=sv_)
        t_ldg_all = (ldg, p.cnt["ldg"])
        t_projs = p.op("vector",
                       lambda e: e.tensor_scalar(out=sproj[:], in0=sproj0[:], scalar1=DATA_NORM,
                                                 scalar2=None, op0=ALU.mult),
                       waits=[t_ldg_all], inc=sv_)
        sgm = p.mksem(ctx, "sgm")
        t_ones = p.op("gpsimd", lambda e: e.memset(svt[:], 1.0), inc=sgm, amt=1)

        xn_done = [None] * 16
        mean_done = [None] * 16
        ev_done = [None] * 16
        out_done = [None] * 16
        am_prev = None
        trbank_prev = {0: None, 1: None}
        for i in range(16):
            j = i % 2
            tl = p.op("sync", lambda e, i=i, j=j: e.dma_start(out=sx[j][:], in_=xa[i * 128:(i + 1) * 128, :]),
                      waits=([xn_done[i - 2], mean_done[i - 2]] if i >= 2 else []), inc=lda[j], amt=16)
            p.op("vector", lambda e, j=j: e.reduce_sum(out=sstat[j][:, 0:1], in_=sx[j][:], axis=AXL.X),
                 waits=[tl] + ([xn_done[i - 2]] if i >= 2 else []), inc=sv_)
            mean_done[i] = p.op("vector", lambda e, j=j: e.tensor_scalar(
                out=sstat[j][:, 0:1], in0=sstat[j][:, 0:1], scalar1=1.0 / D, scalar2=None, op0=ALU.mult),
                inc=sv_)
            t_sq = p.op("scalar", lambda e, j=j: e.activation(out=ssq[:], in_=sx[j][:], func=AF.Square,
                                                              accum_out=sstat[j][:, 1:2]),
                        waits=[tl, mean_done[i]], inc=ss)
            p.op("vector", lambda e, j=j: e.tensor_scalar(out=sstat[j][:, 2:3], in0=sstat[j][:, 1:2],
                                                          scalar1=1.0 / D, scalar2=None, op0=ALU.mult),
                 waits=[t_sq], inc=sv_)
            p.op("vector", lambda e, j=j: e.tensor_tensor(out=sstat[j][:, 3:4], in0=sstat[j][:, 0:1],
                                                          in1=sstat[j][:, 0:1], op=ALU.mult), inc=sv_)
            t_var = p.op("vector", lambda e, j=j: e.tensor_tensor(out=sstat[j][:, 2:3], in0=sstat[j][:, 2:3],
                                                                  in1=sstat[j][:, 3:4], op=ALU.subtract), inc=sv_)
            p.op("scalar", lambda e, j=j: e.activation(out=sstat[j][:, 4:5], in_=sstat[j][:, 2:3],
                                                       func=AF.Ln, bias=LN_EPS, scale=1.0),
                 waits=[t_var], inc=ss)
            t_rstd = p.op("scalar", lambda e, j=j: e.activation(out=sstat[j][:, 4:5], in_=sstat[j][:, 4:5],
                                                                func=AF.Exp, scale=-0.5), inc=ss)
            p.op("vector", lambda e, j=j: e.tensor_tensor(out=sstat[j][:, 5:6], in0=sstat[j][:, 0:1],
                                                          in1=sstat[j][:, 4:5], op=ALU.mult),
                 waits=[t_rstd], inc=sv_)
            t_bneg = p.op("vector", lambda e, j=j: e.tensor_scalar(out=sstat[j][:, 5:6], in0=sstat[j][:, 5:6],
                                                                   scalar1=-1.0, scalar2=None, op0=ALU.mult),
                          inc=sv_)
            xn_done[i] = p.op("vector", lambda e, j=j: e.tensor_scalar(
                out=sxn[j][:], in0=sx[j][:], scalar1=sstat[j][:, 4:5], scalar2=sstat[j][:, 5:6],
                op0=ALU.mult, op1=ALU.add),
                waits=[t_bneg] + ([ev_done[i - 2], out_done[i - 2]] if i >= 2 else []), inc=sv_)
            p.op("vector", lambda e, j=j: e.tensor_reduce(out=sstat[j][:, 6:7], in_=sxn[j][:], axis=AXL.X,
                                                          op=ALU.max, apply_absolute_value=True),
                 waits=[xn_done[i]], inc=sv_)
            if i == 0:
                am_prev = p.op("vector", lambda e, j=j: e.tensor_copy(out=samax[:], in_=sstat[j][:, 6:7]),
                               inc=sv_)
            else:
                am_prev = p.op("vector", lambda e, j=j: e.tensor_tensor(out=samax[:], in0=samax[:],
                                                                        in1=sstat[j][:, 6:7], op=ALU.max),
                               inc=sv_)
            out_done[i] = p.op("gpsimd", lambda e, i=i, j=j: e.dma_start(
                out=o_xn1[i * 128:(i + 1) * 128, :], in_=sxn[j][:]),
                waits=[xn_done[i]], inc=sgx[j], amt=16, dma=True)
            ev_tok = {}
            for k in range(8):
                tb = k % 2
                wt = [xn_done[i], t_ident]
                if k >= 2:
                    wt.append(ev_tok[k - 2])
                else:
                    wt.append(trbank_prev[tb])
                tt = p.op("tensor", lambda e, j=j, k=k, tb=tb: e.transpose(
                    out=p_tr[tb], in_=sxn[j][:, k * 128:(k + 1) * 128], identity=sident[:]),
                    waits=wt, inc=st_)
                ev_tok[k] = p.op("vector", lambda e, i=i, k=k, tb=tb: e.tensor_copy(
                    out=sxnT[:, k * TPC + i * 128: k * TPC + (i + 1) * 128], in_=p_tr[tb]),
                    waits=[tt], inc=sv_)
            trbank_prev = {0: ev_tok[6], 1: ev_tok[7]}
            ev_done[i] = ev_tok[7]

        t_amT = p.op("tensor", lambda e: e.transpose(out=p_xp[0][0:1, 0:128], in_=samax[:],
                                                     identity=sident[:]),
                     waits=[am_prev, trbank_prev[0]], inc=st_)
        t_amE = p.op("vector", lambda e: e.tensor_copy(out=samax_r[:], in_=p_xp[0][0:1, 0:128]),
                     waits=[t_amT], inc=sv_)
        t_amR = p.op("vector", lambda e: e.tensor_reduce(out=samax_f[:], in_=samax_r[:], axis=AXL.X,
                                                         op=ALU.max), waits=[t_amE], inc=sv_)
        t_amO = p.op("gpsimd", lambda e: e.dma_start(out=o_amax, in_=samax_f[:]),
                     waits=[t_amR], inc=sg, amt=16, dma=True)

        if ln_only:
            p.fence("gpsimd", [(sg, p.cnt["sg"]), (sgx[0], p.cnt["sgx0"]), (sgx[1], p.cnt["sgx1"]),
                               t_amO, (sv_, p.cnt["sv_"]), (ss, p.cnt["ss"]), (st_, p.cnt["st_"])])
            p.emit(block)
            return nc
        all_xnT = ev_done[15]
        mm_prev = {0: None, 1: None}
        xp_prev = {0: t_amE, 1: None}
        vt_prev = None
        kv_prev = {0: None, 1: None}
        exp_cons = None
        nm_cons = None
        qp_cons = None
        kv_last_mm = None
        skv_dma_prev = None
        for hp in range(hp_count):
            slab_toks = {}
            for sl, (dst, col0) in enumerate(((sq, 128 * hp), (sk, D + 128 * hp), (sv, 2 * D + 128 * hp))):
                last_ev = None
                for ntc in range(4):
                    pb = (sl * 4 + ntc) % 2
                    t_mm = None
                    for kt in range(8):
                        wt = []
                        if kt == 0:
                            wt = [mm_prev[pb], all_xnT] + ([t_ldg_all, t_bias] if hp == 0 and sl == 0 and ntc == 0 else [])
                            if sl == 0 and ntc == 0:
                                wt.append(kv_last_mm)
                                wt.append(exp_cons)
                        t_mm = p.op("tensor", lambda e, kt=kt, col0=col0, ntc=ntc, pb=pb: e.matmul(
                            out=p_mm[pb][:],
                            lhsT=swq[:, kt * 3 * D + col0: kt * 3 * D + col0 + 128],
                            rhs=sxnT[:, kt * TPC + ntc * 512: kt * TPC + (ntc + 1) * 512],
                            start=(kt == 0), stop=(kt == 7)),
                            waits=wt, inc=(st_ if kt == 7 else None))
                    bias_col = col0 // 128
                    last_ev = p.op("vector", lambda e, dst=dst, ntc=ntc, pb=pb, bias_col=bias_col: e.tensor_scalar(
                        out=dst[:, ntc * 512:(ntc + 1) * 512], in0=p_mm[pb][:],
                        scalar1=sbias[:, bias_col:bias_col + 1], scalar2=None, op0=ALU.add),
                        waits=[t_mm], inc=sv_)
                    mm_prev[pb] = last_ev
                slab_toks[sl] = last_ev

            for ic in range(NT if stage != 'slab' else 0):
                for h2 in range(2):
                    wt = [slab_toks[2], t_ones, vt_prev, t_identb]
                    ttv = p.op("tensor", lambda e, ic=ic, h2=h2: e.transpose(
                        out=p_vt[:], in_=sv[h2 * 64:(h2 + 1) * 64, ic * 128:(ic + 1) * 128],
                        identity=sident_b[h2 * 64:(h2 + 1) * 64, h2 * 64:(h2 + 1) * 64]),
                        waits=wt, inc=st_)
                    vt_prev = p.op("vector", lambda e, ic=ic, h2=h2: e.tensor_copy(
                        out=svt[:, ic * 130 + h2 * 65: ic * 130 + h2 * 65 + 64], in_=p_vt[0:128, 0:64]),
                        waits=[ttv, kv_last_mm if ic == 0 and h2 == 0 else None], inc=sv_)
            vt_all = vt_prev

            kp_last = None
            for src, is_q in (((sq, True), (sk, False)) if stage in ('fm', 'kv') else ()):
                for ic in range(NT):
                    pb = ic % 2
                    mm2 = p.op("tensor", lambda e, src=src, ic=ic, pb=pb, hp=hp: e.matmul(
                        out=p_xp[pb][:], lhsT=src[:, ic * 128:(ic + 1) * 128],
                        rhs=sproj[:, hp * 512:(hp + 1) * 512], start=True, stop=True),
                        waits=[slab_toks[0] if is_q else slab_toks[1], t_projs, xp_prev[pb]], inc=st_)
                    t_nm = p.op("vector", lambda e, pb=pb: e.tensor_reduce(
                        out=snmax[:, 0:2], in_=p_xp[pb][:].rearrange("p (h m) -> p h m", h=2), axis=AXL.X,
                        op=ALU.max, negate=True),
                        waits=[mm2, nm_cons], inc=sv_)
                    e1 = p.op("scalar", lambda e, pb=pb: e.activation(
                        out=sexp[:, 0:256], in_=p_xp[pb][:, 0:256], func=AF.Exp,
                        bias=snmax[:, 0:1], scale=1.0), waits=[t_nm, exp_cons], inc=ss)
                    e2 = p.op("scalar", lambda e, pb=pb: e.activation(
                        out=sexp[:, 256:512], in_=p_xp[pb][:, 256:512], func=AF.Exp,
                        bias=snmax[:, 1:2], scale=1.0), inc=ss)
                    xp_prev[pb] = e2
                    if is_q:
                        t_qp = p.op("vector", lambda e: e.tensor_scalar(
                            out=sqp[:], in0=sexp[:], scalar1=1.0 / 16.0, scalar2=PEPS / 16.0,
                            op0=ALU.mult, op1=ALU.add), waits=[e2, qp_cons], inc=sv_)
                        exp_cons = t_qp
                        nm_cons = e2
                        qp_cons = p.op("gpsimd", lambda e, ic=ic, hp=hp: e.dma_start(
                            out=o_qp.rearrange("h t m -> t h m")[ic * 128:(ic + 1) * 128,
                                                                 2 * hp:2 * hp + 2, :],
                            in_=sqp[:].rearrange("p (h m) -> p h m", h=2)),
                            waits=[t_qp], inc=sgq, amt=16, dma=True)
                    else:
                        kp_last = p.op("vector", lambda e, ic=ic: e.tensor_scalar(
                            out=skp[:, ic * 512:(ic + 1) * 512], in0=sexp[:], scalar1=1.0 / 16.0,
                            scalar2=PEPS / 16.0, op0=ALU.mult, op1=ALU.add),
                            waits=[e2, kv_last_mm if ic == 0 else None], inc=sv_)
                        exp_cons = kp_last
                        nm_cons = e2

            for h2 in range(2 if stage == 'kv' else 0):
                head = 2 * hp + h2
                for mt in range(2):
                    t_mm = None
                    for ic in range(NT):
                        wt = [kp_last, vt_all, kv_prev[mt]] if ic == 0 else []
                        t_mm = p.op("tensor", lambda e, ic=ic, h2=h2, mt=mt: e.matmul(
                            out=p_kv[mt][:],
                            lhsT=skp[:, ic * 512 + h2 * 256 + mt * 128: ic * 512 + h2 * 256 + (mt + 1) * 128],
                            rhs=svt[:, ic * 130 + h2 * 65: ic * 130 + h2 * 65 + 65],
                            start=(ic == 0), stop=(ic == NT - 1)),
                            waits=wt, inc=(st_ if ic == NT - 1 else None))
                        kv_last_mm = t_mm if ic == NT - 1 else kv_last_mm
                    evk = p.op("vector", lambda e, mt=mt: e.tensor_copy(out=skv[:], in_=p_kv[mt][:]),
                               waits=[t_mm, skv_dma_prev], inc=sv_)
                    skv_dma_prev = p.op("gpsimd", lambda e, head=head, mt=mt: e.dma_start(
                        out=o_kv[head, mt, :, :], in_=skv[:]), waits=[evk], inc=sgk, amt=16, dma=True)
                    kv_prev[mt] = skv_dma_prev

        p.fence("gpsimd", [(sg, p.cnt["sg"]), (sgx[0], p.cnt["sgx0"]), (sgx[1], p.cnt["sgx1"]),
                           (sgq, p.cnt["sgq"]), (sgk, p.cnt["sgk"]), t_amO])
        p.emit(block)
    return nc



def build_B():
    nc = bass.Bass("TRN2", target_bir_lowering=False)
    xa = nc.dram_tensor("xa", [TPC, D], F32, kind="ExternalInput").ap()
    xn1T = nc.dram_tensor("xn1T", [D, TPC], F32, kind="ExternalInput").ap()
    qpT = nc.dram_tensor("qpT", [NH, MF, TPC], BF16, kind="ExternalInput").ap()
    kv4 = nc.dram_tensor("kv4", [128, NH * 2 * 4 * 65], F32, kind="ExternalInput").ap()
    amax8 = nc.dram_tensor("amax8", [1, 8], F32, kind="ExternalInput").ap()
    wffnT = nc.dram_tensor("wffnT", [D, D], F32, kind="ExternalInput").ap()
    bffn = nc.dram_tensor("bffn", [1, D], F32, kind="ExternalInput").ap()
    wout = nc.dram_tensor("wout", [D, D], F32, kind="ExternalInput").ap()
    bout = nc.dram_tensor("bout", [1, D], F32, kind="ExternalInput").ap()
    wip = nc.dram_tensor("wip", [D, 2 * D], F32, kind="ExternalInput").ap()
    gwT = nc.dram_tensor("gwT", [D, 8], F32, kind="ExternalInput").ap()
    wexp = nc.dram_tensor("wexp", [8, D, D], F32, kind="ExternalInput").ap()
    cosb = nc.dram_tensor("cosb", [TPC, 512], BF16, kind="ExternalInput").ap()
    sinb = nc.dram_tensor("sinb", [TPC, 512], BF16, kind="ExternalInput").ap()
    ident = nc.dram_tensor("ident", [128, 128], F32, kind="ExternalInput").ap()
    iota8 = nc.dram_tensor("iota8", [1, 8], F32, kind="ExternalInput").ap()

    o_x3p = nc.dram_tensor("o_x3p", [TPC, D], F32, kind="ExternalOutput").ap()
    o_d1 = nc.dram_tensor("o_d1", [128, D], F32, kind="ExternalOutput").ap()
    o_d2 = nc.dram_tensor("o_d2", [128, D], F32, kind="ExternalOutput").ap()
    o_d3 = nc.dram_tensor("o_d3", [128, TPC], F32, kind="ExternalOutput").ap()
    o_d0 = nc.dram_tensor("o_d0", [1, 16], F32, kind="ExternalOutput").ap()
    o_P = nc.dram_tensor("o_P", [TPC, D], BF16, kind="ExternalOutput").ap()
    o_Ph = nc.dram_tensor("o_Ph", [TPC, D], BF16, kind="ExternalOutput").ap()
    o_I = nc.dram_tensor("o_I", [TPC, D], BF16, kind="ExternalOutput").ap()
    o_Ih = nc.dram_tensor("o_Ih", [TPC, D], BF16, kind="ExternalOutput").ap()

    with ExitStack() as ctx:
        E = ctx.enter_context
        p = Prog(nc)
        ld1 = p.mksem(ctx, "ld1")
        ldg1 = p.mksem(ctx, "ldg1")
        ldx = [p.mksem(ctx, "ldx0"), p.mksem(ctx, "ldx1")]
        ldw = [p.mksem(ctx, "ldw0"), p.mksem(ctx, "ldw1")]
        lda = [p.mksem(ctx, "lda0"), p.mksem(ctx, "lda1")]
        sv_ = p.mksem(ctx, "sv_")
        ss = p.mksem(ctx, "ss")
        st_ = p.mksem(ctx, "st_")
        sgo = [p.mksem(ctx, "sgo0"), p.mksem(ctx, "sgo1")]
        block = E(nc.Block())
        p.setup_consts(ctx, [0.0, 1.0, -0.5, -1.0, LN_EPS, 1.0 / D, 448.0, 4.0, 0.5, 1e-12,
                             1000.0, 1e-24, -1000.0], sgo[0])

        sident = E(nc.sbuf_tensor("sident", [128, 128], F32))
        sident_b = E(nc.sbuf_tensor("sident_b", [128, 128], BF16))
        sones = E(nc.sbuf_tensor("sones", [1, 128], F32))
        ssc = E(nc.sbuf_tensor("ssc", [1, 16], F32))
        sscb = E(nc.sbuf_tensor("sscb", [128, 4], F32))
        sx2 = E(nc.sbuf_tensor("sx2", [128, NT * D], F32))
        sstat = E(nc.sbuf_tensor("sstat", [128, 8], F32))
        sbffn = E(nc.sbuf_tensor("sbffn", [128, D], F32))
        sbout = E(nc.sbuf_tensor("sbout", [128, D], F32))
        ssc_f = E(nc.sbuf_tensor("ssc_f", [8, TPC], F32))
        sgw32 = E(nc.sbuf_tensor("sgw32", [128, 64], F32))

        p_a = [E(nc.psum_tensor(f"p_a{j}", [128, 512], F32)) for j in range(4)]
        p_b = [E(nc.psum_tensor(f"p_b{j}", [128, 512], F32)) for j in range(2)]
        p_t = [E(nc.psum_tensor(f"p_t{j}", [128, 128], F32)) for j in range(2)]

        p.op("sync", lambda e: e.dma_start(out=sident[:], in_=ident), inc=ld1, amt=16)
        p.op("sync", lambda e: e.dma_start(out=ssc[0:1, 0:8], in_=amax8), inc=ld1, amt=16)
        p.op("sync", lambda e: e.dma_start(out=sbffn[:], in_=bffn.partition_broadcast(128)), inc=ld1, amt=16)
        p.op("sync", lambda e: e.dma_start(out=sbout[:], in_=bout.partition_broadcast(128)), inc=ld1, amt=16)
        t_ld1 = (ld1, p.cnt["ld1"])
        t_identb = p.op("vector", lambda e: e.tensor_copy(out=sident_b[:], in_=sident[:]),
                        waits=[t_ld1], inc=sv_)
        sgm = p.mksem(ctx, "sgm")
        t_ones = p.op("gpsimd", lambda e: e.memset(sones[:], 1.0), inc=sgm)

        c12 = ctx.enter_context(ExitStack())
        sffo = c12.enter_context(nc.sbuf_tensor("sffo", [128, NT * D], BF16))
        with ExitStack() as c1:
            E1 = c1.enter_context
            sxf = [E1(nc.sbuf_tensor(f"sxf{j}", [128, TPC], F32)) for j in range(2)]
            swfb = [E1(nc.sbuf_tensor(f"swfb{j}", [128, D], F32)) for j in range(2)]
            sx8 = E1(nc.sbuf_tensor("sx8", [128, 8 * TPC], FP8))
            sw8 = E1(nc.sbuf_tensor("sw8", [128, 8 * D], FP8))
            t_am = p.op("vector", lambda e: e.tensor_reduce(out=ssc[0:1, 8:9], in_=ssc[0:1, 0:8],
                                                            axis=AXL.X, op=ALU.max),
                        waits=[t_ld1], inc=sv_)
            wl_prev = {0: None, 1: None}
            am_acc = None
            for k in range(8):
                j = k % 2
                t_wl = p.op("sync", lambda e, k=k, j=j: e.dma_start(
                    out=swfb[j][:], in_=wffnT[k * 128:(k + 1) * 128, :]),
                    waits=[wl_prev[j]], inc=lda[j], amt=16)
                t_r = p.op("vector", lambda e, j=j, k=k: e.tensor_reduce(
                    out=sstat[:, 1:2], in_=swfb[j][:], axis=AXL.X, op=ALU.max,
                    apply_absolute_value=True), waits=[t_wl], inc=sv_)
                if k == 0:
                    am_acc = p.op("vector", lambda e: e.tensor_copy(out=sstat[:, 0:1], in_=sstat[:, 1:2]),
                                  inc=sv_)
                else:
                    am_acc = p.op("vector", lambda e: e.tensor_tensor(
                        out=sstat[:, 0:1], in0=sstat[:, 0:1], in1=sstat[:, 1:2], op=ALU.max), inc=sv_)
                wl_prev[j] = am_acc
            t_wT = p.op("tensor", lambda e: e.transpose(out=p_t[0][0:1, 0:128], in_=sstat[:, 0:1],
                                                        identity=sident[:]), waits=[am_acc, t_ld1], inc=st_)
            t_wr3 = p.op("vector", lambda e: e.tensor_reduce(out=ssc[0:1, 9:10],
                                                             in_=p_t[0][0:1, 0:128], axis=AXL.X,
                                                             op=ALU.max), waits=[t_wT], inc=sv_)

            def scales(e):
                e.tensor_scalar(out=ssc[0:1, 10:11], in0=ssc[0:1, 8:9], scalar1=1e-12, scalar2=None,
                                op0=ALU.max)
                e.drain()
                e.reciprocal(out=ssc[0:1, 10:11], in_=ssc[0:1, 10:11])
                e.drain()
                e.tensor_scalar(out=ssc[0:1, 10:11], in0=ssc[0:1, 10:11], scalar1=448.0, scalar2=448.0,
                                op0=ALU.mult, op1=ALU.min)
                e.drain()
                e.tensor_scalar(out=ssc[0:1, 11:12], in0=ssc[0:1, 9:10], scalar1=1e-12, scalar2=None,
                                op0=ALU.max)
                e.drain()
                e.reciprocal(out=ssc[0:1, 11:12], in_=ssc[0:1, 11:12])
                e.drain()
                e.tensor_scalar(out=ssc[0:1, 11:12], in0=ssc[0:1, 11:12], scalar1=448.0, scalar2=448.0,
                                op0=ALU.mult, op1=ALU.min)
                e.drain()
                e.tensor_tensor(out=ssc[0:1, 12:13], in0=ssc[0:1, 10:11], in1=ssc[0:1, 11:12],
                                op=ALU.mult)
                e.drain()
                e.reciprocal(out=ssc[0:1, 12:13], in_=ssc[0:1, 12:13])
                e.drain()
                e.tensor_scalar(out=ssc[0:1, 12:13], in0=ssc[0:1, 12:13], scalar1=4.0, scalar2=None,
                                op0=ALU.mult)
                e.drain()
                e.tensor_scalar(out=ssc[0:1, 10:11], in0=ssc[0:1, 10:11], scalar1=0.5, scalar2=None,
                                op0=ALU.mult)
                e.drain()
                return e.tensor_scalar(out=ssc[0:1, 11:12], in0=ssc[0:1, 11:12], scalar1=0.5,
                                       scalar2=None, op0=ALU.mult)
            t_sc = p.op("vector", scales, waits=[t_am, t_wr3], inc=sv_)
            t_bc = []
            for i_src, i_dst in ((10, 0), (11, 1), (12, 2)):
                tm = p.op("tensor", lambda e, i_src=i_src: e.matmul(
                    out=p_t[1][:, 0:1], lhsT=sones[:], rhs=ssc[0:1, i_src:i_src + 1],
                    start=True, stop=True), waits=[t_sc, t_ones, t_bc[-1] if t_bc else None], inc=st_)
                t_bc.append(p.op("vector", lambda e, i_dst=i_dst: e.tensor_copy(
                    out=sscb[:, i_dst:i_dst + 1], in_=p_t[1][:, 0:1]), waits=[tm], inc=sv_))
            t_x8 = []
            for k in range(8):
                j = k % 2
                t_xl = p.op("sync", lambda e, k=k, j=j: e.dma_start(
                    out=sxf[j][:], in_=xn1T[k * 128:(k + 1) * 128, :]),
                    waits=[t_x8[k - 2] if k >= 2 else t_sc], inc=lda[j], amt=16)
                t_x8.append(p.op("vector", lambda e, k=k, j=j: e.tensor_scalar(
                    out=sx8[:, k * TPC:(k + 1) * TPC], in0=sxf[j][:],
                    scalar1=sscb[:, 0:1], scalar2=None, op0=ALU.mult),
                    waits=[t_xl, t_bc[0]], inc=sv_))
            t_w8 = []
            for k in range(8):
                j = k % 2
                t_wl = p.op("sync", lambda e, k=k, j=j: e.dma_start(
                    out=swfb[j][:], in_=wffnT[k * 128:(k + 1) * 128, :]),
                    waits=[t_w8[k - 2] if k >= 2 else t_x8[6 + k]], inc=lda[j], amt=16)
                t_w8.append(p.op("vector", lambda e, k=k, j=j: e.tensor_scalar(
                    out=sw8[:, k * D:(k + 1) * D], in0=swfb[j][:],
                    scalar1=sscb[:, 1:2], scalar2=None, op0=ALU.mult),
                    waits=[t_wl, t_bc[1]], inc=sv_))
            prev_ev = {0: None, 1: None}
            for icc in range(NT):
                for mh in range(2):
                    pb = (icc * 2 + mh) % 2
                    t_mm = None
                    for kt in range(8):
                        wt = [prev_ev[pb], t_x8[7], t_w8[7]] if kt == 0 else []
                        t_mm = p.op("tensor", lambda e, icc=icc, mh=mh, kt=kt, pb=pb: e.matmul(
                            out=p_a[pb][:], lhsT=sx8[:, kt * TPC + icc * 128: kt * TPC + (icc + 1) * 128],
                            rhs=sw8[:, kt * D + mh * 512: kt * D + (mh + 1) * 512],
                            start=(kt == 0), stop=(kt == 7)),
                            waits=wt, inc=(st_ if kt == 7 else None))
                    prev_ev[pb] = p.op("vector", lambda e, icc=icc, mh=mh, pb=pb: e.scalar_tensor_tensor(
                        out=sffo[:, icc * D + mh * 512: icc * D + (mh + 1) * 512], in0=p_a[pb][:],
                        scalar=sscb[:, 2:3], in1=sbffn[:, mh * 512:(mh + 1) * 512],
                        op0=ALU.mult, op1=ALU.add),
                        waits=[t_mm, t_bc[2], t_ld1], inc=sv_)
            p.barrier([sv_, ss, st_])
        p.op("gpsimd", lambda e: e.dma_start(out=o_d1, in_=sffo[:, 0:D]), inc=sgo[0], amt=16, dma=True)
        p.op("gpsimd", lambda e: e.dma_start(out=o_d0, in_=ssc[:]), inc=sgo[0], amt=16, dma=True)

        with ExitStack() as c2:
            E2 = c2.enter_context
            skvb = E2(nc.sbuf_tensor("skvb", [128, NH * 2 * 65], BF16))
            with ExitStack() as c2a:
                skvs = c2a.enter_context(nc.sbuf_tensor("skvs", [128, NH * 2 * 4 * 65], F32))
                t_kv = p.op("sync", lambda e: e.dma_start(out=skvs[:], in_=kv4), inc=ld1, amt=16)

                def kvsum(e):
                    v = skvs[:].rearrange("p (h m c f) -> p (h m) c f", h=NH, m=2, c=4)
                    e.tensor_tensor(out=v[:, :, 0, :], in0=v[:, :, 0, :], in1=v[:, :, 1, :], op=ALU.add)
                    e.drain()
                    e.tensor_tensor(out=v[:, :, 2, :], in0=v[:, :, 2, :], in1=v[:, :, 3, :], op=ALU.add)
                    e.drain()
                    e.tensor_tensor(out=v[:, :, 0, :], in0=v[:, :, 0, :], in1=v[:, :, 2, :], op=ALU.add)
                    e.drain()
                    return e.tensor_copy(out=skvb[:].rearrange("p (g f) -> p g f", f=65),
                                         in_=v[:, :, 0, :])
                t_kvs = p.op("vector", kvsum, waits=[t_kv], inc=sv_)
                p.barrier([sv_])
            shc = E2(nc.sbuf_tensor("shc", [128, 8 * TPC], BF16))
            sqp2 = [E2(nc.sbuf_tensor(f"sqp2{j}", [128, 2 * TPC], BF16)) for j in range(2)]
            srec = E2(nc.sbuf_tensor("srec", [1, 512], F32))
            srecb = E2(nc.sbuf_tensor("srecb", [128, 512], BF16))
            swo = E2(nc.sbuf_tensor("swo", [128, 8 * D], BF16))
            sxc = [E2(nc.sbuf_tensor(f"sxc{j}", [128, D], F32)) for j in range(2)]
            for k in range(8):
                p.op("gpsimd", lambda e, k=k: e.dma_start(
                    out=swo[:, k * D:(k + 1) * D], in_=wout[k * 128:(k + 1) * 128, :]),
                    inc=ldg1, amt=16, dma=True)
            t_ldg_all2 = (ldg1, p.cnt["ldg1"])
            hc_toks = []
            pnum_prev = {0: None, 1: None}
            qh_prev = {0: None, 1: None}
            for h in range(NH):
                jh = h % 2
                t_qh = p.op("sync", lambda e, h=h, jh=jh: e.dma_start(
                    out=sqp2[jh][:].rearrange("p (a t) -> p a t", a=2),
                    in_=qpT[h].rearrange("(a p) t -> p a t", p=128)),
                    waits=[qh_prev[jh]], inc=lda[jh], amt=16)
                for icc in range(4):
                    pb = icc % 2
                    t_mm = None
                    for mt in range(2):
                        wt = [t_kvs, t_qh, pnum_prev[pb]] if mt == 0 else []
                        t_mm = p.op("tensor", lambda e, h=h, jh=jh, icc=icc, mt=mt, pb=pb: e.matmul(
                            out=p_a[pb][0:65, :],
                            lhsT=skvb[:, (h * 2 + mt) * 65:(h * 2 + mt + 1) * 65],
                            rhs=sqp2[jh][:, mt * TPC + icc * 512: mt * TPC + (icc + 1) * 512],
                            start=(mt == 0), stop=(mt == 1)),
                            waits=wt, inc=(st_ if mt == 1 else None))
                    def recip_den(e, pb=pb):
                        e.tensor_scalar(out=srec[:], in0=p_a[pb][64:65, :], scalar1=PEPS, scalar2=None,
                                        op0=ALU.add)
                        e.drain()
                        return e.reciprocal(out=srec[:], in_=srec[:])
                    t_rc = p.op("vector", recip_den, waits=[t_mm], inc=sv_)
                    t_rb = p.op("tensor", lambda e: e.matmul(out=p_b[0][:], lhsT=sones[:], rhs=srec[:],
                                                             start=True, stop=True),
                                waits=[t_rc, t_ones], inc=st_)
                    t_re = p.op("vector", lambda e: e.tensor_copy(out=srecb[:], in_=p_b[0][:]),
                                waits=[t_rb], inc=sv_)
                    pnum_prev[pb] = p.op("vector", lambda e, h=h, icc=icc, pb=pb: e.tensor_tensor(
                        out=shc[(h % 2) * 64:(h % 2) * 64 + 64,
                                (h // 2) * TPC + icc * 512:(h // 2) * TPC + (icc + 1) * 512],
                        in0=p_a[pb][0:64, :], in1=srecb[0:64, :], op=ALU.mult),
                        waits=[t_re], inc=sv_)
                    hc_toks.append(pnum_prev[pb])
                qh_prev[jh] = hc_toks[-1]
            hc_all = [hc_toks[-1], hc_toks[-2]]
            x2_done = [None] * NT
            wo_prev = {0: None, 1: None}
            for icc in range(NT):
                jj = icc % 2
                t_xl = p.op("sync", lambda e, icc=icc, jj=jj: e.dma_start(
                    out=sxc[jj][:], in_=xa[icc * 128:(icc + 1) * 128, :]),
                    waits=([x2_done[icc - 2]] if icc >= 2 else []), inc=ldx[jj], amt=16)
                for mh in range(2):
                    pb = mh
                    t_mm = None
                    for kt in range(8):
                        wt = [wo_prev[pb], t_ldg_all2] + hc_all if kt == 0 else []
                        t_mm = p.op("tensor", lambda e, icc=icc, mh=mh, kt=kt, pb=pb: e.matmul(
                            out=p_b[pb][:], lhsT=shc[:, kt * TPC + icc * 128: kt * TPC + (icc + 1) * 128],
                            rhs=swo[:, kt * D + mh * 512: kt * D + (mh + 1) * 512],
                            start=(kt == 0), stop=(kt == 7)),
                            waits=wt, inc=(st_ if kt == 7 else None))
                    p.op("vector", lambda e, icc=icc, mh=mh, pb=pb: e.scalar_tensor_tensor(
                        out=sx2[:, icc * D + mh * 512: icc * D + (mh + 1) * 512], in0=p_b[pb][:],
                        scalar=1.0, in1=sbout[:, mh * 512:(mh + 1) * 512], op0=ALU.mult, op1=ALU.add),
                        waits=[t_mm, t_ld1], inc=sv_)
                    p.op("vector", lambda e, icc=icc, mh=mh, jj=jj: e.tensor_tensor(
                        out=sx2[:, icc * D + mh * 512: icc * D + (mh + 1) * 512],
                        in0=sx2[:, icc * D + mh * 512: icc * D + (mh + 1) * 512],
                        in1=sxc[jj][:, mh * 512:(mh + 1) * 512], op=ALU.add),
                        waits=[t_xl], inc=sv_)
                    wo_prev[pb] = p.op("vector", lambda e, icc=icc, mh=mh: e.tensor_tensor(
                        out=sx2[:, icc * D + mh * 512: icc * D + (mh + 1) * 512],
                        in0=sx2[:, icc * D + mh * 512: icc * D + (mh + 1) * 512],
                        in1=sffo[:, icc * D + mh * 512:icc * D + (mh + 1) * 512],
                        op=ALU.add), inc=sv_)
                x2_done[icc] = wo_prev[1]
            p.barrier([sv_, ss, st_])

        p.op("gpsimd", lambda e: e.dma_start(out=o_d2, in_=sx2[:, 0:D]), inc=sgo[0], amt=16, dma=True)
        p.barrier([sv_, ss, st_])
        c12.close()
        sxn2T = E(nc.sbuf_tensor("sxn2T", [128, 8 * TPC], BF16))
        with ExitStack() as c3:
            E3 = c3.enter_context
            ssq3 = E3(nc.sbuf_tensor("ssq3", [128, D], F32))
            sxn2 = E3(nc.sbuf_tensor("sxn2", [128, D], F32))
            sxn232 = E3(nc.sbuf_tensor("sxn232", [128, 8 * TPC], F32))
            for k in range(0, 8):
                p.op("sync", lambda e, k=k: e.dma_start(out=sgw32[:, k * 8:(k + 1) * 8],
                                                        in_=gwT[k * 128:(k + 1) * 128, :]), inc=ld1, amt=16)
            t_gw32 = (ld1, p.cnt["ld1"])
            trbank_prev = {0: None, 1: None}
            for i in range(NT):
                x2t = sx2[:, i * D:(i + 1) * D]
                p.op("vector", lambda e, x2t=x2t: e.reduce_sum(out=sstat[:, 0:1], in_=x2t, axis=AXL.X),
                     inc=sv_)
                p.op("vector", lambda e: e.tensor_scalar(out=sstat[:, 0:1], in0=sstat[:, 0:1],
                                                         scalar1=1.0 / D, scalar2=None, op0=ALU.mult),
                     inc=sv_)
                t_sq = p.op("scalar", lambda e, x2t=x2t: e.activation(out=ssq3[:], in_=x2t, func=AF.Square,
                                                                     accum_out=sstat[:, 1:2]), inc=ss)
                p.op("vector", lambda e: e.tensor_scalar(out=sstat[:, 2:3], in0=sstat[:, 1:2],
                                                         scalar1=1.0 / D, scalar2=None, op0=ALU.mult),
                     waits=[t_sq], inc=sv_)
                p.op("vector", lambda e: e.tensor_tensor(out=sstat[:, 3:4], in0=sstat[:, 0:1],
                                                         in1=sstat[:, 0:1], op=ALU.mult), inc=sv_)
                t_var = p.op("vector", lambda e: e.tensor_tensor(out=sstat[:, 2:3], in0=sstat[:, 2:3],
                                                                 in1=sstat[:, 3:4], op=ALU.subtract), inc=sv_)
                p.op("scalar", lambda e: e.activation(out=sstat[:, 4:5], in_=sstat[:, 2:3],
                                                      func=AF.Ln, bias=LN_EPS, scale=1.0),
                     waits=[t_var], inc=ss)
                t_rstd = p.op("scalar", lambda e: e.activation(out=sstat[:, 4:5], in_=sstat[:, 4:5],
                                                               func=AF.Exp, scale=-0.5), inc=ss)
                p.op("vector", lambda e: e.tensor_tensor(out=sstat[:, 5:6], in0=sstat[:, 0:1],
                                                         in1=sstat[:, 4:5], op=ALU.mult),
                     waits=[t_rstd], inc=sv_)
                t_bneg = p.op("vector", lambda e: e.tensor_scalar(out=sstat[:, 5:6], in0=sstat[:, 5:6],
                                                                  scalar1=-1.0, scalar2=None, op0=ALU.mult),
                              inc=sv_)
                t_xn = p.op("vector", lambda e, x2t=x2t: e.tensor_scalar(
                    out=sxn2[:], in0=x2t, scalar1=sstat[:, 4:5], scalar2=sstat[:, 5:6],
                    op0=ALU.mult, op1=ALU.add), waits=[t_bneg], inc=sv_)
                ev_tok = {}
                for k in range(8):
                    tb = k % 2
                    wt = [t_xn, t_ld1]
                    if k >= 2:
                        wt.append(ev_tok[k - 2])
                    else:
                        wt.append(trbank_prev[tb])
                    tt = p.op("tensor", lambda e, k=k, tb=tb: e.transpose(
                        out=p_t[tb][:], in_=sxn2[:, k * 128:(k + 1) * 128], identity=sident[:]),
                        waits=wt, inc=st_)
                    ev_tok[k] = p.op("vector", lambda e, i=i, k=k, tb=tb: e.tensor_copy(
                        out=sxn232[:, k * TPC + i * 128: k * TPC + (i + 1) * 128], in_=p_t[tb][:]),
                        waits=[tt], inc=sv_)
                trbank_prev = {0: ev_tok[6], 1: ev_tok[7]}
                p.op("vector", lambda e, x2t=x2t: e.tensor_tensor(
                    out=x2t, in0=x2t, in1=sxn2[:], op=ALU.add), inc=sv_)
            t_cast = None
            for k in range(8):
                t_cast = p.op("vector", lambda e, k=k: e.tensor_copy(
                    out=sxn2T[:, k * TPC:(k + 1) * TPC], in_=sxn232[:, k * TPC:(k + 1) * TPC]),
                    inc=sv_)
            sc_ev = None
            for nch in range(4):
                t_mm = None
                for kt in range(8):
                    wt = [t_gw32, sc_ev, t_cast] if kt == 0 else []
                    t_mm = p.op("tensor", lambda e, nch=nch, kt=kt: e.matmul(
                        out=p_a[0][0:8, :], lhsT=sgw32[:, kt * 8:(kt + 1) * 8],
                        rhs=sxn232[:, kt * TPC + nch * 512:kt * TPC + (nch + 1) * 512],
                        start=(kt == 0), stop=(kt == 7)),
                        waits=wt, inc=(st_ if kt == 7 else None))
                sc_ev = p.op("vector", lambda e, nch=nch: e.tensor_copy(
                    out=ssc_f[:, nch * 512:(nch + 1) * 512], in_=p_a[0][0:8, :]),
                    waits=[t_mm], inc=sv_)
            p.barrier([sv_, ss, st_])

        p.op("gpsimd", lambda e: e.dma_start(out=o_d3, in_=sxn2T[:, 0:TPC]), inc=sgo[0], amt=16, dma=True)
        with ExitStack() as c4:
            E4 = c4.enter_context
            swip = E4(nc.sbuf_tensor("swip", [128, 8 * 2 * D], BF16))
            scs = E4(nc.sbuf_tensor("scs", [128, 2 * D], BF16))
            sipc = [E4(nc.sbuf_tensor(f"sipc{j}", [128, 2 * D], F32)) for j in range(2)]
            srp = [E4(nc.sbuf_tensor(f"srp{j}", [128, 2 * D], BF16)) for j in range(2)]
            srh = [E4(nc.sbuf_tensor(f"srh{j}", [128, 2 * D], BF16)) for j in range(2)]
            snrm = E4(nc.sbuf_tensor("snrm", [128, 8], F32))
            ssq4 = E4(nc.sbuf_tensor("ssq4", [128, D], F32))
            for k in range(8):
                p.op("gpsimd", lambda e, k=k: e.dma_start(
                    out=swip[:, k * 2 * D:(k + 1) * 2 * D], in_=wip[k * 128:(k + 1) * 128, :]),
                    inc=ldg1, amt=16, dma=True)
            t_l4 = (ld1, p.cnt["ld1"])
            t_g4 = (ldg1, p.cnt["ldg1"])
            ip_prev = {0: None, 1: None, 2: None, 3: None}
            buf_prev = {0: None, 1: None}
            st_prev = {0: None, 1: None}
            cs_prev = {0: None, 1: None}
            for i in range(NT):
                jj = i % 2
                t_cs = p.op("sync", lambda e, i=i, jj=jj: e.dma_start(
                    out=scs[:, jj * D: jj * D + 512], in_=cosb[i * 128:(i + 1) * 128, :]),
                    waits=[cs_prev[jj]], inc=lda[jj], amt=16)
                t_sn = p.op("sync", lambda e, i=i, jj=jj: e.dma_start(
                    out=scs[:, jj * D + 512: jj * D + D], in_=sinb[i * 128:(i + 1) * 128, :]),
                    inc=lda[jj], amt=16)
                ev4 = []
                for q4 in range(4):
                    t_mm = None
                    for kt in range(8):
                        wt = [ip_prev[q4], t_g4] if kt == 0 else []
                        t_mm = p.op("tensor", lambda e, i=i, q4=q4, kt=kt: e.matmul(
                            out=p_a[q4][:], lhsT=sxn2T[:, kt * TPC + i * 128: kt * TPC + (i + 1) * 128],
                            rhs=swip[:, kt * 2 * D + q4 * 512: kt * 2 * D + (q4 + 1) * 512],
                            start=(kt == 0), stop=(kt == 7)),
                            waits=wt, inc=(st_ if kt == 7 else None))
                    ev4.append(p.op("vector", lambda e, jj=jj, q4=q4: e.tensor_copy(
                        out=sipc[jj][:, q4 * 512:(q4 + 1) * 512], in_=p_a[q4][:]),
                        waits=[t_mm, buf_prev[jj]], inc=sv_))
                    ip_prev[q4] = ev4[-1]
                def rope2(e, jj=jj, i=i):
                    cc = scs[:, jj * D: jj * D + 512]
                    sn = scs[:, jj * D + 512: jj * D + D]
                    tmp = srh[jj][:, 0:512]
                    for base in (0, D):
                        x1 = sipc[jj][:, base: base + 512]
                        x2_ = sipc[jj][:, base + 512: base + D]
                        o1 = srp[jj][:, base: base + 512]
                        o2 = srp[jj][:, base + 512: base + D]
                        e.tensor_tensor(out=o1, in0=x1, in1=cc, op=ALU.mult)
                        e.drain()
                        e.tensor_tensor(out=tmp, in0=x2_, in1=sn, op=ALU.mult)
                        e.drain()
                        e.tensor_tensor(out=o1, in0=o1, in1=tmp, op=ALU.subtract)
                        e.drain()
                        e.tensor_tensor(out=o2, in0=x2_, in1=cc, op=ALU.mult)
                        e.drain()
                        e.tensor_tensor(out=tmp, in0=x1, in1=sn, op=ALU.mult)
                        e.drain()
                        e.tensor_tensor(out=o2, in0=o2, in1=tmp, op=ALU.add)
                        e.drain()
                    return e.engine_nop()
                t_rope = p.op("vector", rope2, waits=[ev4[3], t_sn, st_prev[jj]], inc=sv_)
                cs_prev[jj] = t_rope
                def norms(e, jj=jj):
                    e.activation(out=ssq4[:], in_=srp[jj][:, 0:D], func=AF.Square,
                                 accum_out=snrm[:, 0:1])
                    e.drain()
                    e.activation(out=ssq4[:], in_=srp[jj][:, D:2 * D], func=AF.Square,
                                 accum_out=snrm[:, 1:2])
                    return e.drain()
                t_sq4 = p.op("scalar", norms, waits=[t_rope], inc=ss, dma=True)
                t_gq = p.op("vector", lambda e: e.tensor_scalar(
                    out=snrm[:, 0:2], in0=snrm[:, 0:2], scalar1=1e-24, scalar2=None, op0=ALU.max),
                    waits=[t_sq4], inc=sv_)

                def norms2(e):
                    e.activation(out=snrm[:, 2:3], in_=snrm[:, 0:1], func=AF.Ln, bias=0.0, scale=1.0)
                    e.drain()
                    e.activation(out=snrm[:, 2:3], in_=snrm[:, 2:3], func=AF.Exp, scale=-0.5)
                    e.drain()
                    e.activation(out=snrm[:, 3:4], in_=snrm[:, 1:2], func=AF.Ln, bias=0.0, scale=1.0)
                    e.drain()
                    return e.activation(out=snrm[:, 3:4], in_=snrm[:, 3:4], func=AF.Exp, scale=-0.5)
                t_nrm = p.op("scalar", norms2, waits=[t_gq], inc=ss)
                t_nh = p.op("vector", lambda e, jj=jj: e.tensor_scalar(
                    out=srh[jj][:, 0:D], in0=srp[jj][:, 0:D], scalar1=snrm[:, 2:3], scalar2=None,
                    op0=ALU.mult), waits=[t_nrm], inc=sv_)
                t_ph = p.op("vector", lambda e, jj=jj: e.tensor_scalar(
                    out=srh[jj][:, D:2 * D], in0=srp[jj][:, D:2 * D], scalar1=snrm[:, 3:4], scalar2=None,
                    op0=ALU.mult), inc=sv_)
                st1 = p.op("gpsimd", lambda e, i=i, jj=jj: e.dma_start(
                    out=o_I[i * 128:(i + 1) * 128, :], in_=srp[jj][:, 0:D]),
                    waits=[t_rope], inc=sgo[jj], amt=16, dma=True)
                st2 = p.op("gpsimd", lambda e, i=i, jj=jj: e.dma_start(
                    out=o_P[i * 128:(i + 1) * 128, :], in_=srp[jj][:, D:2 * D]),
                    waits=[t_rope], inc=sgo[jj], amt=16, dma=True)
                st3 = p.op("gpsimd", lambda e, i=i, jj=jj: e.dma_start(
                    out=o_Ih[i * 128:(i + 1) * 128, :], in_=srh[jj][:, 0:D]),
                    waits=[t_nh], inc=sgo[jj], amt=16, dma=True)
                st4 = p.op("gpsimd", lambda e, i=i, jj=jj: e.dma_start(
                    out=o_Ph[i * 128:(i + 1) * 128, :], in_=srh[jj][:, D:2 * D]),
                    waits=[t_ph], inc=sgo[jj], amt=16, dma=True)
                buf_prev[jj] = t_rope
                st_prev[jj] = st4
            p.barrier([sv_, ss, st_])

        with ExitStack() as c5:
            E5 = c5.enter_context
            scot = E5(nc.sbuf_tensor("scot", [128, 8], F32))
            smk = E5(nc.sbuf_tensor("smk", [128, NT * 8], F32))
            siota = E5(nc.sbuf_tensor("siota", [128, 8], F32))
            swe = [E5(nc.sbuf_tensor(f"swe{j}", [128, 8 * D], BF16)) for j in range(2)]
            t_io = p.op("gpsimd", lambda e: e.dma_start(out=siota[:], in_=iota8.partition_broadcast(128)),
                        inc=ldg1, amt=16, dma=True)
            sc_ev = None
            mk_prev = None
            tr_prev = None
            for i in range(NT):
                tt = p.op("tensor", lambda e, i=i: e.transpose(
                    out=p_t[0][:, 0:8], in_=ssc_f[:, i * 128:(i + 1) * 128],
                    identity=sident[0:8, 0:8]), waits=[sc_ev, tr_prev], inc=st_)
                t_sc2 = p.op("vector", lambda e: e.tensor_copy(out=scot[:], in_=p_t[0][:, 0:8]),
                             waits=[tt], inc=sv_)
                tr_prev = t_sc2

                def mkmask(e, i=i):
                    e.tensor_reduce(out=sstat[:, 6:7], in_=scot[:], axis=AXL.X, op=ALU.max)
                    e.drain()
                    e.tensor_scalar(out=smk[:, i * 8:(i + 1) * 8], in0=scot[:], scalar1=sstat[:, 6:7],
                                    scalar2=None, op0=ALU.is_ge)
                    e.drain()
                    e.tensor_scalar(out=smk[:, i * 8:(i + 1) * 8], in0=smk[:, i * 8:(i + 1) * 8],
                                    scalar1=-1000.0, scalar2=1000.0, op0=ALU.mult, op1=ALU.add)
                    e.drain()
                    e.tensor_tensor(out=smk[:, i * 8:(i + 1) * 8], in0=smk[:, i * 8:(i + 1) * 8],
                                    in1=siota[:], op=ALU.add)
                    e.drain()
                    e.tensor_reduce(out=sstat[:, 7:8], in_=smk[:, i * 8:(i + 1) * 8],
                                    axis=AXL.X, op=ALU.min)
                    e.drain()
                    return e.tensor_scalar(out=smk[:, i * 8:(i + 1) * 8], in0=siota[:],
                                           scalar1=sstat[:, 7:8], scalar2=None, op0=ALU.is_equal)
                mk_prev = p.op("vector", mkmask, waits=[t_sc2, t_io], inc=sv_)
            we_tok = [None, None]
            last_mm_of_expert = {}
            moe_ev = {0: None, 1: None}
            for ee in range(8):
                jw = ee % 2
                we_tok[jw] = p.op("gpsimd", lambda e, ee=ee: e.dma_start(
                    out=swe[ee % 2][:].rearrange("p (k d) -> p k d", k=8),
                    in_=wexp[ee].rearrange("(k p) d -> p k d", p=128)),
                    waits=([last_mm_of_expert.get(ee - 2)]), inc=ldw[jw], amt=16, dma=True)
                for i in range(NT):
                    for mh in range(2):
                        pb = mh
                        t_mm = None
                        for kt in range(8):
                            wt = [we_tok[jw], mk_prev, moe_ev[pb]] if kt == 0 else []
                            t_mm = p.op("tensor", lambda e, ee=ee, i=i, mh=mh, kt=kt, pb=pb, jw=jw: e.matmul(
                                out=p_b[pb][:],
                                lhsT=sxn2T[:, kt * TPC + i * 128: kt * TPC + (i + 1) * 128],
                                rhs=swe[jw][:, kt * D + mh * 512: kt * D + (mh + 1) * 512],
                                start=(kt == 0), stop=(kt == 7)),
                                waits=wt, inc=(st_ if kt == 7 else None))
                        last_mm_of_expert[ee] = t_mm
                        moe_ev[pb] = p.op("vector", lambda e, ee=ee, i=i, mh=mh, pb=pb: e.scalar_tensor_tensor(
                            out=sx2[:, i * D + mh * 512: i * D + (mh + 1) * 512], in0=p_b[pb][:],
                            scalar=smk[:, i * 8 + ee: i * 8 + ee + 1],
                            in1=sx2[:, i * D + mh * 512: i * D + (mh + 1) * 512],
                            op0=ALU.mult, op1=ALU.add),
                            waits=[t_mm], inc=sv_)
            st5 = None
            for i in range(NT):
                st5 = p.op("gpsimd", lambda e, i=i: e.dma_start(
                    out=o_x3p[i * 128:(i + 1) * 128, :], in_=sx2[:, i * D:(i + 1) * D]),
                    waits=[moe_ev[0], moe_ev[1]], inc=sgo[0], amt=16, dma=True)
            p.barrier([sv_, ss, st_])
        p.fence("gpsimd", [(sgo[0], p.cnt["sgo0"]), (sgo[1], p.cnt["sgo1"])])
        p.emit(block)
    return nc



def build_C():
    nc = bass.Bass("TRN2", target_bir_lowering=False)
    PT = nc.dram_tensor("PT", [D, T], BF16, kind="ExternalInput").ap()
    Phat = nc.dram_tensor("Phat", [T, D], BF16, kind="ExternalInput").ap()
    If_ = nc.dram_tensor("If_", [D, TPC], BF16, kind="ExternalInput").ap()
    Ihf = nc.dram_tensor("Ihf", [D, TPC], BF16, kind="ExternalInput").ap()
    x3pT = nc.dram_tensor("x3pT", [D, TPC], F32, kind="ExternalInput").ap()
    w1p = nc.dram_tensor("w1p", [32, 64], F32, kind="ExternalInput").ap()
    b1 = nc.dram_tensor("b1", [64, 1], F32, kind="ExternalInput").ap()
    w2 = nc.dram_tensor("w2", [64, 64], F32, kind="ExternalInput").ap()
    b2 = nc.dram_tensor("b2", [64, 1], F32, kind="ExternalInput").ap()
    wc = nc.dram_tensor("wc", [64, 2], F32, kind="ExternalInput").ap()
    bc = nc.dram_tensor("bc", [2, 1], F32, kind="ExternalInput").ap()
    wm = nc.dram_tensor("wm", [64, 1], F32, kind="ExternalInput").ap()
    bm = nc.dram_tensor("bm", [1, 1], F32, kind="ExternalInput").ap()
    pal = nc.dram_tensor("pal", [D, 256], F32, kind="ExternalInput").ap()
    wo = nc.dram_tensor("wo", [D, D], F32, kind="ExternalInput").ap()
    ident = nc.dram_tensor("ident", [128, 128], F32, kind="ExternalInput").ap()
    cmask = nc.dram_tensor("cmask", [128, 512], F32, kind="ExternalInput").ap()
    rowg = nc.dram_tensor("rowg", [128, 16], F32, kind="ExternalInput").ap()
    iotas = nc.dram_tensor("iotas", [128, 512], I32, kind="ExternalInput").ap()
    iot128 = nc.dram_tensor("iot128", [128, 1], F32, kind="ExternalInput").ap()
    ioy = nc.dram_tensor("ioy", [128, 256], BF16, kind="ExternalInput").ap()
    kmG = nc.dram_tensor("kmG", [32, 256], BF16, kind="ExternalInput").ap()
    kmM = nc.dram_tensor("kmM", [128, 16], F32, kind="ExternalInput").ap()
    o_yf = nc.dram_tensor("o_yf", [D, TPC], F32, kind="ExternalOutput").ap()
    idx_scr = nc.dram_tensor("idx_scr", [128, 16], I32, kind="Internal").ap()
    row_scr = nc.dram_tensor("row_scr", [1, TPC], F32, kind="Internal").ap()
    tok_scr = nc.dram_tensor("tok_scr", [128, 16], F32, kind="Internal").ap()

    with ExitStack() as ctx:
        E = ctx.enter_context
        p = Prog(nc)
        ld1 = p.mksem(ctx, "ld1")
        ldp = [p.mksem(ctx, "ldp0"), p.mksem(ctx, "ldp1")]
        ldb = [p.mksem(ctx, "ldb0"), p.mksem(ctx, "ldb1")]
        ldgx = p.mksem(ctx, "ldgx")
        ldsc = p.mksem(ctx, "ldsc")
        sv_ = p.mksem(ctx, "sv_")
        ss = p.mksem(ctx, "ss")
        st_ = p.mksem(ctx, "st_")
        sgo = p.mksem(ctx, "sgo")
        block = E(nc.Block())
        p.setup_consts(ctx, [0.0, 1.0, -1.0, -0.5, 7.5, 15.0, NEG, PEPS, 1.0 / 8192.0, 1e-24,
                             1.0 / 16.0], sgo)

        sident = E(nc.sbuf_tensor("sident", [128, 128], F32))
        sident_b = E(nc.sbuf_tensor("sident_b", [128, 128], BF16))
        scm = E(nc.sbuf_tensor("scm", [128, 512], F32))
        srowg = E(nc.sbuf_tensor("srowg", [128, 16], F32))
        siotas = E(nc.sbuf_tensor("siotas", [128, 512], I32))
        siot128 = E(nc.sbuf_tensor("siot128", [128, 1], F32))
        sioy = E(nc.sbuf_tensor("sioy", [128, 256], BF16))
        skmG = E(nc.sbuf_tensor("skmG", [32, 256], BF16))
        skmM = E(nc.sbuf_tensor("skmM", [128, 16], F32))
        sw1 = E(nc.sbuf_tensor("sw1", [32, 64], BF16))
        sw2 = E(nc.sbuf_tensor("sw2", [64, 64], BF16))
        swc = E(nc.sbuf_tensor("swc", [64, 2], BF16))
        swm = E(nc.sbuf_tensor("swm", [64, 1], BF16))
        sb1 = E(nc.sbuf_tensor("sb1", [64, 1], F32))
        sb2 = E(nc.sbuf_tensor("sb2", [64, 1], F32))
        sbc = E(nc.sbuf_tensor("sbc", [2, 1], F32))
        sbm = E(nc.sbuf_tensor("sbm", [1, 1], F32))
        sPW = E(nc.sbuf_tensor("sPW", [128, 2 * D], BF16))
        with ExitStack() as c0:
            E0 = c0.enter_context
            spal = E0(nc.sbuf_tensor("spal", [128, 8 * 256], BF16))
            swo0 = E0(nc.sbuf_tensor("swo0", [128, 8 * D], BF16))
            for k in range(8):
                p.op("gpsimd", lambda e, k=k: e.dma_start(
                    out=spal[:, k * 256:(k + 1) * 256], in_=pal[k * 128:(k + 1) * 128, :]),
                    inc=ldgx, amt=16, dma=True)
                p.op("gpsimd", lambda e, k=k: e.dma_start(
                    out=swo0[:, k * D:(k + 1) * D], in_=wo[k * 128:(k + 1) * 128, :]),
                    inc=ldgx, amt=16, dma=True)
            t_pw0 = (ldgx, p.cnt["ldgx"])
            pw_prev = None
            for cc in range(2):
                for nn in range(2):
                    t_mm = None
                    for kt in range(8):
                        wt = [t_pw0, pw_prev] if kt == 0 else []
                        t_mm = p.op("tensor", lambda e, cc=cc, nn=nn, kt=kt: e.matmul(
                            out=p_m[:], lhsT=spal[:, kt * 256 + cc * 128: kt * 256 + (cc + 1) * 128],
                            rhs=swo0[:, kt * D + nn * 512: kt * D + (nn + 1) * 512],
                            start=(kt == 0), stop=(kt == 7)),
                            waits=wt, inc=(st_ if kt == 7 else None))
                    pw_prev = p.op("vector", lambda e, cc=cc, nn=nn: e.tensor_copy(
                        out=sPW[:, cc * D + nn * 512: cc * D + (nn + 1) * 512], in_=p_m[:]),
                        waits=[t_mm], inc=sv_)
            p.barrier([sv_, ss, st_])

        sVp = E(nc.sbuf_tensor("sVp", [128, T], F32))
        sZ = E(nc.sbuf_tensor("sZ", [128, 8 * 4096], BF16))
        sptb = [E(nc.sbuf_tensor(f"sptb{j}", [128, 8 * 512], BF16)) for j in range(2)]
        sifb = [E(nc.sbuf_tensor(f"sifb{j}", [128, 8 * 128], BF16)) for j in range(2)]
        sihb = [E(nc.sbuf_tensor(f"sihb{j}", [128, 8 * 128], BF16)) for j in range(2)]
        sx3b = [E(nc.sbuf_tensor(f"sx3b{j}", [128, 8 * 128], F32)) for j in range(2)]
        sm16 = E(nc.sbuf_tensor("sm16", [128, 16], F32))
        sidx = E(nc.sbuf_tensor("sidx", [128, 16], I32))
        sidxf = E(nc.sbuf_tensor("sidxf", [128, 16], F32))
        sidxg = E(nc.sbuf_tensor("sidxg", [128, 16], I32))
        sgt = [E(nc.sbuf_tensor(f"sgt{j}", [128, D], BF16)) for j in range(4)]
        sG = E(nc.sbuf_tensor("sG", [128, 4096], BF16))
        srel = E(nc.sbuf_tensor("srel", [32, 2048], BF16))
        sh1 = E(nc.sbuf_tensor("sh1", [64, 2048], BF16))
        sh2 = E(nc.sbuf_tensor("sh2", [64, 2048], BF16))
        smixr = E(nc.sbuf_tensor("smixr", [1, 2048], F32))
        szrow = E(nc.sbuf_tensor("szrow", [2, 2048], F32))
        smixt = E(nc.sbuf_tensor("smixt", [128, 16], F32))
        sstat = E(nc.sbuf_tensor("sstat", [128, 8], F32))
        sw_ = E(nc.sbuf_tensor("sw_", [128, 16], F32))
        sfx = E(nc.sbuf_tensor("sfx", [128, 16], F32))
        sfy = E(nc.sbuf_tensor("sfy", [128, 16], F32))
        sfxi = E(nc.sbuf_tensor("sfxi", [128, 16], I32))
        shx = E(nc.sbuf_tensor("shx", [128, 256], BF16))
        shy = E(nc.sbuf_tensor("shy", [128, 256], BF16))
        sYe = E(nc.sbuf_tensor("sYe", [128, 4096], BF16))
        stmp = E(nc.sbuf_tensor("stmp", [128, 256], BF16))
        sXe = E(nc.sbuf_tensor("sXe", [128, 4096], BF16))
        sA = E(nc.sbuf_tensor("sA", [128, 256], F32))
        sAb = E(nc.sbuf_tensor("sAb", [128, 256], BF16))
        sAf = E(nc.sbuf_tensor("sAf", [128, 256], BF16))
        souf = E(nc.sbuf_tensor("souf", [128, 8 * 128], F32))

        p_l = [E(nc.psum_tensor(f"p_l{j}", [128, 512], F32)) for j in range(2)]
        p_g = [E(nc.psum_tensor(f"p_g{j}", [128, 128], F32)) for j in range(2)]
        p_t2 = [E(nc.psum_tensor(f"p_t2{j}", [128, 128], BF16)) for j in range(2)]
        p_m = E(nc.psum_tensor("p_m", [128, 512], F32))

        for dst, srcap in ((sident, ident), (scm, cmask), (siotas, iotas), (siot128, iot128),
                           (srowg, rowg),
                           (skmM, kmM), (sb1, b1), (sb2, b2), (sbc, bc), (sbm, bm),
                           (sioy, ioy), (skmG, kmG)):
            p.op("sync", lambda e, dst=dst, srcap=srcap: e.dma_start(out=dst[:], in_=srcap),
                 inc=ld1, amt=16)
        for dst, srcap in ((sw1, w1p), (sw2, w2), (swc, wc), (swm, wm)):
            p.op("gpsimd", lambda e, dst=dst, srcap=srcap: e.dma_start(out=dst[:], in_=srcap),
                 inc=ld1, amt=16, dma=True)
        t_ld1 = (ld1, p.cnt["ld1"])
        t_identb = p.op("vector", lambda e: e.tensor_copy(out=sident_b[:], in_=sident[:]),
                        waits=[t_ld1], inc=sv_)
        t_zz = p.op("vector", lambda e: e.memset(sZ[:], 0.0), inc=sv_)
        ptb_prev = {0: None, 1: None}
        blk_prev = {0: None, 1: None}
        gx_prev = None
        sc_prev = None
        out_prev = None
        prev_mlp_guard = []
        for rb in range(16):
            nch = rb + 1
            jb = rb % 2
            t_if = p.op("sync", lambda e, rb=rb, jb=jb: e.dma_start(
                out=sifb[jb][:].rearrange("p (k c) -> p k c", k=8),
                in_=If_.rearrange("(k p) t -> p k t", p=128)[:, :, rb * 128:(rb + 1) * 128]),
                waits=[blk_prev[jb]], inc=ldb[jb], amt=16)
            t_ih = p.op("sync", lambda e, rb=rb, jb=jb: e.dma_start(
                out=sihb[jb][:].rearrange("p (k c) -> p k c", k=8),
                in_=Ihf.rearrange("(k p) t -> p k t", p=128)[:, :, rb * 128:(rb + 1) * 128]),
                inc=ldb[jb], amt=16)
            t_x3 = p.op("sync", lambda e, rb=rb, jb=jb: e.dma_start(
                out=sx3b[jb][:].rearrange("p (k c) -> p k c", k=8),
                in_=x3pT.rearrange("(k p) t -> p k t", p=128)[:, :, rb * 128:(rb + 1) * 128]),
                inc=ldb[jb], amt=16)
            ev_prev = None
            for ch in range(nch):
                jp = ch % 2
                t_pt = p.op("sync", lambda e, ch=ch, jp=jp: e.dma_start(
                    out=sptb[jp][:].rearrange("p (k s) -> p k s", k=8),
                    in_=PT.rearrange("(k p) t -> p k t", p=128)[:, :, ch * 512:(ch + 1) * 512]),
                    waits=[ptb_prev[jp]], inc=ldp[jp], amt=16)
                pb = ch % 2
                t_mm = None
                for kt in range(8):
                    wt = [t_pt, t_if] if kt == 0 else []
                    if kt == 0 and ch < 2:
                        wt.extend(prev_mlp_guard)
                    if kt == 0 and ev_prev is not None:
                        wt.append(ev_prev if ch < 2 else ev_toks[ch - 2])
                    t_mm = p.op("tensor", lambda e, ch=ch, kt=kt, pb=pb, rb=rb, jb=jb: e.matmul(
                        out=p_l[pb][:], lhsT=sifb[jb][:, kt * 128:(kt + 1) * 128],
                        rhs=sptb[ch % 2][:, kt * 512:(kt + 1) * 512],
                        start=(kt == 0), stop=(kt == 7)),
                        waits=wt, inc=(st_ if kt == 7 else None))
                if ch == nch - 1:
                    t_mk = p.op("vector", lambda e, pb=pb: e.tensor_tensor(
                        out=p_l[pb][:], in0=p_l[pb][:], in1=scm[:], op=ALU.add),
                        waits=[t_mm, t_ld1], inc=sv_)
                else:
                    t_mk = t_mm
                def tagev(e, ch=ch, pb=pb):
                    e.tensor_scalar(
                        out=sVp[:, ch * 512:(ch + 1) * 512].bitcast(I32),
                        in0=p_l[pb][:].bitcast(I32), scalar1=-8192, scalar2=ch * 512,
                        op0=ALU.bitwise_and, op1=ALU.bitwise_or)
                    e.drain()
                    return e.tensor_tensor(
                        out=sVp[:, ch * 512:(ch + 1) * 512].bitcast(I32),
                        in0=sVp[:, ch * 512:(ch + 1) * 512].bitcast(I32),
                        in1=siotas[:], op=ALU.bitwise_or)
                tg = p.op("vector", tagev, waits=[t_mk], inc=sv_)
                if ch == 0:
                    ev_toks = {}
                ev_toks[ch] = tg
                ev_prev = tg
                ptb_prev[ch % 2] = tg
            def top16(e, nch=nch):
                e.max(out=sm16[:, 0:8], in_=sVp[:, 0:nch * 512])
                e.drain()
                e.match_replace(out=sVp[:, 0:nch * 512], in_to_replace=sm16[:, 0:8],
                                in_values=sVp[:, 0:nch * 512], imm_value=NEG)
                e.drain()
                e.max(out=sm16[:, 8:16], in_=sVp[:, 0:nch * 512])
                e.drain()
                e.tensor_scalar(out=sidx[:], in0=sm16[:].bitcast(I32), scalar1=8191, scalar2=None,
                                op0=ALU.bitwise_and)
                e.drain()
                return e.tensor_copy(out=sidxf[:], in_=sidx[:])
            t_tk = p.op("vector", top16, waits=[ev_prev, sc_prev, gx_prev], inc=sv_)
            if rb == 0:
                gt_prev = {0: None, 1: None, 2: None, 3: None}
            tb_prev = {0: None, 1: None}
            zev = []
            for g in range(15):
                jg = g % 4
                t_ga = p.op("gpsimd", lambda e, g=g, jg=jg: e.indirect_dma_start(
                    out=sgt[jg][:], out_offset=None, in_=Phat,
                    in_offset=bass.IndirectOffsetOnAxis(sidx[:, g:g + 1], 0)),
                    waits=[t_tk, gt_prev[jg]], inc=ldgx, amt=16, dma=True)
                for kt in range(8):
                    tb = kt % 2
                    t_tt = p.op("tensor", lambda e, jg=jg, kt=kt, tb=tb: e.transpose(
                        out=p_t2[tb][:], in_=sgt[jg][:, kt * 128:(kt + 1) * 128],
                        identity=sident_b[:]),
                        waits=[t_ga, tb_prev[tb], t_identb], inc=st_)
                    ev = p.op("vector", lambda e, g=g, kt=kt, tb=tb: e.tensor_copy(
                        out=sZ[:, kt * 4096 + g: (kt + 1) * 4096: 32].rearrange("p f -> p f"),
                        in_=p_t2[tb][:]),
                        waits=[t_tt, t_zz], inc=sv_)
                    tb_prev[tb] = ev
                    if kt == 7:
                        gt_prev[jg] = ev
                        zev.append(ev)
            ih_ev = []
            for kt in range(8):
                ih_ev.append(p.op("vector", lambda e, kt=kt, jb=jb: e.tensor_copy(
                    out=sZ[:, kt * 4096 + 15: (kt + 1) * 4096: 32].rearrange("p f -> p f"),
                    in_=sihb[jb][:, kt * 128:(kt + 1) * 128]),
                    waits=[t_ih, zev[-1]], inc=sv_))
            gr_ev = []
            gr_prev = {0: None, 1: None}
            for g in range(32):
                pgb = g % 2
                t_mm = None
                for kt in range(8):
                    wt = [ih_ev[-1], gr_prev[pgb]] if kt == 0 else []
                    t_mm = p.op("tensor", lambda e, g=g, kt=kt, pgb=pgb: e.matmul(
                        out=p_g[pgb][:], lhsT=sZ[:, kt * 4096 + g * 128: kt * 4096 + (g + 1) * 128],
                        rhs=sZ[:, kt * 4096 + g * 128: kt * 4096 + (g + 1) * 128],
                        start=(kt == 0), stop=(kt == 7)),
                        waits=wt, inc=(st_ if kt == 7 else None))
                gr_prev[pgb] = p.op("vector", lambda e, g=g, pgb=pgb: e.tensor_scalar(
                    out=sG[:, g * 128:(g + 1) * 128], in0=p_g[pgb][:], scalar1=1.0, scalar2=-1.0,
                    op0=ALU.min, op1=ALU.max), waits=[t_mm], inc=sv_)
                gr_ev.append(gr_prev[pgb])
            def relg(e):
                e.memset(srel[:], 0.0)
                e.drain()
                for tsub in range(4):
                    e.tensor_copy(
                        out=srel[0:16, :].rearrange("p (g k) -> p g k", g=32)[:, :, 0:16].rearrange(
                            "p g k -> p (g k)").rearrange("p (g k) -> p g k", g=32)[:, :, 0:16]
                        if False else
                        srel[0:16, tsub * 16 + 0: 2048: 64].rearrange("p f -> p f"),
                        in_=sG[tsub * 32: tsub * 32 + 16, tsub * 32: 4096: 128])
                    e.drain()
                return e.drain()
            def relg2(e):
                e.memset(srel[:], 0.0)
                e.drain()
                for tsub in range(4):
                    src = sG[tsub * 32: tsub * 32 + 16, :].rearrange("p (g c) -> p g c", g=32)[
                        :, :, tsub * 32: tsub * 32 + 16]
                    dst = srel[0:16, :].rearrange("p (g t k) -> p g t k", g=32, t=4)[:, :, tsub, :]
                    e.tensor_copy(out=dst, in_=src)
                return e.drain()
            t_rel = p.op("vector", relg2, waits=[gr_ev[-1]], inc=sv_, dma=True)
            def delta(e, rb=rb):
                e.tensor_scalar(out=sm16[:], in0=sidxf[:], scalar1=srowg[:, rb:rb + 1], scalar2=None,
                                op0=ALU.subtract)
                e.drain()
                e.tensor_scalar(out=sm16[:], in0=sm16[:], scalar1=-1.0 / 8192.0,
                                scalar2=0.0, op0=ALU.mult, op1=ALU.add)
                e.drain()
                return e.tensor_scalar(out=sm16[:], in0=sm16[:], scalar1=0.0, scalar2=None,
                                       op0=ALU.max)
            t_dl = p.op("vector", delta, waits=[t_tk], inc=sv_)
            t_dw = p.op("gpsimd", lambda e: e.dma_start(out=tok_scr, in_=sm16[:]),
                        waits=[t_dl], inc=ldsc, amt=16, dma=True)
            t_dr = p.op("gpsimd", lambda e: e.dma_start(
                out=srel[16:17, :], in_=tok_scr.rearrange("p k -> (p k)")[None, :]),
                waits=[t_dw, t_rel], inc=ldsc, amt=16, dma=True)
            if rb == 0:
                t_rel2 = p.op("vector", lambda e: e.tensor_tensor(
                    out=srel[:, 0:224].bitcast(BF16), in0=srel[:, 0:224].bitcast(BF16),
                    in1=skmG[:, 0:224], op=ALU.mult), waits=[(ldsc, p.cnt["ldsc"]), t_ld1], inc=sv_)
            else:
                t_rel2 = (ldsc, p.cnt["ldsc"])
            mlp_banks = [p_m, p_l[0]]
            mlp_ev2 = {0: None, 1: None}
            mseq = 0
            mlp_ev = []
            for nn in range(4):
                bk = mseq % 2
                t_m1 = p.op("tensor", lambda e, nn=nn, bk=bk: e.matmul(
                    out=mlp_banks[bk][0:64, :], lhsT=sw1[:], rhs=srel[:, nn * 512:(nn + 1) * 512],
                    start=True, stop=True),
                    waits=[t_rel2, t_ld1, mlp_ev2[bk]], inc=st_)
                mlp_ev2[bk] = p.op("scalar", lambda e, nn=nn, bk=bk: e.activation(
                    out=sh1[:, nn * 512:(nn + 1) * 512], in_=mlp_banks[bk][0:64, :], func=AF.Gelu,
                    bias=sb1[:], scale=1.0), waits=[t_m1], inc=ss)
                mlp_ev.append(mlp_ev2[bk])
                mseq += 1
            for nn in range(4):
                bk = mseq % 2
                t_m2 = p.op("tensor", lambda e, nn=nn, bk=bk: e.matmul(
                    out=mlp_banks[bk][0:64, :], lhsT=sw2[:], rhs=sh1[:, nn * 512:(nn + 1) * 512],
                    start=True, stop=True), waits=[mlp_ev[nn], mlp_ev2[bk]], inc=st_)
                mlp_ev2[bk] = p.op("scalar", lambda e, nn=nn, bk=bk: e.activation(
                    out=sh2[:, nn * 512:(nn + 1) * 512], in_=mlp_banks[bk][0:64, :], func=AF.Gelu,
                    bias=sb2[:], scale=1.0), waits=[t_m2], inc=ss)
                mlp_ev.append(mlp_ev2[bk])
                mseq += 1
            zm_ev = []
            for nn in range(4):
                bk = mseq % 2
                t_m3 = p.op("tensor", lambda e, nn=nn, bk=bk: e.matmul(
                    out=mlp_banks[bk][0:2, :], lhsT=swc[:], rhs=sh2[:, nn * 512:(nn + 1) * 512],
                    start=True, stop=True), waits=[mlp_ev[4 + nn], mlp_ev2[bk]], inc=st_)
                mlp_ev2[bk] = p.op("scalar", lambda e, nn=nn, bk=bk: e.activation(
                    out=szrow[:, nn * 512:(nn + 1) * 512], in_=mlp_banks[bk][0:2, :], func=AF.Tanh,
                    bias=sbc[:], scale=1.0), waits=[t_m3], inc=ss)
                zm_ev.append(mlp_ev2[bk])
                mseq += 1
            for nn in range(4):
                bk = mseq % 2
                t_m4 = p.op("tensor", lambda e, nn=nn, bk=bk: e.matmul(
                    out=mlp_banks[bk][0:1, :], lhsT=swm[:], rhs=sh2[:, nn * 512:(nn + 1) * 512],
                    start=True, stop=True), waits=[mlp_ev2[bk]], inc=st_)
                mlp_ev2[bk] = p.op("vector", lambda e, nn=nn, bk=bk: e.tensor_scalar(
                    out=smixr[:, nn * 512:(nn + 1) * 512], in0=mlp_banks[bk][0:1, :], scalar1=sbm[:],
                    scalar2=None, op0=ALU.add), waits=[t_m4], inc=sv_)
                zm_ev.append(mlp_ev2[bk])
                mseq += 1
            mlp_guard = [mlp_ev2[0], mlp_ev2[1]]
            t_mw = p.op("gpsimd", lambda e: e.dma_start(out=row_scr, in_=smixr[:]),
                        waits=[zm_ev[-1]], inc=ldsc, amt=16, dma=True)
            t_mr = p.op("gpsimd", lambda e: e.dma_start(
                out=smixt[:], in_=row_scr.rearrange("o (p k) -> (o p) k", p=128)),
                waits=[t_mw, sc_prev], inc=ldsc, amt=16, dma=True)

            def smax(e, rb=rb):
                e.memset(smixt[:, 15:16], NEG)
                e.drain()
                if rb == 0:
                    e.tensor_tensor(out=smixt[:], in0=smixt[:], in1=skmM[:], op=ALU.add)
                    e.drain()
                e.tensor_reduce(out=sstat[:, 0:1], in_=smixt[:], axis=AXL.X, op=ALU.max, negate=True)
                return e.drain()
            t_sm = p.op("vector", smax, waits=[t_mr, t_ld1], inc=sv_, dma=True)
            t_ex = p.op("scalar", lambda e: e.activation(out=sw_[:], in_=smixt[:], func=AF.Exp,
                                                         bias=sstat[:, 0:1], scale=1.0),
                        waits=[t_sm], inc=ss)

            def wnorm(e):
                e.tensor_reduce(out=sstat[:, 1:2], in_=sw_[:], axis=AXL.X, op=ALU.add)
                e.drain()
                e.reciprocal(out=sstat[:, 1:2], in_=sstat[:, 1:2])
                e.drain()
                return e.tensor_scalar(out=sw_[:], in0=sw_[:], scalar1=sstat[:, 1:2], scalar2=None,
                                       op0=ALU.mult)
            t_wn = p.op("vector", wnorm, waits=[t_ex], inc=sv_)
            t_zw = p.op("gpsimd", lambda e: e.dma_start(out=row_scr, in_=szrow[0:1, :]),
                        waits=[zm_ev[-1], t_mr], inc=ldsc, amt=16, dma=True)
            t_zr = p.op("gpsimd", lambda e: e.dma_start(
                out=sfx[:], in_=row_scr.rearrange("o (p k) -> (o p) k", p=128)),
                waits=[t_zw], inc=ldsc, amt=16, dma=True)
            t_zw2 = p.op("gpsimd", lambda e: e.dma_start(out=row_scr, in_=szrow[1:2, :]),
                         waits=[t_zr], inc=ldsc, amt=16, dma=True)
            t_zr2 = p.op("gpsimd", lambda e: e.dma_start(
                out=sfy[:], in_=row_scr.rearrange("o (p k) -> (o p) k", p=128)),
                waits=[t_zw2], inc=ldsc, amt=16, dma=True)

            def mkf(e, buf):
                e.tensor_scalar(out=buf[:], in0=buf[:], scalar1=7.5, scalar2=7.5,
                                op0=ALU.mult, op1=ALU.add)
                e.drain()
                e.tensor_scalar(out=buf[:], in0=buf[:], scalar1=0.0, scalar2=15.0,
                                op0=ALU.max, op1=ALU.min)
                return e.drain()
            t_fx = p.op("vector", lambda e: mkf(e, sfx), waits=[t_zr], inc=sv_, dma=True)
            t_fy = p.op("vector", lambda e: mkf(e, sfy), waits=[t_zr2], inc=sv_, dma=True)
            def hats(e, src, dst, wmul):
                for j in range(16):
                    e.tensor_copy(out=dst[:, j * 16:(j + 1) * 16], in_=src[:])
                e.drain()
                e.tensor_tensor(out=dst[:], in0=dst[:], in1=sioy[:], op=ALU.subtract)
                e.drain()
                e.tensor_scalar(out=stmp[:], in0=dst[:], scalar1=1.0, scalar2=None, op0=ALU.add)
                e.drain()
                e.tensor_scalar(out=dst[:], in0=dst[:], scalar1=-1.0, scalar2=1.0,
                                op0=ALU.mult, op1=ALU.add)
                e.drain()
                e.tensor_tensor(out=dst[:], in0=dst[:], in1=stmp[:], op=ALU.min)
                e.drain()
                e.tensor_scalar(out=dst[:], in0=dst[:], scalar1=0.0, scalar2=None, op0=ALU.max)
                e.drain()
                if wmul:
                    for j in range(16):
                        e.tensor_tensor(out=dst[:, j * 16:(j + 1) * 16],
                                        in0=dst[:, j * 16:(j + 1) * 16], in1=sw_[:], op=ALU.mult)
                    e.drain()
                return e.drain()
            t_hx = p.op("vector", lambda e: hats(e, sfx, shx, False), waits=[t_fx], inc=sv_, dma=True)
            t_hy = p.op("vector", lambda e: hats(e, sfy, shy, True), waits=[t_fy, t_wn], inc=sv_, dma=True)
            def abuild(e):
                for x in range(16):
                    e.tensor_copy(out=sYe[:].rearrange("p (y x k) -> p y x k", y=16, x=16)[:, :, x, :],
                                  in_=shy[:].rearrange("p (y k) -> p y k", y=16))
                for y in range(16):
                    e.tensor_copy(out=sXe[:, y * 256:(y + 1) * 256], in_=shx[:])
                e.drain()
                e.tensor_tensor(out=sYe[:], in0=sYe[:], in1=sXe[:], op=ALU.mult)
                e.drain()
                e.tensor_reduce(out=sA[:], in_=sYe[:].rearrange("p (c k) -> p c k", k=16),
                                axis=AXL.X, op=ALU.add)
                e.drain()
                e.tensor_copy(out=sAb[:], in_=sA[:])
                return e.drain()
            t_ab = p.op("vector", abuild, waits=[t_hx, t_hy], inc=sv_, dma=True)
            ta_ev = []
            for half in range(2):
                t_at = p.op("tensor", lambda e, half=half: e.transpose(
                    out=p_t2[half][:], in_=sAb[:, half * 128:(half + 1) * 128],
                    identity=sident_b[:]), waits=[t_ab, ta_ev[-1] if ta_ev else None], inc=st_)
                ta_ev.append(p.op("vector", lambda e, half=half: e.tensor_copy(
                    out=sAf[:, half * 128:(half + 1) * 128], in_=p_t2[half][:]),
                    waits=[t_at], inc=sv_))
                tb_prev[half] = ta_ev[-1]
            o_ev = []
            for mt in range(8):
                t_my = p.op("tensor", lambda e, mt=mt: e.matmul(
                    out=p_m[:, 0:128], lhsT=sPW[:, 0 * D + mt * 128: 0 * D + (mt + 1) * 128],
                    rhs=sAf[:, 0:128], start=True, stop=False),
                    waits=[ta_ev[-1]] + ([o_ev[-1]] if o_ev else [out_prev]), inc=None)
                t_my = p.op("tensor", lambda e, mt=mt: e.matmul(
                    out=p_m[:, 0:128], lhsT=sPW[:, 1 * D + mt * 128: 1 * D + (mt + 1) * 128],
                    rhs=sAf[:, 128:256], start=False, stop=True), inc=st_)
                o_ev.append(p.op("vector", lambda e, mt=mt, jb=jb: e.tensor_tensor(
                    out=souf[:, mt * 128:(mt + 1) * 128], in0=p_m[:, 0:128],
                    in1=sx3b[jb][:, mt * 128:(mt + 1) * 128], op=ALU.add),
                    waits=[t_my, t_x3], inc=sv_))
            out_prev = p.op("gpsimd", lambda e, rb=rb: e.dma_start(
                out=o_yf.rearrange("(k p) t -> p k t", p=128)[:, :, rb * 128:(rb + 1) * 128],
                in_=souf[:].rearrange("p (k c) -> p k c", k=8)),
                waits=[o_ev[-1]], inc=sgo, amt=16, dma=True)
            blk_prev[jb] = o_ev[-1]
            sc_prev = t_wn
            gx_prev = (ldgx, p.cnt["ldgx"])
            prev_mlp_guard = mlp_guard
        p.fence("gpsimd", [(sgo, p.cnt["sgo"])])
        p.barrier([sv_, ss, st_])
        p.emit(block)
    return nc



_IDENT = np.eye(128, dtype=np.float32)


def _prep_A(inputs):
    x = inputs["L0_attn_x"]
    wqkv = np.ascontiguousarray(inputs["L0_attn_w_qkv"])
    bqkv = np.ascontiguousarray(inputs["L0_attn_b_qkv"].reshape(24, 128).T)
    pt = inputs["L0_attn_proj_matrix"].transpose(0, 2, 1)
    proj = np.zeros((128, 8 * 2 * MF), dtype=np.float32)
    for hp in range(8):
        proj[0:64, hp * 512: hp * 512 + 256] = pt[2 * hp]
        proj[64:128, hp * 512 + 256: hp * 512 + 512] = pt[2 * hp + 1]
    proj = np.ascontiguousarray(proj)
    maps = []
    for c in range(NCORE):
        b = c // 4
        xa = np.ascontiguousarray(x[b][tok_index(c)])
        maps.append({"xa": xa, "wqkv": wqkv, "bqkv": bqkv, "proj": proj, "ident": _IDENT})
    return maps


def run_A(inputs, **kw):
    nc = build_A(**kw)
    maps = _prep_A(inputs)
    res = run_bass_kernel_spmd(nc, maps, core_ids=list(range(NCORE)))
    return res.results




def _performer_ref_np(xn, w_qkv, b_qkv, w_out, b_out, proj):
    qkv = xn @ w_qkv + b_qkv
    q, k, v = np.split(qkv, 3, axis=-1)
    DATA_NORM_ = 64 ** -0.25
    out = np.zeros_like(xn)
    for h in range(16):
        qh = q[:, h * 64:(h + 1) * 64] * DATA_NORM_
        kh = k[:, h * 64:(h + 1) * 64] * DATA_NORM_
        vh = v[:, h * 64:(h + 1) * 64]

        def fm(z):
            xp = z @ proj[h].T
            xp = xp - xp.max(-1, keepdims=True)
            return (np.exp(xp) + 1e-6) / 16.0
        qp = fm(qh)
        kp = fm(kh)
        kv = kp.T @ vh
        ks = kp.sum(0)
        num = qp @ kv
        den = qp @ ks + 1e-6
        out[:, h * 64:(h + 1) * 64] = num / den[:, None]
    return out @ w_out + b_out


def _prep_B(inputs, resA):
    x = inputs["L0_attn_x"]
    wffnT = np.ascontiguousarray(inputs["L0_ffn_weight"].T)
    bffn = np.ascontiguousarray(inputs["L0_ffn_bias"][None, :])
    wout = np.ascontiguousarray(inputs["L0_attn_w_out"])
    bout = np.ascontiguousarray(inputs["L0_attn_b_out"][None, :])
    wip = np.ascontiguousarray(np.concatenate([inputs["L1_attn_Wi"], inputs["L1_attn_Wp"]], axis=1))
    gwT = np.ascontiguousarray(inputs["L1_ffn_gate_w"].T)
    wexp = np.ascontiguousarray(inputs["L1_ffn_expert_w"].transpose(0, 2, 1))
    amax8 = np.ascontiguousarray(np.stack([resA[c]["o_amax"][0, 0] for c in range(NCORE)])[None, :])
    iota8 = np.arange(8, dtype=np.float32)[None, :]
    import ml_dtypes
    half = 512
    inv_freq = (1.0 / 10000.0 ** (np.arange(half, dtype=np.float32) / half)).astype(np.float32)
    maps = []
    for c in range(NCORE):
        b = c // 4
        tok = tok_index(c)
        ang = (tok[:, None].astype(np.float32) * inv_freq[None, :]).astype(np.float32)
        cosb = np.cos(ang).astype(ml_dtypes.bfloat16)
        sinb = np.sin(ang).astype(ml_dtypes.bfloat16)
        kvstack = np.stack([resA[4 * b + j]["o_kv"] for j in range(4)])
        kv4 = np.ascontiguousarray(kvstack.transpose(3, 1, 2, 0, 4).reshape(128, -1))
        maps.append({
            "xa": np.ascontiguousarray(x[b][tok]),
            "xn1T": np.ascontiguousarray(resA[c]["o_xn1"].T),
            "qpT": np.ascontiguousarray(resA[c]["o_qp"].transpose(0, 2, 1)),
            "kv4": kv4, "amax8": amax8, "wffnT": wffnT, "bffn": bffn,
            "wout": wout, "bout": bout, "wip": wip, "gwT": gwT, "wexp": wexp,
            "cosb": cosb, "sinb": sinb, "ident": _IDENT, "iota8": iota8,
        })
    return maps


def run_B(inputs, resA):
    nc = build_B()
    maps = _prep_B(inputs, resA)
    res = run_bass_kernel_spmd(nc, maps, core_ids=list(range(NCORE)))
    return res.results




def _prep_C(inputs, resB):
    import ml_dtypes
    Pfull = [np.zeros((T, D), ml_dtypes.bfloat16) for _ in range(B)]
    Phfull = [np.zeros((T, D), ml_dtypes.bfloat16) for _ in range(B)]
    for c in range(NCORE):
        b = c // 4
        Pfull[b][tok_index(c)] = resB[c]["o_P"]
        Phfull[b][tok_index(c)] = resB[c]["o_Ph"]
    PTb = [np.ascontiguousarray(Pfull[b].T) for b in range(B)]
    w1p = np.zeros((32, 64), np.float32)
    w1p[0:17] = inputs["L1_attn_W1"]
    iotas = np.tile(np.arange(512, dtype=np.int32)[None, :], (128, 1))
    iot128 = np.arange(128, dtype=np.float32)[:, None]
    ioy = np.tile(np.repeat(np.arange(16, dtype=np.float32), 16)[None, :], (128, 1)).astype(ml_dtypes.bfloat16)
    kmG0 = np.ones((32, 256), np.float32)
    for t in range(14):
        for k in range(16):
            for j in range(15):
                kmG0[j, 16 * t + k] = float((k <= t) and (j <= t))
            kmG0[15, 16 * t + k] = float(k <= t)
            kmG0[16, 16 * t + k] = float(k <= t)
    kmM0 = np.zeros((128, 16), np.float32)
    for t in range(14):
        for k in range(16):
            if k > t:
                kmM0[t, k] = NEG
    ones_kmG = np.ones((32, 256), np.float32)
    zeros_kmM = np.zeros((128, 16), np.float32)
    maps = []
    for c in range(NCORE):
        b = c // 4
        m = c % 4
        blocks = tok_blocks(c)
        sp = np.arange(512)[None, :]
        pp = np.arange(128)[:, None]
        cm = np.where(sp > 128 * m + pp, NEG, 0.0).astype(np.float32)
        rowg_np = np.stack([gb * 128 + np.arange(128, dtype=np.float32) for gb in blocks], 1)
        maps.append({
            "PT": PTb[b], "Phat": np.ascontiguousarray(Phfull[b]),
            "If_": np.ascontiguousarray(resB[c]["o_I"].T),
            "Ihf": np.ascontiguousarray(resB[c]["o_Ih"].T),
            "x3pT": np.ascontiguousarray(resB[c]["o_x3p"].T),
            "w1p": w1p, "b1": np.ascontiguousarray(inputs["L1_attn_b1"][:, None]),
            "w2": np.ascontiguousarray(inputs["L1_attn_W2"]),
            "b2": np.ascontiguousarray(inputs["L1_attn_b2"][:, None]),
            "wc": np.ascontiguousarray(inputs["L1_attn_Wc"]),
            "bc": np.ascontiguousarray(inputs["L1_attn_bc"][:, None]),
            "wm": np.ascontiguousarray(inputs["L1_attn_Wm"]),
            "bm": np.ascontiguousarray(inputs["L1_attn_bm"][:, None]),
            "pal": np.ascontiguousarray(inputs["L1_attn_palette"].reshape(D, 256)),
            "wo": np.ascontiguousarray(inputs["L1_attn_Wo"][0]),
            "ident": _IDENT, "cmask": cm, "rowg": np.ascontiguousarray(rowg_np),
            "iotas": iotas, "iot128": iot128,
            "ioy": np.ascontiguousarray(ioy),
            "kmG": np.ascontiguousarray((kmG0 if m == 0 else ones_kmG).astype(ml_dtypes.bfloat16)),
            "kmM": (kmM0 if m == 0 else zeros_kmM),
        })
    return maps


def run_C(inputs, resB):
    maps = _prep_C(inputs, resB)
    nc = build_C()
    res = run_bass_kernel_spmd(nc, maps, core_ids=list(range(NCORE)))
    return res.results


def kernel(**inputs):
    inputs = {k: np.asarray(v) for k, v in inputs.items()}
    resA = run_A(inputs)
    resB = run_B(inputs, resA)
    resC = run_C(inputs, resB)
    out = np.zeros((B, T, D), dtype=np.float32)
    for c in range(NCORE):
        b = c // 4
        out[b][tok_index(c)] = resC[c]["o_yf"].T
    return out


# version 05
# speedup vs baseline: 3.6834x; 1.3099x over previous
import math
import numpy as np
from contextlib import ExitStack

import concourse.bass as bass
import concourse.mybir as mybir
from concourse.bass_utils import run_bass_kernel_spmd

F32 = mybir.dt.float32
BF16 = mybir.dt.bfloat16
FP8 = mybir.dt.float8e4
I32 = mybir.dt.int32
U32 = mybir.dt.uint32
AF = mybir.ActivationFunctionType
ALU = mybir.AluOpType
AXL = mybir.AxisListType

B, T, D, NCORE = 2, 8192, 1024, 8
TPC = 2048
NH, HD, MF = 16, 64, 256
KSEL = 15
LN_EPS = 1e-5
PEPS = 1e-6
DATA_NORM = HD ** -0.25
NEG = -1.0e38
ENGS = ("sync", "vector", "scalar", "tensor", "gpsimd")
NT = TPC // 128


def tok_blocks(c):
    return list(range(c % 4, 64, 4))


def tok_index(c):
    return np.concatenate([np.arange(128) + 128 * bb for bb in tok_blocks(c)])


class Prog:

    def __init__(self, nc):
        self.nc = nc
        self.q = {e: [] for e in ENGS}
        self.cnt = {}

    def mksem(self, ctx, name):
        h = ctx.enter_context(self.nc.semaphore(name))
        self.cnt[name] = 0
        return h

    def op(self, eng, build, waits=(), inc=None, amt=1, dma=False):
        waits = tuple(w for w in waits if w is not None)
        tok = None
        if inc is not None:
            self.cnt[inc.name] += amt
            tok = (inc, self.cnt[inc.name])
        need_drain = eng in ("vector", "scalar", "gpsimd", "tensor") and not dma

        def f(e, build=build, waits=waits, inc=inc, amt=amt, need_drain=need_drain):
            for (s, v) in waits:
                e.wait_ge(s, v)
            ins = build(e)
            if inc is not None:
                if need_drain:
                    ins = e.drain()
                ins.then_inc(inc, amt)

        self.q[eng].append(f)
        return tok

    def barrier(self, sems):
        toks = [(s, self.cnt[s.name]) for s in sems]
        for e in ENGS:
            self.fence(e, toks)

    def fence(self, eng, waits):
        waits = tuple(w for w in waits if w is not None)

        def f(e, waits=waits):
            for (s, v) in waits:
                e.wait_ge(s, v)

        self.q[eng].append(f)

    def setup_consts(self, ctx, values, sem=None):
        sem = self.mksem(ctx, "scst")
        E = ctx.enter_context
        vals = sorted(set(float(v) for v in values))
        tile = E(self.nc.sbuf_tensor("consts", [128, max(1, len(vals))], F32))
        for i, v in enumerate(vals):
            self.nc.const_aps.register(F32, v, tile[:, i:i + 1])

        def f(e, vals=vals, tile=tile):
            ins = None
            for i, v in enumerate(vals):
                ins = e.memset(tile[:, i:i + 1], v)
            ins.then_inc(sem, 1)
        self.cnt[sem.name] += 1
        tok = (sem, self.cnt[sem.name])
        self.q["gpsimd"].append(f)
        for e in ENGS:
            if e != "gpsimd":
                self.fence(e, [tok])
        return tok

    def emit(self, block):
        for e in ENGS:
            fns = self.q[e]

            def runner(eng, fns=fns):
                for f in fns:
                    f(eng)

            getattr(block, e)(runner)


def build_A(hp_count=8, ln_only=False, stage='kv'):
    nc = bass.Bass("TRN2", target_bir_lowering=False)
    xa = nc.dram_tensor("xa", [TPC, D], F32, kind="ExternalInput").ap()
    wqkv = nc.dram_tensor("wqkv", [D, 3 * D], F32, kind="ExternalInput").ap()
    bqkv = nc.dram_tensor("bqkv", [128, 24], F32, kind="ExternalInput").ap()
    proj = nc.dram_tensor("proj", [128, 8 * 2 * MF], F32, kind="ExternalInput").ap()
    ident = nc.dram_tensor("ident", [128, 128], F32, kind="ExternalInput").ap()

    o_xn1 = nc.dram_tensor("o_xn1", [TPC, D], F32, kind="ExternalOutput").ap()
    o_qp = nc.dram_tensor("o_qp", [NH, TPC, MF], BF16, kind="ExternalOutput").ap()
    o_kv = nc.dram_tensor("o_kv", [NH, 2, 128, 65], F32, kind="ExternalOutput").ap()
    o_amax = nc.dram_tensor("o_amax", [1, 1], F32, kind="ExternalOutput").ap()

    with ExitStack() as ctx:
        E = ctx.enter_context
        sx = [E(nc.sbuf_tensor(f"sx{j}", [128, D], F32)) for j in range(2)]
        sxn = [E(nc.sbuf_tensor(f"sxn{j}", [128, D], F32)) for j in range(2)]
        ssq = E(nc.sbuf_tensor("ssq", [128, D], F32))
        sstat = [E(nc.sbuf_tensor(f"sstat{j}", [128, 8], F32)) for j in range(2)]
        samax = E(nc.sbuf_tensor("samax", [128, 1], F32))
        samax_r = E(nc.sbuf_tensor("samax_r", [1, 128], F32))
        samax_f = E(nc.sbuf_tensor("samax_f", [1, 1], F32))
        sident = E(nc.sbuf_tensor("sident", [128, 128], F32))
        sident_b = E(nc.sbuf_tensor("sident_b", [128, 128], BF16))
        sxnT = E(nc.sbuf_tensor("sxnT", [128, 8 * TPC], BF16))
        swq = E(nc.sbuf_tensor("swq", [128, 8 * 3 * D], BF16))
        sbias = E(nc.sbuf_tensor("sbias", [128, 24], F32))
        sproj = E(nc.sbuf_tensor("sproj", [128, 8 * 2 * MF], BF16))
        sproj0 = E(nc.sbuf_tensor("sproj0", [128, 8 * 2 * MF], BF16))
        sq = E(nc.sbuf_tensor("sq", [128, TPC], BF16))
        sk = E(nc.sbuf_tensor("sk", [128, TPC], BF16))
        sv = E(nc.sbuf_tensor("svv", [128, TPC], BF16))
        sexp = E(nc.sbuf_tensor("sexp", [128, 512], F32))
        snmax = E(nc.sbuf_tensor("snmax", [128, 4], F32))
        sqp = E(nc.sbuf_tensor("sqp", [128, 512], BF16))
        skp = E(nc.sbuf_tensor("skp", [128, 16 * 512], BF16))
        svt = E(nc.sbuf_tensor("svt", [128, 16 * 130], BF16))
        skv = E(nc.sbuf_tensor("skv", [128, 65], F32))

        p_mm = [E(nc.psum_tensor(f"p_mm{j}", [128, 512], F32)) for j in range(2)]
        p_xp = [E(nc.psum_tensor(f"p_xp{j}", [128, 512], F32)) for j in range(2)]
        p_vt = E(nc.psum_tensor("p_vt", [128, 64], BF16))
        p_kv = [E(nc.psum_tensor(f"p_kv{j}", [128, 65], F32)) for j in range(2)]
        p_tr = [p_xp[0][:, 0:128], p_xp[1][:, 0:128]]

        p = Prog(nc)
        ld = p.mksem(ctx, "ld")
        lda = [p.mksem(ctx, "lda0"), p.mksem(ctx, "lda1")]
        ldg = p.mksem(ctx, "ldg")
        sv_ = p.mksem(ctx, "sv_")
        ss = p.mksem(ctx, "ss")
        st_ = p.mksem(ctx, "st_")
        sg = p.mksem(ctx, "sg")
        sgx = [p.mksem(ctx, "sgx0"), p.mksem(ctx, "sgx1")]
        sgq = p.mksem(ctx, "sgq")
        sgk = p.mksem(ctx, "sgk")
        block = E(nc.Block())
        p.setup_consts(ctx, [0.0, 1.0, -0.5, -1.0, LN_EPS, 1.0 / D, DATA_NORM, 1.0 / 16.0, PEPS / 16.0], sg)

        t_ident = p.op("sync", lambda e: e.dma_start(out=sident[:], in_=ident), inc=ld, amt=16)
        t_bias = p.op("sync", lambda e: e.dma_start(out=sbias[:], in_=bqkv), inc=ld, amt=16)
        t_ident = t_bias = (ld, 32)
        t_wq = []
        for k in range(8):
            t_wq.append(p.op("gpsimd",
                             lambda e, k=k: e.dma_start(out=swq[:, k * 3 * D:(k + 1) * 3 * D],
                                                        in_=wqkv[k * 128:(k + 1) * 128, :]),
                             inc=ldg, amt=16, dma=True))
        t_proj = p.op("gpsimd", lambda e: e.dma_start(out=sproj0[:, :], in_=proj), inc=ldg, amt=16, dma=True)
        t_identb = p.op("vector", lambda e: e.tensor_copy(out=sident_b[:], in_=sident[:]),
                        waits=[t_ident], inc=sv_)
        t_ldg_all = (ldg, p.cnt["ldg"])
        t_projs = p.op("vector",
                       lambda e: e.tensor_scalar(out=sproj[:], in0=sproj0[:], scalar1=DATA_NORM,
                                                 scalar2=None, op0=ALU.mult),
                       waits=[t_ldg_all], inc=sv_)
        sgm = p.mksem(ctx, "sgm")
        t_ones = p.op("gpsimd", lambda e: e.memset(svt[:], 1.0), inc=sgm, amt=1)

        xn_done = [None] * 16
        mean_done = [None] * 16
        ev_done = [None] * 16
        out_done = [None] * 16
        am_prev = None
        trbank_prev = {0: None, 1: None}
        for i in range(16):
            j = i % 2
            tl = p.op("sync", lambda e, i=i, j=j: e.dma_start(out=sx[j][:], in_=xa[i * 128:(i + 1) * 128, :]),
                      waits=([xn_done[i - 2], mean_done[i - 2]] if i >= 2 else []), inc=lda[j], amt=16)
            p.op("vector", lambda e, j=j: e.reduce_sum(out=sstat[j][:, 0:1], in_=sx[j][:], axis=AXL.X),
                 waits=[tl] + ([xn_done[i - 2]] if i >= 2 else []), inc=sv_)
            mean_done[i] = p.op("vector", lambda e, j=j: e.tensor_scalar(
                out=sstat[j][:, 0:1], in0=sstat[j][:, 0:1], scalar1=1.0 / D, scalar2=None, op0=ALU.mult),
                inc=sv_)
            t_sq = p.op("scalar", lambda e, j=j: e.activation(out=ssq[:], in_=sx[j][:], func=AF.Square,
                                                              accum_out=sstat[j][:, 1:2]),
                        waits=[tl, mean_done[i]], inc=ss)
            p.op("vector", lambda e, j=j: e.tensor_scalar(out=sstat[j][:, 2:3], in0=sstat[j][:, 1:2],
                                                          scalar1=1.0 / D, scalar2=None, op0=ALU.mult),
                 waits=[t_sq], inc=sv_)
            p.op("vector", lambda e, j=j: e.tensor_tensor(out=sstat[j][:, 3:4], in0=sstat[j][:, 0:1],
                                                          in1=sstat[j][:, 0:1], op=ALU.mult), inc=sv_)
            t_var = p.op("vector", lambda e, j=j: e.tensor_tensor(out=sstat[j][:, 2:3], in0=sstat[j][:, 2:3],
                                                                  in1=sstat[j][:, 3:4], op=ALU.subtract), inc=sv_)
            p.op("scalar", lambda e, j=j: e.activation(out=sstat[j][:, 4:5], in_=sstat[j][:, 2:3],
                                                       func=AF.Ln, bias=LN_EPS, scale=1.0),
                 waits=[t_var], inc=ss)
            t_rstd = p.op("scalar", lambda e, j=j: e.activation(out=sstat[j][:, 4:5], in_=sstat[j][:, 4:5],
                                                                func=AF.Exp, scale=-0.5), inc=ss)
            p.op("vector", lambda e, j=j: e.tensor_tensor(out=sstat[j][:, 5:6], in0=sstat[j][:, 0:1],
                                                          in1=sstat[j][:, 4:5], op=ALU.mult),
                 waits=[t_rstd], inc=sv_)
            t_bneg = p.op("vector", lambda e, j=j: e.tensor_scalar(out=sstat[j][:, 5:6], in0=sstat[j][:, 5:6],
                                                                   scalar1=-1.0, scalar2=None, op0=ALU.mult),
                          inc=sv_)
            xn_done[i] = p.op("vector", lambda e, j=j: e.tensor_scalar(
                out=sxn[j][:], in0=sx[j][:], scalar1=sstat[j][:, 4:5], scalar2=sstat[j][:, 5:6],
                op0=ALU.mult, op1=ALU.add),
                waits=[t_bneg] + ([ev_done[i - 2], out_done[i - 2]] if i >= 2 else []), inc=sv_)
            p.op("vector", lambda e, j=j: e.tensor_reduce(out=sstat[j][:, 6:7], in_=sxn[j][:], axis=AXL.X,
                                                          op=ALU.max, apply_absolute_value=True),
                 waits=[xn_done[i]], inc=sv_)
            if i == 0:
                am_prev = p.op("vector", lambda e, j=j: e.tensor_copy(out=samax[:], in_=sstat[j][:, 6:7]),
                               inc=sv_)
            else:
                am_prev = p.op("vector", lambda e, j=j: e.tensor_tensor(out=samax[:], in0=samax[:],
                                                                        in1=sstat[j][:, 6:7], op=ALU.max),
                               inc=sv_)
            out_done[i] = p.op("gpsimd", lambda e, i=i, j=j: e.dma_start(
                out=o_xn1[i * 128:(i + 1) * 128, :], in_=sxn[j][:]),
                waits=[xn_done[i]], inc=sgx[j], amt=16, dma=True)
            ev_tok = {}
            for k in range(8):
                tb = k % 2
                wt = [xn_done[i], t_ident]
                if k >= 2:
                    wt.append(ev_tok[k - 2])
                else:
                    wt.append(trbank_prev[tb])
                tt = p.op("tensor", lambda e, j=j, k=k, tb=tb: e.transpose(
                    out=p_tr[tb], in_=sxn[j][:, k * 128:(k + 1) * 128], identity=sident[:]),
                    waits=wt, inc=st_)
                ev_tok[k] = p.op("vector", lambda e, i=i, k=k, tb=tb: e.tensor_copy(
                    out=sxnT[:, k * TPC + i * 128: k * TPC + (i + 1) * 128], in_=p_tr[tb]),
                    waits=[tt], inc=sv_)
            trbank_prev = {0: ev_tok[6], 1: ev_tok[7]}
            ev_done[i] = ev_tok[7]

        t_amT = p.op("tensor", lambda e: e.transpose(out=p_xp[0][0:1, 0:128], in_=samax[:],
                                                     identity=sident[:]),
                     waits=[am_prev, trbank_prev[0]], inc=st_)
        t_amE = p.op("vector", lambda e: e.tensor_copy(out=samax_r[:], in_=p_xp[0][0:1, 0:128]),
                     waits=[t_amT], inc=sv_)
        t_amR = p.op("vector", lambda e: e.tensor_reduce(out=samax_f[:], in_=samax_r[:], axis=AXL.X,
                                                         op=ALU.max), waits=[t_amE], inc=sv_)
        t_amO = p.op("gpsimd", lambda e: e.dma_start(out=o_amax, in_=samax_f[:]),
                     waits=[t_amR], inc=sg, amt=16, dma=True)

        if ln_only:
            p.fence("gpsimd", [(sg, p.cnt["sg"]), (sgx[0], p.cnt["sgx0"]), (sgx[1], p.cnt["sgx1"]),
                               t_amO, (sv_, p.cnt["sv_"]), (ss, p.cnt["ss"]), (st_, p.cnt["st_"])])
            p.emit(block)
            return nc
        all_xnT = ev_done[15]
        mm_prev = {0: None, 1: None}
        xp_prev = {0: t_amE, 1: None}
        vt_prev = None
        kv_prev = {0: None, 1: None}
        exp_cons = None
        nm_cons = None
        qp_cons = None
        kv_last_mm = None
        skv_dma_prev = None
        for hp in range(hp_count):
            slab_toks = {}
            for sl, (dst, col0) in enumerate(((sq, 128 * hp), (sk, D + 128 * hp), (sv, 2 * D + 128 * hp))):
                last_ev = None
                for ntc in range(4):
                    pb = (sl * 4 + ntc) % 2
                    t_mm = None
                    for kt in range(8):
                        wt = []
                        if kt == 0:
                            wt = [mm_prev[pb], all_xnT] + ([t_ldg_all, t_bias] if hp == 0 and sl == 0 and ntc == 0 else [])
                            if sl == 0 and ntc == 0:
                                wt.append(kv_last_mm)
                                wt.append(exp_cons)
                        t_mm = p.op("tensor", lambda e, kt=kt, col0=col0, ntc=ntc, pb=pb: e.matmul(
                            out=p_mm[pb][:],
                            lhsT=swq[:, kt * 3 * D + col0: kt * 3 * D + col0 + 128],
                            rhs=sxnT[:, kt * TPC + ntc * 512: kt * TPC + (ntc + 1) * 512],
                            start=(kt == 0), stop=(kt == 7)),
                            waits=wt, inc=(st_ if kt == 7 else None))
                    bias_col = col0 // 128
                    last_ev = p.op("vector", lambda e, dst=dst, ntc=ntc, pb=pb, bias_col=bias_col: e.tensor_scalar(
                        out=dst[:, ntc * 512:(ntc + 1) * 512], in0=p_mm[pb][:],
                        scalar1=sbias[:, bias_col:bias_col + 1], scalar2=None, op0=ALU.add),
                        waits=[t_mm], inc=sv_)
                    mm_prev[pb] = last_ev
                slab_toks[sl] = last_ev

            for ic in range(NT if stage != 'slab' else 0):
                for h2 in range(2):
                    wt = [slab_toks[2], t_ones, vt_prev, t_identb]
                    ttv = p.op("tensor", lambda e, ic=ic, h2=h2: e.transpose(
                        out=p_vt[:], in_=sv[h2 * 64:(h2 + 1) * 64, ic * 128:(ic + 1) * 128],
                        identity=sident_b[h2 * 64:(h2 + 1) * 64, h2 * 64:(h2 + 1) * 64]),
                        waits=wt, inc=st_)
                    vt_prev = p.op("vector", lambda e, ic=ic, h2=h2: e.tensor_copy(
                        out=svt[:, ic * 130 + h2 * 65: ic * 130 + h2 * 65 + 64], in_=p_vt[0:128, 0:64]),
                        waits=[ttv, kv_last_mm if ic == 0 and h2 == 0 else None], inc=sv_)
            vt_all = vt_prev

            kp_last = None
            for src, is_q in (((sq, True), (sk, False)) if stage in ('fm', 'kv') else ()):
                for ic in range(NT):
                    pb = ic % 2
                    mm2 = p.op("tensor", lambda e, src=src, ic=ic, pb=pb, hp=hp: e.matmul(
                        out=p_xp[pb][:], lhsT=src[:, ic * 128:(ic + 1) * 128],
                        rhs=sproj[:, hp * 512:(hp + 1) * 512], start=True, stop=True),
                        waits=[slab_toks[0] if is_q else slab_toks[1], t_projs, xp_prev[pb]], inc=st_)
                    t_nm = p.op("vector", lambda e, pb=pb: e.tensor_reduce(
                        out=snmax[:, 0:2], in_=p_xp[pb][:].rearrange("p (h m) -> p h m", h=2), axis=AXL.X,
                        op=ALU.max, negate=True),
                        waits=[mm2, nm_cons], inc=sv_)
                    e1 = p.op("scalar", lambda e, pb=pb: e.activation(
                        out=sexp[:, 0:256], in_=p_xp[pb][:, 0:256], func=AF.Exp,
                        bias=snmax[:, 0:1], scale=1.0), waits=[t_nm, exp_cons], inc=ss)
                    e2 = p.op("scalar", lambda e, pb=pb: e.activation(
                        out=sexp[:, 256:512], in_=p_xp[pb][:, 256:512], func=AF.Exp,
                        bias=snmax[:, 1:2], scale=1.0), inc=ss)
                    xp_prev[pb] = e2
                    if is_q:
                        t_qp = p.op("vector", lambda e: e.tensor_scalar(
                            out=sqp[:], in0=sexp[:], scalar1=1.0 / 16.0, scalar2=PEPS / 16.0,
                            op0=ALU.mult, op1=ALU.add), waits=[e2, qp_cons], inc=sv_)
                        exp_cons = t_qp
                        nm_cons = e2
                        qp_cons = p.op("gpsimd", lambda e, ic=ic, hp=hp: e.dma_start(
                            out=o_qp.rearrange("h t m -> t h m")[ic * 128:(ic + 1) * 128,
                                                                 2 * hp:2 * hp + 2, :],
                            in_=sqp[:].rearrange("p (h m) -> p h m", h=2)),
                            waits=[t_qp], inc=sgq, amt=16, dma=True)
                    else:
                        kp_last = p.op("vector", lambda e, ic=ic: e.tensor_scalar(
                            out=skp[:, ic * 512:(ic + 1) * 512], in0=sexp[:], scalar1=1.0 / 16.0,
                            scalar2=PEPS / 16.0, op0=ALU.mult, op1=ALU.add),
                            waits=[e2, kv_last_mm if ic == 0 else None], inc=sv_)
                        exp_cons = kp_last
                        nm_cons = e2

            for h2 in range(2 if stage == 'kv' else 0):
                head = 2 * hp + h2
                for mt in range(2):
                    t_mm = None
                    for ic in range(NT):
                        wt = [kp_last, vt_all, kv_prev[mt]] if ic == 0 else []
                        t_mm = p.op("tensor", lambda e, ic=ic, h2=h2, mt=mt: e.matmul(
                            out=p_kv[mt][:],
                            lhsT=skp[:, ic * 512 + h2 * 256 + mt * 128: ic * 512 + h2 * 256 + (mt + 1) * 128],
                            rhs=svt[:, ic * 130 + h2 * 65: ic * 130 + h2 * 65 + 65],
                            start=(ic == 0), stop=(ic == NT - 1)),
                            waits=wt, inc=(st_ if ic == NT - 1 else None))
                        kv_last_mm = t_mm if ic == NT - 1 else kv_last_mm
                    evk = p.op("vector", lambda e, mt=mt: e.tensor_copy(out=skv[:], in_=p_kv[mt][:]),
                               waits=[t_mm, skv_dma_prev], inc=sv_)
                    skv_dma_prev = p.op("gpsimd", lambda e, head=head, mt=mt: e.dma_start(
                        out=o_kv[head, mt, :, :], in_=skv[:]), waits=[evk], inc=sgk, amt=16, dma=True)
                    kv_prev[mt] = skv_dma_prev

        p.fence("gpsimd", [(sg, p.cnt["sg"]), (sgx[0], p.cnt["sgx0"]), (sgx[1], p.cnt["sgx1"]),
                           (sgq, p.cnt["sgq"]), (sgk, p.cnt["sgk"]), t_amO])
        p.emit(block)
    return nc



def build_B():
    nc = bass.Bass("TRN2", target_bir_lowering=False)
    xa = nc.dram_tensor("xa", [TPC, D], F32, kind="ExternalInput").ap()
    xn1T = nc.dram_tensor("xn1T", [D, TPC], F32, kind="ExternalInput").ap()
    qpT = nc.dram_tensor("qpT", [NH, MF, TPC], BF16, kind="ExternalInput").ap()
    kv4 = nc.dram_tensor("kv4", [128, NH * 2 * 4 * 65], F32, kind="ExternalInput").ap()
    amax8 = nc.dram_tensor("amax8", [1, 8], F32, kind="ExternalInput").ap()
    wffnT = nc.dram_tensor("wffnT", [D, D], F32, kind="ExternalInput").ap()
    bffn = nc.dram_tensor("bffn", [1, D], F32, kind="ExternalInput").ap()
    wout = nc.dram_tensor("wout", [D, D], F32, kind="ExternalInput").ap()
    bout = nc.dram_tensor("bout", [1, D], F32, kind="ExternalInput").ap()
    wip = nc.dram_tensor("wip", [D, 2 * D], F32, kind="ExternalInput").ap()
    gwT = nc.dram_tensor("gwT", [D, 8], F32, kind="ExternalInput").ap()
    wexp = nc.dram_tensor("wexp", [8, D, D], F32, kind="ExternalInput").ap()
    cosb = nc.dram_tensor("cosb", [TPC, 512], BF16, kind="ExternalInput").ap()
    sinb = nc.dram_tensor("sinb", [TPC, 512], BF16, kind="ExternalInput").ap()
    ident = nc.dram_tensor("ident", [128, 128], F32, kind="ExternalInput").ap()
    iota8 = nc.dram_tensor("iota8", [1, 8], F32, kind="ExternalInput").ap()

    o_x3p = nc.dram_tensor("o_x3p", [TPC, D], F32, kind="ExternalOutput").ap()
    o_d1 = nc.dram_tensor("o_d1", [128, D], F32, kind="ExternalOutput").ap()
    o_d2 = nc.dram_tensor("o_d2", [128, D], F32, kind="ExternalOutput").ap()
    o_d3 = nc.dram_tensor("o_d3", [128, TPC], F32, kind="ExternalOutput").ap()
    o_d0 = nc.dram_tensor("o_d0", [1, 16], F32, kind="ExternalOutput").ap()
    o_P = nc.dram_tensor("o_P", [TPC, D], BF16, kind="ExternalOutput").ap()
    o_Ph = nc.dram_tensor("o_Ph", [TPC, D], BF16, kind="ExternalOutput").ap()
    o_I = nc.dram_tensor("o_I", [TPC, D], BF16, kind="ExternalOutput").ap()
    o_Ih = nc.dram_tensor("o_Ih", [TPC, D], BF16, kind="ExternalOutput").ap()

    with ExitStack() as ctx:
        E = ctx.enter_context
        p = Prog(nc)
        ld1 = p.mksem(ctx, "ld1")
        ldg1 = p.mksem(ctx, "ldg1")
        ldx = [p.mksem(ctx, "ldx0"), p.mksem(ctx, "ldx1")]
        ldw = [p.mksem(ctx, "ldw0"), p.mksem(ctx, "ldw1")]
        lda = [p.mksem(ctx, "lda0"), p.mksem(ctx, "lda1")]
        sv_ = p.mksem(ctx, "sv_")
        ss = p.mksem(ctx, "ss")
        st_ = p.mksem(ctx, "st_")
        sgo = [p.mksem(ctx, "sgo0"), p.mksem(ctx, "sgo1")]
        block = E(nc.Block())
        p.setup_consts(ctx, [0.0, 1.0, -0.5, -1.0, LN_EPS, 1.0 / D, 448.0, 4.0, 0.5, 1e-12,
                             1000.0, 1e-24, -1000.0], sgo[0])

        sident = E(nc.sbuf_tensor("sident", [128, 128], F32))
        sident_b = E(nc.sbuf_tensor("sident_b", [128, 128], BF16))
        sones = E(nc.sbuf_tensor("sones", [1, 128], F32))
        ssc = E(nc.sbuf_tensor("ssc", [1, 16], F32))
        sscb = E(nc.sbuf_tensor("sscb", [128, 4], F32))
        sx2 = E(nc.sbuf_tensor("sx2", [128, NT * D], F32))
        sstat = E(nc.sbuf_tensor("sstat", [128, 8], F32))
        sbffn = E(nc.sbuf_tensor("sbffn", [128, D], F32))
        sbout = E(nc.sbuf_tensor("sbout", [128, D], F32))
        ssc_f = E(nc.sbuf_tensor("ssc_f", [8, TPC], F32))
        sgw32 = E(nc.sbuf_tensor("sgw32", [128, 64], F32))

        p_a = [E(nc.psum_tensor(f"p_a{j}", [128, 512], F32)) for j in range(4)]
        p_b = [E(nc.psum_tensor(f"p_b{j}", [128, 512], F32)) for j in range(2)]
        p_t = [E(nc.psum_tensor(f"p_t{j}", [128, 128], F32)) for j in range(2)]

        p.op("sync", lambda e: e.dma_start(out=sident[:], in_=ident), inc=ld1, amt=16)
        p.op("sync", lambda e: e.dma_start(out=ssc[0:1, 0:8], in_=amax8), inc=ld1, amt=16)
        p.op("sync", lambda e: e.dma_start(out=sbffn[:], in_=bffn.partition_broadcast(128)), inc=ld1, amt=16)
        p.op("sync", lambda e: e.dma_start(out=sbout[:], in_=bout.partition_broadcast(128)), inc=ld1, amt=16)
        t_ld1 = (ld1, p.cnt["ld1"])
        t_identb = p.op("vector", lambda e: e.tensor_copy(out=sident_b[:], in_=sident[:]),
                        waits=[t_ld1], inc=sv_)
        sgm = p.mksem(ctx, "sgm")
        t_ones = p.op("gpsimd", lambda e: e.memset(sones[:], 1.0), inc=sgm)

        c12 = ctx.enter_context(ExitStack())
        sffo = c12.enter_context(nc.sbuf_tensor("sffo", [128, NT * D], BF16))
        with ExitStack() as c1:
            E1 = c1.enter_context
            sxf = [E1(nc.sbuf_tensor(f"sxf{j}", [128, TPC], F32)) for j in range(2)]
            swfb = [E1(nc.sbuf_tensor(f"swfb{j}", [128, D], F32)) for j in range(2)]
            sx8 = E1(nc.sbuf_tensor("sx8", [128, 8 * TPC], FP8))
            sw8 = E1(nc.sbuf_tensor("sw8", [128, 8 * D], FP8))
            t_am = p.op("vector", lambda e: e.tensor_reduce(out=ssc[0:1, 8:9], in_=ssc[0:1, 0:8],
                                                            axis=AXL.X, op=ALU.max),
                        waits=[t_ld1], inc=sv_)
            wl_prev = {0: None, 1: None}
            am_acc = None
            for k in range(8):
                j = k % 2
                t_wl = p.op("sync", lambda e, k=k, j=j: e.dma_start(
                    out=swfb[j][:], in_=wffnT[k * 128:(k + 1) * 128, :]),
                    waits=[wl_prev[j]], inc=lda[j], amt=16)
                t_r = p.op("vector", lambda e, j=j, k=k: e.tensor_reduce(
                    out=sstat[:, 1:2], in_=swfb[j][:], axis=AXL.X, op=ALU.max,
                    apply_absolute_value=True), waits=[t_wl], inc=sv_)
                if k == 0:
                    am_acc = p.op("vector", lambda e: e.tensor_copy(out=sstat[:, 0:1], in_=sstat[:, 1:2]),
                                  inc=sv_)
                else:
                    am_acc = p.op("vector", lambda e: e.tensor_tensor(
                        out=sstat[:, 0:1], in0=sstat[:, 0:1], in1=sstat[:, 1:2], op=ALU.max), inc=sv_)
                wl_prev[j] = am_acc
            t_wT = p.op("tensor", lambda e: e.transpose(out=p_t[0][0:1, 0:128], in_=sstat[:, 0:1],
                                                        identity=sident[:]), waits=[am_acc, t_ld1], inc=st_)
            t_wr3 = p.op("vector", lambda e: e.tensor_reduce(out=ssc[0:1, 9:10],
                                                             in_=p_t[0][0:1, 0:128], axis=AXL.X,
                                                             op=ALU.max), waits=[t_wT], inc=sv_)

            def scales(e):
                e.tensor_scalar(out=ssc[0:1, 10:11], in0=ssc[0:1, 8:9], scalar1=1e-12, scalar2=None,
                                op0=ALU.max)
                e.drain()
                e.reciprocal(out=ssc[0:1, 10:11], in_=ssc[0:1, 10:11])
                e.drain()
                e.tensor_scalar(out=ssc[0:1, 10:11], in0=ssc[0:1, 10:11], scalar1=448.0, scalar2=448.0,
                                op0=ALU.mult, op1=ALU.min)
                e.drain()
                e.tensor_scalar(out=ssc[0:1, 11:12], in0=ssc[0:1, 9:10], scalar1=1e-12, scalar2=None,
                                op0=ALU.max)
                e.drain()
                e.reciprocal(out=ssc[0:1, 11:12], in_=ssc[0:1, 11:12])
                e.drain()
                e.tensor_scalar(out=ssc[0:1, 11:12], in0=ssc[0:1, 11:12], scalar1=448.0, scalar2=448.0,
                                op0=ALU.mult, op1=ALU.min)
                e.drain()
                e.tensor_tensor(out=ssc[0:1, 12:13], in0=ssc[0:1, 10:11], in1=ssc[0:1, 11:12],
                                op=ALU.mult)
                e.drain()
                e.reciprocal(out=ssc[0:1, 12:13], in_=ssc[0:1, 12:13])
                e.drain()
                e.tensor_scalar(out=ssc[0:1, 12:13], in0=ssc[0:1, 12:13], scalar1=4.0, scalar2=None,
                                op0=ALU.mult)
                e.drain()
                e.tensor_scalar(out=ssc[0:1, 10:11], in0=ssc[0:1, 10:11], scalar1=0.5, scalar2=None,
                                op0=ALU.mult)
                e.drain()
                return e.tensor_scalar(out=ssc[0:1, 11:12], in0=ssc[0:1, 11:12], scalar1=0.5,
                                       scalar2=None, op0=ALU.mult)
            t_sc = p.op("vector", scales, waits=[t_am, t_wr3], inc=sv_)
            t_bc = []
            for i_src, i_dst in ((10, 0), (11, 1), (12, 2)):
                tm = p.op("tensor", lambda e, i_src=i_src: e.matmul(
                    out=p_t[1][:, 0:1], lhsT=sones[:], rhs=ssc[0:1, i_src:i_src + 1],
                    start=True, stop=True), waits=[t_sc, t_ones, t_bc[-1] if t_bc else None], inc=st_)
                t_bc.append(p.op("vector", lambda e, i_dst=i_dst: e.tensor_copy(
                    out=sscb[:, i_dst:i_dst + 1], in_=p_t[1][:, 0:1]), waits=[tm], inc=sv_))
            t_x8 = []
            for k in range(8):
                j = k % 2
                t_xl = p.op("sync", lambda e, k=k, j=j: e.dma_start(
                    out=sxf[j][:], in_=xn1T[k * 128:(k + 1) * 128, :]),
                    waits=[t_x8[k - 2] if k >= 2 else t_sc], inc=lda[j], amt=16)
                t_x8.append(p.op("vector", lambda e, k=k, j=j: e.tensor_scalar(
                    out=sx8[:, k * TPC:(k + 1) * TPC], in0=sxf[j][:],
                    scalar1=sscb[:, 0:1], scalar2=None, op0=ALU.mult),
                    waits=[t_xl, t_bc[0]], inc=sv_))
            t_w8 = []
            for k in range(8):
                j = k % 2
                t_wl = p.op("sync", lambda e, k=k, j=j: e.dma_start(
                    out=swfb[j][:], in_=wffnT[k * 128:(k + 1) * 128, :]),
                    waits=[t_w8[k - 2] if k >= 2 else t_x8[6 + k]], inc=lda[j], amt=16)
                t_w8.append(p.op("vector", lambda e, k=k, j=j: e.tensor_scalar(
                    out=sw8[:, k * D:(k + 1) * D], in0=swfb[j][:],
                    scalar1=sscb[:, 1:2], scalar2=None, op0=ALU.mult),
                    waits=[t_wl, t_bc[1]], inc=sv_))
            prev_ev = {0: None, 1: None}
            for icc in range(NT):
                for mh in range(2):
                    pb = (icc * 2 + mh) % 2
                    t_mm = None
                    for kt in range(8):
                        wt = [prev_ev[pb], t_x8[7], t_w8[7]] if kt == 0 else []
                        t_mm = p.op("tensor", lambda e, icc=icc, mh=mh, kt=kt, pb=pb: e.matmul(
                            out=p_a[pb][:], lhsT=sx8[:, kt * TPC + icc * 128: kt * TPC + (icc + 1) * 128],
                            rhs=sw8[:, kt * D + mh * 512: kt * D + (mh + 1) * 512],
                            start=(kt == 0), stop=(kt == 7)),
                            waits=wt, inc=(st_ if kt == 7 else None))
                    prev_ev[pb] = p.op("vector", lambda e, icc=icc, mh=mh, pb=pb: e.scalar_tensor_tensor(
                        out=sffo[:, icc * D + mh * 512: icc * D + (mh + 1) * 512], in0=p_a[pb][:],
                        scalar=sscb[:, 2:3], in1=sbffn[:, mh * 512:(mh + 1) * 512],
                        op0=ALU.mult, op1=ALU.add),
                        waits=[t_mm, t_bc[2], t_ld1], inc=sv_)
            p.barrier([sv_, ss, st_])
        p.op("gpsimd", lambda e: e.dma_start(out=o_d1, in_=sffo[:, 0:D]), inc=sgo[0], amt=16, dma=True)
        p.op("gpsimd", lambda e: e.dma_start(out=o_d0, in_=ssc[:]), inc=sgo[0], amt=16, dma=True)

        with ExitStack() as c2:
            E2 = c2.enter_context
            skvb = E2(nc.sbuf_tensor("skvb", [128, NH * 2 * 65], BF16))
            with ExitStack() as c2a:
                skvs = c2a.enter_context(nc.sbuf_tensor("skvs", [128, NH * 2 * 4 * 65], F32))
                t_kv = p.op("sync", lambda e: e.dma_start(out=skvs[:], in_=kv4), inc=ld1, amt=16)

                def kvsum(e):
                    v = skvs[:].rearrange("p (h m c f) -> p (h m) c f", h=NH, m=2, c=4)
                    e.tensor_tensor(out=v[:, :, 0, :], in0=v[:, :, 0, :], in1=v[:, :, 1, :], op=ALU.add)
                    e.drain()
                    e.tensor_tensor(out=v[:, :, 2, :], in0=v[:, :, 2, :], in1=v[:, :, 3, :], op=ALU.add)
                    e.drain()
                    e.tensor_tensor(out=v[:, :, 0, :], in0=v[:, :, 0, :], in1=v[:, :, 2, :], op=ALU.add)
                    e.drain()
                    return e.tensor_copy(out=skvb[:].rearrange("p (g f) -> p g f", f=65),
                                         in_=v[:, :, 0, :])
                t_kvs = p.op("vector", kvsum, waits=[t_kv], inc=sv_)
                p.barrier([sv_])
            shc = E2(nc.sbuf_tensor("shc", [128, 8 * TPC], BF16))
            sqp2 = [E2(nc.sbuf_tensor(f"sqp2{j}", [128, 2 * TPC], BF16)) for j in range(2)]
            srec = E2(nc.sbuf_tensor("srec", [1, 512], F32))
            srecb = E2(nc.sbuf_tensor("srecb", [128, 512], BF16))
            swo = E2(nc.sbuf_tensor("swo", [128, 8 * D], BF16))
            sxc = [E2(nc.sbuf_tensor(f"sxc{j}", [128, D], F32)) for j in range(2)]
            for k in range(8):
                p.op("gpsimd", lambda e, k=k: e.dma_start(
                    out=swo[:, k * D:(k + 1) * D], in_=wout[k * 128:(k + 1) * 128, :]),
                    inc=ldg1, amt=16, dma=True)
            t_ldg_all2 = (ldg1, p.cnt["ldg1"])
            hc_toks = []
            pnum_prev = {0: None, 1: None}
            qh_prev = {0: None, 1: None}
            for h in range(NH):
                jh = h % 2
                t_qh = p.op("sync", lambda e, h=h, jh=jh: e.dma_start(
                    out=sqp2[jh][:].rearrange("p (a t) -> p a t", a=2),
                    in_=qpT[h].rearrange("(a p) t -> p a t", p=128)),
                    waits=[qh_prev[jh]], inc=lda[jh], amt=16)
                for icc in range(4):
                    pb = icc % 2
                    t_mm = None
                    for mt in range(2):
                        wt = [t_kvs, t_qh, pnum_prev[pb]] if mt == 0 else []
                        t_mm = p.op("tensor", lambda e, h=h, jh=jh, icc=icc, mt=mt, pb=pb: e.matmul(
                            out=p_a[pb][0:65, :],
                            lhsT=skvb[:, (h * 2 + mt) * 65:(h * 2 + mt + 1) * 65],
                            rhs=sqp2[jh][:, mt * TPC + icc * 512: mt * TPC + (icc + 1) * 512],
                            start=(mt == 0), stop=(mt == 1)),
                            waits=wt, inc=(st_ if mt == 1 else None))
                    def recip_den(e, pb=pb):
                        e.tensor_scalar(out=srec[:], in0=p_a[pb][64:65, :], scalar1=PEPS, scalar2=None,
                                        op0=ALU.add)
                        e.drain()
                        return e.reciprocal(out=srec[:], in_=srec[:])
                    t_rc = p.op("vector", recip_den, waits=[t_mm], inc=sv_)
                    t_rb = p.op("tensor", lambda e: e.matmul(out=p_b[0][:], lhsT=sones[:], rhs=srec[:],
                                                             start=True, stop=True),
                                waits=[t_rc, t_ones], inc=st_)
                    t_re = p.op("vector", lambda e: e.tensor_copy(out=srecb[:], in_=p_b[0][:]),
                                waits=[t_rb], inc=sv_)
                    pnum_prev[pb] = p.op("vector", lambda e, h=h, icc=icc, pb=pb: e.tensor_tensor(
                        out=shc[(h % 2) * 64:(h % 2) * 64 + 64,
                                (h // 2) * TPC + icc * 512:(h // 2) * TPC + (icc + 1) * 512],
                        in0=p_a[pb][0:64, :], in1=srecb[0:64, :], op=ALU.mult),
                        waits=[t_re], inc=sv_)
                    hc_toks.append(pnum_prev[pb])
                qh_prev[jh] = hc_toks[-1]
            hc_all = [hc_toks[-1], hc_toks[-2]]
            x2_done = [None] * NT
            wo_prev = {0: None, 1: None}
            for icc in range(NT):
                jj = icc % 2
                t_xl = p.op("sync", lambda e, icc=icc, jj=jj: e.dma_start(
                    out=sxc[jj][:], in_=xa[icc * 128:(icc + 1) * 128, :]),
                    waits=([x2_done[icc - 2]] if icc >= 2 else []), inc=ldx[jj], amt=16)
                for mh in range(2):
                    pb = mh
                    t_mm = None
                    for kt in range(8):
                        wt = [wo_prev[pb], t_ldg_all2] + hc_all if kt == 0 else []
                        t_mm = p.op("tensor", lambda e, icc=icc, mh=mh, kt=kt, pb=pb: e.matmul(
                            out=p_b[pb][:], lhsT=shc[:, kt * TPC + icc * 128: kt * TPC + (icc + 1) * 128],
                            rhs=swo[:, kt * D + mh * 512: kt * D + (mh + 1) * 512],
                            start=(kt == 0), stop=(kt == 7)),
                            waits=wt, inc=(st_ if kt == 7 else None))
                    p.op("vector", lambda e, icc=icc, mh=mh, pb=pb: e.scalar_tensor_tensor(
                        out=sx2[:, icc * D + mh * 512: icc * D + (mh + 1) * 512], in0=p_b[pb][:],
                        scalar=1.0, in1=sbout[:, mh * 512:(mh + 1) * 512], op0=ALU.mult, op1=ALU.add),
                        waits=[t_mm, t_ld1], inc=sv_)
                    p.op("vector", lambda e, icc=icc, mh=mh, jj=jj: e.tensor_tensor(
                        out=sx2[:, icc * D + mh * 512: icc * D + (mh + 1) * 512],
                        in0=sx2[:, icc * D + mh * 512: icc * D + (mh + 1) * 512],
                        in1=sxc[jj][:, mh * 512:(mh + 1) * 512], op=ALU.add),
                        waits=[t_xl], inc=sv_)
                    wo_prev[pb] = p.op("vector", lambda e, icc=icc, mh=mh: e.tensor_tensor(
                        out=sx2[:, icc * D + mh * 512: icc * D + (mh + 1) * 512],
                        in0=sx2[:, icc * D + mh * 512: icc * D + (mh + 1) * 512],
                        in1=sffo[:, icc * D + mh * 512:icc * D + (mh + 1) * 512],
                        op=ALU.add), inc=sv_)
                x2_done[icc] = wo_prev[1]
            p.barrier([sv_, ss, st_])

        p.op("gpsimd", lambda e: e.dma_start(out=o_d2, in_=sx2[:, 0:D]), inc=sgo[0], amt=16, dma=True)
        p.barrier([sv_, ss, st_])
        c12.close()
        sxn2T = E(nc.sbuf_tensor("sxn2T", [128, 8 * TPC], BF16))
        with ExitStack() as c3:
            E3 = c3.enter_context
            ssq3 = E3(nc.sbuf_tensor("ssq3", [128, D], F32))
            sxn2 = E3(nc.sbuf_tensor("sxn2", [128, D], F32))
            sxn232 = E3(nc.sbuf_tensor("sxn232", [128, 8 * TPC], F32))
            for k in range(0, 8):
                p.op("sync", lambda e, k=k: e.dma_start(out=sgw32[:, k * 8:(k + 1) * 8],
                                                        in_=gwT[k * 128:(k + 1) * 128, :]), inc=ld1, amt=16)
            t_gw32 = (ld1, p.cnt["ld1"])
            trbank_prev = {0: None, 1: None}
            for i in range(NT):
                x2t = sx2[:, i * D:(i + 1) * D]
                p.op("vector", lambda e, x2t=x2t: e.reduce_sum(out=sstat[:, 0:1], in_=x2t, axis=AXL.X),
                     inc=sv_)
                p.op("vector", lambda e: e.tensor_scalar(out=sstat[:, 0:1], in0=sstat[:, 0:1],
                                                         scalar1=1.0 / D, scalar2=None, op0=ALU.mult),
                     inc=sv_)
                t_sq = p.op("scalar", lambda e, x2t=x2t: e.activation(out=ssq3[:], in_=x2t, func=AF.Square,
                                                                     accum_out=sstat[:, 1:2]), inc=ss)
                p.op("vector", lambda e: e.tensor_scalar(out=sstat[:, 2:3], in0=sstat[:, 1:2],
                                                         scalar1=1.0 / D, scalar2=None, op0=ALU.mult),
                     waits=[t_sq], inc=sv_)
                p.op("vector", lambda e: e.tensor_tensor(out=sstat[:, 3:4], in0=sstat[:, 0:1],
                                                         in1=sstat[:, 0:1], op=ALU.mult), inc=sv_)
                t_var = p.op("vector", lambda e: e.tensor_tensor(out=sstat[:, 2:3], in0=sstat[:, 2:3],
                                                                 in1=sstat[:, 3:4], op=ALU.subtract), inc=sv_)
                p.op("scalar", lambda e: e.activation(out=sstat[:, 4:5], in_=sstat[:, 2:3],
                                                      func=AF.Ln, bias=LN_EPS, scale=1.0),
                     waits=[t_var], inc=ss)
                t_rstd = p.op("scalar", lambda e: e.activation(out=sstat[:, 4:5], in_=sstat[:, 4:5],
                                                               func=AF.Exp, scale=-0.5), inc=ss)
                p.op("vector", lambda e: e.tensor_tensor(out=sstat[:, 5:6], in0=sstat[:, 0:1],
                                                         in1=sstat[:, 4:5], op=ALU.mult),
                     waits=[t_rstd], inc=sv_)
                t_bneg = p.op("vector", lambda e: e.tensor_scalar(out=sstat[:, 5:6], in0=sstat[:, 5:6],
                                                                  scalar1=-1.0, scalar2=None, op0=ALU.mult),
                              inc=sv_)
                t_xn = p.op("vector", lambda e, x2t=x2t: e.tensor_scalar(
                    out=sxn2[:], in0=x2t, scalar1=sstat[:, 4:5], scalar2=sstat[:, 5:6],
                    op0=ALU.mult, op1=ALU.add), waits=[t_bneg], inc=sv_)
                ev_tok = {}
                for k in range(8):
                    tb = k % 2
                    wt = [t_xn, t_ld1]
                    if k >= 2:
                        wt.append(ev_tok[k - 2])
                    else:
                        wt.append(trbank_prev[tb])
                    tt = p.op("tensor", lambda e, k=k, tb=tb: e.transpose(
                        out=p_t[tb][:], in_=sxn2[:, k * 128:(k + 1) * 128], identity=sident[:]),
                        waits=wt, inc=st_)
                    ev_tok[k] = p.op("vector", lambda e, i=i, k=k, tb=tb: e.tensor_copy(
                        out=sxn232[:, k * TPC + i * 128: k * TPC + (i + 1) * 128], in_=p_t[tb][:]),
                        waits=[tt], inc=sv_)
                trbank_prev = {0: ev_tok[6], 1: ev_tok[7]}
                p.op("vector", lambda e, x2t=x2t: e.tensor_tensor(
                    out=x2t, in0=x2t, in1=sxn2[:], op=ALU.add), inc=sv_)
            t_cast = None
            for k in range(8):
                t_cast = p.op("vector", lambda e, k=k: e.tensor_copy(
                    out=sxn2T[:, k * TPC:(k + 1) * TPC], in_=sxn232[:, k * TPC:(k + 1) * TPC]),
                    inc=sv_)
            sc_ev = None
            for nch in range(4):
                t_mm = None
                for kt in range(8):
                    wt = [t_gw32, sc_ev, t_cast] if kt == 0 else []
                    t_mm = p.op("tensor", lambda e, nch=nch, kt=kt: e.matmul(
                        out=p_a[0][0:8, :], lhsT=sgw32[:, kt * 8:(kt + 1) * 8],
                        rhs=sxn232[:, kt * TPC + nch * 512:kt * TPC + (nch + 1) * 512],
                        start=(kt == 0), stop=(kt == 7)),
                        waits=wt, inc=(st_ if kt == 7 else None))
                sc_ev = p.op("vector", lambda e, nch=nch: e.tensor_copy(
                    out=ssc_f[:, nch * 512:(nch + 1) * 512], in_=p_a[0][0:8, :]),
                    waits=[t_mm], inc=sv_)
            p.barrier([sv_, ss, st_])

        p.op("gpsimd", lambda e: e.dma_start(out=o_d3, in_=sxn2T[:, 0:TPC]), inc=sgo[0], amt=16, dma=True)
        with ExitStack() as c4:
            E4 = c4.enter_context
            swip = E4(nc.sbuf_tensor("swip", [128, 8 * 2 * D], BF16))
            scs = E4(nc.sbuf_tensor("scs", [128, 2 * D], BF16))
            sipc = [E4(nc.sbuf_tensor(f"sipc{j}", [128, 2 * D], F32)) for j in range(2)]
            srp = [E4(nc.sbuf_tensor(f"srp{j}", [128, 2 * D], BF16)) for j in range(2)]
            srh = [E4(nc.sbuf_tensor(f"srh{j}", [128, 2 * D], BF16)) for j in range(2)]
            snrm = E4(nc.sbuf_tensor("snrm", [128, 8], F32))
            ssq4 = E4(nc.sbuf_tensor("ssq4", [128, D], F32))
            for k in range(8):
                p.op("gpsimd", lambda e, k=k: e.dma_start(
                    out=swip[:, k * 2 * D:(k + 1) * 2 * D], in_=wip[k * 128:(k + 1) * 128, :]),
                    inc=ldg1, amt=16, dma=True)
            t_l4 = (ld1, p.cnt["ld1"])
            t_g4 = (ldg1, p.cnt["ldg1"])
            ip_prev = {0: None, 1: None, 2: None, 3: None}
            buf_prev = {0: None, 1: None}
            st_prev = {0: None, 1: None}
            cs_prev = {0: None, 1: None}
            for i in range(NT):
                jj = i % 2
                t_cs = p.op("sync", lambda e, i=i, jj=jj: e.dma_start(
                    out=scs[:, jj * D: jj * D + 512], in_=cosb[i * 128:(i + 1) * 128, :]),
                    waits=[cs_prev[jj]], inc=lda[jj], amt=16)
                t_sn = p.op("sync", lambda e, i=i, jj=jj: e.dma_start(
                    out=scs[:, jj * D + 512: jj * D + D], in_=sinb[i * 128:(i + 1) * 128, :]),
                    inc=lda[jj], amt=16)
                ev4 = []
                for q4 in range(4):
                    t_mm = None
                    for kt in range(8):
                        wt = [ip_prev[q4], t_g4] if kt == 0 else []
                        t_mm = p.op("tensor", lambda e, i=i, q4=q4, kt=kt: e.matmul(
                            out=p_a[q4][:], lhsT=sxn2T[:, kt * TPC + i * 128: kt * TPC + (i + 1) * 128],
                            rhs=swip[:, kt * 2 * D + q4 * 512: kt * 2 * D + (q4 + 1) * 512],
                            start=(kt == 0), stop=(kt == 7)),
                            waits=wt, inc=(st_ if kt == 7 else None))
                    ev4.append(p.op("vector", lambda e, jj=jj, q4=q4: e.tensor_copy(
                        out=sipc[jj][:, q4 * 512:(q4 + 1) * 512], in_=p_a[q4][:]),
                        waits=[t_mm, buf_prev[jj]], inc=sv_))
                    ip_prev[q4] = ev4[-1]
                def rope2(e, jj=jj, i=i):
                    cc = scs[:, jj * D: jj * D + 512]
                    sn = scs[:, jj * D + 512: jj * D + D]
                    tmp = srh[jj][:, 0:512]
                    for base in (0, D):
                        x1 = sipc[jj][:, base: base + 512]
                        x2_ = sipc[jj][:, base + 512: base + D]
                        o1 = srp[jj][:, base: base + 512]
                        o2 = srp[jj][:, base + 512: base + D]
                        e.tensor_tensor(out=o1, in0=x1, in1=cc, op=ALU.mult)
                        e.drain()
                        e.tensor_tensor(out=tmp, in0=x2_, in1=sn, op=ALU.mult)
                        e.drain()
                        e.tensor_tensor(out=o1, in0=o1, in1=tmp, op=ALU.subtract)
                        e.drain()
                        e.tensor_tensor(out=o2, in0=x2_, in1=cc, op=ALU.mult)
                        e.drain()
                        e.tensor_tensor(out=tmp, in0=x1, in1=sn, op=ALU.mult)
                        e.drain()
                        e.tensor_tensor(out=o2, in0=o2, in1=tmp, op=ALU.add)
                        e.drain()
                    return e.engine_nop()
                t_rope = p.op("vector", rope2, waits=[ev4[3], t_sn, st_prev[jj]], inc=sv_)
                cs_prev[jj] = t_rope
                def norms(e, jj=jj):
                    e.activation(out=ssq4[:], in_=srp[jj][:, 0:D], func=AF.Square,
                                 accum_out=snrm[:, 0:1])
                    e.drain()
                    e.activation(out=ssq4[:], in_=srp[jj][:, D:2 * D], func=AF.Square,
                                 accum_out=snrm[:, 1:2])
                    return e.drain()
                t_sq4 = p.op("scalar", norms, waits=[t_rope], inc=ss, dma=True)
                t_gq = p.op("vector", lambda e: e.tensor_scalar(
                    out=snrm[:, 0:2], in0=snrm[:, 0:2], scalar1=1e-24, scalar2=None, op0=ALU.max),
                    waits=[t_sq4], inc=sv_)

                def norms2(e):
                    e.activation(out=snrm[:, 2:3], in_=snrm[:, 0:1], func=AF.Ln, bias=0.0, scale=1.0)
                    e.drain()
                    e.activation(out=snrm[:, 2:3], in_=snrm[:, 2:3], func=AF.Exp, scale=-0.5)
                    e.drain()
                    e.activation(out=snrm[:, 3:4], in_=snrm[:, 1:2], func=AF.Ln, bias=0.0, scale=1.0)
                    e.drain()
                    return e.activation(out=snrm[:, 3:4], in_=snrm[:, 3:4], func=AF.Exp, scale=-0.5)
                t_nrm = p.op("scalar", norms2, waits=[t_gq], inc=ss)
                t_nh = p.op("vector", lambda e, jj=jj: e.tensor_scalar(
                    out=srh[jj][:, 0:D], in0=srp[jj][:, 0:D], scalar1=snrm[:, 2:3], scalar2=None,
                    op0=ALU.mult), waits=[t_nrm], inc=sv_)
                t_ph = p.op("vector", lambda e, jj=jj: e.tensor_scalar(
                    out=srh[jj][:, D:2 * D], in0=srp[jj][:, D:2 * D], scalar1=snrm[:, 3:4], scalar2=None,
                    op0=ALU.mult), inc=sv_)
                st1 = p.op("gpsimd", lambda e, i=i, jj=jj: e.dma_start(
                    out=o_I[i * 128:(i + 1) * 128, :], in_=srp[jj][:, 0:D]),
                    waits=[t_rope], inc=sgo[jj], amt=16, dma=True)
                st2 = p.op("gpsimd", lambda e, i=i, jj=jj: e.dma_start(
                    out=o_P[i * 128:(i + 1) * 128, :], in_=srp[jj][:, D:2 * D]),
                    waits=[t_rope], inc=sgo[jj], amt=16, dma=True)
                st3 = p.op("gpsimd", lambda e, i=i, jj=jj: e.dma_start(
                    out=o_Ih[i * 128:(i + 1) * 128, :], in_=srh[jj][:, 0:D]),
                    waits=[t_nh], inc=sgo[jj], amt=16, dma=True)
                st4 = p.op("gpsimd", lambda e, i=i, jj=jj: e.dma_start(
                    out=o_Ph[i * 128:(i + 1) * 128, :], in_=srh[jj][:, D:2 * D]),
                    waits=[t_ph], inc=sgo[jj], amt=16, dma=True)
                buf_prev[jj] = t_rope
                st_prev[jj] = st4
            p.barrier([sv_, ss, st_])

        with ExitStack() as c5:
            E5 = c5.enter_context
            scot = E5(nc.sbuf_tensor("scot", [128, 8], F32))
            smk = E5(nc.sbuf_tensor("smk", [128, NT * 8], F32))
            siota = E5(nc.sbuf_tensor("siota", [128, 8], F32))
            swe = [E5(nc.sbuf_tensor(f"swe{j}", [128, 8 * D], BF16)) for j in range(2)]
            t_io = p.op("gpsimd", lambda e: e.dma_start(out=siota[:], in_=iota8.partition_broadcast(128)),
                        inc=ldg1, amt=16, dma=True)
            sc_ev = None
            mk_prev = None
            tr_prev = None
            for i in range(NT):
                tt = p.op("tensor", lambda e, i=i: e.transpose(
                    out=p_t[0][:, 0:8], in_=ssc_f[:, i * 128:(i + 1) * 128],
                    identity=sident[0:8, 0:8]), waits=[sc_ev, tr_prev], inc=st_)
                t_sc2 = p.op("vector", lambda e: e.tensor_copy(out=scot[:], in_=p_t[0][:, 0:8]),
                             waits=[tt], inc=sv_)
                tr_prev = t_sc2

                def mkmask(e, i=i):
                    e.tensor_reduce(out=sstat[:, 6:7], in_=scot[:], axis=AXL.X, op=ALU.max)
                    e.drain()
                    e.tensor_scalar(out=smk[:, i * 8:(i + 1) * 8], in0=scot[:], scalar1=sstat[:, 6:7],
                                    scalar2=None, op0=ALU.is_ge)
                    e.drain()
                    e.tensor_scalar(out=smk[:, i * 8:(i + 1) * 8], in0=smk[:, i * 8:(i + 1) * 8],
                                    scalar1=-1000.0, scalar2=1000.0, op0=ALU.mult, op1=ALU.add)
                    e.drain()
                    e.tensor_tensor(out=smk[:, i * 8:(i + 1) * 8], in0=smk[:, i * 8:(i + 1) * 8],
                                    in1=siota[:], op=ALU.add)
                    e.drain()
                    e.tensor_reduce(out=sstat[:, 7:8], in_=smk[:, i * 8:(i + 1) * 8],
                                    axis=AXL.X, op=ALU.min)
                    e.drain()
                    return e.tensor_scalar(out=smk[:, i * 8:(i + 1) * 8], in0=siota[:],
                                           scalar1=sstat[:, 7:8], scalar2=None, op0=ALU.is_equal)
                mk_prev = p.op("vector", mkmask, waits=[t_sc2, t_io], inc=sv_)
            we_tok = [None, None]
            last_mm_of_expert = {}
            moe_ev = {0: None, 1: None}
            for ee in range(8):
                jw = ee % 2
                we_tok[jw] = p.op("gpsimd", lambda e, ee=ee: e.dma_start(
                    out=swe[ee % 2][:].rearrange("p (k d) -> p k d", k=8),
                    in_=wexp[ee].rearrange("(k p) d -> p k d", p=128)),
                    waits=([last_mm_of_expert.get(ee - 2)]), inc=ldw[jw], amt=16, dma=True)
                for i in range(NT):
                    for mh in range(2):
                        pb = mh
                        t_mm = None
                        for kt in range(8):
                            wt = [we_tok[jw], mk_prev, moe_ev[pb]] if kt == 0 else []
                            t_mm = p.op("tensor", lambda e, ee=ee, i=i, mh=mh, kt=kt, pb=pb, jw=jw: e.matmul(
                                out=p_b[pb][:],
                                lhsT=sxn2T[:, kt * TPC + i * 128: kt * TPC + (i + 1) * 128],
                                rhs=swe[jw][:, kt * D + mh * 512: kt * D + (mh + 1) * 512],
                                start=(kt == 0), stop=(kt == 7)),
                                waits=wt, inc=(st_ if kt == 7 else None))
                        last_mm_of_expert[ee] = t_mm
                        moe_ev[pb] = p.op("vector", lambda e, ee=ee, i=i, mh=mh, pb=pb: e.scalar_tensor_tensor(
                            out=sx2[:, i * D + mh * 512: i * D + (mh + 1) * 512], in0=p_b[pb][:],
                            scalar=smk[:, i * 8 + ee: i * 8 + ee + 1],
                            in1=sx2[:, i * D + mh * 512: i * D + (mh + 1) * 512],
                            op0=ALU.mult, op1=ALU.add),
                            waits=[t_mm], inc=sv_)
            st5 = None
            for i in range(NT):
                st5 = p.op("gpsimd", lambda e, i=i: e.dma_start(
                    out=o_x3p[i * 128:(i + 1) * 128, :], in_=sx2[:, i * D:(i + 1) * D]),
                    waits=[moe_ev[0], moe_ev[1]], inc=sgo[0], amt=16, dma=True)
            p.barrier([sv_, ss, st_])
        p.fence("gpsimd", [(sgo[0], p.cnt["sgo0"]), (sgo[1], p.cnt["sgo1"])])
        p.emit(block)
    return nc



def build_C():
    nc = bass.Bass("TRN2", target_bir_lowering=False)
    PT = nc.dram_tensor("PT", [D, T], BF16, kind="ExternalInput").ap()
    Phat = nc.dram_tensor("Phat", [T, D], BF16, kind="ExternalInput").ap()
    If_ = nc.dram_tensor("If_", [D, TPC], BF16, kind="ExternalInput").ap()
    Ihf = nc.dram_tensor("Ihf", [D, TPC], BF16, kind="ExternalInput").ap()
    x3pT = nc.dram_tensor("x3pT", [D, TPC], F32, kind="ExternalInput").ap()
    w1p = nc.dram_tensor("w1p", [32, 64], F32, kind="ExternalInput").ap()
    b1 = nc.dram_tensor("b1", [64, 1], F32, kind="ExternalInput").ap()
    w2 = nc.dram_tensor("w2", [64, 64], F32, kind="ExternalInput").ap()
    b2 = nc.dram_tensor("b2", [64, 1], F32, kind="ExternalInput").ap()
    wc = nc.dram_tensor("wc", [64, 2], F32, kind="ExternalInput").ap()
    bc = nc.dram_tensor("bc", [2, 1], F32, kind="ExternalInput").ap()
    wm = nc.dram_tensor("wm", [64, 1], F32, kind="ExternalInput").ap()
    bm = nc.dram_tensor("bm", [1, 1], F32, kind="ExternalInput").ap()
    pal = nc.dram_tensor("pal", [D, 256], F32, kind="ExternalInput").ap()
    wo = nc.dram_tensor("wo", [D, D], F32, kind="ExternalInput").ap()
    ident = nc.dram_tensor("ident", [128, 128], F32, kind="ExternalInput").ap()
    cmask = nc.dram_tensor("cmask", [128, 512], F32, kind="ExternalInput").ap()
    rowg = nc.dram_tensor("rowg", [128, 16], F32, kind="ExternalInput").ap()
    iotas = nc.dram_tensor("iotas", [128, 512], I32, kind="ExternalInput").ap()
    iot128 = nc.dram_tensor("iot128", [128, 1], F32, kind="ExternalInput").ap()
    ioy = nc.dram_tensor("ioy", [128, 256], BF16, kind="ExternalInput").ap()
    kmG = nc.dram_tensor("kmG", [32, 256], BF16, kind="ExternalInput").ap()
    kmM = nc.dram_tensor("kmM", [128, 16], F32, kind="ExternalInput").ap()
    o_yf = nc.dram_tensor("o_yf", [D, TPC], F32, kind="ExternalOutput").ap()
    idx_scr = nc.dram_tensor("idx_scr", [128, 16], I32, kind="Internal").ap()
    row_scr = nc.dram_tensor("row_scr", [1, TPC], F32, kind="Internal").ap()
    tok_scr = nc.dram_tensor("tok_scr", [128, 16], F32, kind="Internal").ap()

    with ExitStack() as ctx:
        E = ctx.enter_context
        p = Prog(nc)
        ld1 = p.mksem(ctx, "ld1")
        ldp = [p.mksem(ctx, "ldp0"), p.mksem(ctx, "ldp1")]
        ldb = [p.mksem(ctx, "ldb0"), p.mksem(ctx, "ldb1")]
        ldgx = p.mksem(ctx, "ldgx")
        ldsc = p.mksem(ctx, "ldsc")
        sv_ = p.mksem(ctx, "sv_")
        ss = p.mksem(ctx, "ss")
        st_ = p.mksem(ctx, "st_")
        sgo = p.mksem(ctx, "sgo")
        block = E(nc.Block())
        p.setup_consts(ctx, [0.0, 1.0, -1.0, -0.5, 7.5, 15.0, NEG, PEPS, 1.0 / 8192.0, 1e-24,
                             1.0 / 16.0], sgo)

        sident = E(nc.sbuf_tensor("sident", [128, 128], F32))
        sident_b = E(nc.sbuf_tensor("sident_b", [128, 128], BF16))
        scm = E(nc.sbuf_tensor("scm", [128, 512], F32))
        srowg = E(nc.sbuf_tensor("srowg", [128, 16], F32))
        siotas = E(nc.sbuf_tensor("siotas", [128, 512], I32))
        siot128 = E(nc.sbuf_tensor("siot128", [128, 1], F32))
        sioy = E(nc.sbuf_tensor("sioy", [128, 256], BF16))
        skmG = E(nc.sbuf_tensor("skmG", [32, 256], BF16))
        skmM = E(nc.sbuf_tensor("skmM", [128, 16], F32))
        sw1 = E(nc.sbuf_tensor("sw1", [32, 64], BF16))
        sw2 = E(nc.sbuf_tensor("sw2", [64, 64], BF16))
        swc = E(nc.sbuf_tensor("swc", [64, 2], BF16))
        swm = E(nc.sbuf_tensor("swm", [64, 1], BF16))
        sb1 = E(nc.sbuf_tensor("sb1", [64, 1], F32))
        sb2 = E(nc.sbuf_tensor("sb2", [64, 1], F32))
        sbc = E(nc.sbuf_tensor("sbc", [2, 1], F32))
        sbm = E(nc.sbuf_tensor("sbm", [1, 1], F32))
        sPW = E(nc.sbuf_tensor("sPW", [128, 2 * D], BF16))
        with ExitStack() as c0:
            E0 = c0.enter_context
            spal = E0(nc.sbuf_tensor("spal", [128, 8 * 256], BF16))
            swo0 = E0(nc.sbuf_tensor("swo0", [128, 8 * D], BF16))
            for k in range(8):
                p.op("gpsimd", lambda e, k=k: e.dma_start(
                    out=spal[:, k * 256:(k + 1) * 256], in_=pal[k * 128:(k + 1) * 128, :]),
                    inc=ldgx, amt=16, dma=True)
                p.op("gpsimd", lambda e, k=k: e.dma_start(
                    out=swo0[:, k * D:(k + 1) * D], in_=wo[k * 128:(k + 1) * 128, :]),
                    inc=ldgx, amt=16, dma=True)
            t_pw0 = (ldgx, p.cnt["ldgx"])
            pw_prev = None
            for cc in range(2):
                for nn in range(2):
                    t_mm = None
                    for kt in range(8):
                        wt = [t_pw0, pw_prev] if kt == 0 else []
                        t_mm = p.op("tensor", lambda e, cc=cc, nn=nn, kt=kt: e.matmul(
                            out=p_m[:], lhsT=spal[:, kt * 256 + cc * 128: kt * 256 + (cc + 1) * 128],
                            rhs=swo0[:, kt * D + nn * 512: kt * D + (nn + 1) * 512],
                            start=(kt == 0), stop=(kt == 7)),
                            waits=wt, inc=(st_ if kt == 7 else None))
                    pw_prev = p.op("vector", lambda e, cc=cc, nn=nn: e.tensor_copy(
                        out=sPW[:, cc * D + nn * 512: cc * D + (nn + 1) * 512], in_=p_m[:]),
                        waits=[t_mm], inc=sv_)
            p.barrier([sv_, ss, st_])

        sVp = E(nc.sbuf_tensor("sVp", [128, T], F32))
        sZ = E(nc.sbuf_tensor("sZ", [128, 8 * 4096], BF16))
        sptb = [E(nc.sbuf_tensor(f"sptb{j}", [128, 8 * 512], BF16)) for j in range(2)]
        sifb = [E(nc.sbuf_tensor(f"sifb{j}", [128, 8 * 128], BF16)) for j in range(2)]
        sihb = [E(nc.sbuf_tensor(f"sihb{j}", [128, 8 * 128], BF16)) for j in range(2)]
        sx3b = [E(nc.sbuf_tensor(f"sx3b{j}", [128, 8 * 128], F32)) for j in range(2)]
        sm16 = E(nc.sbuf_tensor("sm16", [128, 16], F32))
        sidx = E(nc.sbuf_tensor("sidx", [128, 16], I32))
        sidxf = E(nc.sbuf_tensor("sidxf", [128, 16], F32))
        sidxg = E(nc.sbuf_tensor("sidxg", [128, 16], I32))
        sgt = [E(nc.sbuf_tensor(f"sgt{j}", [128, D], BF16)) for j in range(4)]
        sG = E(nc.sbuf_tensor("sG", [128, 4096], BF16))
        srel = E(nc.sbuf_tensor("srel", [32, 2048], BF16))
        sh1 = E(nc.sbuf_tensor("sh1", [64, 2048], BF16))
        sh2 = E(nc.sbuf_tensor("sh2", [64, 2048], BF16))
        smixr = E(nc.sbuf_tensor("smixr", [1, 2048], F32))
        szrow = E(nc.sbuf_tensor("szrow", [2, 2048], F32))
        smixt = E(nc.sbuf_tensor("smixt", [128, 16], F32))
        sstat = E(nc.sbuf_tensor("sstat", [128, 8], F32))
        sw_ = E(nc.sbuf_tensor("sw_", [128, 16], F32))
        sfx = E(nc.sbuf_tensor("sfx", [128, 16], F32))
        sfy = E(nc.sbuf_tensor("sfy", [128, 16], F32))
        sfxi = E(nc.sbuf_tensor("sfxi", [128, 16], I32))
        shx = E(nc.sbuf_tensor("shx", [128, 256], BF16))
        shy = E(nc.sbuf_tensor("shy", [128, 256], BF16))
        sYe = E(nc.sbuf_tensor("sYe", [128, 4096], BF16))
        stmp = E(nc.sbuf_tensor("stmp", [128, 256], BF16))
        sXe = E(nc.sbuf_tensor("sXe", [128, 4096], BF16))
        sA = E(nc.sbuf_tensor("sA", [128, 256], F32))
        sAb = E(nc.sbuf_tensor("sAb", [128, 256], BF16))
        sAf = E(nc.sbuf_tensor("sAf", [128, 256], BF16))
        souf = E(nc.sbuf_tensor("souf", [128, 8 * 128], F32))

        p_l = [E(nc.psum_tensor(f"p_l{j}", [128, 512], F32)) for j in range(2)]
        p_g = [E(nc.psum_tensor(f"p_g{j}", [128, 128], F32)) for j in range(2)]
        p_t2 = [E(nc.psum_tensor(f"p_t2{j}", [128, 128], BF16)) for j in range(2)]
        p_m = E(nc.psum_tensor("p_m", [128, 512], F32))

        for dst, srcap in ((sident, ident), (scm, cmask), (siotas, iotas), (siot128, iot128),
                           (srowg, rowg),
                           (skmM, kmM), (sb1, b1), (sb2, b2), (sbc, bc), (sbm, bm),
                           (sioy, ioy), (skmG, kmG)):
            p.op("sync", lambda e, dst=dst, srcap=srcap: e.dma_start(out=dst[:], in_=srcap),
                 inc=ld1, amt=16)
        for dst, srcap in ((sw1, w1p), (sw2, w2), (swc, wc), (swm, wm)):
            p.op("gpsimd", lambda e, dst=dst, srcap=srcap: e.dma_start(out=dst[:], in_=srcap),
                 inc=ld1, amt=16, dma=True)
        t_ld1 = (ld1, p.cnt["ld1"])
        t_identb = p.op("vector", lambda e: e.tensor_copy(out=sident_b[:], in_=sident[:]),
                        waits=[t_ld1], inc=sv_)
        t_zz = p.op("vector", lambda e: e.memset(sZ[:], 0.0), inc=sv_)
        ptb_prev = {0: None, 1: None}
        blk_prev = {0: None, 1: None}
        gx_prev = None
        sc_prev = None
        out_prev = None
        prev_mlp_guard = []
        for rb in range(16):
            nch = rb + 1
            jb = rb % 2
            t_if = p.op("sync", lambda e, rb=rb, jb=jb: e.dma_start(
                out=sifb[jb][:].rearrange("p (k c) -> p k c", k=8),
                in_=If_.rearrange("(k p) t -> p k t", p=128)[:, :, rb * 128:(rb + 1) * 128]),
                waits=[blk_prev[jb]], inc=ldb[jb], amt=16)
            t_ih = p.op("sync", lambda e, rb=rb, jb=jb: e.dma_start(
                out=sihb[jb][:].rearrange("p (k c) -> p k c", k=8),
                in_=Ihf.rearrange("(k p) t -> p k t", p=128)[:, :, rb * 128:(rb + 1) * 128]),
                inc=ldb[jb], amt=16)
            t_x3 = p.op("sync", lambda e, rb=rb, jb=jb: e.dma_start(
                out=sx3b[jb][:].rearrange("p (k c) -> p k c", k=8),
                in_=x3pT.rearrange("(k p) t -> p k t", p=128)[:, :, rb * 128:(rb + 1) * 128]),
                inc=ldb[jb], amt=16)
            ev_prev = None
            for ch in range(nch):
                jp = ch % 2
                t_pt = p.op("sync", lambda e, ch=ch, jp=jp: e.dma_start(
                    out=sptb[jp][:].rearrange("p (k s) -> p k s", k=8),
                    in_=PT.rearrange("(k p) t -> p k t", p=128)[:, :, ch * 512:(ch + 1) * 512]),
                    waits=[ptb_prev[jp]], inc=ldp[jp], amt=16)
                pb = ch % 2
                t_mm = None
                for kt in range(8):
                    wt = [t_pt, t_if] if kt == 0 else []
                    if kt == 0 and ch < 2:
                        wt.extend(prev_mlp_guard)
                    if kt == 0 and ev_prev is not None:
                        wt.append(ev_prev if ch < 2 else ev_toks[ch - 2])
                    t_mm = p.op("tensor", lambda e, ch=ch, kt=kt, pb=pb, rb=rb, jb=jb: e.matmul(
                        out=p_l[pb][:], lhsT=sifb[jb][:, kt * 128:(kt + 1) * 128],
                        rhs=sptb[ch % 2][:, kt * 512:(kt + 1) * 512],
                        start=(kt == 0), stop=(kt == 7)),
                        waits=wt, inc=(st_ if kt == 7 else None))
                if ch == nch - 1:
                    t_mk = p.op("vector", lambda e, pb=pb: e.tensor_tensor(
                        out=p_l[pb][:], in0=p_l[pb][:], in1=scm[:], op=ALU.add),
                        waits=[t_mm, t_ld1], inc=sv_)
                else:
                    t_mk = t_mm
                def tagev(e, ch=ch, pb=pb):
                    e.tensor_scalar(
                        out=sVp[:, ch * 512:(ch + 1) * 512].bitcast(I32),
                        in0=p_l[pb][:].bitcast(I32), scalar1=-8192, scalar2=ch * 512,
                        op0=ALU.bitwise_and, op1=ALU.bitwise_or)
                    e.drain()
                    return e.tensor_tensor(
                        out=sVp[:, ch * 512:(ch + 1) * 512].bitcast(I32),
                        in0=sVp[:, ch * 512:(ch + 1) * 512].bitcast(I32),
                        in1=siotas[:], op=ALU.bitwise_or)
                tg = p.op("vector", tagev, waits=[t_mk], inc=sv_)
                if ch == 0:
                    ev_toks = {}
                ev_toks[ch] = tg
                ev_prev = tg
                ptb_prev[ch % 2] = tg
            def top16(e, nch=nch):
                e.max(out=sm16[:, 0:8], in_=sVp[:, 0:nch * 512])
                e.drain()
                e.match_replace(out=sVp[:, 0:nch * 512], in_to_replace=sm16[:, 0:8],
                                in_values=sVp[:, 0:nch * 512], imm_value=NEG)
                e.drain()
                e.max(out=sm16[:, 8:16], in_=sVp[:, 0:nch * 512])
                e.drain()
                e.tensor_scalar(out=sidx[:], in0=sm16[:].bitcast(I32), scalar1=8191, scalar2=None,
                                op0=ALU.bitwise_and)
                e.drain()
                return e.tensor_copy(out=sidxf[:], in_=sidx[:])
            t_tk = p.op("vector", top16, waits=[ev_prev, sc_prev, gx_prev], inc=sv_)
            if rb == 0:
                gt_prev = {0: None, 1: None, 2: None, 3: None}
            tb_prev = {0: None, 1: None}
            zev = []
            for g in range(15):
                jg = g % 4
                t_ga = p.op("gpsimd", lambda e, g=g, jg=jg: e.indirect_dma_start(
                    out=sgt[jg][:], out_offset=None, in_=Phat,
                    in_offset=bass.IndirectOffsetOnAxis(sidx[:, g:g + 1], 0)),
                    waits=[t_tk, gt_prev[jg]], inc=ldgx, amt=16, dma=True)
                for kt in range(8):
                    tb = kt % 2
                    t_tt = p.op("tensor", lambda e, jg=jg, kt=kt, tb=tb: e.transpose(
                        out=p_t2[tb][:], in_=sgt[jg][:, kt * 128:(kt + 1) * 128],
                        identity=sident_b[:]),
                        waits=[t_ga, tb_prev[tb], t_identb], inc=st_)
                    ev = p.op("vector", lambda e, g=g, kt=kt, tb=tb: e.tensor_copy(
                        out=sZ[:, kt * 4096 + g: (kt + 1) * 4096: 32].rearrange("p f -> p f"),
                        in_=p_t2[tb][:]),
                        waits=[t_tt, t_zz], inc=sv_)
                    tb_prev[tb] = ev
                    if kt == 7:
                        gt_prev[jg] = ev
                        zev.append(ev)
            ih_ev = []
            for kt in range(8):
                ih_ev.append(p.op("vector", lambda e, kt=kt, jb=jb: e.tensor_copy(
                    out=sZ[:, kt * 4096 + 15: (kt + 1) * 4096: 32].rearrange("p f -> p f"),
                    in_=sihb[jb][:, kt * 128:(kt + 1) * 128]),
                    waits=[t_ih, zev[-1]], inc=sv_))
            gr_ev = []
            gr_prev = {0: None, 1: None}
            for g in range(32):
                pgb = g % 2
                t_mm = None
                for kt in range(8):
                    wt = [ih_ev[-1], gr_prev[pgb]] if kt == 0 else []
                    t_mm = p.op("tensor", lambda e, g=g, kt=kt, pgb=pgb: e.matmul(
                        out=p_g[pgb][:], lhsT=sZ[:, kt * 4096 + g * 128: kt * 4096 + (g + 1) * 128],
                        rhs=sZ[:, kt * 4096 + g * 128: kt * 4096 + (g + 1) * 128],
                        start=(kt == 0), stop=(kt == 7)),
                        waits=wt, inc=(st_ if kt == 7 else None))
                gr_prev[pgb] = p.op("vector", lambda e, g=g, pgb=pgb: e.tensor_scalar(
                    out=sG[:, g * 128:(g + 1) * 128], in0=p_g[pgb][:], scalar1=1.0, scalar2=-1.0,
                    op0=ALU.min, op1=ALU.max), waits=[t_mm], inc=sv_)
                gr_ev.append(gr_prev[pgb])
            def relg(e):
                e.memset(srel[:], 0.0)
                e.drain()
                for tsub in range(4):
                    e.tensor_copy(
                        out=srel[0:16, :].rearrange("p (g k) -> p g k", g=32)[:, :, 0:16].rearrange(
                            "p g k -> p (g k)").rearrange("p (g k) -> p g k", g=32)[:, :, 0:16]
                        if False else
                        srel[0:16, tsub * 16 + 0: 2048: 64].rearrange("p f -> p f"),
                        in_=sG[tsub * 32: tsub * 32 + 16, tsub * 32: 4096: 128])
                    e.drain()
                return e.drain()
            def relg2(e):
                e.memset(srel[:], 0.0)
                e.drain()
                for tsub in range(4):
                    src = sG[tsub * 32: tsub * 32 + 16, :].rearrange("p (g c) -> p g c", g=32)[
                        :, :, tsub * 32: tsub * 32 + 16]
                    dst = srel[0:16, :].rearrange("p (g t k) -> p g t k", g=32, t=4)[:, :, tsub, :]
                    e.tensor_copy(out=dst, in_=src)
                return e.drain()
            t_rel = p.op("vector", relg2, waits=[gr_ev[-1]], inc=sv_, dma=True)
            def delta(e, rb=rb):
                e.tensor_scalar(out=sm16[:], in0=sidxf[:], scalar1=srowg[:, rb:rb + 1], scalar2=None,
                                op0=ALU.subtract)
                e.drain()
                e.tensor_scalar(out=sm16[:], in0=sm16[:], scalar1=-1.0 / 8192.0,
                                scalar2=0.0, op0=ALU.mult, op1=ALU.add)
                e.drain()
                return e.tensor_scalar(out=sm16[:], in0=sm16[:], scalar1=0.0, scalar2=None,
                                       op0=ALU.max)
            t_dl = p.op("vector", delta, waits=[t_tk], inc=sv_)
            t_dw = p.op("gpsimd", lambda e: e.dma_start(out=tok_scr, in_=sm16[:]),
                        waits=[t_dl], inc=ldsc, amt=16, dma=True)
            t_dr = p.op("gpsimd", lambda e: e.dma_start(
                out=srel[16:17, :], in_=tok_scr.rearrange("p k -> (p k)")[None, :]),
                waits=[t_dw, t_rel], inc=ldsc, amt=16, dma=True)
            if rb == 0:
                t_rel2 = p.op("vector", lambda e: e.tensor_tensor(
                    out=srel[:, 0:224].bitcast(BF16), in0=srel[:, 0:224].bitcast(BF16),
                    in1=skmG[:, 0:224], op=ALU.mult), waits=[(ldsc, p.cnt["ldsc"]), t_ld1], inc=sv_)
            else:
                t_rel2 = (ldsc, p.cnt["ldsc"])
            mlp_banks = [p_m, p_l[0]]
            mlp_ev2 = {0: None, 1: None}
            mseq = 0
            mlp_ev = []
            for nn in range(4):
                bk = mseq % 2
                t_m1 = p.op("tensor", lambda e, nn=nn, bk=bk: e.matmul(
                    out=mlp_banks[bk][0:64, :], lhsT=sw1[:], rhs=srel[:, nn * 512:(nn + 1) * 512],
                    start=True, stop=True),
                    waits=[t_rel2, t_ld1, mlp_ev2[bk]], inc=st_)
                mlp_ev2[bk] = p.op("scalar", lambda e, nn=nn, bk=bk: e.activation(
                    out=sh1[:, nn * 512:(nn + 1) * 512], in_=mlp_banks[bk][0:64, :], func=AF.Gelu,
                    bias=sb1[:], scale=1.0), waits=[t_m1], inc=ss)
                mlp_ev.append(mlp_ev2[bk])
                mseq += 1
            for nn in range(4):
                bk = mseq % 2
                t_m2 = p.op("tensor", lambda e, nn=nn, bk=bk: e.matmul(
                    out=mlp_banks[bk][0:64, :], lhsT=sw2[:], rhs=sh1[:, nn * 512:(nn + 1) * 512],
                    start=True, stop=True), waits=[mlp_ev[nn], mlp_ev2[bk]], inc=st_)
                mlp_ev2[bk] = p.op("scalar", lambda e, nn=nn, bk=bk: e.activation(
                    out=sh2[:, nn * 512:(nn + 1) * 512], in_=mlp_banks[bk][0:64, :], func=AF.Gelu,
                    bias=sb2[:], scale=1.0), waits=[t_m2], inc=ss)
                mlp_ev.append(mlp_ev2[bk])
                mseq += 1
            zm_ev = []
            for nn in range(4):
                bk = mseq % 2
                t_m3 = p.op("tensor", lambda e, nn=nn, bk=bk: e.matmul(
                    out=mlp_banks[bk][0:2, :], lhsT=swc[:], rhs=sh2[:, nn * 512:(nn + 1) * 512],
                    start=True, stop=True), waits=[mlp_ev[4 + nn], mlp_ev2[bk]], inc=st_)
                mlp_ev2[bk] = p.op("scalar", lambda e, nn=nn, bk=bk: e.activation(
                    out=szrow[:, nn * 512:(nn + 1) * 512], in_=mlp_banks[bk][0:2, :], func=AF.Tanh,
                    bias=sbc[:], scale=1.0), waits=[t_m3], inc=ss)
                zm_ev.append(mlp_ev2[bk])
                mseq += 1
            for nn in range(4):
                bk = mseq % 2
                t_m4 = p.op("tensor", lambda e, nn=nn, bk=bk: e.matmul(
                    out=mlp_banks[bk][0:1, :], lhsT=swm[:], rhs=sh2[:, nn * 512:(nn + 1) * 512],
                    start=True, stop=True), waits=[mlp_ev2[bk]], inc=st_)
                mlp_ev2[bk] = p.op("vector", lambda e, nn=nn, bk=bk: e.tensor_scalar(
                    out=smixr[:, nn * 512:(nn + 1) * 512], in0=mlp_banks[bk][0:1, :], scalar1=sbm[:],
                    scalar2=None, op0=ALU.add), waits=[t_m4], inc=sv_)
                zm_ev.append(mlp_ev2[bk])
                mseq += 1
            mlp_guard = [mlp_ev2[0], mlp_ev2[1]]
            t_mw = p.op("gpsimd", lambda e: e.dma_start(out=row_scr, in_=smixr[:]),
                        waits=[zm_ev[-1]], inc=ldsc, amt=16, dma=True)
            t_mr = p.op("gpsimd", lambda e: e.dma_start(
                out=smixt[:], in_=row_scr.rearrange("o (p k) -> (o p) k", p=128)),
                waits=[t_mw, sc_prev], inc=ldsc, amt=16, dma=True)

            def smax(e, rb=rb):
                e.memset(smixt[:, 15:16], NEG)
                e.drain()
                if rb == 0:
                    e.tensor_tensor(out=smixt[:], in0=smixt[:], in1=skmM[:], op=ALU.add)
                    e.drain()
                e.tensor_reduce(out=sstat[:, 0:1], in_=smixt[:], axis=AXL.X, op=ALU.max, negate=True)
                return e.drain()
            t_sm = p.op("vector", smax, waits=[t_mr, t_ld1], inc=sv_, dma=True)
            t_ex = p.op("scalar", lambda e: e.activation(out=sw_[:], in_=smixt[:], func=AF.Exp,
                                                         bias=sstat[:, 0:1], scale=1.0),
                        waits=[t_sm], inc=ss)

            def wnorm(e):
                e.tensor_reduce(out=sstat[:, 1:2], in_=sw_[:], axis=AXL.X, op=ALU.add)
                e.drain()
                e.reciprocal(out=sstat[:, 1:2], in_=sstat[:, 1:2])
                e.drain()
                return e.tensor_scalar(out=sw_[:], in0=sw_[:], scalar1=sstat[:, 1:2], scalar2=None,
                                       op0=ALU.mult)
            t_wn = p.op("vector", wnorm, waits=[t_ex], inc=sv_)
            t_zw = p.op("gpsimd", lambda e: e.dma_start(out=row_scr, in_=szrow[0:1, :]),
                        waits=[zm_ev[-1], t_mr], inc=ldsc, amt=16, dma=True)
            t_zr = p.op("gpsimd", lambda e: e.dma_start(
                out=sfx[:], in_=row_scr.rearrange("o (p k) -> (o p) k", p=128)),
                waits=[t_zw], inc=ldsc, amt=16, dma=True)
            t_zw2 = p.op("gpsimd", lambda e: e.dma_start(out=row_scr, in_=szrow[1:2, :]),
                         waits=[t_zr], inc=ldsc, amt=16, dma=True)
            t_zr2 = p.op("gpsimd", lambda e: e.dma_start(
                out=sfy[:], in_=row_scr.rearrange("o (p k) -> (o p) k", p=128)),
                waits=[t_zw2], inc=ldsc, amt=16, dma=True)

            def mkf(e, buf):
                e.tensor_scalar(out=buf[:], in0=buf[:], scalar1=7.5, scalar2=7.5,
                                op0=ALU.mult, op1=ALU.add)
                e.drain()
                e.tensor_scalar(out=buf[:], in0=buf[:], scalar1=0.0, scalar2=15.0,
                                op0=ALU.max, op1=ALU.min)
                return e.drain()
            t_fx = p.op("vector", lambda e: mkf(e, sfx), waits=[t_zr], inc=sv_, dma=True)
            t_fy = p.op("vector", lambda e: mkf(e, sfy), waits=[t_zr2], inc=sv_, dma=True)
            def hats(e, src, dst, wmul):
                for j in range(16):
                    e.tensor_copy(out=dst[:, j * 16:(j + 1) * 16], in_=src[:])
                e.drain()
                e.tensor_tensor(out=dst[:], in0=dst[:], in1=sioy[:], op=ALU.subtract)
                e.drain()
                e.tensor_scalar(out=stmp[:], in0=dst[:], scalar1=1.0, scalar2=None, op0=ALU.add)
                e.drain()
                e.tensor_scalar(out=dst[:], in0=dst[:], scalar1=-1.0, scalar2=1.0,
                                op0=ALU.mult, op1=ALU.add)
                e.drain()
                e.tensor_tensor(out=dst[:], in0=dst[:], in1=stmp[:], op=ALU.min)
                e.drain()
                e.tensor_scalar(out=dst[:], in0=dst[:], scalar1=0.0, scalar2=None, op0=ALU.max)
                e.drain()
                if wmul:
                    for j in range(16):
                        e.tensor_tensor(out=dst[:, j * 16:(j + 1) * 16],
                                        in0=dst[:, j * 16:(j + 1) * 16], in1=sw_[:], op=ALU.mult)
                    e.drain()
                return e.drain()
            t_hx = p.op("vector", lambda e: hats(e, sfx, shx, False), waits=[t_fx], inc=sv_, dma=True)
            t_hy = p.op("vector", lambda e: hats(e, sfy, shy, True), waits=[t_fy, t_wn], inc=sv_, dma=True)
            def abuild(e):
                for x in range(16):
                    e.tensor_copy(out=sYe[:].rearrange("p (y x k) -> p y x k", y=16, x=16)[:, :, x, :],
                                  in_=shy[:].rearrange("p (y k) -> p y k", y=16))
                for y in range(16):
                    e.tensor_copy(out=sXe[:, y * 256:(y + 1) * 256], in_=shx[:])
                e.drain()
                e.tensor_tensor(out=sYe[:], in0=sYe[:], in1=sXe[:], op=ALU.mult)
                e.drain()
                e.tensor_reduce(out=sA[:], in_=sYe[:].rearrange("p (c k) -> p c k", k=16),
                                axis=AXL.X, op=ALU.add)
                e.drain()
                e.tensor_copy(out=sAb[:], in_=sA[:])
                return e.drain()
            t_ab = p.op("vector", abuild, waits=[t_hx, t_hy], inc=sv_, dma=True)
            ta_ev = []
            for half in range(2):
                t_at = p.op("tensor", lambda e, half=half: e.transpose(
                    out=p_t2[half][:], in_=sAb[:, half * 128:(half + 1) * 128],
                    identity=sident_b[:]), waits=[t_ab, ta_ev[-1] if ta_ev else None], inc=st_)
                ta_ev.append(p.op("vector", lambda e, half=half: e.tensor_copy(
                    out=sAf[:, half * 128:(half + 1) * 128], in_=p_t2[half][:]),
                    waits=[t_at], inc=sv_))
                tb_prev[half] = ta_ev[-1]
            o_ev = []
            for mt in range(8):
                t_my = p.op("tensor", lambda e, mt=mt: e.matmul(
                    out=p_m[:, 0:128], lhsT=sPW[:, 0 * D + mt * 128: 0 * D + (mt + 1) * 128],
                    rhs=sAf[:, 0:128], start=True, stop=False),
                    waits=[ta_ev[-1]] + ([o_ev[-1]] if o_ev else [out_prev]), inc=None)
                t_my = p.op("tensor", lambda e, mt=mt: e.matmul(
                    out=p_m[:, 0:128], lhsT=sPW[:, 1 * D + mt * 128: 1 * D + (mt + 1) * 128],
                    rhs=sAf[:, 128:256], start=False, stop=True), inc=st_)
                o_ev.append(p.op("vector", lambda e, mt=mt, jb=jb: e.tensor_tensor(
                    out=souf[:, mt * 128:(mt + 1) * 128], in0=p_m[:, 0:128],
                    in1=sx3b[jb][:, mt * 128:(mt + 1) * 128], op=ALU.add),
                    waits=[t_my, t_x3], inc=sv_))
            out_prev = p.op("gpsimd", lambda e, rb=rb: e.dma_start(
                out=o_yf.rearrange("(k p) t -> p k t", p=128)[:, :, rb * 128:(rb + 1) * 128],
                in_=souf[:].rearrange("p (k c) -> p k c", k=8)),
                waits=[o_ev[-1]], inc=sgo, amt=16, dma=True)
            blk_prev[jb] = o_ev[-1]
            sc_prev = t_wn
            gx_prev = (ldgx, p.cnt["ldgx"])
            prev_mlp_guard = mlp_guard
        p.fence("gpsimd", [(sgo, p.cnt["sgo"])])
        p.barrier([sv_, ss, st_])
        p.emit(block)
    return nc



_IDENT = np.eye(128, dtype=np.float32)


def _prep_A(inputs):
    x = inputs["L0_attn_x"]
    wqkv = np.ascontiguousarray(inputs["L0_attn_w_qkv"])
    bqkv = np.ascontiguousarray(inputs["L0_attn_b_qkv"].reshape(24, 128).T)
    pt = inputs["L0_attn_proj_matrix"].transpose(0, 2, 1)
    proj = np.zeros((128, 8 * 2 * MF), dtype=np.float32)
    for hp in range(8):
        proj[0:64, hp * 512: hp * 512 + 256] = pt[2 * hp]
        proj[64:128, hp * 512 + 256: hp * 512 + 512] = pt[2 * hp + 1]
    proj = np.ascontiguousarray(proj)
    maps = []
    for c in range(NCORE):
        b = c // 4
        xa = np.ascontiguousarray(x[b][tok_index(c)])
        maps.append({"xa": xa, "wqkv": wqkv, "bqkv": bqkv, "proj": proj, "ident": _IDENT})
    return maps


def run_A(inputs, **kw):
    nc = build_A(**kw)
    maps = _prep_A(inputs)
    res = run_bass_kernel_spmd(nc, maps, core_ids=list(range(NCORE)))
    return res.results




def _performer_ref_np(xn, w_qkv, b_qkv, w_out, b_out, proj):
    qkv = xn @ w_qkv + b_qkv
    q, k, v = np.split(qkv, 3, axis=-1)
    DATA_NORM_ = 64 ** -0.25
    out = np.zeros_like(xn)
    for h in range(16):
        qh = q[:, h * 64:(h + 1) * 64] * DATA_NORM_
        kh = k[:, h * 64:(h + 1) * 64] * DATA_NORM_
        vh = v[:, h * 64:(h + 1) * 64]

        def fm(z):
            xp = z @ proj[h].T
            xp = xp - xp.max(-1, keepdims=True)
            return (np.exp(xp) + 1e-6) / 16.0
        qp = fm(qh)
        kp = fm(kh)
        kv = kp.T @ vh
        ks = kp.sum(0)
        num = qp @ kv
        den = qp @ ks + 1e-6
        out[:, h * 64:(h + 1) * 64] = num / den[:, None]
    return out @ w_out + b_out


def _prep_B(inputs, resA):
    x = inputs["L0_attn_x"]
    wffnT = np.ascontiguousarray(inputs["L0_ffn_weight"].T)
    bffn = np.ascontiguousarray(inputs["L0_ffn_bias"][None, :])
    wout = np.ascontiguousarray(inputs["L0_attn_w_out"])
    bout = np.ascontiguousarray(inputs["L0_attn_b_out"][None, :])
    wip = np.ascontiguousarray(np.concatenate([inputs["L1_attn_Wi"], inputs["L1_attn_Wp"]], axis=1))
    gwT = np.ascontiguousarray(inputs["L1_ffn_gate_w"].T)
    wexp = np.ascontiguousarray(inputs["L1_ffn_expert_w"].transpose(0, 2, 1))
    amax8 = np.ascontiguousarray(np.stack([resA[c]["o_amax"][0, 0] for c in range(NCORE)])[None, :])
    iota8 = np.arange(8, dtype=np.float32)[None, :]
    import ml_dtypes
    half = 512
    inv_freq = (1.0 / 10000.0 ** (np.arange(half, dtype=np.float32) / half)).astype(np.float32)
    maps = []
    for c in range(NCORE):
        b = c // 4
        tok = tok_index(c)
        ang = (tok[:, None].astype(np.float32) * inv_freq[None, :]).astype(np.float32)
        cosb = np.cos(ang).astype(ml_dtypes.bfloat16)
        sinb = np.sin(ang).astype(ml_dtypes.bfloat16)
        kvstack = np.stack([resA[4 * b + j]["o_kv"] for j in range(4)])
        kv4 = np.ascontiguousarray(kvstack.transpose(3, 1, 2, 0, 4).reshape(128, -1))
        maps.append({
            "xa": np.ascontiguousarray(x[b][tok]),
            "xn1T": np.ascontiguousarray(resA[c]["o_xn1"].T),
            "qpT": np.ascontiguousarray(resA[c]["o_qp"].transpose(0, 2, 1)),
            "kv4": kv4, "amax8": amax8, "wffnT": wffnT, "bffn": bffn,
            "wout": wout, "bout": bout, "wip": wip, "gwT": gwT, "wexp": wexp,
            "cosb": cosb, "sinb": sinb, "ident": _IDENT, "iota8": iota8,
        })
    return maps


def run_B(inputs, resA):
    nc = build_B()
    maps = _prep_B(inputs, resA)
    res = run_bass_kernel_spmd(nc, maps, core_ids=list(range(NCORE)))
    return res.results




def _prep_C(inputs, resB):
    import ml_dtypes
    Pfull = [np.zeros((T, D), ml_dtypes.bfloat16) for _ in range(B)]
    Phfull = [np.zeros((T, D), ml_dtypes.bfloat16) for _ in range(B)]
    for c in range(NCORE):
        b = c // 4
        Pfull[b][tok_index(c)] = resB[c]["o_P"]
        Phfull[b][tok_index(c)] = resB[c]["o_Ph"]
    PTb = [np.ascontiguousarray(Pfull[b].T) for b in range(B)]
    w1p = np.zeros((32, 64), np.float32)
    w1p[0:17] = inputs["L1_attn_W1"]
    iotas = np.tile(np.arange(512, dtype=np.int32)[None, :], (128, 1))
    iot128 = np.arange(128, dtype=np.float32)[:, None]
    ioy = np.tile(np.repeat(np.arange(16, dtype=np.float32), 16)[None, :], (128, 1)).astype(ml_dtypes.bfloat16)
    kmG0 = np.ones((32, 256), np.float32)
    for t in range(14):
        for k in range(16):
            for j in range(15):
                kmG0[j, 16 * t + k] = float((k <= t) and (j <= t))
            kmG0[15, 16 * t + k] = float(k <= t)
            kmG0[16, 16 * t + k] = float(k <= t)
    kmM0 = np.zeros((128, 16), np.float32)
    for t in range(14):
        for k in range(16):
            if k > t:
                kmM0[t, k] = NEG
    ones_kmG = np.ones((32, 256), np.float32)
    zeros_kmM = np.zeros((128, 16), np.float32)
    maps = []
    for c in range(NCORE):
        b = c // 4
        m = c % 4
        blocks = tok_blocks(c)
        sp = np.arange(512)[None, :]
        pp = np.arange(128)[:, None]
        cm = np.where(sp > 128 * m + pp, NEG, 0.0).astype(np.float32)
        rowg_np = np.stack([gb * 128 + np.arange(128, dtype=np.float32) for gb in blocks], 1)
        maps.append({
            "PT": PTb[b], "Phat": np.ascontiguousarray(Phfull[b]),
            "If_": np.ascontiguousarray(resB[c]["o_I"].T),
            "Ihf": np.ascontiguousarray(resB[c]["o_Ih"].T),
            "x3pT": np.ascontiguousarray(resB[c]["o_x3p"].T),
            "w1p": w1p, "b1": np.ascontiguousarray(inputs["L1_attn_b1"][:, None]),
            "w2": np.ascontiguousarray(inputs["L1_attn_W2"]),
            "b2": np.ascontiguousarray(inputs["L1_attn_b2"][:, None]),
            "wc": np.ascontiguousarray(inputs["L1_attn_Wc"]),
            "bc": np.ascontiguousarray(inputs["L1_attn_bc"][:, None]),
            "wm": np.ascontiguousarray(inputs["L1_attn_Wm"]),
            "bm": np.ascontiguousarray(inputs["L1_attn_bm"][:, None]),
            "pal": np.ascontiguousarray(inputs["L1_attn_palette"].reshape(D, 256)),
            "wo": np.ascontiguousarray(inputs["L1_attn_Wo"][0]),
            "ident": _IDENT, "cmask": cm, "rowg": np.ascontiguousarray(rowg_np),
            "iotas": iotas, "iot128": iot128,
            "ioy": np.ascontiguousarray(ioy),
            "kmG": np.ascontiguousarray((kmG0 if m == 0 else ones_kmG).astype(ml_dtypes.bfloat16)),
            "kmM": (kmM0 if m == 0 else zeros_kmM),
        })
    return maps


def run_C(inputs, resB):
    maps = _prep_C(inputs, resB)
    nc = build_C()
    res = run_bass_kernel_spmd(nc, maps, core_ids=list(range(NCORE)))
    return res.results


def kernel(**inputs):
    inputs = {k: np.asarray(v) for k, v in inputs.items()}
    resA = run_A(inputs)
    resB = run_B(inputs, resA)
    resC = run_C(inputs, resB)
    out = np.zeros((B, T, D), dtype=np.float32)
    for c in range(NCORE):
        b = c // 4
        out[b][tok_index(c)] = resC[c]["o_yf"].T
    return out
